# version 31
# baseline (speedup 1.0000x reference)
import numpy as np
import ml_dtypes
import concourse.bass as bass
import concourse.mybir as mybir
from concourse.alu_op_type import AluOpType
from concourse.bass_utils import run_bass_kernel_spmd

F32 = mybir.dt.float32
BF16 = mybir.dt.bfloat16
I32 = mybir.dt.int32
AF = mybir.ActivationFunctionType
AX = mybir.AxisListType

D = 1024
KC = D // 128
H = 16
DKH = 64
LOWR = 20
LEN = 200
IFF = 2816
ICH = IFF // 128
NEXP = 8
WARM = 128
BF16NP = ml_dtypes.bfloat16
ENGINES = ("sy", "v", "a", "g", "t")


def _split(total, step):
    out, o = [], 0
    while o < total:
        n = min(step, total - o)
        out.append((o, n))
        o += n
    return out



class Op:
    __slots__ = ("eng", "fn", "idx", "pos", "is_dma", "chan", "chan_val",
                 "need_inc", "inc_seq", "waits", "drain")

    def __init__(self, eng, fn, is_dma, chan):
        self.eng = eng
        self.fn = fn
        self.is_dma = is_dma
        self.chan = chan
        self.chan_val = 0
        self.need_inc = False
        self.inc_seq = 0
        self.waits = []
        self.drain = False


class MS:
    def __init__(self, nc):
        self.nc = nc
        self.ops = []
        self.by_eng = {e: [] for e in ENGINES}
        self.res = {}
        self.chan_tot = {}
        self.sems = {}
        self.barrier_fns = {}

    def _edge(self, p, c, raw=True):
        if p is c:
            return
        if p.is_dma:
            if c.is_dma and c.chan == p.chan:
                return
            c.waits.append(("ch", p.chan, p.chan_val))
        elif p.eng == c.eng:
            if (raw and p.eng in ("v", "a", "g")) or (not raw and p.eng in ("v", "a")):
                p.need_inc = True
                c.waits.append(("op", p))
        else:
            p.need_inc = True
            c.waits.append(("op", p))

    def op(self, eng, fn, reads=(), writes=(), dma_chan=None):
        o = Op(eng, fn, dma_chan is not None, dma_chan)
        if o.is_dma:
            self.chan_tot[o.chan] = self.chan_tot.get(o.chan, 0) + 16
            o.chan_val = self.chan_tot[o.chan]
        for r in reads:
            s = self.res.get(r)
            if s and s["w"] is not None:
                self._edge(s["w"], o)
        for r in writes:
            s = self.res.get(r)
            if s:
                if s["w"] is not None:
                    self._edge(s["w"], o, raw=False)
                for rd in s["r"]:
                    self._edge(rd, o, raw=False)

            self.res[r] = {"w": o, "r": []}
        for r in reads:
            self.res.setdefault(r, {"w": None, "r": []})["r"].append(o)
        o.idx = len(self.ops)
        self.ops.append(o)
        o.pos = len(self.by_eng[o.eng])
        self.by_eng[o.eng].append(o)
        return o

    def phase(self):
        lasts = []
        for e in ENGINES:
            for o in reversed(self.by_eng[e]):
                if not o.is_dma:
                    o.need_inc = True
                    lasts.append(o)
                    break
        chans = dict(self.chan_tot)
        for e in ENGINES:
            bf = self.barrier_fns.get(e)
            o = Op(e, bf() if bf else (lambda en: en.nop()), False, None)
            for p in lasts:
                if p.eng != e:
                    o.waits.append(("op", p))
            for ch, v in chans.items():
                o.waits.append(("ch", ch, v))
            o.idx = len(self.ops)
            self.ops.append(o)
            o.pos = len(self.by_eng[e])
            self.by_eng[e].append(o)
        self.res = {}

    def finalize(self, block):
        inc_cnt = {e: 0 for e in ENGINES}
        for o in self.ops:
            if o.need_inc and not o.is_dma:
                inc_cnt[o.eng] += 1
                o.inc_seq = inc_cnt[o.eng]
        for e in ENGINES:
            if inc_cnt[e]:
                self.sems["eng:" + e] = self.nc.semaphore("sm_" + e).__enter__()
        for ch in self.chan_tot:
            self.sems["ch:" + ch] = self.nc.semaphore("sc_" + ch).__enter__()
        ms = self

        def mk(eng_name):
            def inner(e):
                waited = {}
                for o in ms.by_eng[eng_name]:
                    wl = {}
                    for w in o.waits:
                        if w[0] == "ch":
                            key, val = "ch:" + w[1], w[2]
                        else:
                            key, val = "eng:" + w[1].eng, w[1].inc_seq
                        wl[key] = max(wl.get(key, 0), val)
                    for key, val in sorted(wl.items()):
                        if waited.get(key, -1) < val:
                            e.wait_ge(ms.sems[key], val)
                            waited[key] = val
                    if o.drain:
                        e.drain()
                    inst = o.fn(e)
                    if o.is_dma:
                        inst.then_inc(ms.sems["ch:" + o.chan], 16)
                    elif o.need_inc:
                        if inst is None:
                            e.sem_inc(ms.sems["eng:" + o.eng], 1)
                        else:
                            inst.then_inc(ms.sems["eng:" + o.eng], 1)
            return inner

        for name, deco in [("sy", block.sync), ("v", block.vector),
                           ("a", block.scalar), ("g", block.gpsimd),
                           ("t", block.tensor)]:
            if self.by_eng[name]:
                deco(mk(name))


class P:
    pass



def build_program(T_MAIN, debug=False):
    TP = WARM + T_MAIN
    NT = TP // 128
    NTM = T_MAIN // 128
    GRA = _split(TP, 512)
    GRP = _split(TP, 256)
    GRPC = _split(TP, 512)
    GRPM = _split(T_MAIN, 512)

    nc = bass.Bass("TRN2", target_bir_lowering=False)
    ms = MS(nc)

    apin = {}

    def din(name, shape, dtype=F32):
        apin[name] = nc.dram_tensor(name, list(shape), dtype, kind="ExternalInput").ap()

    din("xin", (TP, D)); din("xg", (LEN, D)); din("mwarm", (128, 1))
    din("Wq", (D, D)); din("Wk", (D, D)); din("Wo0", (D, D))
    din("bq", (1, D)); din("bk", (1, D)); din("bo", (1, D))
    din("WeL", (H, LEN, LOWR)); din("WrL", (H, LEN, LOWR))
    din("Wc", (2 * LOWR, DKH)); din("bc", (128, 1))
    din("wg_nat", (IFF, D)); din("wu_nat", (IFF, D)); din("wd_nat", (D, IFF))
    din("Wi", (D, D)); din("Wf", (D, D)); din("Wgr", (D, D)); din("Wo1", (D, D))
    din("conv_i", (128, KC * 4)); din("conv_f", (128, KC * 4))
    din("gnw", (128, KC))
    din("WeE", (NEXP * D, D)); din("beE", (NEXP, D))
    din("WgE", (D, NEXP)); din("bgE", (1, NEXP))

    yout = nc.dram_tensor("y", [T_MAIN, D], F32, kind="ExternalOutput").ap()
    s_scr = nc.dram_tensor("s_scr", [TP, D], F32, kind="Internal").ap()
    x1_scr = nc.dram_tensor("x1_scr", [TP, D], F32, kind="Internal").ap()
    s1_scr = nc.dram_tensor("s1_scr", [T_MAIN, D], F32, kind="Internal").ap()
    wgu_scr = nc.dram_tensor("wgu_scr", [2 * ICH * 128, KC * 128], BF16, kind="Internal").ap()
    wd_scr = nc.dram_tensor("wd_scr", [KC * 128, ICH * 128], BF16, kind="Internal").ap()
    dbg = {}
    if debug:
        for nm, shp in [("dbg_s", (TP, D)), ("dbg_x1", (TP, D)),
                        ("dbg_h", (128, TP)), ("dbg_moe", (T_MAIN, D)),
                        ("dbg_alpha", (T_MAIN, NEXP))]:
            dbg[nm] = nc.dram_tensor(nm, list(shp), F32, kind="ExternalOutput").ap()

    groups = {}

    def alloc(group, name, shape, dtype=F32):
        g = nc.sbuf_tensor(name, list(shape), dtype)
        hh = g.__enter__()
        groups.setdefault(group, []).append(g)
        return hh

    def free_group(group):
        for g in reversed(groups.pop(group, [])):
            g.__exit__(None, None, None)

    pp = []
    for i in range(8):
        g = nc.psum_tensor("pp%d" % i, [128, 512], F32)
        pp.append(g.__enter__())
    PPK = [("pp", i) for i in range(8)]

    ones1 = alloc("főr", "ones1", (1, 128), BF16)
    onesf = alloc("főr", "onesf", (128, 1), F32)
    mwarm_sb = alloc("főr", "mwarm_sb", (128, 1), F32)
    bq_sb = alloc("főr", "bq_sb", (1, D), BF16)
    bk_sb = alloc("főr", "bk_sb", (1, D), BF16)
    bo_sb = alloc("főr", "bo_sb", (1, D), BF16)
    bg_sb = alloc("főr", "bg_sb", (1, NEXP), BF16)
    bc_sb = alloc("főr", "bc_sb", (128, 1), F32)
    ln_ab = alloc("főr", "ln_ab", (128, 2 * (NT + 2)), F32)
    xbuf2 = alloc("főr", "xbuf2", (128, 2 * D), F32)
    bscr = alloc("főr", "bscr", (1, 8), F32)
    _bar_i = [0]

    def _g_barrier_fn():
        i = _bar_i[0]
        _bar_i[0] += 1
        return lambda en, i=i: en.memset(bscr[0:1, i:i + 1], 0.0)

    ms.barrier_fns["g"] = _g_barrier_fn

    stg_state = P()
    lnv = P()
    lnv.bn6 = alloc("főr", "ln_bn6", (128, 12), F32)
    lnv.mv = alloc("főr", "ln_mv", (128, 2), F32)
    lnv.var = alloc("főr", "ln_var", (128, 1), F32)

    qb = P()
    QW = 1024
    qb.amax = alloc("gB", "q_amax", (128, QW // 16), F32)
    qb.v = alloc("gB", "q_v", (128, QW), F32)
    qb.f1 = alloc("gB", "q_f1", (128, QW), F32)
    qb.g = alloc("gB", "q_g", (128, QW), F32)
    qb.sg = alloc("gB", "q_sg", (128, QW), BF16)

    stg_state.a = alloc("gA", "wstW", (128, D), F32)
    stg_state.b = alloc("gA", "wstW2", (128, D), F32)
    wq_sb = alloc("gA", "wq_sb", (128, KC * D), BF16)
    wk_sb = alloc("gA", "wk_sb", (128, KC * D), BF16)
    wo_sb = alloc("gA", "wo_sb", (128, KC * D), BF16)
    ablk = alloc("gA", "ablk", (128, KC * 128), BF16)
    bblk = alloc("gA", "bblk", (128, KC * 128), BF16)

    def dma(dst_ap, src_ap, chan, reads, writes, transpose=False, eng="sy"):
        if transpose:
            ms.op(eng, lambda e, d=dst_ap, s=src_ap: e.dma_start_transpose(d, s),
                  reads=reads, writes=writes, dma_chan=chan)
        else:
            ms.op(eng, lambda e, d=dst_ap, s=src_ap: e.dma_start(out=d, in_=s),
                  reads=reads, writes=writes, dma_chan=chan)

    def vop(fn, reads, writes, eng="v"):
        ms.op(eng, fn, reads=reads, writes=writes)

    def mm(ps_ap, lhs_ap, rhs_ap, reads, writes, start, stop, tile_position=None):
        if not isinstance(ps_ap, bass.AP):
            ps_ap = ps_ap[:]
        if not isinstance(lhs_ap, bass.AP):
            lhs_ap = lhs_ap[:]
        if not isinstance(rhs_ap, bass.AP):
            rhs_ap = rhs_ap[:]
        ms.op("t", lambda e, o=ps_ap, l=lhs_ap, r=rhs_ap: e.matmul(
            o, l, r, start=start, stop=stop, tile_position=tile_position),
            reads=reads, writes=writes)

    def cast_weight(srcname, dst_sb, dstkey, nchunks=KC, width=D):
        for k in range(nchunks):
            stg, skey, ch = ((stg_state.a, ("wst",), "ch_wst") if k % 2 == 0
                             else (stg_state.b, ("wst2",), "ch_wst2"))
            dma(stg[:, 0:width], apin[srcname][128 * k:128 * (k + 1), :],
                chan=ch, reads=[skey], writes=[skey], eng="a")
            vop(lambda e, k=k, stg=stg: e.tensor_copy(
                out=dst_sb[:, width * k:width * (k + 1)], in_=stg[:, 0:width]),
                reads=[skey], writes=[(dstkey, k)])

    def emit_quant_p1(x_key, x_ap, F, P_, q, vkey, v_ap, sgkey, sg_ap):
        vop(lambda e: e.activation(out=v_ap, in_=x_ap, func=AF.Abs),
            reads=[x_key], writes=[vkey], eng="a")
        vop(lambda e: e.activation(out=sg_ap, in_=x_ap, func=AF.Sign),
            reads=[x_key], writes=[sgkey], eng="a")

    def emit_quant_p2(out_key, out_ap, F, P_, q, vkey, v_ap, sgkey, sg_ap):
        nb = F // 16
        vop(lambda e: e.tensor_reduce(out=q.amax[0:P_, 0:nb],
                                      in_=v_ap.rearrange("p (b w) -> p b w", w=16),
                                      axis=AX.X, op=AluOpType.max),
            reads=[vkey], writes=[("q.amax",)])
        vop(lambda e: e.tensor_scalar(out=q.amax[0:P_, 0:nb], in0=q.amax[0:P_, 0:nb],
                                      scalar1=1.0 / 6.0, scalar2=1e-12,
                                      op0=AluOpType.mult, op1=AluOpType.max),
            reads=[("q.amax",)], writes=[("q.amax",)])
        vop(lambda e: e.reciprocal(out=q.amax[0:P_, 0:nb], in_=q.amax[0:P_, 0:nb]),
            reads=[("q.amax",)], writes=[("q.amax",)])
        rb = q.amax[0:P_, 0:nb].unsqueeze(2).broadcast_to([P_, nb, 16])
        vop(lambda e: e.tensor_tensor(out=v_ap.rearrange("p (b w) -> p b w", w=16),
                                      in0=v_ap.rearrange("p (b w) -> p b w", w=16),
                                      in1=rb, op=AluOpType.mult),
            reads=[vkey, ("q.amax",)], writes=[vkey])
        vop(lambda e: e.tensor_scalar(out=q.g[0:P_, 0:F], in0=v_ap,
                                      scalar1=5.0, scalar2=2.0,
                                      op0=AluOpType.is_gt, op1=AluOpType.mult),
            reads=[vkey], writes=[("q.g",)])
        for thr in (2.5, 3.5):
            vop(lambda e, thr=thr: e.scalar_tensor_tensor(
                out=q.g[0:P_, 0:F], in0=v_ap, scalar=thr,
                in1=q.g[0:P_, 0:F], op0=AluOpType.is_gt, op1=AluOpType.add),
                reads=[vkey, ("q.g",)], writes=[("q.g",)])
        vop(lambda e: e.tensor_scalar(out=q.f1[0:P_, 0:F], in0=v_ap,
                                      scalar1=0.25, scalar2=None, op0=AluOpType.is_gt),
            reads=[vkey], writes=[("q.f1",)])
        for thr in (0.75, 1.25, 1.75):
            vop(lambda e, thr=thr: e.scalar_tensor_tensor(
                out=q.f1[0:P_, 0:F], in0=v_ap, scalar=thr,
                in1=q.f1[0:P_, 0:F], op0=AluOpType.is_gt, op1=AluOpType.add),
                reads=[vkey, ("q.f1",)], writes=[("q.f1",)])
        vop(lambda e: e.scalar_tensor_tensor(out=q.g[0:P_, 0:F], in0=q.f1[0:P_, 0:F],
                                             scalar=0.5, in1=q.g[0:P_, 0:F],
                                             op0=AluOpType.mult, op1=AluOpType.add),
            reads=[("q.f1",), ("q.g",)], writes=[("q.g",)])
        vop(lambda e: e.tensor_tensor(out=q.g[0:P_, 0:F], in0=q.g[0:P_, 0:F],
                                      in1=sg_ap, op=AluOpType.mult),
            reads=[("q.g",), sgkey], writes=[("q.g",)])
        vop(lambda e: e.reciprocal(out=q.amax[0:P_, 0:nb], in_=q.amax[0:P_, 0:nb]),
            reads=[("q.amax",)], writes=[("q.amax",)])
        sb2 = q.amax[0:P_, 0:nb].unsqueeze(2).broadcast_to([P_, nb, 16])
        vop(lambda e: e.tensor_tensor(out=out_ap.rearrange("p (b w) -> p b w", w=16),
                                      in0=q.g[0:P_, 0:F].rearrange("p (b w) -> p b w", w=16),
                                      in1=sb2, op=AluOpType.mult),
            reads=[("q.g",), ("q.amax",)], writes=[out_key])

    def emit_quant(x_key, x_ap, out_key, out_ap, F, P_=128, q=None):
        if q is None:
            q = qb
        vk, sk = ("q.v",), ("q.sg",)
        emit_quant_p1(x_key, x_ap, F, P_, q, vk, q.v[0:P_, 0:F], sk, q.sg[0:P_, 0:F])
        emit_quant_p2(out_key, out_ap, F, P_, q, vk, q.v[0:P_, 0:F], sk, q.sg[0:P_, 0:F])

    def emit_ln(x_key, x_ap, ab_sb, ab_key, slot, outs, P_=128,
                scr=None, scr_key=None):
        a_ap = ab_sb[0:P_, 2 * slot:2 * slot + 1]
        b_ap = ab_sb[0:P_, 2 * slot + 1:2 * slot + 2]
        vop(lambda e: e.bn_stats(out=lnv.bn6[0:P_, 0:6], in_=x_ap[:, 0:D // 2]),
            reads=[x_key], writes=[("ln.bn6",)])
        vop(lambda e: e.bn_stats(out=lnv.bn6[0:P_, 6:12], in_=x_ap[:, D // 2:D]),
            reads=[x_key], writes=[("ln.bn6",)])
        vop(lambda e: e.bn_aggr(out=lnv.mv[0:P_, :], in_=lnv.bn6[0:P_, 0:12]),
            reads=[("ln.bn6",)], writes=[("ln.mv",)])
        vop(lambda e: e.tensor_scalar(out=lnv.var[0:P_, :], in0=lnv.mv[0:P_, 1:2],
                                      scalar1=1e-5, scalar2=None, op0=AluOpType.add),
            reads=[("ln.mv",)], writes=[("ln.var",)])
        vop(lambda e: e.activation(out=lnv.var[0:P_, :], in_=lnv.var[0:P_, :],
                                   func=AF.Sqrt),
            reads=[("ln.var",)], writes=[("ln.var",)], eng="a")
        vop(lambda e: e.reciprocal(out=a_ap, in_=lnv.var[0:P_, :]),
            reads=[("ln.var",)], writes=[(ab_key, slot, "a")])
        vop(lambda e: e.scalar_tensor_tensor(out=b_ap, in0=lnv.mv[0:P_, 0:1],
                                             scalar=-1.0, in1=a_ap,
                                             op0=AluOpType.mult, op1=AluOpType.mult),
            reads=[("ln.mv",), (ab_key, slot, "a")], writes=[(ab_key, slot, "b")])
        for okey, oap in outs:
            vop(lambda e, oap=oap: e.tensor_scalar(out=oap, in_=None, in0=x_ap,
                                                   scalar1=a_ap, scalar2=b_ap,
                                                   op0=AluOpType.mult, op1=AluOpType.add)
                if False else
                e.tensor_scalar(out=oap, in0=x_ap, scalar1=a_ap, scalar2=b_ap,
                                op0=AluOpType.mult, op1=AluOpType.add),
                reads=[x_key, (ab_key, slot, "a"), (ab_key, slot, "b")],
                writes=[okey])

    dma(mwarm_sb[:], apin["mwarm"][:], chan="c_mw", reads=[], writes=[("mwarm",)])
    dma(bc_sb[:], apin["bc"][:], chan="c_bc", reads=[], writes=[("bc",)])
    vop(lambda e: e.memset(ones1[:], 1.0), reads=[], writes=[("ones1",)])
    vop(lambda e: e.memset(onesf[:], 1.0), reads=[], writes=[("onesf",)])
    vop(lambda e: e.memset(ablk[:], 0.0), reads=[], writes=[("ablk", "z")])
    vop(lambda e: e.memset(bblk[:], 0.0), reads=[], writes=[("bblk", "z")])

    for nm, dst, w in [("bq", bq_sb, D), ("bk", bk_sb, D), ("bo", bo_sb, D),
                       ("bgE", bg_sb, NEXP)]:
        dma(stg_state.a[0:1, 0:w], apin[nm][:], chan="ch_wst", reads=[("wst",)], writes=[("wst",)])
        vop(lambda e, dst=dst, w=w, stg=stg_state.a: e.tensor_copy(
            out=dst[0:1, 0:w], in_=stg[0:1, 0:w]),
            reads=[("wst",)], writes=[("b_" + nm,)])

    cast_weight("Wq", wq_sb, "wq")
    cast_weight("Wk", wk_sb, "wk")
    cast_weight("Wo0", wo_sb, "wo")

    xg_sb = alloc("gW1", "xg_sb", (128, 2 * D), F32)
    xng_sb = alloc("gW1", "xng_sb", (128, 2 * D), BF16)
    wel_sb = alloc("gW1", "wel_sb", (128, 2 * LOWR), BF16)
    wrl_sb = alloc("gW1", "wrl_sb", (128, 2 * LOWR), BF16)
    wc_sb = alloc("gW1", "wc_sb", (LOWR, 2 * DKH), BF16)
    wet_sb = alloc("gW1", "wet_sb", (LOWR, 2 * DKH), BF16)
    LEN2 = LEN - 128

    dma(xg_sb[0:128, 0:D], apin["xg"][0:128, :], chan="cxg", reads=[],
        writes=[("xg", 0), ("xg", 1)])
    dma(xg_sb[0:LEN2, D:2 * D], apin["xg"][128:LEN, :], chan="cxg", reads=[],
        writes=[("xg", 0), ("xg", 1)])
    dma(stg_state.a[0:LOWR, 0:DKH], apin["Wc"][0:LOWR, :], chan="ch_wst",
        reads=[("wst",)], writes=[("wst",)])
    dma(stg_state.a[0:LOWR, DKH:2 * DKH], apin["Wc"][LOWR:2 * LOWR, :], chan="ch_wst",
        reads=[("wst",)], writes=[("wst",)])
    vop(lambda e, stg=stg_state.a: e.tensor_copy(out=wc_sb[:], in_=stg[0:LOWR, 0:2 * DKH]),
        reads=[("wst",)], writes=[("wc",)])
    emit_ln(("xg", 0), xg_sb[0:128, 0:D], ln_ab, "lng", NT, [(("xng", 0), xng_sb[0:128, 0:D])])
    emit_ln(("xg", 1), xg_sb[0:LEN2, D:2 * D], ln_ab, "lng", NT + 1,
            [(("xng", 1), xng_sb[0:LEN2, D:2 * D])], P_=LEN2)

    ps_we = pp[0][0:LOWR, 0:DKH]
    ps_ab = pp[1]
    for h in range(H):
        j, par = h // 2, h % 2
        dma(stg_state.a[0:128, 0:LOWR], apin["WeL"][h, 0:128, :], chan="ch_wst",
            reads=[("wst",)], writes=[("wst",)])
        dma(stg_state.a[0:LEN2, LOWR:2 * LOWR], apin["WeL"][h, 128:LEN, :], chan="ch_wst",
            reads=[("wst",)], writes=[("wst",)])
        vop(lambda e, stg=stg_state.a: e.tensor_copy(out=wel_sb[:, :], in_=stg[0:128, 0:2 * LOWR]),
            reads=[("wst",)], writes=[("wel",)])
        dma(stg_state.a[0:128, 0:LOWR], apin["WrL"][h, 0:128, :], chan="ch_wst",
            reads=[("wst",)], writes=[("wst",)])
        dma(stg_state.a[0:LEN2, LOWR:2 * LOWR], apin["WrL"][h, 128:LEN, :], chan="ch_wst",
            reads=[("wst",)], writes=[("wst",)])
        vop(lambda e, stg=stg_state.a: e.tensor_copy(out=wrl_sb[:, :], in_=stg[0:128, 0:2 * LOWR]),
            reads=[("wst",)], writes=[("wrl",)])
        for src_sb, srckey, half, blk, blkkey in [
                (wel_sb, ("wel",), 0, ablk, "ablk"), (wrl_sb, ("wrl",), 1, bblk, "bblk")]:
            wcs = wc_sb[0:LOWR, 0:DKH] if half == 0 else wc_sb[0:LOWR, DKH:2 * DKH]
            mm(ps_we, src_sb[0:128, 0:LOWR], xng_sb[0:128, DKH * h:DKH * (h + 1)],
               reads=[srckey, ("xng", 0)], writes=[PPK[0]], start=True, stop=False)
            mm(ps_we, src_sb[0:LEN2, LOWR:2 * LOWR],
               xng_sb[0:LEN2, D + DKH * h:D + DKH * (h + 1)],
               reads=[srckey, ("xng", 1)], writes=[PPK[0]], start=False, stop=True)
            wt = wet_sb[:, DKH * half:DKH * (half + 1)]
            vop(lambda e, wt=wt: e.activation(out=wt, in_=ps_we, func=AF.Copy),
                reads=[PPK[0]], writes=[("wet", half)], eng="a")
            mm(ps_ab[64 * par:64 * (par + 1), 0:DKH], wt, wcs,
               reads=[("wet", half), ("wc",)], writes=[PPK[1]],
               start=True, stop=True, tile_position=(0, 64 * par))
            dst = blk[64 * par:64 * (par + 1), 128 * j + 64 * par:128 * j + 64 * (par + 1)]
            vop(lambda e, dst=dst, par=par: e.activation(
                out=dst, in_=ps_ab[64 * par:64 * (par + 1), 0:DKH], func=AF.Copy),
                reads=[PPK[1], (blkkey, "z")], writes=[(blkkey, j, par)], eng="a")

    ms.phase()
    free_group("gW1")

    qa = P()
    QA = 2048
    qa.amax = alloc("gAq", "qa_amax", (128, QA // 16), F32)
    qa.v2 = [alloc("gAq", "qa_v%d" % i, (128, QA), F32) for i in range(2)]
    qa.f1 = alloc("gAq", "qa_f1", (128, QA), F32)
    qa.g = alloc("gAq", "qa_g", (128, QA), F32)
    qa.sg2 = [alloc("gAq", "qa_sg%d" % i, (128, QA), BF16) for i in range(2)]
    wstA = [alloc("gAq", "wstA%d" % i, (128, QA), F32) for i in range(2)]
    wqqA = alloc("gAq", "wqqA", (128, QA), BF16)
    recst = [alloc("gAq", "recst%d" % i, (128, 1408), BF16) for i in range(2)]
    wq_jobs = []

    def job_width_src(job):
        if job[0] == "wgu":
            return D * len(job[3])
        return 1408

    def job_stage1(job, par):
        stg = wstA[par]
        skey = ("wstA", par)
        vk, sk = ("qa.v", par), ("qa.sg", par)
        if job[0] == "wgu":
            _, srcname, base, sls = job
            for i, sl in enumerate(sls):
                dma(stg[:, D * i:D * (i + 1)], apin[srcname][128 * sl:128 * (sl + 1), :],
                    chan="ch_wstA%d" % par, reads=[skey], writes=[skey], eng="a")
            F = D * len(sls)
        else:
            _, sl, half = job
            so, F = half * 1408, 1408
            dma(stg[:, 0:F], apin["wd_nat"][128 * sl:128 * (sl + 1), so:so + F],
                chan="ch_wstA%d" % par, reads=[skey], writes=[skey], eng="a")
        emit_quant_p1(skey, stg[:, 0:F], F, 128, qa, vk, qa.v2[par][:, 0:F],
                      sk, qa.sg2[par][:, 0:F])

    def job_stage2(job, par):
        F = job_width_src(job)
        vk, sk = ("qa.v", par), ("qa.sg", par)
        emit_quant_p2(("wqqA",), wqqA[:, 0:F], F, 128, qa,
                      vk, qa.v2[par][:, 0:F], sk, qa.sg2[par][:, 0:F])
        rp = par
        rkey = ("recst", rp)
        if job[0] == "wgu":
            _, srcname, base, sls = job
            for i, sl in enumerate(sls):
                dma(recst[rp][:, 0:D].rearrange("p (j c) -> p j c", c=128),
                    wqqA[:, D * i:D * (i + 1)].rearrange("p (j c) -> p j c", c=128),
                    chan="tx_rec%d" % rp, reads=[("wqqA",), rkey], writes=[rkey],
                    transpose=True)
                dma(wgu_scr[128 * (base + sl):128 * (base + sl + 1), :],
                    recst[rp][:, 0:D],
                    chan="st_rec%d" % rp, reads=[rkey],
                    writes=[("wgu_scr", base + sl)], eng="a")
        else:
            _, sl, half = job
            so = half * 1408
            dma(recst[rp][:, 0:F].rearrange("p (j c) -> p j c", c=128),
                wqqA[:, 0:F].rearrange("p (j c) -> p j c", c=128),
                chan="tx_rec%d" % rp, reads=[("wqqA",), rkey], writes=[rkey],
                transpose=True)
            dma(wd_scr[128 * sl:128 * (sl + 1), so:so + F], recst[rp][:, 0:F],
                chan="st_rec%d" % rp, reads=[rkey], writes=[("wd_scr", sl, half)],
                eng="a")

    _g3 = [list(range(i, min(i + 2, ICH))) for i in range(0, ICH, 2)]
    for gi3, sls in enumerate(_g3):
        wq_jobs.append(("wgu", "wg_nat", 0, sls))
    for gi3, sls in enumerate(_g3):
        wq_jobs.append(("wgu", "wu_nat", ICH, sls))
    for sl in range(KC):
        for half in range(2):
            wq_jobs.append(("wd", sl, half))
    wq_jobs.reverse()
    _jobstate = {"n": 0, "pending": []}

    def quant_weight_slice():
        if wq_jobs:
            job = wq_jobs.pop()
            par = _jobstate["n"] & 1
            _jobstate["n"] += 1
            job_stage1(job, par)
            _jobstate["pending"].append((job, par))
        if len(_jobstate["pending"]) > 1 or (not wq_jobs and _jobstate["pending"]):
            job, par = _jobstate["pending"].pop(0)
            job_stage2(job, par)

    def drain_jobs():
        while wq_jobs or _jobstate["pending"]:
            quant_weight_slice()

    GA = 512
    xnT = alloc("gA2", "xnT", (128, KC * GA), BF16)
    qnT = alloc("gA2", "qnT", (128, KC * GA), BF16)
    knT = alloc("gA2", "knT", (128, KC * GA), BF16)
    mT = alloc("gA2", "mT", (128, KC * GA), BF16)
    qss = alloc("gA2", "qss", (128, H), F32)
    qrv = alloc("gA2", "qrv", (128, H), F32)
    qn_bf = alloc("gA2", "qn_bf", (128, D), BF16)
    kn_bf = alloc("gA2", "kn_bf", (128, D), BF16)
    xnbf = alloc("gA2", "xnbf", (128, D), BF16)
    qf = alloc("gA2", "qf", (128, D), F32)
    kf = qf
    s_t = qf

    def l2norm(src, srckey, dst, dstkey):
        vop(lambda e: e.tensor_tensor(out=qb.v[:, 0:D], in0=src[:], in1=src[:],
                                      op=AluOpType.mult),
            reads=[srckey], writes=[("q.v",)])
        vop(lambda e: e.tensor_reduce(out=qss[:], in_=qb.v[:, 0:D].rearrange(
            "p (h d) -> p h d", d=DKH), axis=AX.X, op=AluOpType.add),
            reads=[("q.v",)], writes=[("qss",)])
        vop(lambda e: e.activation(out=qrv[:], in_=qss[:], func=AF.Sqrt),
            reads=[("qss",)], writes=[("qrv",)], eng="a")
        vop(lambda e: e.tensor_scalar(out=qrv[:], in0=qrv[:], scalar1=1e-12,
                                      scalar2=None, op0=AluOpType.max),
            reads=[("qrv",)], writes=[("qrv",)])
        vop(lambda e: e.reciprocal(out=qrv[:], in_=qrv[:]),
            reads=[("qrv",)], writes=[("qrv",)])
        rb = qrv[:, :].unsqueeze(2).broadcast_to([128, H, DKH])
        vop(lambda e: e.tensor_tensor(out=dst.rearrange("p (h d) -> p h d", d=DKH),
                                      in0=src.rearrange("p (h d) -> p h d", d=DKH),
                                      in1=rb, op=AluOpType.mult),
            reads=[srckey, ("qrv",)], writes=[dstkey])

    for go, gn in GRA:
        ntile = gn // 128
        for ti in range(ntile):
            for _ in range(2):
                if wq_jobs:
                    quant_weight_slice()
            t0 = go + ti * 128
            tidx = t0 // 128
            par = tidx % 2
            xb = xbuf2[:, D * par:D * (par + 1)]
            xbk = ("xbuf", par)
            dma(xb, apin["xin"][t0:t0 + 128, :], chan="ch_x%d" % par,
                reads=[xbk], writes=[xbk])
            emit_ln(xbk, xb, ln_ab, "ln0", tidx, [(("xnbf",), xnbf[:])])
            wall = [("xnT", j2, ti) for j2 in range(KC)]
            dma(xnT[:, :].rearrange("p (j c) -> p j c", c=GA)[:, :, 128 * ti:128 * (ti + 1)],
                xnbf[:, :].rearrange("p (j c) -> p j c", c=128),
                chan="tx_xnT%d" % ti, reads=[("xnbf",)], writes=wall, transpose=True)
            for ww, wwkey, ppi, bap, bkey in [
                    (wq_sb, "wq", (0, 1), bq_sb, ("b_bq",)),
                    (wk_sb, "wk", (2, 3), bk_sb, ("b_bk",))]:
                for n in range(2):
                    pidx = ppi[n]
                    mm(pp[pidx], ones1[:], bap[0:1, 512 * n:512 * (n + 1)],
                       reads=[("ones1",), bkey], writes=[PPK[pidx]], start=True, stop=False)
                    for k in range(KC):
                        mm(pp[pidx], xnT[:, GA * k + 128 * ti:GA * k + 128 * (ti + 1)],
                           ww[:, D * k + 512 * n:D * k + 512 * (n + 1)],
                           reads=[("xnT", k, ti), (wwkey, k)], writes=[PPK[pidx]],
                           start=False, stop=(k == KC - 1))
            for n in range(2):
                vop(lambda e, n=n: e.activation(out=qf[:, 512 * n:512 * (n + 1)],
                                                in_=pp[n][:], func=AF.Copy),
                    reads=[PPK[n], ("qf",)], writes=[("qf",)], eng="a")
            l2norm(qf, ("qf",), qn_bf[:, :], ("qn",))
            for n in range(2):
                vop(lambda e, n=n: e.activation(out=kf[:, 512 * n:512 * (n + 1)],
                                                in_=pp[2 + n][:], func=AF.Copy),
                    reads=[PPK[2 + n], ("qf",)], writes=[("qf",)], eng="a")
            l2norm(kf, ("qf",), kn_bf[:, :], ("kn",))
            wq_all = [("qnT", j2, ti) for j2 in range(KC)]
            wk_all = [("knT", j2, ti) for j2 in range(KC)]
            dma(qnT[:, :].rearrange("p (j c) -> p j c", c=GA)[:, :, 128 * ti:128 * (ti + 1)],
                qn_bf[:, :].rearrange("p (j c) -> p j c", c=128),
                chan="tx_qnT%d" % ti, reads=[("qn",)], writes=wq_all, transpose=True)
            dma(knT[:, :].rearrange("p (j c) -> p j c", c=GA)[:, :, 128 * ti:128 * (ti + 1)],
                kn_bf[:, :].rearrange("p (j c) -> p j c", c=128),
                chan="tx_knT%d" % ti, reads=[("kn",)], writes=wk_all, transpose=True)
        for jj in range(KC):
            rq = [("qnT", jj, ti) for ti in range(ntile)]
            rk = [("knT", jj, ti) for ti in range(ntile)]
            mm(pp[4][:, 0:gn], ablk[:, 128 * jj:128 * (jj + 1)], qnT[:, GA * jj:GA * jj + gn],
               reads=rq + [("ablk", jj, 0), ("ablk", jj, 1)], writes=[PPK[4]],
               start=True, stop=False)
            mm(pp[4][:, 0:gn], bblk[:, 128 * jj:128 * (jj + 1)], knT[:, GA * jj:GA * jj + gn],
               reads=rk + [("bblk", jj, 0), ("bblk", jj, 1)], writes=[PPK[4]],
               start=False, stop=True)
            vop(lambda e, jj=jj, gn=gn: e.tensor_scalar(
                out=mT[:, GA * jj:GA * jj + gn], in0=pp[4][:, 0:gn],
                scalar1=bc_sb[:, 0:1], scalar2=None, op0=AluOpType.add),
                reads=[PPK[4], ("bc",)], writes=[("mT", jj)])
        for ti in range(ntile):
            t0 = go + ti * 128
            for n in range(2):
                pidx = 5 + n
                mm(pp[pidx], ones1[:], bo_sb[0:1, 512 * n:512 * (n + 1)],
                   reads=[("ones1",), ("b_bo",)], writes=[PPK[pidx]], start=True, stop=False)
                for k in range(KC):
                    mm(pp[pidx], mT[:, GA * k + 128 * ti:GA * k + 128 * (ti + 1)],
                       wo_sb[:, D * k + 512 * n:D * k + 512 * (n + 1)],
                       reads=[("mT", k), ("wo", k)], writes=[PPK[pidx]],
                       start=False, stop=(k == KC - 1))
            for n in range(2):
                pidx = 5 + n
                vop(lambda e, n=n, pidx=pidx: e.activation(
                    out=s_t[:, 512 * n:512 * (n + 1)], in_=pp[pidx][:], func=AF.Copy),
                    reads=[PPK[pidx], ("qf",)], writes=[("qf",)], eng="a")
            dma(s_scr[t0:t0 + 128, :], s_t[:], chan="st_s", reads=[("qf",)],
                writes=[("s_scr", t0 // 128)])
            if debug:
                dma(dbg["dbg_s"][t0:t0 + 128, :], s_t[:], chan="dbgc",
                    reads=[("qf",)], writes=[])

    drain_jobs()
    ms.phase()
    free_group("gA2")
    free_group("gAq")
    free_group("gA")

    wgT = alloc("gB2", "wgT", (128, KC * IFF), BF16)
    wuT = alloc("gB2", "wuT", (128, KC * IFF), BF16)
    wdT = alloc("gB2", "wdT", (128, ICH * D), BF16)
    for wi_, (dst_sb, dkey, base) in enumerate([(wgT, "wgT", 0), (wuT, "wuT", ICH)]):
        for sl in range(ICH):
            wall = [(dkey, j2) for j2 in range(KC)]
            dma(dst_sb[:, :].rearrange("p (j c) -> p j c", c=IFF)[:, :, 128 * sl:128 * (sl + 1)],
                wgu_scr[128 * (base + sl):128 * (base + sl + 1), :]
                .rearrange("p (j c) -> p j c", c=128),
                chan="ld_" + dkey, reads=[("wgu_scr", base + sl)], writes=wall)
    for sl in range(KC):
        wall = [("wdT", k2) for k2 in range(ICH)]
        dma(wdT[:, :].rearrange("p (j c) -> p j c", c=D)[:, :, 128 * sl:128 * (sl + 1)],
            wd_scr[128 * sl:128 * (sl + 1), :].rearrange("p (j c) -> p j c", c=128),
            chan="ld_wdT", reads=[("wd_scr", sl, 0), ("wd_scr", sl, 1)], writes=wall)
    xqbf = alloc("gB2", "xqbf", (128, D), BF16)
    tmpf = alloc("gB2", "tmpf", (128, D), F32)
    sldb = alloc("gB2", "sldb", (128, D), F32)
    xn32 = tmpf
    sload = sldb
    xqT = alloc("gB2", "xqT", (128, KC * 256), BF16)
    hqT = alloc("gB2", "hqT", (128, ICH * 256), BF16)
    h_sl = alloc("gB2", "h_sl", (128, 1024), F32)
    hq_sl = xqbf
    g_t = alloc("gB2", "g_t", (128, D), F32)
    x1_t = alloc("gB2", "x1_t", (128, D), F32)

    NSL = _split(IFF, 512)
    hq_bf = xqbf
    xb4ext = alloc("gB2", "xb4ext", (128, 2 * D), F32)

    def b_xb(tidx):
        p4 = tidx % 4
        base = xbuf2 if p4 < 2 else xb4ext
        return base[:, D * (p4 % 2):D * (p4 % 2 + 1)], ("xbuf", p4), p4

    def b_xload(tidx):
        t0 = tidx * 128
        xb, xbk, p4 = b_xb(tidx)
        dma(xb, apin["xin"][t0:t0 + 128, :], chan="ch_x%d" % p4,
            reads=[xbk], writes=[xbk])

    def b_front(tidx):
        t0 = tidx * 128
        par = tidx % 2
        xb, xbk, p4 = b_xb(tidx)
        a_ap = ln_ab[:, 2 * tidx:2 * tidx + 1]
        b_ap = ln_ab[:, 2 * tidx + 1:2 * tidx + 2]
        vop(lambda e, a_ap=a_ap, b_ap=b_ap, xb=xb: e.tensor_scalar(
            out=xn32[:], in0=xb, scalar1=a_ap, scalar2=b_ap,
            op0=AluOpType.mult, op1=AluOpType.add),
            reads=[xbk], writes=[("tmpf",)])
        emit_quant(("tmpf",), xn32[:], ("xq",), xqbf[:], D)
        wall = [("xqT", j2, par) for j2 in range(KC)]
        dma(xqT[:, :].rearrange("p (j c) -> p j c", c=256)[:, :, 128 * par:128 * (par + 1)],
            xqbf[:, :].rearrange("p (j c) -> p j c", c=128),
            chan="tx_xqT%d" % par, reads=[("xq",)], writes=wall, transpose=True)
        if tidx + 1 < TP // 128:
            b_xload(tidx + 1)

    def b_back(tidx):
        t0 = tidx * 128
        par = tidx % 2
        xb, xbk, p4 = b_xb(tidx)
        for sb_i in range(3):
            sls = NSL[2 * sb_i:2 * sb_i + 2]
            sb0 = sls[0][0]
            sbw = sum(sn for _, sn in sls)
            for k in range(KC):
                lhs = xqT[:, 256 * k + 128 * par:256 * k + 128 * (par + 1)]
                for si, (so, sn) in enumerate(sls):
                    mm(pp[si][:, 0:sn], lhs, wgT[:, IFF * k + so:IFF * k + so + sn],
                       reads=[("xqT", k, par), ("wgT", k)], writes=[PPK[si]],
                       start=(k == 0), stop=(k == KC - 1))
                    mm(pp[2 + si][:, 0:sn], lhs, wuT[:, IFF * k + so:IFF * k + so + sn],
                       reads=[("xqT", k, par), ("wuT", k)], writes=[PPK[2 + si]],
                       start=(k == 0), stop=(k == KC - 1))
            for si, (so, sn) in enumerate(sls):
                vop(lambda e, si=si, so=so, sn=sn, sb0=sb0: e.activation(
                    out=h_sl[:, so - sb0:so - sb0 + sn], in_=pp[si][:, 0:sn],
                    func=AF.Silu),
                    reads=[PPK[si]], writes=[("h_sl",)], eng="a")
                vop(lambda e, si=si, so=so, sn=sn, sb0=sb0: e.tensor_tensor(
                    out=h_sl[:, so - sb0:so - sb0 + sn],
                    in0=h_sl[:, so - sb0:so - sb0 + sn],
                    in1=pp[2 + si][:, 0:sn], op=AluOpType.mult),
                    reads=[("h_sl",), PPK[2 + si]], writes=[("h_sl",)])
            emit_quant(("h_sl",), h_sl[:, 0:sbw], ("xq",), hq_bf[:, 0:sbw], sbw)
            wall = [("hqT", k2, par) for k2 in range(ICH)]
            nblk = sbw // 128
            dma(hqT[:, :].rearrange("p (j c) -> p j c", c=256)[
                :, sb0 // 128:sb0 // 128 + nblk, 128 * par:128 * (par + 1)],
                hq_bf[:, 0:sbw].rearrange("p (j c) -> p j c", c=128),
                chan="tx_hqT%d" % par, reads=[("xq",)], writes=wall, transpose=True)
        for n in range(2):
            pidx = 6 + n
            for k in range(ICH):
                mm(pp[pidx], hqT[:, 256 * k + 128 * par:256 * k + 128 * (par + 1)],
                   wdT[:, D * k + 512 * n:D * k + 512 * (n + 1)],
                   reads=[("hqT", k, par), ("wdT", k)], writes=[PPK[pidx]],
                   start=(k == 0), stop=(k == ICH - 1))
        gk = ("g_t",)
        for n in range(2):
            pidx = 6 + n
            vop(lambda e, n=n, pidx=pidx: e.activation(
                out=g_t[:, 512 * n:512 * (n + 1)], in_=pp[pidx][:], func=AF.Silu),
                reads=[PPK[pidx]], writes=[gk], eng="a")
        dma(sload[:], s_scr[t0:t0 + 128, :], chan="ch_sload%d" % (tidx % 2),
            reads=[("s_scr", tidx), ("sldb",)], writes=[("sldb",)], eng="g")
        vop(lambda e: e.tensor_tensor(out=x1_t[:, 0:D], in0=sload[:],
                                      in1=g_t[:], op=AluOpType.mult),
            reads=[("sldb",), gk], writes=[("x1_t",)], eng="g")
        vop(lambda e, xb=xb: e.tensor_tensor(out=x1_t[:, 0:D], in0=x1_t[:, 0:D],
                                             in1=xb, op=AluOpType.add),
            reads=[("x1_t",), xbk], writes=[("x1_t",)], eng="g")
        dma(x1_scr[t0:t0 + 128, :], x1_t[:, 0:D], chan="st_x1_%d" % (tidx % 2),
            reads=[("x1_t",)], writes=[("x1_scr", tidx)], eng="g")
        if debug:
            dma(dbg["dbg_x1"][t0:t0 + 128, :], x1_t[:, 0:D], chan="dbgc",
                reads=[("x1_t",)], writes=[])

    b_xload(0)
    prev_t = None
    for tidx in range(TP // 128):
        b_front(tidx)
        if prev_t is not None:
            b_back(prev_t)
        prev_t = tidx
    b_back(prev_t)

    ms.phase()
    free_group("gB2")
    free_group("gB")

    xn1T = alloc("gD", "xn1T", (128, KC * TP), BF16)
    xnbf = alloc("gD", "xnbf_c", (128, D), BF16)
    stg_state.a = alloc("gD", "wstC", (128, D), F32)
    stg_state.b = alloc("gD", "wstC2", (128, D), F32)
    z_sb = alloc("gD", "z_sb", (128, NEXP * NTM), F32)
    alp_sb = alloc("gD", "alp_sb", (128, NEXP * NTM), F32)
    zm1 = alloc("gD", "zm1", (128, 1), F32)
    zm2 = alloc("gD", "zm2", (128, 1), F32)
    znm = alloc("gD", "znm", (128, 1), F32)
    zscr = alloc("gD", "zscr", (128, NEXP), F32)

    hbuf = alloc("gC2", "hbuf", (128, KC * TP), BF16)
    slg = alloc("gC2", "slg", (128, KC * T_MAIN), BF16)
    sqacc = alloc("gC2", "sqacc", (128, T_MAIN), F32)
    wo1_sb = alloc("gC2", "wo1_sb", (128, KC * D), BF16)
    gnw_sb = alloc("gC2", "gnw_sb", (128, KC), F32)

    wim = [alloc("gC1", "wim%d" % i, (128, 3 * D), BF16) for i in range(2)]
    wg1_sb = alloc("gC1", "wg1_sb", (128, KC * NEXP), BF16)
    conv_i_sb = alloc("gC1", "conv_i_sb", (128, KC * 4), F32)
    conv_f_sb = alloc("gC1", "conv_f_sb", (128, KC * 4), F32)
    iraw = alloc("gC1", "iraw", (128, TP), F32)
    fraw = alloc("gC1", "fraw", (128, TP), F32)
    ci_b = alloc("gC1", "ci_b", (128, TP), F32)
    cf_b = alloc("gC1", "cf_b", (128, TP), F32)

    cast_weight("Wo1", wo1_sb, "wo1")
    cast_weight("WgE", wg1_sb, "wg1", width=NEXP)
    dma(conv_i_sb[:], apin["conv_i"][:], chan="c_cvi", reads=[], writes=[("convi",)])
    dma(conv_f_sb[:], apin["conv_f"][:], chan="c_cvf", reads=[], writes=[("convf",)])
    dma(gnw_sb[:], apin["gnw"][:], chan="c_gnw", reads=[], writes=[("gnw",)])

    for go, gn in GRPC:
        for ti in range(gn // 128):
            t0 = go + ti * 128
            tidx = t0 // 128
            par = tidx % 2
            xb = xbuf2[:, D * par:D * (par + 1)]
            xbk = ("xbuf", par)
            dma(xb, x1_scr[t0:t0 + 128, :], chan="ch_x%d" % par,
                reads=[xbk, ("x1_scr", tidx)], writes=[xbk])
            emit_ln(xbk, xb, ln_ab, "ln1", tidx, [(("xnbf",), xnbf[:])])
            wall = [("xn1T", j2, tidx) for j2 in range(KC)]
            dma(xn1T[:, :].rearrange("p (j c) -> p j c", c=TP)[:, :, t0:t0 + 128],
                xnbf[:, :].rearrange("p (j c) -> p j c", c=128),
                chan="tx_xn1T", reads=[("xnbf",)], writes=wall, transpose=True)

    for tm in range(NTM):
        t0 = WARM + tm * 128
        zps = pp[3][:, 0:NEXP]
        mm(zps, ones1[:], bg_sb[0:1, 0:NEXP], reads=[("ones1",), ("b_bgE",)],
           writes=[PPK[3]], start=True, stop=False)
        for k in range(KC):
            mm(zps, xn1T[:, TP * k + t0:TP * k + t0 + 128],
               wg1_sb[:, NEXP * k:NEXP * (k + 1)],
               reads=[("xn1T", k, t0 // 128), ("wg1", k)], writes=[PPK[3]],
               start=False, stop=(k == KC - 1))
        zs = z_sb[:, NEXP * tm:NEXP * (tm + 1)]
        als = alp_sb[:, NEXP * tm:NEXP * (tm + 1)]
        vop(lambda e, zs=zs: e.activation(out=zs, in_=pp[3][:, 0:NEXP], func=AF.Copy),
            reads=[PPK[3]], writes=[("z", tm)], eng="a")
        vop(lambda e, zs=zs: e.tensor_reduce(out=zm1[:], in_=zs, axis=AX.X,
                                             op=AluOpType.max),
            reads=[("z", tm)], writes=[("zm1",)])
        vop(lambda e, zs=zs: e.tensor_scalar(out=zscr[:], in0=zs, scalar1=zm1[:, 0:1],
                                             scalar2=-1e30, op0=AluOpType.is_ge,
                                             op1=AluOpType.mult),
            reads=[("z", tm), ("zm1",)], writes=[("zscr",)])
        vop(lambda e, zs=zs: e.tensor_tensor(out=zscr[:], in0=zscr[:], in1=zs,
                                             op=AluOpType.add),
            reads=[("zscr",), ("z", tm)], writes=[("zscr",)])
        vop(lambda e: e.tensor_reduce(out=zm2[:], in_=zscr[:], axis=AX.X,
                                      op=AluOpType.max),
            reads=[("zscr",)], writes=[("zm2",)])
        vop(lambda e, zs=zs, als=als: e.tensor_scalar(out=als, in0=zs,
                                                      scalar1=zm2[:, 0:1], scalar2=None,
                                                      op0=AluOpType.is_ge),
            reads=[("z", tm), ("zm2",)], writes=[("alp", tm)])
        vop(lambda e: e.tensor_scalar(out=znm[:], in0=zm1[:], scalar1=-1.0,
                                      scalar2=None, op0=AluOpType.mult),
            reads=[("zm1",)], writes=[("znm",)])
        vop(lambda e, zs=zs: e.activation(out=zscr[:], in_=zs, func=AF.Exp,
                                          bias=znm[:, 0:1], scale=1.0),
            reads=[("z", tm), ("znm",)], writes=[("zscr",)], eng="a")
        vop(lambda e: e.tensor_tensor(out=zm2[:], in0=zm2[:], in1=zm1[:],
                                      op=AluOpType.subtract),
            reads=[("zm2",), ("zm1",)], writes=[("zm2",)])
        vop(lambda e: e.activation(out=zm2[:], in_=zm2[:], func=AF.Exp),
            reads=[("zm2",)], writes=[("zm2",)], eng="a")
        vop(lambda e: e.tensor_scalar(out=zm2[:], in0=zm2[:], scalar1=1.0,
                                      scalar2=None, op0=AluOpType.add),
            reads=[("zm2",)], writes=[("zm2",)])
        vop(lambda e: e.reciprocal(out=zm2[:], in_=zm2[:]),
            reads=[("zm2",)], writes=[("zm2",)])
        vop(lambda e, als=als: e.tensor_tensor(out=als, in0=als, in1=zscr[:],
                                               op=AluOpType.mult),
            reads=[("alp", tm), ("zscr",)], writes=[("alp", tm)])
        vop(lambda e, als=als: e.tensor_scalar_mul(out=als, in0=als,
                                                   scalar1=zm2[:, 0:1]),
            reads=[("alp", tm), ("zm2",)], writes=[("alpha", tm)])
        if debug:
            vop(lambda e, als=als: e.tensor_copy(out=zscr[:], in_=als),
                reads=[("alpha", tm)], writes=[("zscr",)])
            dma(dbg["dbg_alpha"][128 * tm:128 * (tm + 1), :], zscr[:],
                chan="dbgc", reads=[("zscr",)], writes=[])

    vop(lambda e: e.memset(sqacc[:], 0.0), reads=[], writes=[("sqacc",)])
    for m in range(KC):
        par = m % 2
        for wi_, wname in enumerate(["Wi", "Wf", "Wgr"]):
            stg, skey, ch = ((stg_state.a, ("wst",), "ch_wst") if (m * 3 + wi_) % 2 == 0
                             else (stg_state.b, ("wst2",), "ch_wst2"))
            srcap = apin[wname][:, 128 * m:128 * (m + 1)].rearrange(
                "(a p) c -> p a c", p=128)
            dma(stg[:, 0:D].rearrange("p (a c) -> p a c", c=128), srcap,
                chan=ch, reads=[skey], writes=[skey], eng="a")
            vop(lambda e, par=par, wi_=wi_, stg=stg: e.tensor_copy(
                out=wim[par][:, D * wi_:D * (wi_ + 1)], in_=stg[:, 0:D]),
                reads=[skey], writes=[("wim", par, wi_)])
        for gi, (go, gn) in enumerate(GRPC):
            rx = [("xn1T", k, (go // 128) + tt) for k in range(KC)
                  for tt in range(gn // 128)]
            for pse, psk, wi_ in [(0, PPK[0], 0), (1, PPK[1], 1)]:
                for k in range(KC):
                    mm(pp[pse][:, 0:gn], wim[par][:, D * wi_ + 128 * k:D * wi_ + 128 * (k + 1)],
                       xn1T[:, TP * k + go:TP * k + go + gn],
                       reads=[("wim", par, wi_)] + rx, writes=[psk],
                       start=(k == 0), stop=(k == KC - 1))
            vop(lambda e, go=go, gn=gn: e.activation(out=iraw[:, go:go + gn],
                                                     in_=pp[0][:, 0:gn], func=AF.Copy),
                reads=[PPK[0]], writes=[("iraw",)], eng="a")
            vop(lambda e, go=go, gn=gn: e.activation(out=fraw[:, go:go + gn],
                                                     in_=pp[1][:, 0:gn], func=AF.Copy),
                reads=[PPK[1]], writes=[("fraw",)], eng="a")
        for go, gn in GRPM:
            gow = WARM + go
            rx = [("xn1T", k, (gow // 128) + tt) for k in range(KC)
                  for tt in range(gn // 128)]
            for k in range(KC):
                mm(pp[2][:, 0:gn], wim[par][:, 2 * D + 128 * k:2 * D + 128 * (k + 1)],
                   xn1T[:, TP * k + gow:TP * k + gow + gn],
                   reads=[("wim", par, 2)] + rx, writes=[PPK[2]],
                   start=(k == 0), stop=(k == KC - 1))
            vop(lambda e, m=m, go=go, gn=gn: e.activation(
                out=slg[:, T_MAIN * m + go:T_MAIN * m + go + gn],
                in_=pp[2][:, 0:gn], func=AF.Silu),
                reads=[PPK[2]], writes=[("slg", m)], eng="a")
        vop(lambda e: e.tensor_scalar_mul(out=iraw[:, 0:WARM], in0=iraw[:, 0:WARM],
                                          scalar1=mwarm_sb[:, 0:1]),
            reads=[("iraw",), ("mwarm",)], writes=[("iraw",)])
        vop(lambda e: e.tensor_scalar_mul(out=fraw[:, 0:WARM], in0=fraw[:, 0:WARM],
                                          scalar1=mwarm_sb[:, 0:1]),
            reads=[("fraw",), ("mwarm",)], writes=[("fraw",)])
        for src, skey, cout, ckey, csb, cwkey in [
                (iraw, ("iraw",), ci_b, ("ci",), conv_i_sb, ("convi",)),
                (fraw, ("fraw",), cf_b, ("cf",), conv_f_sb, ("convf",))]:
            vop(lambda e, cout=cout: e.memset(cout[:, 0:3], 0.0),
                reads=[], writes=[ckey])
            vop(lambda e, src=src, cout=cout, csb=csb, m=m: e.tensor_scalar_mul(
                out=cout[:, 3:TP], in0=src[:, 3:TP],
                scalar1=csb[:, 4 * m + 3:4 * m + 4]),
                reads=[skey, cwkey], writes=[ckey])
            for tap in range(3):
                vop(lambda e, src=src, cout=cout, csb=csb, m=m, tap=tap:
                    e.scalar_tensor_tensor(out=cout[:, 3:TP], in0=src[:, tap:TP - 3 + tap],
                                           scalar=csb[:, 4 * m + tap:4 * m + tap + 1],
                                           in1=cout[:, 3:TP],
                                           op0=AluOpType.mult, op1=AluOpType.add),
                    reads=[skey, cwkey, ckey], writes=[ckey])
        vop(lambda e: e.activation(out=cf_b[:], in_=cf_b[:], func=AF.Sigmoid),
            reads=[("cf",)], writes=[("cf",)], eng="a")
        vop(lambda e: e.activation(out=ci_b[:], in_=ci_b[:], func=AF.Silu),
            reads=[("ci",)], writes=[("ci",)], eng="a")
        vop(lambda e: e.tensor_scalar(out=iraw[:], in0=cf_b[:], scalar1=-1.0,
                                      scalar2=1.0, op0=AluOpType.mult, op1=AluOpType.add),
            reads=[("cf",)], writes=[("iraw",)])
        vop(lambda e: e.tensor_tensor(out=ci_b[:], in0=ci_b[:], in1=iraw[:],
                                      op=AluOpType.mult),
            reads=[("ci",), ("iraw",)], writes=[("ci",)])
        for so, sn in _split(TP, 544):
            init = 0.0 if so == 0 else hbuf[:, TP * m + so - 1:TP * m + so]
            vop(lambda e, m=m, so=so, sn=sn, init=init: e.tensor_tensor_scan(
                out=hbuf[:, TP * m + so:TP * m + so + sn],
                data0=cf_b[:, so:so + sn], data1=ci_b[:, so:so + sn],
                initial=init, op0=AluOpType.mult, op1=AluOpType.add),
                reads=[("cf",), ("ci",), ("h", m)], writes=[("h", m)])
        vop(lambda e, m=m: e.tensor_tensor(out=fraw[:, 0:T_MAIN],
                                           in0=hbuf[:, TP * m + WARM:TP * (m + 1)],
                                           in1=hbuf[:, TP * m + WARM:TP * (m + 1)],
                                           op=AluOpType.mult),
            reads=[("h", m), ("fraw",)], writes=[("fraw",)], eng="g")
        vop(lambda e: e.tensor_tensor(out=sqacc[:], in0=sqacc[:], in1=fraw[:, 0:T_MAIN],
                                      op=AluOpType.add),
            reads=[("sqacc",), ("fraw",)], writes=[("sqacc",)], eng="g")
        if debug and m == 0:
            vop(lambda e: e.tensor_copy(out=ci_b[:, 0:T_MAIN],
                                        in_=hbuf[:, WARM:TP]),
                reads=[("h", 0), ("ci",)], writes=[("ci",)])
            dma(dbg["dbg_h"][:, WARM:TP], ci_b[:, 0:T_MAIN], chan="dbgc",
                reads=[("ci",)], writes=[])

    ms.phase()
    free_group("gC1")

    rms_b = alloc("gC2b", "rms_b", (128, T_MAIN), F32)
    oT = alloc("gC2b", "oT", (128, KC * T_MAIN), BF16)
    s1_t = alloc("gC2b", "s1_t", (128, D), F32)
    sqtmp = alloc("gC2b", "sqtmp", (128, T_MAIN), F32)

    for gi, (go, gn) in enumerate(GRPM):
        mm(pp[4][0:1, 0:gn], onesf[:], sqacc[:, go:go + gn],
           reads=[("onesf",), ("sqacc",)], writes=[PPK[4]], start=True, stop=True)
        vop(lambda e, go=go, gn=gn: e.tensor_scalar(out=rms_b[0:1, go:go + gn],
                                                    in0=pp[4][0:1, 0:gn],
                                                    scalar1=1.0 / D, scalar2=1e-5,
                                                    op0=AluOpType.mult, op1=AluOpType.add),
            reads=[PPK[4]], writes=[("rms", 0)])
        vop(lambda e, go=go, gn=gn: e.activation(out=rms_b[0:1, go:go + gn],
                                                 in_=rms_b[0:1, go:go + gn], func=AF.Sqrt),
            reads=[("rms", 0)], writes=[("rms", 0)], eng="a")
        vop(lambda e, go=go, gn=gn: e.reciprocal(out=rms_b[0:1, go:go + gn],
                                                 in_=rms_b[0:1, go:go + gn]),
            reads=[("rms", 0)], writes=[("rms", 0)])
    kk = 1
    while kk < 128:
        dma(rms_b[kk:2 * kk, :], rms_b[0:kk, :], chan="ch_rms%d" % kk,
            reads=[("rms", kk // 2) if kk > 1 else ("rms", 0)], writes=[("rms", kk)])
        kk *= 2
    RMSD = ("rms", 64)

    for m in range(KC):
        vop(lambda e, m=m: e.tensor_tensor(out=sqtmp[:],
                                           in0=hbuf[:, TP * m + WARM:TP * (m + 1)],
                                           in1=rms_b[:], op=AluOpType.mult),
            reads=[("h", m), RMSD], writes=[("sqtmp",)])
        vop(lambda e, m=m: e.tensor_tensor(out=sqtmp[:], in0=sqtmp[:],
                                           in1=slg[:, T_MAIN * m:T_MAIN * (m + 1)],
                                           op=AluOpType.mult),
            reads=[("sqtmp",), ("slg", m)], writes=[("sqtmp",)])
        vop(lambda e, m=m: e.tensor_scalar(out=oT[:, T_MAIN * m:T_MAIN * (m + 1)],
                                           in0=sqtmp[:],
                                           scalar1=gnw_sb[:, m:m + 1], scalar2=None,
                                           op0=AluOpType.mult),
            reads=[("sqtmp",), ("gnw",)], writes=[("oT", m)])
    for tm in range(NTM):
        for n in range(2):
            pidx = 5 + n
            for k in range(KC):
                mm(pp[pidx], oT[:, T_MAIN * k + 128 * tm:T_MAIN * k + 128 * (tm + 1)],
                   wo1_sb[:, D * k + 512 * n:D * k + 512 * (n + 1)],
                   reads=[("oT", k), ("wo1", k)], writes=[PPK[pidx]],
                   start=(k == 0), stop=(k == KC - 1))
        for n in range(2):
            pidx = 5 + n
            vop(lambda e, n=n, pidx=pidx: e.activation(
                out=s1_t[:, 512 * n:512 * (n + 1)], in_=pp[pidx][:], func=AF.Copy),
                reads=[PPK[pidx]], writes=[("s1_t",)], eng="a")
        dma(s1_scr[128 * tm:128 * (tm + 1), :], s1_t[:], chan="st_s1",
            reads=[("s1_t",)], writes=[("s1_scr", tm)])

    ms.phase()
    free_group("gC2b")
    free_group("gC2")

    weE_sb = [alloc("gDD", "weE%d" % i, (128, KC * D), BF16) for i in range(2)]
    be_sb = alloc("gDD", "be_sb", (1, 2 * D), BF16)
    acc = alloc("gDD", "acc", (128, NTM * D), F32)
    g1_t = alloc("gDD", "g1_t", (128, D), F32)
    y_t = alloc("gDD", "y_t", (128, D), F32)
    s1l = alloc("gDD", "s1l", (128, 2 * D), F32)

    for e_i in range(NEXP):
        sl = e_i % 2
        for k in range(KC):
            stg, skey, ch = ((stg_state.a, ("wst",), "ch_wst") if k % 2 == 0
                             else (stg_state.b, ("wst2",), "ch_wst2"))
            dma(stg[:, 0:D], apin["WeE"][D * e_i + 128 * k:D * e_i + 128 * (k + 1), :],
                chan=ch, reads=[skey], writes=[skey], eng="a")
            vop(lambda e, sl=sl, k=k, stg=stg: e.tensor_copy(
                out=weE_sb[sl][:, D * k:D * (k + 1)], in_=stg[:, 0:D]),
                reads=[skey], writes=[("weE", sl, k)])
        dma(stg_state.a[0:1, 0:D], apin["beE"][e_i:e_i + 1, :], chan="ch_wst",
            reads=[("wst",)], writes=[("wst",)])
        vop(lambda e, sl=sl, stg=stg_state.a: e.tensor_copy(
            out=be_sb[0:1, D * sl:D * (sl + 1)], in_=stg[0:1, 0:D]),
            reads=[("wst",)], writes=[("beE", sl)])
        pb = 2 * (e_i % 3)
        for tm in range(NTM):
            t0 = WARM + 128 * tm
            for n in range(2):
                mm(pp[pb + n], ones1[:], be_sb[0:1, D * sl + 512 * n:D * sl + 512 * (n + 1)],
                   reads=[("ones1",), ("beE", sl)], writes=[PPK[pb + n]],
                   start=True, stop=False)
                for k in range(KC):
                    mm(pp[pb + n], xn1T[:, TP * k + t0:TP * k + t0 + 128],
                       weE_sb[sl][:, D * k + 512 * n:D * k + 512 * (n + 1)],
                       reads=[("xn1T", k, t0 // 128), ("weE", sl, k)],
                       writes=[PPK[pb + n]], start=False, stop=(k == KC - 1))
            for n in range(2):
                pidx = pb + n
                alc = alp_sb[:, NEXP * tm + e_i:NEXP * tm + e_i + 1]
                if e_i == 0:
                    vop(lambda e, tm=tm, n=n, alc=alc, pidx=pidx: e.tensor_scalar(
                        out=acc[:, D * tm + 512 * n:D * tm + 512 * (n + 1)],
                        in0=pp[pidx][:], scalar1=alc, scalar2=None,
                        op0=AluOpType.mult),
                        reads=[PPK[pidx], ("alpha", tm)], writes=[("acc", tm)])
                else:
                    vop(lambda e, tm=tm, n=n, alc=alc, pidx=pidx: e.scalar_tensor_tensor(
                        out=acc[:, D * tm + 512 * n:D * tm + 512 * (n + 1)],
                        in0=pp[pidx][:], scalar=alc,
                        in1=acc[:, D * tm + 512 * n:D * tm + 512 * (n + 1)],
                        op0=AluOpType.mult, op1=AluOpType.add),
                        reads=[PPK[pidx], ("alpha", tm), ("acc", tm)],
                        writes=[("acc", tm)])

    for tm in range(NTM):
        t0 = WARM + 128 * tm
        par = tm % 2
        xb = xbuf2[:, D * par:D * (par + 1)]
        xbk = ("xbuf", par)
        dma(xb, x1_scr[t0:t0 + 128, :], chan="ch_x%d" % par,
            reads=[xbk, ("x1_scr", t0 // 128)], writes=[xbk])
        s1v = s1l[:, D * par:D * (par + 1)]
        dma(s1v, s1_scr[128 * tm:128 * (tm + 1), :], chan="ch_s1l%d" % par,
            reads=[("s1l", par), ("s1_scr", tm)], writes=[("s1l", par)])
        vop(lambda e, tm=tm: e.activation(out=g1_t[:], in_=acc[:, D * tm:D * (tm + 1)],
                                          func=AF.Silu),
            reads=[("acc", tm)], writes=[("g1_t",)], eng="a")
        if debug:
            dma(dbg["dbg_moe"][128 * tm:128 * (tm + 1), :],
                acc[:, D * tm:D * (tm + 1)], chan="dbgc",
                reads=[("acc", tm)], writes=[])
        vop(lambda e, s1v=s1v: e.tensor_tensor(out=g1_t[:], in0=g1_t[:], in1=s1v,
                                      op=AluOpType.mult),
            reads=[("g1_t",), ("s1l", par)], writes=[("g1_t",)])
        vop(lambda e, xb=xb: e.tensor_tensor(out=y_t[:], in0=g1_t[:], in1=xb,
                                             op=AluOpType.add),
            reads=[("g1_t",), xbk], writes=[("y_t",)], eng="g")
        dma(yout[128 * tm:128 * (tm + 1), :], y_t[:], chan="st_y%d" % par,
            reads=[("y_t",)], writes=[("yout", tm)], eng="g")

    ms.op("sy", lambda e: e.nop(), reads=[("yout", tm) for tm in range(NTM)], writes=[])

    with nc.Block() as block:
        ms.finalize(block)

    return nc, apin



def _host_stage(inputs, T_MAIN, L):
    x = np.asarray(inputs["L0_attn_x"], dtype=np.float32)
    B = x.shape[0]
    idx = np.linspace(0.0, L - 1, LEN).astype(np.int32)
    conv_i = np.ascontiguousarray(
        np.asarray(inputs["L1_attn_conv_i"], np.float32).reshape(KC, 128, 4)
        .transpose(1, 0, 2).reshape(128, KC * 4))
    conv_f = np.ascontiguousarray(
        np.asarray(inputs["L1_attn_conv_f"], np.float32).reshape(KC, 128, 4)
        .transpose(1, 0, 2).reshape(128, KC * 4))
    gnw = np.ascontiguousarray(
        np.asarray(inputs["L1_attn_g_norm_weight"], np.float32).reshape(KC, 128).T)
    shared = {
        "Wq": np.asarray(inputs["L0_attn_Wq"], np.float32),
        "Wk": np.asarray(inputs["L0_attn_Wk"], np.float32),
        "Wo0": np.asarray(inputs["L0_attn_Wo"], np.float32),
        "bq": np.asarray(inputs["L0_attn_bq"], np.float32).reshape(1, D),
        "bk": np.asarray(inputs["L0_attn_bk"], np.float32).reshape(1, D),
        "bo": np.asarray(inputs["L0_attn_bo"], np.float32).reshape(1, D),
        "WeL": np.asarray(inputs["L0_attn_We"], np.float32),
        "WrL": np.asarray(inputs["L0_attn_Wr"], np.float32),
        "Wc": np.asarray(inputs["L0_attn_Wc"], np.float32),
        "bc": np.tile(np.asarray(inputs["L0_attn_bc"], np.float32), 2).reshape(128, 1),
        "wg_nat": np.asarray(inputs["L0_ffn_w_gate"], np.float32),
        "wu_nat": np.asarray(inputs["L0_ffn_w_up"], np.float32),
        "wd_nat": np.asarray(inputs["L0_ffn_w_down"], np.float32),
        "Wi": np.asarray(inputs["L1_attn_Wi"], np.float32),
        "Wf": np.asarray(inputs["L1_attn_Wf"], np.float32),
        "Wgr": np.asarray(inputs["L1_attn_Wg"], np.float32),
        "Wo1": np.asarray(inputs["L1_attn_Wo"], np.float32),
        "conv_i": conv_i, "conv_f": conv_f, "gnw": gnw,
        "WeE": np.asarray(inputs["L1_ffn_We"], np.float32).reshape(NEXP * D, D),
        "beE": np.asarray(inputs["L1_ffn_be"], np.float32),
        "WgE": np.asarray(inputs["L1_ffn_Wg"], np.float32),
        "bgE": np.asarray(inputs["L1_ffn_bg"], np.float32).reshape(1, NEXP),
    }
    in_maps = []
    per_b = L // T_MAIN
    for c in range(B * per_b):
        b, pos = c // per_b, c % per_b
        t0 = pos * T_MAIN
        if pos == 0:
            xin = np.concatenate([x[b, 0:WARM], x[b, t0:t0 + T_MAIN]], axis=0)
            mw = np.zeros((128, 1), np.float32)
        else:
            xin = x[b, t0 - WARM:t0 + T_MAIN]
            mw = np.ones((128, 1), np.float32)
        m = dict(shared)
        m["xin"] = np.ascontiguousarray(xin)
        m["xg"] = np.ascontiguousarray(x[b, idx, :])
        m["mwarm"] = mw
        in_maps.append(m)
    return in_maps, B, per_b


def kernel(**inputs):
    T_MAIN, L = 2048, 8192
    in_maps, B, per_b = _host_stage(inputs, T_MAIN, L)
    nc, apin = build_program(T_MAIN)
    res = run_bass_kernel_spmd(nc, in_maps, core_ids=list(range(B * per_b)))
    shards = [np.asarray(r["y"]) for r in res.results]
    return np.stack([np.concatenate(shards[b * per_b:(b + 1) * per_b], axis=0)
                     for b in range(B)], axis=0)


# version 32
# speedup vs baseline: 1.0027x; 1.0027x over previous
import numpy as np
import ml_dtypes
import concourse.bass as bass
import concourse.mybir as mybir
from concourse.alu_op_type import AluOpType
from concourse.bass_utils import run_bass_kernel_spmd

F32 = mybir.dt.float32
BF16 = mybir.dt.bfloat16
I32 = mybir.dt.int32
AF = mybir.ActivationFunctionType
AX = mybir.AxisListType

D = 1024
KC = D // 128
H = 16
DKH = 64
LOWR = 20
LEN = 200
IFF = 2816
ICH = IFF // 128
NEXP = 8
WARM = 128
BF16NP = ml_dtypes.bfloat16
ENGINES = ("sy", "v", "a", "g", "t")


def _split(total, step):
    out, o = [], 0
    while o < total:
        n = min(step, total - o)
        out.append((o, n))
        o += n
    return out



class Op:
    __slots__ = ("eng", "fn", "idx", "pos", "is_dma", "chan", "chan_val",
                 "need_inc", "inc_seq", "waits", "drain")

    def __init__(self, eng, fn, is_dma, chan):
        self.eng = eng
        self.fn = fn
        self.is_dma = is_dma
        self.chan = chan
        self.chan_val = 0
        self.need_inc = False
        self.inc_seq = 0
        self.waits = []
        self.drain = False


class MS:
    def __init__(self, nc):
        self.nc = nc
        self.ops = []
        self.by_eng = {e: [] for e in ENGINES}
        self.res = {}
        self.chan_tot = {}
        self.sems = {}
        self.barrier_fns = {}

    def _edge(self, p, c, raw=True):
        if p is c:
            return
        if p.is_dma:
            if c.is_dma and c.chan == p.chan:
                return
            c.waits.append(("ch", p.chan, p.chan_val))
        elif p.eng == c.eng:
            if (raw and p.eng in ("v", "a", "g")) or (not raw and p.eng in ("v", "a")):
                p.need_inc = True
                c.waits.append(("op", p))
        else:
            p.need_inc = True
            c.waits.append(("op", p))

    def op(self, eng, fn, reads=(), writes=(), dma_chan=None):
        o = Op(eng, fn, dma_chan is not None, dma_chan)
        if o.is_dma:
            self.chan_tot[o.chan] = self.chan_tot.get(o.chan, 0) + 16
            o.chan_val = self.chan_tot[o.chan]
        for r in reads:
            s = self.res.get(r)
            if s and s["w"] is not None:
                self._edge(s["w"], o)
        for r in writes:
            s = self.res.get(r)
            if s:
                if s["w"] is not None:
                    self._edge(s["w"], o, raw=False)
                for rd in s["r"]:
                    self._edge(rd, o, raw=False)

            self.res[r] = {"w": o, "r": []}
        for r in reads:
            self.res.setdefault(r, {"w": None, "r": []})["r"].append(o)
        o.idx = len(self.ops)
        self.ops.append(o)
        o.pos = len(self.by_eng[o.eng])
        self.by_eng[o.eng].append(o)
        return o

    def phase(self):
        lasts = []
        for e in ENGINES:
            for o in reversed(self.by_eng[e]):
                if not o.is_dma:
                    o.need_inc = True
                    lasts.append(o)
                    break
        chans = dict(self.chan_tot)
        for e in ENGINES:
            bf = self.barrier_fns.get(e)
            o = Op(e, bf() if bf else (lambda en: en.nop()), False, None)
            for p in lasts:
                if p.eng != e:
                    o.waits.append(("op", p))
            for ch, v in chans.items():
                o.waits.append(("ch", ch, v))
            o.idx = len(self.ops)
            self.ops.append(o)
            o.pos = len(self.by_eng[e])
            self.by_eng[e].append(o)
        self.res = {}

    def finalize(self, block):
        inc_cnt = {e: 0 for e in ENGINES}
        for o in self.ops:
            if o.need_inc and not o.is_dma:
                inc_cnt[o.eng] += 1
                o.inc_seq = inc_cnt[o.eng]
        for e in ENGINES:
            if inc_cnt[e]:
                self.sems["eng:" + e] = self.nc.semaphore("sm_" + e).__enter__()
        for ch in self.chan_tot:
            self.sems["ch:" + ch] = self.nc.semaphore("sc_" + ch).__enter__()
        ms = self

        def mk(eng_name):
            def inner(e):
                waited = {}
                for o in ms.by_eng[eng_name]:
                    wl = {}
                    for w in o.waits:
                        if w[0] == "ch":
                            key, val = "ch:" + w[1], w[2]
                        else:
                            key, val = "eng:" + w[1].eng, w[1].inc_seq
                        wl[key] = max(wl.get(key, 0), val)
                    for key, val in sorted(wl.items()):
                        if waited.get(key, -1) < val:
                            e.wait_ge(ms.sems[key], val)
                            waited[key] = val
                    if o.drain:
                        e.drain()
                    inst = o.fn(e)
                    if o.is_dma:
                        inst.then_inc(ms.sems["ch:" + o.chan], 16)
                    elif o.need_inc:
                        if inst is None:
                            e.sem_inc(ms.sems["eng:" + o.eng], 1)
                        else:
                            inst.then_inc(ms.sems["eng:" + o.eng], 1)
            return inner

        for name, deco in [("sy", block.sync), ("v", block.vector),
                           ("a", block.scalar), ("g", block.gpsimd),
                           ("t", block.tensor)]:
            if self.by_eng[name]:
                deco(mk(name))


class P:
    pass



def build_program(T_MAIN, debug=False):
    TP = WARM + T_MAIN
    NT = TP // 128
    NTM = T_MAIN // 128
    GRA = _split(TP, 512)
    GRP = _split(TP, 256)
    GRPC = _split(TP, 512)
    GRPM = _split(T_MAIN, 512)

    nc = bass.Bass("TRN2", target_bir_lowering=False)
    ms = MS(nc)

    apin = {}

    def din(name, shape, dtype=F32):
        apin[name] = nc.dram_tensor(name, list(shape), dtype, kind="ExternalInput").ap()

    din("xin", (TP, D)); din("xg", (LEN, D)); din("mwarm", (128, 1))
    din("Wq", (D, D)); din("Wk", (D, D)); din("Wo0", (D, D))
    din("bq", (1, D)); din("bk", (1, D)); din("bo", (1, D))
    din("WeL", (H, LEN, LOWR)); din("WrL", (H, LEN, LOWR))
    din("Wc", (2 * LOWR, DKH)); din("bc", (128, 1))
    din("wg_nat", (IFF, D)); din("wu_nat", (IFF, D)); din("wd_nat", (D, IFF))
    din("Wi", (D, D)); din("Wf", (D, D)); din("Wgr", (D, D)); din("Wo1", (D, D))
    din("conv_i", (128, KC * 4)); din("conv_f", (128, KC * 4))
    din("gnw", (128, KC))
    din("WeE", (NEXP * D, D)); din("beE", (NEXP, D))
    din("WgE", (D, NEXP)); din("bgE", (1, NEXP))

    yout = nc.dram_tensor("y", [T_MAIN, D], F32, kind="ExternalOutput").ap()
    s_scr = nc.dram_tensor("s_scr", [TP, D], F32, kind="Internal").ap()
    x1_scr = nc.dram_tensor("x1_scr", [TP, D], F32, kind="Internal").ap()
    s1_scr = nc.dram_tensor("s1_scr", [T_MAIN, D], F32, kind="Internal").ap()
    wgu_scr = nc.dram_tensor("wgu_scr", [2 * ICH * 128, KC * 128], BF16, kind="Internal").ap()
    wd_scr = nc.dram_tensor("wd_scr", [KC * 128, ICH * 128], BF16, kind="Internal").ap()
    dbg = {}
    if debug:
        for nm, shp in [("dbg_s", (TP, D)), ("dbg_x1", (TP, D)),
                        ("dbg_h", (128, TP)), ("dbg_moe", (T_MAIN, D)),
                        ("dbg_alpha", (T_MAIN, NEXP))]:
            dbg[nm] = nc.dram_tensor(nm, list(shp), F32, kind="ExternalOutput").ap()

    groups = {}

    def alloc(group, name, shape, dtype=F32):
        g = nc.sbuf_tensor(name, list(shape), dtype)
        hh = g.__enter__()
        groups.setdefault(group, []).append(g)
        return hh

    def free_group(group):
        for g in reversed(groups.pop(group, [])):
            g.__exit__(None, None, None)

    pp = []
    for i in range(8):
        g = nc.psum_tensor("pp%d" % i, [128, 512], F32)
        pp.append(g.__enter__())
    PPK = [("pp", i) for i in range(8)]

    ones1 = alloc("főr", "ones1", (1, 128), BF16)
    onesf = alloc("főr", "onesf", (128, 1), F32)
    mwarm_sb = alloc("főr", "mwarm_sb", (128, 1), F32)
    bq_sb = alloc("főr", "bq_sb", (1, D), BF16)
    bk_sb = alloc("főr", "bk_sb", (1, D), BF16)
    bo_sb = alloc("főr", "bo_sb", (1, D), BF16)
    bg_sb = alloc("főr", "bg_sb", (1, NEXP), BF16)
    bc_sb = alloc("főr", "bc_sb", (128, 1), F32)
    ln_ab = alloc("főr", "ln_ab", (128, 2 * (NT + 2)), F32)
    xbuf2 = alloc("főr", "xbuf2", (128, 2 * D), F32)
    bscr = alloc("főr", "bscr", (1, 8), F32)
    _bar_i = [0]

    def _g_barrier_fn():
        i = _bar_i[0]
        _bar_i[0] += 1
        return lambda en, i=i: en.memset(bscr[0:1, i:i + 1], 0.0)

    ms.barrier_fns["g"] = _g_barrier_fn

    stg_state = P()
    lnv = P()
    lnv.bn6 = alloc("főr", "ln_bn6", (128, 12), F32)
    lnv.mv = alloc("főr", "ln_mv", (128, 2), F32)
    lnv.var = alloc("főr", "ln_var", (128, 1), F32)

    qb = P()
    QW = 1024
    qb.amax = alloc("gB", "q_amax", (128, QW // 16), F32)
    qb.v = alloc("gB", "q_v", (128, QW), F32)
    qb.f1 = alloc("gB", "q_f1", (128, QW), F32)
    qb.g = alloc("gB", "q_g", (128, QW), F32)
    qb.sg = alloc("gB", "q_sg", (128, QW), BF16)

    stg_state.a = alloc("gA", "wstW", (128, D), F32)
    stg_state.b = alloc("gA", "wstW2", (128, D), F32)
    wq_sb = alloc("gA", "wq_sb", (128, KC * D), BF16)
    wk_sb = alloc("gA", "wk_sb", (128, KC * D), BF16)
    wo_sb = alloc("gA", "wo_sb", (128, KC * D), BF16)
    ablk = alloc("gA", "ablk", (128, KC * 128), BF16)
    bblk = alloc("gA", "bblk", (128, KC * 128), BF16)

    def dma(dst_ap, src_ap, chan, reads, writes, transpose=False, eng="sy"):
        if transpose:
            ms.op(eng, lambda e, d=dst_ap, s=src_ap: e.dma_start_transpose(d, s),
                  reads=reads, writes=writes, dma_chan=chan)
        else:
            ms.op(eng, lambda e, d=dst_ap, s=src_ap: e.dma_start(out=d, in_=s),
                  reads=reads, writes=writes, dma_chan=chan)

    def vop(fn, reads, writes, eng="v"):
        ms.op(eng, fn, reads=reads, writes=writes)

    def mm(ps_ap, lhs_ap, rhs_ap, reads, writes, start, stop, tile_position=None):
        if not isinstance(ps_ap, bass.AP):
            ps_ap = ps_ap[:]
        if not isinstance(lhs_ap, bass.AP):
            lhs_ap = lhs_ap[:]
        if not isinstance(rhs_ap, bass.AP):
            rhs_ap = rhs_ap[:]
        ms.op("t", lambda e, o=ps_ap, l=lhs_ap, r=rhs_ap: e.matmul(
            o, l, r, start=start, stop=stop, tile_position=tile_position),
            reads=reads, writes=writes)

    def cast_weight(srcname, dst_sb, dstkey, nchunks=KC, width=D):
        for k in range(nchunks):
            stg, skey, ch = ((stg_state.a, ("wst",), "ch_wst") if k % 2 == 0
                             else (stg_state.b, ("wst2",), "ch_wst2"))
            dma(stg[:, 0:width], apin[srcname][128 * k:128 * (k + 1), :],
                chan=ch, reads=[skey], writes=[skey], eng="a")
            vop(lambda e, k=k, stg=stg: e.tensor_copy(
                out=dst_sb[:, width * k:width * (k + 1)], in_=stg[:, 0:width]),
                reads=[skey], writes=[(dstkey, k)])

    def emit_quant_p1(x_key, x_ap, F, P_, q, vkey, v_ap, sgkey, sg_ap):
        vop(lambda e: e.activation(out=v_ap, in_=x_ap, func=AF.Abs),
            reads=[x_key], writes=[vkey], eng="a")
        vop(lambda e: e.activation(out=sg_ap, in_=x_ap, func=AF.Sign),
            reads=[x_key], writes=[sgkey], eng="a")

    def emit_quant_p2(out_key, out_ap, F, P_, q, vkey, v_ap, sgkey, sg_ap):
        nb = F // 16
        vop(lambda e: e.tensor_reduce(out=q.amax[0:P_, 0:nb],
                                      in_=v_ap.rearrange("p (b w) -> p b w", w=16),
                                      axis=AX.X, op=AluOpType.max),
            reads=[vkey], writes=[("q.amax",)])
        vop(lambda e: e.tensor_scalar(out=q.amax[0:P_, 0:nb], in0=q.amax[0:P_, 0:nb],
                                      scalar1=1.0 / 6.0, scalar2=1e-12,
                                      op0=AluOpType.mult, op1=AluOpType.max),
            reads=[("q.amax",)], writes=[("q.amax",)])
        vop(lambda e: e.reciprocal(out=q.amax[0:P_, 0:nb], in_=q.amax[0:P_, 0:nb]),
            reads=[("q.amax",)], writes=[("q.amax",)])
        rb = q.amax[0:P_, 0:nb].unsqueeze(2).broadcast_to([P_, nb, 16])
        vop(lambda e: e.tensor_tensor(out=v_ap.rearrange("p (b w) -> p b w", w=16),
                                      in0=v_ap.rearrange("p (b w) -> p b w", w=16),
                                      in1=rb, op=AluOpType.mult),
            reads=[vkey, ("q.amax",)], writes=[vkey])
        vop(lambda e: e.tensor_scalar(out=q.g[0:P_, 0:F], in0=v_ap,
                                      scalar1=5.0, scalar2=2.0,
                                      op0=AluOpType.is_gt, op1=AluOpType.mult),
            reads=[vkey], writes=[("q.g",)])
        for thr in (2.5, 3.5):
            vop(lambda e, thr=thr: e.scalar_tensor_tensor(
                out=q.g[0:P_, 0:F], in0=v_ap, scalar=thr,
                in1=q.g[0:P_, 0:F], op0=AluOpType.is_gt, op1=AluOpType.add),
                reads=[vkey, ("q.g",)], writes=[("q.g",)])
        vop(lambda e: e.tensor_scalar(out=q.f1[0:P_, 0:F], in0=v_ap,
                                      scalar1=0.25, scalar2=None, op0=AluOpType.is_gt),
            reads=[vkey], writes=[("q.f1",)])
        for thr in (0.75, 1.25, 1.75):
            vop(lambda e, thr=thr: e.scalar_tensor_tensor(
                out=q.f1[0:P_, 0:F], in0=v_ap, scalar=thr,
                in1=q.f1[0:P_, 0:F], op0=AluOpType.is_gt, op1=AluOpType.add),
                reads=[vkey, ("q.f1",)], writes=[("q.f1",)])
        vop(lambda e: e.scalar_tensor_tensor(out=q.g[0:P_, 0:F], in0=q.f1[0:P_, 0:F],
                                             scalar=0.5, in1=q.g[0:P_, 0:F],
                                             op0=AluOpType.mult, op1=AluOpType.add),
            reads=[("q.f1",), ("q.g",)], writes=[("q.g",)])
        vop(lambda e: e.tensor_tensor(out=q.g[0:P_, 0:F], in0=q.g[0:P_, 0:F],
                                      in1=sg_ap, op=AluOpType.mult),
            reads=[("q.g",), sgkey], writes=[("q.g",)])
        vop(lambda e: e.reciprocal(out=q.amax[0:P_, 0:nb], in_=q.amax[0:P_, 0:nb]),
            reads=[("q.amax",)], writes=[("q.amax",)])
        sb2 = q.amax[0:P_, 0:nb].unsqueeze(2).broadcast_to([P_, nb, 16])
        vop(lambda e: e.tensor_tensor(out=out_ap.rearrange("p (b w) -> p b w", w=16),
                                      in0=q.g[0:P_, 0:F].rearrange("p (b w) -> p b w", w=16),
                                      in1=sb2, op=AluOpType.mult),
            reads=[("q.g",), ("q.amax",)], writes=[out_key])

    def emit_quant(x_key, x_ap, out_key, out_ap, F, P_=128, q=None):
        if q is None:
            q = qb
        vk, sk = ("q.v",), ("q.sg",)
        emit_quant_p1(x_key, x_ap, F, P_, q, vk, q.v[0:P_, 0:F], sk, q.sg[0:P_, 0:F])
        emit_quant_p2(out_key, out_ap, F, P_, q, vk, q.v[0:P_, 0:F], sk, q.sg[0:P_, 0:F])

    def emit_ln(x_key, x_ap, ab_sb, ab_key, slot, outs, P_=128,
                scr=None, scr_key=None):
        a_ap = ab_sb[0:P_, 2 * slot:2 * slot + 1]
        b_ap = ab_sb[0:P_, 2 * slot + 1:2 * slot + 2]
        vop(lambda e: e.bn_stats(out=lnv.bn6[0:P_, 0:6], in_=x_ap[:, 0:D // 2]),
            reads=[x_key], writes=[("ln.bn6",)])
        vop(lambda e: e.bn_stats(out=lnv.bn6[0:P_, 6:12], in_=x_ap[:, D // 2:D]),
            reads=[x_key], writes=[("ln.bn6",)])
        vop(lambda e: e.bn_aggr(out=lnv.mv[0:P_, :], in_=lnv.bn6[0:P_, 0:12]),
            reads=[("ln.bn6",)], writes=[("ln.mv",)])
        vop(lambda e: e.tensor_scalar(out=lnv.var[0:P_, :], in0=lnv.mv[0:P_, 1:2],
                                      scalar1=1e-5, scalar2=None, op0=AluOpType.add),
            reads=[("ln.mv",)], writes=[("ln.var",)])
        vop(lambda e: e.activation(out=lnv.var[0:P_, :], in_=lnv.var[0:P_, :],
                                   func=AF.Sqrt),
            reads=[("ln.var",)], writes=[("ln.var",)], eng="a")
        vop(lambda e: e.reciprocal(out=a_ap, in_=lnv.var[0:P_, :]),
            reads=[("ln.var",)], writes=[(ab_key, slot, "a")])
        vop(lambda e: e.scalar_tensor_tensor(out=b_ap, in0=lnv.mv[0:P_, 0:1],
                                             scalar=-1.0, in1=a_ap,
                                             op0=AluOpType.mult, op1=AluOpType.mult),
            reads=[("ln.mv",), (ab_key, slot, "a")], writes=[(ab_key, slot, "b")])
        for okey, oap in outs:
            vop(lambda e, oap=oap: e.tensor_scalar(out=oap, in_=None, in0=x_ap,
                                                   scalar1=a_ap, scalar2=b_ap,
                                                   op0=AluOpType.mult, op1=AluOpType.add)
                if False else
                e.tensor_scalar(out=oap, in0=x_ap, scalar1=a_ap, scalar2=b_ap,
                                op0=AluOpType.mult, op1=AluOpType.add),
                reads=[x_key, (ab_key, slot, "a"), (ab_key, slot, "b")],
                writes=[okey])

    dma(mwarm_sb[:], apin["mwarm"][:], chan="c_mw", reads=[], writes=[("mwarm",)])
    dma(bc_sb[:], apin["bc"][:], chan="c_bc", reads=[], writes=[("bc",)])
    vop(lambda e: e.memset(ones1[:], 1.0), reads=[], writes=[("ones1",)])
    vop(lambda e: e.memset(onesf[:], 1.0), reads=[], writes=[("onesf",)])
    vop(lambda e: e.memset(ablk[:], 0.0), reads=[], writes=[("ablk", "z")])
    vop(lambda e: e.memset(bblk[:], 0.0), reads=[], writes=[("bblk", "z")])

    for nm, dst, w in [("bq", bq_sb, D), ("bk", bk_sb, D), ("bo", bo_sb, D),
                       ("bgE", bg_sb, NEXP)]:
        dma(stg_state.a[0:1, 0:w], apin[nm][:], chan="ch_wst", reads=[("wst",)], writes=[("wst",)])
        vop(lambda e, dst=dst, w=w, stg=stg_state.a: e.tensor_copy(
            out=dst[0:1, 0:w], in_=stg[0:1, 0:w]),
            reads=[("wst",)], writes=[("b_" + nm,)])

    cast_weight("Wq", wq_sb, "wq")
    cast_weight("Wk", wk_sb, "wk")
    cast_weight("Wo0", wo_sb, "wo")

    xg_sb = alloc("gW1", "xg_sb", (128, 2 * D), F32)
    xng_sb = alloc("gW1", "xng_sb", (128, 2 * D), BF16)
    wel_sb = alloc("gW1", "wel_sb", (128, 2 * LOWR), BF16)
    wrl_sb = alloc("gW1", "wrl_sb", (128, 2 * LOWR), BF16)
    wc_sb = alloc("gW1", "wc_sb", (LOWR, 2 * DKH), BF16)
    wet_sb = alloc("gW1", "wet_sb", (LOWR, 2 * DKH), BF16)
    LEN2 = LEN - 128

    dma(xg_sb[0:128, 0:D], apin["xg"][0:128, :], chan="cxg", reads=[],
        writes=[("xg", 0), ("xg", 1)])
    dma(xg_sb[0:LEN2, D:2 * D], apin["xg"][128:LEN, :], chan="cxg", reads=[],
        writes=[("xg", 0), ("xg", 1)])
    dma(stg_state.a[0:LOWR, 0:DKH], apin["Wc"][0:LOWR, :], chan="ch_wst",
        reads=[("wst",)], writes=[("wst",)])
    dma(stg_state.a[0:LOWR, DKH:2 * DKH], apin["Wc"][LOWR:2 * LOWR, :], chan="ch_wst",
        reads=[("wst",)], writes=[("wst",)])
    vop(lambda e, stg=stg_state.a: e.tensor_copy(out=wc_sb[:], in_=stg[0:LOWR, 0:2 * DKH]),
        reads=[("wst",)], writes=[("wc",)])
    emit_ln(("xg", 0), xg_sb[0:128, 0:D], ln_ab, "lng", NT, [(("xng", 0), xng_sb[0:128, 0:D])])
    emit_ln(("xg", 1), xg_sb[0:LEN2, D:2 * D], ln_ab, "lng", NT + 1,
            [(("xng", 1), xng_sb[0:LEN2, D:2 * D])], P_=LEN2)

    ps_we = pp[0][0:LOWR, 0:DKH]
    ps_ab = pp[1]
    for h in range(H):
        j, par = h // 2, h % 2
        dma(stg_state.a[0:128, 0:LOWR], apin["WeL"][h, 0:128, :], chan="ch_wst",
            reads=[("wst",)], writes=[("wst",)])
        dma(stg_state.a[0:LEN2, LOWR:2 * LOWR], apin["WeL"][h, 128:LEN, :], chan="ch_wst",
            reads=[("wst",)], writes=[("wst",)])
        vop(lambda e, stg=stg_state.a: e.tensor_copy(out=wel_sb[:, :], in_=stg[0:128, 0:2 * LOWR]),
            reads=[("wst",)], writes=[("wel",)])
        dma(stg_state.a[0:128, 0:LOWR], apin["WrL"][h, 0:128, :], chan="ch_wst",
            reads=[("wst",)], writes=[("wst",)])
        dma(stg_state.a[0:LEN2, LOWR:2 * LOWR], apin["WrL"][h, 128:LEN, :], chan="ch_wst",
            reads=[("wst",)], writes=[("wst",)])
        vop(lambda e, stg=stg_state.a: e.tensor_copy(out=wrl_sb[:, :], in_=stg[0:128, 0:2 * LOWR]),
            reads=[("wst",)], writes=[("wrl",)])
        for src_sb, srckey, half, blk, blkkey in [
                (wel_sb, ("wel",), 0, ablk, "ablk"), (wrl_sb, ("wrl",), 1, bblk, "bblk")]:
            wcs = wc_sb[0:LOWR, 0:DKH] if half == 0 else wc_sb[0:LOWR, DKH:2 * DKH]
            mm(ps_we, src_sb[0:128, 0:LOWR], xng_sb[0:128, DKH * h:DKH * (h + 1)],
               reads=[srckey, ("xng", 0)], writes=[PPK[0]], start=True, stop=False)
            mm(ps_we, src_sb[0:LEN2, LOWR:2 * LOWR],
               xng_sb[0:LEN2, D + DKH * h:D + DKH * (h + 1)],
               reads=[srckey, ("xng", 1)], writes=[PPK[0]], start=False, stop=True)
            wt = wet_sb[:, DKH * half:DKH * (half + 1)]
            vop(lambda e, wt=wt: e.activation(out=wt, in_=ps_we, func=AF.Copy),
                reads=[PPK[0]], writes=[("wet", half)], eng="a")
            mm(ps_ab[64 * par:64 * (par + 1), 0:DKH], wt, wcs,
               reads=[("wet", half), ("wc",)], writes=[PPK[1]],
               start=True, stop=True, tile_position=(0, 64 * par))
            dst = blk[64 * par:64 * (par + 1), 128 * j + 64 * par:128 * j + 64 * (par + 1)]
            vop(lambda e, dst=dst, par=par: e.activation(
                out=dst, in_=ps_ab[64 * par:64 * (par + 1), 0:DKH], func=AF.Copy),
                reads=[PPK[1], (blkkey, "z")], writes=[(blkkey, j, par)], eng="a")

    ms.phase()
    free_group("gW1")

    qa = P()
    QA = 2048
    qa.amax = alloc("gAq", "qa_amax", (128, QA // 16), F32)
    qa.v2 = [alloc("gAq", "qa_v%d" % i, (128, QA), F32) for i in range(2)]
    qa.f1 = alloc("gAq", "qa_f1", (128, QA), F32)
    qa.g = alloc("gAq", "qa_g", (128, QA), F32)
    qa.sg2 = [alloc("gAq", "qa_sg%d" % i, (128, QA), BF16) for i in range(2)]
    wstA = [alloc("gAq", "wstA%d" % i, (128, QA), F32) for i in range(2)]
    wqqA = alloc("gAq", "wqqA", (128, QA), BF16)
    recst = [alloc("gAq", "recst%d" % i, (128, 1408), BF16) for i in range(2)]
    wq_jobs = []

    def job_width_src(job):
        if job[0] == "wgu":
            return D * len(job[3])
        return 1408

    def job_stage1(job, par):
        stg = wstA[par]
        skey = ("wstA", par)
        vk, sk = ("qa.v", par), ("qa.sg", par)
        if job[0] == "wgu":
            _, srcname, base, sls = job
            for i, sl in enumerate(sls):
                dma(stg[:, D * i:D * (i + 1)], apin[srcname][128 * sl:128 * (sl + 1), :],
                    chan="ch_wstA%d" % par, reads=[skey], writes=[skey], eng="a")
            F = D * len(sls)
        else:
            _, sl, half = job
            so, F = half * 1408, 1408
            dma(stg[:, 0:F], apin["wd_nat"][128 * sl:128 * (sl + 1), so:so + F],
                chan="ch_wstA%d" % par, reads=[skey], writes=[skey], eng="a")
        emit_quant_p1(skey, stg[:, 0:F], F, 128, qa, vk, qa.v2[par][:, 0:F],
                      sk, qa.sg2[par][:, 0:F])

    def job_stage2(job, par):
        F = job_width_src(job)
        vk, sk = ("qa.v", par), ("qa.sg", par)
        emit_quant_p2(("wqqA",), wqqA[:, 0:F], F, 128, qa,
                      vk, qa.v2[par][:, 0:F], sk, qa.sg2[par][:, 0:F])
        rp = par
        rkey = ("recst", rp)
        if job[0] == "wgu":
            _, srcname, base, sls = job
            for i, sl in enumerate(sls):
                dma(recst[rp][:, 0:D].rearrange("p (j c) -> p j c", c=128),
                    wqqA[:, D * i:D * (i + 1)].rearrange("p (j c) -> p j c", c=128),
                    chan="tx_rec%d" % rp, reads=[("wqqA",), rkey], writes=[rkey],
                    transpose=True)
                dma(wgu_scr[128 * (base + sl):128 * (base + sl + 1), :],
                    recst[rp][:, 0:D],
                    chan="st_rec%d" % rp, reads=[rkey],
                    writes=[("wgu_scr", base + sl)], eng="a")
        else:
            _, sl, half = job
            so = half * 1408
            dma(recst[rp][:, 0:F].rearrange("p (j c) -> p j c", c=128),
                wqqA[:, 0:F].rearrange("p (j c) -> p j c", c=128),
                chan="tx_rec%d" % rp, reads=[("wqqA",), rkey], writes=[rkey],
                transpose=True)
            dma(wd_scr[128 * sl:128 * (sl + 1), so:so + F], recst[rp][:, 0:F],
                chan="st_rec%d" % rp, reads=[rkey], writes=[("wd_scr", sl, half)],
                eng="a")

    _g3 = [list(range(i, min(i + 2, ICH))) for i in range(0, ICH, 2)]
    for gi3, sls in enumerate(_g3):
        wq_jobs.append(("wgu", "wg_nat", 0, sls))
    for gi3, sls in enumerate(_g3):
        wq_jobs.append(("wgu", "wu_nat", ICH, sls))
    for sl in range(KC):
        for half in range(2):
            wq_jobs.append(("wd", sl, half))
    wq_jobs.reverse()
    _jobstate = {"n": 0, "pending": []}

    def quant_weight_slice():
        if wq_jobs:
            job = wq_jobs.pop()
            par = _jobstate["n"] & 1
            _jobstate["n"] += 1
            job_stage1(job, par)
            _jobstate["pending"].append((job, par))
        if len(_jobstate["pending"]) > 1 or (not wq_jobs and _jobstate["pending"]):
            job, par = _jobstate["pending"].pop(0)
            job_stage2(job, par)

    def drain_jobs():
        while wq_jobs or _jobstate["pending"]:
            quant_weight_slice()

    GA = 512
    xnT = alloc("gA2", "xnT", (128, KC * GA), BF16)
    qnT = alloc("gA2", "qnT", (128, KC * GA), BF16)
    knT = alloc("gA2", "knT", (128, KC * GA), BF16)
    mT = alloc("gA2", "mT", (128, KC * GA), BF16)
    qss = alloc("gA2", "qss", (128, H), F32)
    qrv = alloc("gA2", "qrv", (128, H), F32)
    qn_bf = alloc("gA2", "qn_bf", (128, D), BF16)
    kn_bf = alloc("gA2", "kn_bf", (128, D), BF16)
    xnbf = alloc("gA2", "xnbf", (128, D), BF16)
    qf = alloc("gA2", "qf", (128, D), F32)
    kf = qf
    s_t = qf

    def l2norm(src, srckey, dst, dstkey):
        vop(lambda e: e.tensor_tensor(out=qb.v[:, 0:D], in0=src[:], in1=src[:],
                                      op=AluOpType.mult),
            reads=[srckey], writes=[("q.v",)])
        vop(lambda e: e.tensor_reduce(out=qss[:], in_=qb.v[:, 0:D].rearrange(
            "p (h d) -> p h d", d=DKH), axis=AX.X, op=AluOpType.add),
            reads=[("q.v",)], writes=[("qss",)])
        vop(lambda e: e.activation(out=qrv[:], in_=qss[:], func=AF.Sqrt),
            reads=[("qss",)], writes=[("qrv",)], eng="a")
        vop(lambda e: e.tensor_scalar(out=qrv[:], in0=qrv[:], scalar1=1e-12,
                                      scalar2=None, op0=AluOpType.max),
            reads=[("qrv",)], writes=[("qrv",)])
        vop(lambda e: e.reciprocal(out=qrv[:], in_=qrv[:]),
            reads=[("qrv",)], writes=[("qrv",)])
        rb = qrv[:, :].unsqueeze(2).broadcast_to([128, H, DKH])
        vop(lambda e: e.tensor_tensor(out=dst.rearrange("p (h d) -> p h d", d=DKH),
                                      in0=src.rearrange("p (h d) -> p h d", d=DKH),
                                      in1=rb, op=AluOpType.mult),
            reads=[srckey, ("qrv",)], writes=[dstkey])

    for go, gn in GRA:
        ntile = gn // 128
        for ti in range(ntile):
            for _ in range(2):
                if wq_jobs:
                    quant_weight_slice()
            t0 = go + ti * 128
            tidx = t0 // 128
            par = tidx % 2
            xb = xbuf2[:, D * par:D * (par + 1)]
            xbk = ("xbuf", par)
            dma(xb, apin["xin"][t0:t0 + 128, :], chan="ch_x%d" % par,
                reads=[xbk], writes=[xbk])
            emit_ln(xbk, xb, ln_ab, "ln0", tidx, [(("xnbf",), xnbf[:])])
            wall = [("xnT", j2, ti) for j2 in range(KC)]
            dma(xnT[:, :].rearrange("p (j c) -> p j c", c=GA)[:, :, 128 * ti:128 * (ti + 1)],
                xnbf[:, :].rearrange("p (j c) -> p j c", c=128),
                chan="tx_xnT%d" % ti, reads=[("xnbf",)], writes=wall, transpose=True)
            for ww, wwkey, ppi, bap, bkey in [
                    (wq_sb, "wq", (0, 1), bq_sb, ("b_bq",)),
                    (wk_sb, "wk", (2, 3), bk_sb, ("b_bk",))]:
                for n in range(2):
                    pidx = ppi[n]
                    mm(pp[pidx], ones1[:], bap[0:1, 512 * n:512 * (n + 1)],
                       reads=[("ones1",), bkey], writes=[PPK[pidx]], start=True, stop=False)
                    for k in range(KC):
                        mm(pp[pidx], xnT[:, GA * k + 128 * ti:GA * k + 128 * (ti + 1)],
                           ww[:, D * k + 512 * n:D * k + 512 * (n + 1)],
                           reads=[("xnT", k, ti), (wwkey, k)], writes=[PPK[pidx]],
                           start=False, stop=(k == KC - 1))
            for n in range(2):
                vop(lambda e, n=n: e.activation(out=qf[:, 512 * n:512 * (n + 1)],
                                                in_=pp[n][:], func=AF.Copy),
                    reads=[PPK[n], ("qf",)], writes=[("qf",)], eng="a")
            l2norm(qf, ("qf",), qn_bf[:, :], ("qn",))
            for n in range(2):
                vop(lambda e, n=n: e.activation(out=kf[:, 512 * n:512 * (n + 1)],
                                                in_=pp[2 + n][:], func=AF.Copy),
                    reads=[PPK[2 + n], ("qf",)], writes=[("qf",)], eng="a")
            l2norm(kf, ("qf",), kn_bf[:, :], ("kn",))
            wq_all = [("qnT", j2, ti) for j2 in range(KC)]
            wk_all = [("knT", j2, ti) for j2 in range(KC)]
            dma(qnT[:, :].rearrange("p (j c) -> p j c", c=GA)[:, :, 128 * ti:128 * (ti + 1)],
                qn_bf[:, :].rearrange("p (j c) -> p j c", c=128),
                chan="tx_qnT%d" % ti, reads=[("qn",)], writes=wq_all, transpose=True)
            dma(knT[:, :].rearrange("p (j c) -> p j c", c=GA)[:, :, 128 * ti:128 * (ti + 1)],
                kn_bf[:, :].rearrange("p (j c) -> p j c", c=128),
                chan="tx_knT%d" % ti, reads=[("kn",)], writes=wk_all, transpose=True)
        for jj in range(KC):
            rq = [("qnT", jj, ti) for ti in range(ntile)]
            rk = [("knT", jj, ti) for ti in range(ntile)]
            mm(pp[4][:, 0:gn], ablk[:, 128 * jj:128 * (jj + 1)], qnT[:, GA * jj:GA * jj + gn],
               reads=rq + [("ablk", jj, 0), ("ablk", jj, 1)], writes=[PPK[4]],
               start=True, stop=False)
            mm(pp[4][:, 0:gn], bblk[:, 128 * jj:128 * (jj + 1)], knT[:, GA * jj:GA * jj + gn],
               reads=rk + [("bblk", jj, 0), ("bblk", jj, 1)], writes=[PPK[4]],
               start=False, stop=True)
            vop(lambda e, jj=jj, gn=gn: e.tensor_scalar(
                out=mT[:, GA * jj:GA * jj + gn], in0=pp[4][:, 0:gn],
                scalar1=bc_sb[:, 0:1], scalar2=None, op0=AluOpType.add),
                reads=[PPK[4], ("bc",)], writes=[("mT", jj)])
        for ti in range(ntile):
            t0 = go + ti * 128
            for n in range(2):
                pidx = 5 + n
                mm(pp[pidx], ones1[:], bo_sb[0:1, 512 * n:512 * (n + 1)],
                   reads=[("ones1",), ("b_bo",)], writes=[PPK[pidx]], start=True, stop=False)
                for k in range(KC):
                    mm(pp[pidx], mT[:, GA * k + 128 * ti:GA * k + 128 * (ti + 1)],
                       wo_sb[:, D * k + 512 * n:D * k + 512 * (n + 1)],
                       reads=[("mT", k), ("wo", k)], writes=[PPK[pidx]],
                       start=False, stop=(k == KC - 1))
            for n in range(2):
                pidx = 5 + n
                vop(lambda e, n=n, pidx=pidx: e.activation(
                    out=s_t[:, 512 * n:512 * (n + 1)], in_=pp[pidx][:], func=AF.Copy),
                    reads=[PPK[pidx], ("qf",)], writes=[("qf",)], eng="a")
            dma(s_scr[t0:t0 + 128, :], s_t[:], chan="st_s", reads=[("qf",)],
                writes=[("s_scr", t0 // 128)])
            if debug:
                dma(dbg["dbg_s"][t0:t0 + 128, :], s_t[:], chan="dbgc",
                    reads=[("qf",)], writes=[])

    drain_jobs()
    ms.phase()
    free_group("gA2")
    free_group("gAq")
    free_group("gA")

    wgT = alloc("gB2", "wgT", (128, KC * IFF), BF16)
    wuT = alloc("gB2", "wuT", (128, KC * IFF), BF16)
    wdT = alloc("gB2", "wdT", (128, ICH * D), BF16)
    for wi_, (dst_sb, dkey, base) in enumerate([(wgT, "wgT", 0), (wuT, "wuT", ICH)]):
        for sl in range(ICH):
            wall = [(dkey, j2) for j2 in range(KC)]
            dma(dst_sb[:, :].rearrange("p (j c) -> p j c", c=IFF)[:, :, 128 * sl:128 * (sl + 1)],
                wgu_scr[128 * (base + sl):128 * (base + sl + 1), :]
                .rearrange("p (j c) -> p j c", c=128),
                chan="ld_" + dkey, reads=[("wgu_scr", base + sl)], writes=wall)
    for sl in range(KC):
        wall = [("wdT", k2) for k2 in range(ICH)]
        dma(wdT[:, :].rearrange("p (j c) -> p j c", c=D)[:, :, 128 * sl:128 * (sl + 1)],
            wd_scr[128 * sl:128 * (sl + 1), :].rearrange("p (j c) -> p j c", c=128),
            chan="ld_wdT", reads=[("wd_scr", sl, 0), ("wd_scr", sl, 1)], writes=wall)
    xqbf = alloc("gB2", "xqbf", (128, D), BF16)
    tmpf = alloc("gB2", "tmpf", (128, D), F32)
    sldb = alloc("gB2", "sldb", (128, D), F32)
    xn32 = tmpf
    sload = sldb
    xqT = alloc("gB2", "xqT", (128, KC * 256), BF16)
    hqT = alloc("gB2", "hqT", (128, ICH * 256), BF16)
    h_sl = alloc("gB2", "h_sl", (128, 1024), F32)
    hq_sl = xqbf
    g_t = alloc("gB2", "g_t", (128, D), F32)
    x1_t = alloc("gB2", "x1_t", (128, D), F32)

    NSL = _split(IFF, 512)
    hq_bf = xqbf
    xb4ext = alloc("gB2", "xb4ext", (128, 2 * D), F32)

    def b_xb(tidx):
        p4 = tidx % 4
        base = xbuf2 if p4 < 2 else xb4ext
        return base[:, D * (p4 % 2):D * (p4 % 2 + 1)], ("xbuf", p4), p4

    def b_xload(tidx):
        t0 = tidx * 128
        xb, xbk, p4 = b_xb(tidx)
        dma(xb, apin["xin"][t0:t0 + 128, :], chan="ch_x%d" % p4,
            reads=[xbk], writes=[xbk])

    def b_front(tidx):
        t0 = tidx * 128
        par = tidx % 2
        xb, xbk, p4 = b_xb(tidx)
        b_xload(tidx)
        a_ap = ln_ab[:, 2 * tidx:2 * tidx + 1]
        b_ap = ln_ab[:, 2 * tidx + 1:2 * tidx + 2]
        vop(lambda e, a_ap=a_ap, b_ap=b_ap, xb=xb: e.tensor_scalar(
            out=xn32[:], in0=xb, scalar1=a_ap, scalar2=b_ap,
            op0=AluOpType.mult, op1=AluOpType.add),
            reads=[xbk], writes=[("tmpf",)])
        emit_quant(("tmpf",), xn32[:], ("xq",), xqbf[:], D)
        wall = [("xqT", j2, par) for j2 in range(KC)]
        dma(xqT[:, :].rearrange("p (j c) -> p j c", c=256)[:, :, 128 * par:128 * (par + 1)],
            xqbf[:, :].rearrange("p (j c) -> p j c", c=128),
            chan="tx_xqT%d" % par, reads=[("xq",)], writes=wall, transpose=True)

    def b_back(tidx):
        t0 = tidx * 128
        par = tidx % 2
        xb, xbk, p4 = b_xb(tidx)
        for sb_i in range(3):
            sls = NSL[2 * sb_i:2 * sb_i + 2]
            sb0 = sls[0][0]
            sbw = sum(sn for _, sn in sls)
            for k in range(KC):
                lhs = xqT[:, 256 * k + 128 * par:256 * k + 128 * (par + 1)]
                for si, (so, sn) in enumerate(sls):
                    mm(pp[si][:, 0:sn], lhs, wgT[:, IFF * k + so:IFF * k + so + sn],
                       reads=[("xqT", k, par), ("wgT", k)], writes=[PPK[si]],
                       start=(k == 0), stop=(k == KC - 1))
                    mm(pp[2 + si][:, 0:sn], lhs, wuT[:, IFF * k + so:IFF * k + so + sn],
                       reads=[("xqT", k, par), ("wuT", k)], writes=[PPK[2 + si]],
                       start=(k == 0), stop=(k == KC - 1))
            for si, (so, sn) in enumerate(sls):
                vop(lambda e, si=si, so=so, sn=sn, sb0=sb0: e.activation(
                    out=h_sl[:, so - sb0:so - sb0 + sn], in_=pp[si][:, 0:sn],
                    func=AF.Silu),
                    reads=[PPK[si]], writes=[("h_sl",)], eng="a")
                vop(lambda e, si=si, so=so, sn=sn, sb0=sb0: e.tensor_tensor(
                    out=h_sl[:, so - sb0:so - sb0 + sn],
                    in0=h_sl[:, so - sb0:so - sb0 + sn],
                    in1=pp[2 + si][:, 0:sn], op=AluOpType.mult),
                    reads=[("h_sl",), PPK[2 + si]], writes=[("h_sl",)])
            emit_quant(("h_sl",), h_sl[:, 0:sbw], ("xq",), hq_bf[:, 0:sbw], sbw)
            wall = [("hqT", k2, par) for k2 in range(ICH)]
            nblk = sbw // 128
            dma(hqT[:, :].rearrange("p (j c) -> p j c", c=256)[
                :, sb0 // 128:sb0 // 128 + nblk, 128 * par:128 * (par + 1)],
                hq_bf[:, 0:sbw].rearrange("p (j c) -> p j c", c=128),
                chan="tx_hqT%d" % par, reads=[("xq",)], writes=wall, transpose=True)
        for n in range(2):
            pidx = 6 + n
            for k in range(ICH):
                mm(pp[pidx], hqT[:, 256 * k + 128 * par:256 * k + 128 * (par + 1)],
                   wdT[:, D * k + 512 * n:D * k + 512 * (n + 1)],
                   reads=[("hqT", k, par), ("wdT", k)], writes=[PPK[pidx]],
                   start=(k == 0), stop=(k == ICH - 1))
        gk = ("g_t",)
        for n in range(2):
            pidx = 6 + n
            vop(lambda e, n=n, pidx=pidx: e.activation(
                out=g_t[:, 512 * n:512 * (n + 1)], in_=pp[pidx][:], func=AF.Silu),
                reads=[PPK[pidx]], writes=[gk], eng="a")
        dma(sload[:], s_scr[t0:t0 + 128, :], chan="ch_sload%d" % (tidx % 2),
            reads=[("s_scr", tidx), ("sldb",)], writes=[("sldb",)], eng="g")
        vop(lambda e: e.tensor_tensor(out=x1_t[:, 0:D], in0=sload[:],
                                      in1=g_t[:], op=AluOpType.mult),
            reads=[("sldb",), gk], writes=[("x1_t",)], eng="g")
        vop(lambda e, xb=xb: e.tensor_tensor(out=x1_t[:, 0:D], in0=x1_t[:, 0:D],
                                             in1=xb, op=AluOpType.add),
            reads=[("x1_t",), xbk], writes=[("x1_t",)], eng="g")
        dma(x1_scr[t0:t0 + 128, :], x1_t[:, 0:D], chan="st_x1_%d" % (tidx % 2),
            reads=[("x1_t",)], writes=[("x1_scr", tidx)], eng="g")
        if debug:
            dma(dbg["dbg_x1"][t0:t0 + 128, :], x1_t[:, 0:D], chan="dbgc",
                reads=[("x1_t",)], writes=[])

    prev_t = None
    for tidx in range(TP // 128):
        b_front(tidx)
        if prev_t is not None:
            b_back(prev_t)
        prev_t = tidx
    b_back(prev_t)

    ms.phase()
    free_group("gB2")
    free_group("gB")

    xn1T = alloc("gD", "xn1T", (128, KC * TP), BF16)
    xnbf = alloc("gD", "xnbf_c", (128, D), BF16)
    stg_state.a = alloc("gD", "wstC", (128, D), F32)
    stg_state.b = alloc("gD", "wstC2", (128, D), F32)
    z_sb = alloc("gD", "z_sb", (128, NEXP * NTM), F32)
    alp_sb = alloc("gD", "alp_sb", (128, NEXP * NTM), F32)
    zm1 = alloc("gD", "zm1", (128, 1), F32)
    zm2 = alloc("gD", "zm2", (128, 1), F32)
    znm = alloc("gD", "znm", (128, 1), F32)
    zscr = alloc("gD", "zscr", (128, NEXP), F32)

    hbuf = alloc("gC2", "hbuf", (128, KC * TP), BF16)
    slg = alloc("gC2", "slg", (128, KC * T_MAIN), BF16)
    sqacc = alloc("gC2", "sqacc", (128, T_MAIN), F32)
    wo1_sb = alloc("gC2", "wo1_sb", (128, KC * D), BF16)
    gnw_sb = alloc("gC2", "gnw_sb", (128, KC), F32)

    wim = [alloc("gC1", "wim%d" % i, (128, 3 * D), BF16) for i in range(2)]
    wg1_sb = alloc("gC1", "wg1_sb", (128, KC * NEXP), BF16)
    conv_i_sb = alloc("gC1", "conv_i_sb", (128, KC * 4), F32)
    conv_f_sb = alloc("gC1", "conv_f_sb", (128, KC * 4), F32)
    iraw = alloc("gC1", "iraw", (128, TP), F32)
    fraw = alloc("gC1", "fraw", (128, TP), F32)
    ci_b = alloc("gC1", "ci_b", (128, TP), F32)
    cf_b = alloc("gC1", "cf_b", (128, TP), F32)

    cast_weight("Wo1", wo1_sb, "wo1")
    cast_weight("WgE", wg1_sb, "wg1", width=NEXP)
    dma(conv_i_sb[:], apin["conv_i"][:], chan="c_cvi", reads=[], writes=[("convi",)])
    dma(conv_f_sb[:], apin["conv_f"][:], chan="c_cvf", reads=[], writes=[("convf",)])
    dma(gnw_sb[:], apin["gnw"][:], chan="c_gnw", reads=[], writes=[("gnw",)])

    for go, gn in GRPC:
        for ti in range(gn // 128):
            t0 = go + ti * 128
            tidx = t0 // 128
            par = tidx % 2
            xb = xbuf2[:, D * par:D * (par + 1)]
            xbk = ("xbuf", par)
            dma(xb, x1_scr[t0:t0 + 128, :], chan="ch_x%d" % par,
                reads=[xbk, ("x1_scr", tidx)], writes=[xbk])
            emit_ln(xbk, xb, ln_ab, "ln1", tidx, [(("xnbf",), xnbf[:])])
            wall = [("xn1T", j2, tidx) for j2 in range(KC)]
            dma(xn1T[:, :].rearrange("p (j c) -> p j c", c=TP)[:, :, t0:t0 + 128],
                xnbf[:, :].rearrange("p (j c) -> p j c", c=128),
                chan="tx_xn1T", reads=[("xnbf",)], writes=wall, transpose=True)

    for tm in range(NTM):
        t0 = WARM + tm * 128
        zps = pp[3][:, 0:NEXP]
        mm(zps, ones1[:], bg_sb[0:1, 0:NEXP], reads=[("ones1",), ("b_bgE",)],
           writes=[PPK[3]], start=True, stop=False)
        for k in range(KC):
            mm(zps, xn1T[:, TP * k + t0:TP * k + t0 + 128],
               wg1_sb[:, NEXP * k:NEXP * (k + 1)],
               reads=[("xn1T", k, t0 // 128), ("wg1", k)], writes=[PPK[3]],
               start=False, stop=(k == KC - 1))
        zs = z_sb[:, NEXP * tm:NEXP * (tm + 1)]
        als = alp_sb[:, NEXP * tm:NEXP * (tm + 1)]
        vop(lambda e, zs=zs: e.activation(out=zs, in_=pp[3][:, 0:NEXP], func=AF.Copy),
            reads=[PPK[3]], writes=[("z", tm)], eng="a")
        vop(lambda e, zs=zs: e.tensor_reduce(out=zm1[:], in_=zs, axis=AX.X,
                                             op=AluOpType.max),
            reads=[("z", tm)], writes=[("zm1",)])
        vop(lambda e, zs=zs: e.tensor_scalar(out=zscr[:], in0=zs, scalar1=zm1[:, 0:1],
                                             scalar2=-1e30, op0=AluOpType.is_ge,
                                             op1=AluOpType.mult),
            reads=[("z", tm), ("zm1",)], writes=[("zscr",)])
        vop(lambda e, zs=zs: e.tensor_tensor(out=zscr[:], in0=zscr[:], in1=zs,
                                             op=AluOpType.add),
            reads=[("zscr",), ("z", tm)], writes=[("zscr",)])
        vop(lambda e: e.tensor_reduce(out=zm2[:], in_=zscr[:], axis=AX.X,
                                      op=AluOpType.max),
            reads=[("zscr",)], writes=[("zm2",)])
        vop(lambda e, zs=zs, als=als: e.tensor_scalar(out=als, in0=zs,
                                                      scalar1=zm2[:, 0:1], scalar2=None,
                                                      op0=AluOpType.is_ge),
            reads=[("z", tm), ("zm2",)], writes=[("alp", tm)])
        vop(lambda e: e.tensor_scalar(out=znm[:], in0=zm1[:], scalar1=-1.0,
                                      scalar2=None, op0=AluOpType.mult),
            reads=[("zm1",)], writes=[("znm",)])
        vop(lambda e, zs=zs: e.activation(out=zscr[:], in_=zs, func=AF.Exp,
                                          bias=znm[:, 0:1], scale=1.0),
            reads=[("z", tm), ("znm",)], writes=[("zscr",)], eng="a")
        vop(lambda e: e.tensor_tensor(out=zm2[:], in0=zm2[:], in1=zm1[:],
                                      op=AluOpType.subtract),
            reads=[("zm2",), ("zm1",)], writes=[("zm2",)])
        vop(lambda e: e.activation(out=zm2[:], in_=zm2[:], func=AF.Exp),
            reads=[("zm2",)], writes=[("zm2",)], eng="a")
        vop(lambda e: e.tensor_scalar(out=zm2[:], in0=zm2[:], scalar1=1.0,
                                      scalar2=None, op0=AluOpType.add),
            reads=[("zm2",)], writes=[("zm2",)])
        vop(lambda e: e.reciprocal(out=zm2[:], in_=zm2[:]),
            reads=[("zm2",)], writes=[("zm2",)])
        vop(lambda e, als=als: e.tensor_tensor(out=als, in0=als, in1=zscr[:],
                                               op=AluOpType.mult),
            reads=[("alp", tm), ("zscr",)], writes=[("alp", tm)])
        vop(lambda e, als=als: e.tensor_scalar_mul(out=als, in0=als,
                                                   scalar1=zm2[:, 0:1]),
            reads=[("alp", tm), ("zm2",)], writes=[("alpha", tm)])
        if debug:
            vop(lambda e, als=als: e.tensor_copy(out=zscr[:], in_=als),
                reads=[("alpha", tm)], writes=[("zscr",)])
            dma(dbg["dbg_alpha"][128 * tm:128 * (tm + 1), :], zscr[:],
                chan="dbgc", reads=[("zscr",)], writes=[])

    vop(lambda e: e.memset(sqacc[:], 0.0), reads=[], writes=[("sqacc",)])
    for m in range(KC):
        par = m % 2
        for wi_, wname in enumerate(["Wi", "Wf", "Wgr"]):
            stg, skey, ch = ((stg_state.a, ("wst",), "ch_wst") if (m * 3 + wi_) % 2 == 0
                             else (stg_state.b, ("wst2",), "ch_wst2"))
            srcap = apin[wname][:, 128 * m:128 * (m + 1)].rearrange(
                "(a p) c -> p a c", p=128)
            dma(stg[:, 0:D].rearrange("p (a c) -> p a c", c=128), srcap,
                chan=ch, reads=[skey], writes=[skey], eng="a")
            vop(lambda e, par=par, wi_=wi_, stg=stg: e.tensor_copy(
                out=wim[par][:, D * wi_:D * (wi_ + 1)], in_=stg[:, 0:D]),
                reads=[skey], writes=[("wim", par, wi_)])
        for gi, (go, gn) in enumerate(GRPC):
            rx = [("xn1T", k, (go // 128) + tt) for k in range(KC)
                  for tt in range(gn // 128)]
            for pse, psk, wi_ in [(0, PPK[0], 0), (1, PPK[1], 1)]:
                for k in range(KC):
                    mm(pp[pse][:, 0:gn], wim[par][:, D * wi_ + 128 * k:D * wi_ + 128 * (k + 1)],
                       xn1T[:, TP * k + go:TP * k + go + gn],
                       reads=[("wim", par, wi_)] + rx, writes=[psk],
                       start=(k == 0), stop=(k == KC - 1))
            vop(lambda e, go=go, gn=gn: e.activation(out=iraw[:, go:go + gn],
                                                     in_=pp[0][:, 0:gn], func=AF.Copy),
                reads=[PPK[0]], writes=[("iraw",)], eng="a")
            vop(lambda e, go=go, gn=gn: e.activation(out=fraw[:, go:go + gn],
                                                     in_=pp[1][:, 0:gn], func=AF.Copy),
                reads=[PPK[1]], writes=[("fraw",)], eng="a")
        for go, gn in GRPM:
            gow = WARM + go
            rx = [("xn1T", k, (gow // 128) + tt) for k in range(KC)
                  for tt in range(gn // 128)]
            for k in range(KC):
                mm(pp[2][:, 0:gn], wim[par][:, 2 * D + 128 * k:2 * D + 128 * (k + 1)],
                   xn1T[:, TP * k + gow:TP * k + gow + gn],
                   reads=[("wim", par, 2)] + rx, writes=[PPK[2]],
                   start=(k == 0), stop=(k == KC - 1))
            vop(lambda e, m=m, go=go, gn=gn: e.activation(
                out=slg[:, T_MAIN * m + go:T_MAIN * m + go + gn],
                in_=pp[2][:, 0:gn], func=AF.Silu),
                reads=[PPK[2]], writes=[("slg", m)], eng="a")
        vop(lambda e: e.tensor_scalar_mul(out=iraw[:, 0:WARM], in0=iraw[:, 0:WARM],
                                          scalar1=mwarm_sb[:, 0:1]),
            reads=[("iraw",), ("mwarm",)], writes=[("iraw",)])
        vop(lambda e: e.tensor_scalar_mul(out=fraw[:, 0:WARM], in0=fraw[:, 0:WARM],
                                          scalar1=mwarm_sb[:, 0:1]),
            reads=[("fraw",), ("mwarm",)], writes=[("fraw",)])
        for src, skey, cout, ckey, csb, cwkey in [
                (iraw, ("iraw",), ci_b, ("ci",), conv_i_sb, ("convi",)),
                (fraw, ("fraw",), cf_b, ("cf",), conv_f_sb, ("convf",))]:
            vop(lambda e, cout=cout: e.memset(cout[:, 0:3], 0.0),
                reads=[], writes=[ckey])
            vop(lambda e, src=src, cout=cout, csb=csb, m=m: e.tensor_scalar_mul(
                out=cout[:, 3:TP], in0=src[:, 3:TP],
                scalar1=csb[:, 4 * m + 3:4 * m + 4]),
                reads=[skey, cwkey], writes=[ckey])
            for tap in range(3):
                vop(lambda e, src=src, cout=cout, csb=csb, m=m, tap=tap:
                    e.scalar_tensor_tensor(out=cout[:, 3:TP], in0=src[:, tap:TP - 3 + tap],
                                           scalar=csb[:, 4 * m + tap:4 * m + tap + 1],
                                           in1=cout[:, 3:TP],
                                           op0=AluOpType.mult, op1=AluOpType.add),
                    reads=[skey, cwkey, ckey], writes=[ckey])
        vop(lambda e: e.activation(out=cf_b[:], in_=cf_b[:], func=AF.Sigmoid),
            reads=[("cf",)], writes=[("cf",)], eng="a")
        vop(lambda e: e.activation(out=ci_b[:], in_=ci_b[:], func=AF.Silu),
            reads=[("ci",)], writes=[("ci",)], eng="a")
        vop(lambda e: e.tensor_scalar(out=iraw[:], in0=cf_b[:], scalar1=-1.0,
                                      scalar2=1.0, op0=AluOpType.mult, op1=AluOpType.add),
            reads=[("cf",)], writes=[("iraw",)])
        vop(lambda e: e.tensor_tensor(out=ci_b[:], in0=ci_b[:], in1=iraw[:],
                                      op=AluOpType.mult),
            reads=[("ci",), ("iraw",)], writes=[("ci",)])
        for so, sn in _split(TP, 544):
            init = 0.0 if so == 0 else hbuf[:, TP * m + so - 1:TP * m + so]
            vop(lambda e, m=m, so=so, sn=sn, init=init: e.tensor_tensor_scan(
                out=hbuf[:, TP * m + so:TP * m + so + sn],
                data0=cf_b[:, so:so + sn], data1=ci_b[:, so:so + sn],
                initial=init, op0=AluOpType.mult, op1=AluOpType.add),
                reads=[("cf",), ("ci",), ("h", m)], writes=[("h", m)])
        vop(lambda e, m=m: e.tensor_tensor(out=fraw[:, 0:T_MAIN],
                                           in0=hbuf[:, TP * m + WARM:TP * (m + 1)],
                                           in1=hbuf[:, TP * m + WARM:TP * (m + 1)],
                                           op=AluOpType.mult),
            reads=[("h", m), ("fraw",)], writes=[("fraw",)], eng="g")
        vop(lambda e: e.tensor_tensor(out=sqacc[:], in0=sqacc[:], in1=fraw[:, 0:T_MAIN],
                                      op=AluOpType.add),
            reads=[("sqacc",), ("fraw",)], writes=[("sqacc",)], eng="g")
        if debug and m == 0:
            vop(lambda e: e.tensor_copy(out=ci_b[:, 0:T_MAIN],
                                        in_=hbuf[:, WARM:TP]),
                reads=[("h", 0), ("ci",)], writes=[("ci",)])
            dma(dbg["dbg_h"][:, WARM:TP], ci_b[:, 0:T_MAIN], chan="dbgc",
                reads=[("ci",)], writes=[])

    ms.phase()
    free_group("gC1")

    rms_b = alloc("gC2b", "rms_b", (128, T_MAIN), F32)
    oT = alloc("gC2b", "oT", (128, KC * T_MAIN), BF16)
    s1_t = alloc("gC2b", "s1_t", (128, D), F32)
    sqtmp = alloc("gC2b", "sqtmp", (128, T_MAIN), F32)

    for gi, (go, gn) in enumerate(GRPM):
        mm(pp[4][0:1, 0:gn], onesf[:], sqacc[:, go:go + gn],
           reads=[("onesf",), ("sqacc",)], writes=[PPK[4]], start=True, stop=True)
        vop(lambda e, go=go, gn=gn: e.tensor_scalar(out=rms_b[0:1, go:go + gn],
                                                    in0=pp[4][0:1, 0:gn],
                                                    scalar1=1.0 / D, scalar2=1e-5,
                                                    op0=AluOpType.mult, op1=AluOpType.add),
            reads=[PPK[4]], writes=[("rms", 0)])
        vop(lambda e, go=go, gn=gn: e.activation(out=rms_b[0:1, go:go + gn],
                                                 in_=rms_b[0:1, go:go + gn], func=AF.Sqrt),
            reads=[("rms", 0)], writes=[("rms", 0)], eng="a")
        vop(lambda e, go=go, gn=gn: e.reciprocal(out=rms_b[0:1, go:go + gn],
                                                 in_=rms_b[0:1, go:go + gn]),
            reads=[("rms", 0)], writes=[("rms", 0)])
    kk = 1
    while kk < 128:
        dma(rms_b[kk:2 * kk, :], rms_b[0:kk, :], chan="ch_rms%d" % kk,
            reads=[("rms", kk // 2) if kk > 1 else ("rms", 0)], writes=[("rms", kk)])
        kk *= 2
    RMSD = ("rms", 64)

    for m in range(KC):
        vop(lambda e, m=m: e.tensor_tensor(out=sqtmp[:],
                                           in0=hbuf[:, TP * m + WARM:TP * (m + 1)],
                                           in1=rms_b[:], op=AluOpType.mult),
            reads=[("h", m), RMSD], writes=[("sqtmp",)])
        vop(lambda e, m=m: e.tensor_tensor(out=sqtmp[:], in0=sqtmp[:],
                                           in1=slg[:, T_MAIN * m:T_MAIN * (m + 1)],
                                           op=AluOpType.mult),
            reads=[("sqtmp",), ("slg", m)], writes=[("sqtmp",)])
        vop(lambda e, m=m: e.tensor_scalar(out=oT[:, T_MAIN * m:T_MAIN * (m + 1)],
                                           in0=sqtmp[:],
                                           scalar1=gnw_sb[:, m:m + 1], scalar2=None,
                                           op0=AluOpType.mult),
            reads=[("sqtmp",), ("gnw",)], writes=[("oT", m)])
    for tm in range(NTM):
        for n in range(2):
            pidx = 5 + n
            for k in range(KC):
                mm(pp[pidx], oT[:, T_MAIN * k + 128 * tm:T_MAIN * k + 128 * (tm + 1)],
                   wo1_sb[:, D * k + 512 * n:D * k + 512 * (n + 1)],
                   reads=[("oT", k), ("wo1", k)], writes=[PPK[pidx]],
                   start=(k == 0), stop=(k == KC - 1))
        for n in range(2):
            pidx = 5 + n
            vop(lambda e, n=n, pidx=pidx: e.activation(
                out=s1_t[:, 512 * n:512 * (n + 1)], in_=pp[pidx][:], func=AF.Copy),
                reads=[PPK[pidx]], writes=[("s1_t",)], eng="a")
        dma(s1_scr[128 * tm:128 * (tm + 1), :], s1_t[:], chan="st_s1",
            reads=[("s1_t",)], writes=[("s1_scr", tm)])

    ms.phase()
    free_group("gC2b")
    free_group("gC2")

    weE_sb = [alloc("gDD", "weE%d" % i, (128, KC * D), BF16) for i in range(2)]
    be_sb = alloc("gDD", "be_sb", (1, 2 * D), BF16)
    acc = alloc("gDD", "acc", (128, NTM * D), F32)
    g1_t = alloc("gDD", "g1_t", (128, D), F32)
    y_t = alloc("gDD", "y_t", (128, D), F32)
    s1l = alloc("gDD", "s1l", (128, 2 * D), F32)

    for e_i in range(NEXP):
        sl = e_i % 2
        for k in range(KC):
            stg, skey, ch = ((stg_state.a, ("wst",), "ch_wst") if k % 2 == 0
                             else (stg_state.b, ("wst2",), "ch_wst2"))
            dma(stg[:, 0:D], apin["WeE"][D * e_i + 128 * k:D * e_i + 128 * (k + 1), :],
                chan=ch, reads=[skey], writes=[skey], eng="a")
            vop(lambda e, sl=sl, k=k, stg=stg: e.tensor_copy(
                out=weE_sb[sl][:, D * k:D * (k + 1)], in_=stg[:, 0:D]),
                reads=[skey], writes=[("weE", sl, k)])
        dma(stg_state.a[0:1, 0:D], apin["beE"][e_i:e_i + 1, :], chan="ch_wst",
            reads=[("wst",)], writes=[("wst",)])
        vop(lambda e, sl=sl, stg=stg_state.a: e.tensor_copy(
            out=be_sb[0:1, D * sl:D * (sl + 1)], in_=stg[0:1, 0:D]),
            reads=[("wst",)], writes=[("beE", sl)])
        pb = 2 * (e_i % 3)
        for tm in range(NTM):
            t0 = WARM + 128 * tm
            for n in range(2):
                mm(pp[pb + n], ones1[:], be_sb[0:1, D * sl + 512 * n:D * sl + 512 * (n + 1)],
                   reads=[("ones1",), ("beE", sl)], writes=[PPK[pb + n]],
                   start=True, stop=False)
                for k in range(KC):
                    mm(pp[pb + n], xn1T[:, TP * k + t0:TP * k + t0 + 128],
                       weE_sb[sl][:, D * k + 512 * n:D * k + 512 * (n + 1)],
                       reads=[("xn1T", k, t0 // 128), ("weE", sl, k)],
                       writes=[PPK[pb + n]], start=False, stop=(k == KC - 1))
            for n in range(2):
                pidx = pb + n
                alc = alp_sb[:, NEXP * tm + e_i:NEXP * tm + e_i + 1]
                if e_i == 0:
                    vop(lambda e, tm=tm, n=n, alc=alc, pidx=pidx: e.tensor_scalar(
                        out=acc[:, D * tm + 512 * n:D * tm + 512 * (n + 1)],
                        in0=pp[pidx][:], scalar1=alc, scalar2=None,
                        op0=AluOpType.mult),
                        reads=[PPK[pidx], ("alpha", tm)], writes=[("acc", tm)])
                else:
                    vop(lambda e, tm=tm, n=n, alc=alc, pidx=pidx: e.scalar_tensor_tensor(
                        out=acc[:, D * tm + 512 * n:D * tm + 512 * (n + 1)],
                        in0=pp[pidx][:], scalar=alc,
                        in1=acc[:, D * tm + 512 * n:D * tm + 512 * (n + 1)],
                        op0=AluOpType.mult, op1=AluOpType.add),
                        reads=[PPK[pidx], ("alpha", tm), ("acc", tm)],
                        writes=[("acc", tm)])

    for tm in range(NTM):
        t0 = WARM + 128 * tm
        par = tm % 2
        xb = xbuf2[:, D * par:D * (par + 1)]
        xbk = ("xbuf", par)
        dma(xb, x1_scr[t0:t0 + 128, :], chan="ch_x%d" % par,
            reads=[xbk, ("x1_scr", t0 // 128)], writes=[xbk])
        s1v = s1l[:, D * par:D * (par + 1)]
        dma(s1v, s1_scr[128 * tm:128 * (tm + 1), :], chan="ch_s1l%d" % par,
            reads=[("s1l", par), ("s1_scr", tm)], writes=[("s1l", par)])
        vop(lambda e, tm=tm: e.activation(out=g1_t[:], in_=acc[:, D * tm:D * (tm + 1)],
                                          func=AF.Silu),
            reads=[("acc", tm)], writes=[("g1_t",)], eng="a")
        if debug:
            dma(dbg["dbg_moe"][128 * tm:128 * (tm + 1), :],
                acc[:, D * tm:D * (tm + 1)], chan="dbgc",
                reads=[("acc", tm)], writes=[])
        vop(lambda e, s1v=s1v: e.tensor_tensor(out=g1_t[:], in0=g1_t[:], in1=s1v,
                                      op=AluOpType.mult),
            reads=[("g1_t",), ("s1l", par)], writes=[("g1_t",)])
        vop(lambda e, xb=xb: e.tensor_tensor(out=y_t[:], in0=g1_t[:], in1=xb,
                                             op=AluOpType.add),
            reads=[("g1_t",), xbk], writes=[("y_t",)], eng="g")
        dma(yout[128 * tm:128 * (tm + 1), :], y_t[:], chan="st_y%d" % par,
            reads=[("y_t",)], writes=[("yout", tm)], eng="g")

    ms.op("sy", lambda e: e.nop(), reads=[("yout", tm) for tm in range(NTM)], writes=[])

    with nc.Block() as block:
        ms.finalize(block)

    return nc, apin



def _host_stage(inputs, T_MAIN, L):
    x = np.asarray(inputs["L0_attn_x"], dtype=np.float32)
    B = x.shape[0]
    idx = np.linspace(0.0, L - 1, LEN).astype(np.int32)
    conv_i = np.ascontiguousarray(
        np.asarray(inputs["L1_attn_conv_i"], np.float32).reshape(KC, 128, 4)
        .transpose(1, 0, 2).reshape(128, KC * 4))
    conv_f = np.ascontiguousarray(
        np.asarray(inputs["L1_attn_conv_f"], np.float32).reshape(KC, 128, 4)
        .transpose(1, 0, 2).reshape(128, KC * 4))
    gnw = np.ascontiguousarray(
        np.asarray(inputs["L1_attn_g_norm_weight"], np.float32).reshape(KC, 128).T)
    shared = {
        "Wq": np.asarray(inputs["L0_attn_Wq"], np.float32),
        "Wk": np.asarray(inputs["L0_attn_Wk"], np.float32),
        "Wo0": np.asarray(inputs["L0_attn_Wo"], np.float32),
        "bq": np.asarray(inputs["L0_attn_bq"], np.float32).reshape(1, D),
        "bk": np.asarray(inputs["L0_attn_bk"], np.float32).reshape(1, D),
        "bo": np.asarray(inputs["L0_attn_bo"], np.float32).reshape(1, D),
        "WeL": np.asarray(inputs["L0_attn_We"], np.float32),
        "WrL": np.asarray(inputs["L0_attn_Wr"], np.float32),
        "Wc": np.asarray(inputs["L0_attn_Wc"], np.float32),
        "bc": np.tile(np.asarray(inputs["L0_attn_bc"], np.float32), 2).reshape(128, 1),
        "wg_nat": np.asarray(inputs["L0_ffn_w_gate"], np.float32),
        "wu_nat": np.asarray(inputs["L0_ffn_w_up"], np.float32),
        "wd_nat": np.asarray(inputs["L0_ffn_w_down"], np.float32),
        "Wi": np.asarray(inputs["L1_attn_Wi"], np.float32),
        "Wf": np.asarray(inputs["L1_attn_Wf"], np.float32),
        "Wgr": np.asarray(inputs["L1_attn_Wg"], np.float32),
        "Wo1": np.asarray(inputs["L1_attn_Wo"], np.float32),
        "conv_i": conv_i, "conv_f": conv_f, "gnw": gnw,
        "WeE": np.asarray(inputs["L1_ffn_We"], np.float32).reshape(NEXP * D, D),
        "beE": np.asarray(inputs["L1_ffn_be"], np.float32),
        "WgE": np.asarray(inputs["L1_ffn_Wg"], np.float32),
        "bgE": np.asarray(inputs["L1_ffn_bg"], np.float32).reshape(1, NEXP),
    }
    in_maps = []
    per_b = L // T_MAIN
    for c in range(B * per_b):
        b, pos = c // per_b, c % per_b
        t0 = pos * T_MAIN
        if pos == 0:
            xin = np.concatenate([x[b, 0:WARM], x[b, t0:t0 + T_MAIN]], axis=0)
            mw = np.zeros((128, 1), np.float32)
        else:
            xin = x[b, t0 - WARM:t0 + T_MAIN]
            mw = np.ones((128, 1), np.float32)
        m = dict(shared)
        m["xin"] = np.ascontiguousarray(xin)
        m["xg"] = np.ascontiguousarray(x[b, idx, :])
        m["mwarm"] = mw
        in_maps.append(m)
    return in_maps, B, per_b


def kernel(**inputs):
    T_MAIN, L = 2048, 8192
    in_maps, B, per_b = _host_stage(inputs, T_MAIN, L)
    nc, apin = build_program(T_MAIN)
    res = run_bass_kernel_spmd(nc, in_maps, core_ids=list(range(B * per_b)))
    shards = [np.asarray(r["y"]) for r in res.results]
    return np.stack([np.concatenate(shards[b * per_b:(b + 1) * per_b], axis=0)
                     for b in range(B)], axis=0)


# version 33
# speedup vs baseline: 1.0190x; 1.0162x over previous
import numpy as np
import ml_dtypes
import concourse.bass as bass
import concourse.mybir as mybir
from concourse.alu_op_type import AluOpType
from concourse.bass_utils import run_bass_kernel_spmd

F32 = mybir.dt.float32
BF16 = mybir.dt.bfloat16
I32 = mybir.dt.int32
AF = mybir.ActivationFunctionType
AX = mybir.AxisListType

D = 1024
KC = D // 128
H = 16
DKH = 64
LOWR = 20
LEN = 200
IFF = 2816
ICH = IFF // 128
NEXP = 8
WARM = 128
BF16NP = ml_dtypes.bfloat16
ENGINES = ("sy", "v", "a", "g", "t")


def _split(total, step):
    out, o = [], 0
    while o < total:
        n = min(step, total - o)
        out.append((o, n))
        o += n
    return out



class Op:
    __slots__ = ("eng", "fn", "idx", "pos", "is_dma", "chan", "chan_val",
                 "need_inc", "inc_seq", "waits", "drain")

    def __init__(self, eng, fn, is_dma, chan):
        self.eng = eng
        self.fn = fn
        self.is_dma = is_dma
        self.chan = chan
        self.chan_val = 0
        self.need_inc = False
        self.inc_seq = 0
        self.waits = []
        self.drain = False


class MS:
    def __init__(self, nc):
        self.nc = nc
        self.ops = []
        self.by_eng = {e: [] for e in ENGINES}
        self.res = {}
        self.chan_tot = {}
        self.sems = {}
        self.barrier_fns = {}

    def _edge(self, p, c, raw=True):
        if p is c:
            return
        if p.is_dma:
            if c.is_dma and c.chan == p.chan:
                return
            c.waits.append(("ch", p.chan, p.chan_val))
        elif p.eng == c.eng:
            if (raw and p.eng in ("v", "a", "g")) or (not raw and p.eng == "a"):
                p.need_inc = True
                c.waits.append(("op", p))
        else:
            p.need_inc = True
            c.waits.append(("op", p))

    def op(self, eng, fn, reads=(), writes=(), dma_chan=None):
        o = Op(eng, fn, dma_chan is not None, dma_chan)
        if o.is_dma:
            self.chan_tot[o.chan] = self.chan_tot.get(o.chan, 0) + 16
            o.chan_val = self.chan_tot[o.chan]
        for r in reads:
            s = self.res.get(r)
            if s and s["w"] is not None:
                self._edge(s["w"], o)
        for r in writes:
            s = self.res.get(r)
            if s:
                if s["w"] is not None:
                    self._edge(s["w"], o, raw=False)
                for rd in s["r"]:
                    self._edge(rd, o, raw=False)

            self.res[r] = {"w": o, "r": []}
        for r in reads:
            self.res.setdefault(r, {"w": None, "r": []})["r"].append(o)
        o.idx = len(self.ops)
        self.ops.append(o)
        o.pos = len(self.by_eng[o.eng])
        self.by_eng[o.eng].append(o)
        return o

    def phase(self):
        lasts = []
        for e in ENGINES:
            for o in reversed(self.by_eng[e]):
                if not o.is_dma:
                    o.need_inc = True
                    lasts.append(o)
                    break
        chans = dict(self.chan_tot)
        for e in ENGINES:
            bf = self.barrier_fns.get(e)
            o = Op(e, bf() if bf else (lambda en: en.nop()), False, None)
            for p in lasts:
                if p.eng != e:
                    o.waits.append(("op", p))
            for ch, v in chans.items():
                o.waits.append(("ch", ch, v))
            o.idx = len(self.ops)
            self.ops.append(o)
            o.pos = len(self.by_eng[e])
            self.by_eng[e].append(o)
        self.res = {}

    def finalize(self, block):
        inc_cnt = {e: 0 for e in ENGINES}
        for o in self.ops:
            if o.need_inc and not o.is_dma:
                inc_cnt[o.eng] += 1
                o.inc_seq = inc_cnt[o.eng]
        for e in ENGINES:
            if inc_cnt[e]:
                self.sems["eng:" + e] = self.nc.semaphore("sm_" + e).__enter__()
        for ch in self.chan_tot:
            self.sems["ch:" + ch] = self.nc.semaphore("sc_" + ch).__enter__()
        ms = self

        def mk(eng_name):
            def inner(e):
                waited = {}
                for o in ms.by_eng[eng_name]:
                    wl = {}
                    for w in o.waits:
                        if w[0] == "ch":
                            key, val = "ch:" + w[1], w[2]
                        else:
                            key, val = "eng:" + w[1].eng, w[1].inc_seq
                        wl[key] = max(wl.get(key, 0), val)
                    for key, val in sorted(wl.items()):
                        if waited.get(key, -1) < val:
                            e.wait_ge(ms.sems[key], val)
                            waited[key] = val
                    if o.drain:
                        e.drain()
                    inst = o.fn(e)
                    if o.is_dma:
                        inst.then_inc(ms.sems["ch:" + o.chan], 16)
                    elif o.need_inc:
                        if inst is None:
                            e.sem_inc(ms.sems["eng:" + o.eng], 1)
                        else:
                            inst.then_inc(ms.sems["eng:" + o.eng], 1)
            return inner

        for name, deco in [("sy", block.sync), ("v", block.vector),
                           ("a", block.scalar), ("g", block.gpsimd),
                           ("t", block.tensor)]:
            if self.by_eng[name]:
                deco(mk(name))


class P:
    pass



def build_program(T_MAIN, debug=False):
    TP = WARM + T_MAIN
    NT = TP // 128
    NTM = T_MAIN // 128
    GRA = _split(TP, 512)
    GRP = _split(TP, 256)
    GRPC = _split(TP, 512)
    GRPM = _split(T_MAIN, 512)

    nc = bass.Bass("TRN2", target_bir_lowering=False)
    ms = MS(nc)

    apin = {}

    def din(name, shape, dtype=F32):
        apin[name] = nc.dram_tensor(name, list(shape), dtype, kind="ExternalInput").ap()

    din("xin", (TP, D)); din("xg", (LEN, D)); din("mwarm", (128, 1))
    din("Wq", (D, D)); din("Wk", (D, D)); din("Wo0", (D, D))
    din("bq", (1, D)); din("bk", (1, D)); din("bo", (1, D))
    din("WeL", (H, LEN, LOWR)); din("WrL", (H, LEN, LOWR))
    din("Wc", (2 * LOWR, DKH)); din("bc", (128, 1))
    din("wg_nat", (IFF, D)); din("wu_nat", (IFF, D)); din("wd_nat", (D, IFF))
    din("Wi", (D, D)); din("Wf", (D, D)); din("Wgr", (D, D)); din("Wo1", (D, D))
    din("conv_i", (128, KC * 4)); din("conv_f", (128, KC * 4))
    din("gnw", (128, KC))
    din("WeE", (NEXP * D, D)); din("beE", (NEXP, D))
    din("WgE", (D, NEXP)); din("bgE", (1, NEXP))

    yout = nc.dram_tensor("y", [T_MAIN, D], F32, kind="ExternalOutput").ap()
    s_scr = nc.dram_tensor("s_scr", [TP, D], F32, kind="Internal").ap()
    x1_scr = nc.dram_tensor("x1_scr", [TP, D], F32, kind="Internal").ap()
    s1_scr = nc.dram_tensor("s1_scr", [T_MAIN, D], F32, kind="Internal").ap()
    wgu_scr = nc.dram_tensor("wgu_scr", [2 * ICH * 128, KC * 128], BF16, kind="Internal").ap()
    wd_scr = nc.dram_tensor("wd_scr", [KC * 128, ICH * 128], BF16, kind="Internal").ap()
    dbg = {}
    if debug:
        for nm, shp in [("dbg_s", (TP, D)), ("dbg_x1", (TP, D)),
                        ("dbg_h", (128, TP)), ("dbg_moe", (T_MAIN, D)),
                        ("dbg_alpha", (T_MAIN, NEXP))]:
            dbg[nm] = nc.dram_tensor(nm, list(shp), F32, kind="ExternalOutput").ap()

    groups = {}

    def alloc(group, name, shape, dtype=F32):
        g = nc.sbuf_tensor(name, list(shape), dtype)
        hh = g.__enter__()
        groups.setdefault(group, []).append(g)
        return hh

    def free_group(group):
        for g in reversed(groups.pop(group, [])):
            g.__exit__(None, None, None)

    pp = []
    for i in range(8):
        g = nc.psum_tensor("pp%d" % i, [128, 512], F32)
        pp.append(g.__enter__())
    PPK = [("pp", i) for i in range(8)]

    ones1 = alloc("főr", "ones1", (1, 128), BF16)
    onesf = alloc("főr", "onesf", (128, 1), F32)
    mwarm_sb = alloc("főr", "mwarm_sb", (128, 1), F32)
    bq_sb = alloc("főr", "bq_sb", (1, D), BF16)
    bk_sb = alloc("főr", "bk_sb", (1, D), BF16)
    bo_sb = alloc("főr", "bo_sb", (1, D), BF16)
    bg_sb = alloc("főr", "bg_sb", (1, NEXP), BF16)
    bc_sb = alloc("főr", "bc_sb", (128, 1), F32)
    ln_ab = alloc("főr", "ln_ab", (128, 2 * (NT + 2)), F32)
    xbuf2 = alloc("főr", "xbuf2", (128, 2 * D), F32)
    bscr = alloc("főr", "bscr", (1, 8), F32)
    _bar_i = [0]

    def _g_barrier_fn():
        i = _bar_i[0]
        _bar_i[0] += 1
        return lambda en, i=i: en.memset(bscr[0:1, i:i + 1], 0.0)

    ms.barrier_fns["g"] = _g_barrier_fn

    stg_state = P()
    lnv = P()
    lnv.bn6 = alloc("főr", "ln_bn6", (128, 12), F32)
    lnv.mv = alloc("főr", "ln_mv", (128, 2), F32)
    lnv.var = alloc("főr", "ln_var", (128, 1), F32)

    qb = P()
    QW = 1024
    qb.amax = alloc("gB", "q_amax", (128, QW // 16), F32)
    qb.scale = alloc("gB", "q_scale", (128, QW // 16), F32)
    qb.v = alloc("gB", "q_v", (128, QW), F32)
    qb.f1 = alloc("gB", "q_f1", (128, QW), F32)
    qb.g = alloc("gB", "q_g", (128, QW), F32)
    qb.sg = alloc("gB", "q_sg", (128, QW), BF16)

    stg_state.a = alloc("gA", "wstW", (128, D), F32)
    stg_state.b = alloc("gA", "wstW2", (128, D), F32)
    wq_sb = alloc("gA", "wq_sb", (128, KC * D), BF16)
    wk_sb = alloc("gA", "wk_sb", (128, KC * D), BF16)
    wo_sb = alloc("gA", "wo_sb", (128, KC * D), BF16)
    ablk = alloc("gA", "ablk", (128, KC * 128), BF16)
    bblk = alloc("gA", "bblk", (128, KC * 128), BF16)

    def dma(dst_ap, src_ap, chan, reads, writes, transpose=False, eng="sy"):
        if transpose:
            ms.op(eng, lambda e, d=dst_ap, s=src_ap: e.dma_start_transpose(d, s),
                  reads=reads, writes=writes, dma_chan=chan)
        else:
            ms.op(eng, lambda e, d=dst_ap, s=src_ap: e.dma_start(out=d, in_=s),
                  reads=reads, writes=writes, dma_chan=chan)

    def vop(fn, reads, writes, eng="v"):
        ms.op(eng, fn, reads=reads, writes=writes)

    def mm(ps_ap, lhs_ap, rhs_ap, reads, writes, start, stop, tile_position=None):
        if not isinstance(ps_ap, bass.AP):
            ps_ap = ps_ap[:]
        if not isinstance(lhs_ap, bass.AP):
            lhs_ap = lhs_ap[:]
        if not isinstance(rhs_ap, bass.AP):
            rhs_ap = rhs_ap[:]
        ms.op("t", lambda e, o=ps_ap, l=lhs_ap, r=rhs_ap: e.matmul(
            o, l, r, start=start, stop=stop, tile_position=tile_position),
            reads=reads, writes=writes)

    def cast_weight(srcname, dst_sb, dstkey, nchunks=KC, width=D):
        for k in range(nchunks):
            stg, skey, ch = ((stg_state.a, ("wst",), "ch_wst") if k % 2 == 0
                             else (stg_state.b, ("wst2",), "ch_wst2"))
            dma(stg[:, 0:width], apin[srcname][128 * k:128 * (k + 1), :],
                chan=ch, reads=[skey], writes=[skey], eng="a")
            vop(lambda e, k=k, stg=stg: e.tensor_copy(
                out=dst_sb[:, width * k:width * (k + 1)], in_=stg[:, 0:width]),
                reads=[skey], writes=[(dstkey, k)])

    def emit_quant_p1(x_key, x_ap, F, P_, q, vkey, v_ap, sgkey, sg_ap):
        vop(lambda e: e.activation(out=v_ap, in_=x_ap, func=AF.Abs),
            reads=[x_key], writes=[vkey], eng="a")
        vop(lambda e: e.activation(out=sg_ap, in_=x_ap, func=AF.Sign),
            reads=[x_key], writes=[sgkey], eng="a")

    def emit_quant_p2(out_key, out_ap, F, P_, q, vkey, v_ap, sgkey, sg_ap):
        nb = F // 16
        vop(lambda e: e.tensor_reduce(out=q.amax[0:P_, 0:nb],
                                      in_=v_ap.rearrange("p (b w) -> p b w", w=16),
                                      axis=AX.X, op=AluOpType.max),
            reads=[vkey], writes=[("q.amax",)])
        vop(lambda e: e.tensor_scalar(out=q.scale[0:P_, 0:nb], in0=q.amax[0:P_, 0:nb],
                                      scalar1=1.0 / 6.0, scalar2=1e-12,
                                      op0=AluOpType.mult, op1=AluOpType.max),
            reads=[("q.amax",)], writes=[("q.scale",)])
        vop(lambda e: e.reciprocal(out=q.amax[0:P_, 0:nb], in_=q.scale[0:P_, 0:nb]),
            reads=[("q.scale",)], writes=[("q.amax",)])
        rb = q.amax[0:P_, 0:nb].unsqueeze(2).broadcast_to([P_, nb, 16])
        vop(lambda e: e.tensor_tensor(out=v_ap.rearrange("p (b w) -> p b w", w=16),
                                      in0=v_ap.rearrange("p (b w) -> p b w", w=16),
                                      in1=rb, op=AluOpType.mult),
            reads=[vkey, ("q.amax",)], writes=[vkey])
        vop(lambda e: e.tensor_scalar(out=q.g[0:P_, 0:F], in0=v_ap,
                                      scalar1=5.0, scalar2=2.0,
                                      op0=AluOpType.is_gt, op1=AluOpType.mult),
            reads=[vkey], writes=[("q.g",)])
        for thr in (2.5, 3.5):
            vop(lambda e, thr=thr: e.scalar_tensor_tensor(
                out=q.g[0:P_, 0:F], in0=v_ap, scalar=thr,
                in1=q.g[0:P_, 0:F], op0=AluOpType.is_gt, op1=AluOpType.add),
                reads=[vkey, ("q.g",)], writes=[("q.g",)])
        vop(lambda e: e.tensor_scalar(out=q.f1[0:P_, 0:F], in0=v_ap,
                                      scalar1=0.25, scalar2=None, op0=AluOpType.is_gt),
            reads=[vkey], writes=[("q.f1",)])
        for thr in (0.75, 1.25, 1.75):
            vop(lambda e, thr=thr: e.scalar_tensor_tensor(
                out=q.f1[0:P_, 0:F], in0=v_ap, scalar=thr,
                in1=q.f1[0:P_, 0:F], op0=AluOpType.is_gt, op1=AluOpType.add),
                reads=[vkey, ("q.f1",)], writes=[("q.f1",)])
        vop(lambda e: e.scalar_tensor_tensor(out=q.g[0:P_, 0:F], in0=q.f1[0:P_, 0:F],
                                             scalar=0.5, in1=q.g[0:P_, 0:F],
                                             op0=AluOpType.mult, op1=AluOpType.add),
            reads=[("q.f1",), ("q.g",)], writes=[("q.g",)])
        vop(lambda e: e.tensor_tensor(out=q.g[0:P_, 0:F], in0=q.g[0:P_, 0:F],
                                      in1=sg_ap, op=AluOpType.mult),
            reads=[("q.g",), sgkey], writes=[("q.g",)])
        sb2 = q.scale[0:P_, 0:nb].unsqueeze(2).broadcast_to([P_, nb, 16])
        vop(lambda e: e.tensor_tensor(out=out_ap.rearrange("p (b w) -> p b w", w=16),
                                      in0=q.g[0:P_, 0:F].rearrange("p (b w) -> p b w", w=16),
                                      in1=sb2, op=AluOpType.mult),
            reads=[("q.g",), ("q.scale",)], writes=[out_key])

    def emit_quant(x_key, x_ap, out_key, out_ap, F, P_=128, q=None):
        if q is None:
            q = qb
        vk, sk = ("q.v",), ("q.sg",)
        emit_quant_p1(x_key, x_ap, F, P_, q, vk, q.v[0:P_, 0:F], sk, q.sg[0:P_, 0:F])
        emit_quant_p2(out_key, out_ap, F, P_, q, vk, q.v[0:P_, 0:F], sk, q.sg[0:P_, 0:F])

    def emit_ln(x_key, x_ap, ab_sb, ab_key, slot, outs, P_=128,
                scr=None, scr_key=None):
        a_ap = ab_sb[0:P_, 2 * slot:2 * slot + 1]
        b_ap = ab_sb[0:P_, 2 * slot + 1:2 * slot + 2]
        vop(lambda e: e.bn_stats(out=lnv.bn6[0:P_, 0:6], in_=x_ap[:, 0:D // 2]),
            reads=[x_key], writes=[("ln.bn6",)])
        vop(lambda e: e.bn_stats(out=lnv.bn6[0:P_, 6:12], in_=x_ap[:, D // 2:D]),
            reads=[x_key], writes=[("ln.bn6",)])
        vop(lambda e: e.bn_aggr(out=lnv.mv[0:P_, :], in_=lnv.bn6[0:P_, 0:12]),
            reads=[("ln.bn6",)], writes=[("ln.mv",)])
        vop(lambda e: e.tensor_scalar(out=lnv.var[0:P_, :], in0=lnv.mv[0:P_, 1:2],
                                      scalar1=1e-5, scalar2=None, op0=AluOpType.add),
            reads=[("ln.mv",)], writes=[("ln.var",)])
        vop(lambda e: e.activation(out=lnv.var[0:P_, :], in_=lnv.var[0:P_, :],
                                   func=AF.Sqrt),
            reads=[("ln.var",)], writes=[("ln.var",)], eng="a")
        vop(lambda e: e.reciprocal(out=a_ap, in_=lnv.var[0:P_, :]),
            reads=[("ln.var",)], writes=[(ab_key, slot, "a")])
        vop(lambda e: e.scalar_tensor_tensor(out=b_ap, in0=lnv.mv[0:P_, 0:1],
                                             scalar=-1.0, in1=a_ap,
                                             op0=AluOpType.mult, op1=AluOpType.mult),
            reads=[("ln.mv",), (ab_key, slot, "a")], writes=[(ab_key, slot, "b")])
        for okey, oap in outs:
            vop(lambda e, oap=oap: e.tensor_scalar(out=oap, in_=None, in0=x_ap,
                                                   scalar1=a_ap, scalar2=b_ap,
                                                   op0=AluOpType.mult, op1=AluOpType.add)
                if False else
                e.tensor_scalar(out=oap, in0=x_ap, scalar1=a_ap, scalar2=b_ap,
                                op0=AluOpType.mult, op1=AluOpType.add),
                reads=[x_key, (ab_key, slot, "a"), (ab_key, slot, "b")],
                writes=[okey])

    dma(mwarm_sb[:], apin["mwarm"][:], chan="c_mw", reads=[], writes=[("mwarm",)])
    dma(bc_sb[:], apin["bc"][:], chan="c_bc", reads=[], writes=[("bc",)])
    vop(lambda e: e.memset(ones1[:], 1.0), reads=[], writes=[("ones1",)])
    vop(lambda e: e.memset(onesf[:], 1.0), reads=[], writes=[("onesf",)])
    vop(lambda e: e.memset(ablk[:], 0.0), reads=[], writes=[("ablk", "z")])
    vop(lambda e: e.memset(bblk[:], 0.0), reads=[], writes=[("bblk", "z")])

    for nm, dst, w in [("bq", bq_sb, D), ("bk", bk_sb, D), ("bo", bo_sb, D),
                       ("bgE", bg_sb, NEXP)]:
        dma(stg_state.a[0:1, 0:w], apin[nm][:], chan="ch_wst", reads=[("wst",)], writes=[("wst",)])
        vop(lambda e, dst=dst, w=w, stg=stg_state.a: e.tensor_copy(
            out=dst[0:1, 0:w], in_=stg[0:1, 0:w]),
            reads=[("wst",)], writes=[("b_" + nm,)])

    cast_weight("Wq", wq_sb, "wq")
    cast_weight("Wk", wk_sb, "wk")
    cast_weight("Wo0", wo_sb, "wo")

    xg_sb = alloc("gW1", "xg_sb", (128, 2 * D), F32)
    xng_sb = alloc("gW1", "xng_sb", (128, 2 * D), BF16)
    wel_sb = alloc("gW1", "wel_sb", (128, 2 * LOWR), BF16)
    wrl_sb = alloc("gW1", "wrl_sb", (128, 2 * LOWR), BF16)
    wc_sb = alloc("gW1", "wc_sb", (LOWR, 2 * DKH), BF16)
    wet_sb = alloc("gW1", "wet_sb", (LOWR, 2 * DKH), BF16)
    LEN2 = LEN - 128

    dma(xg_sb[0:128, 0:D], apin["xg"][0:128, :], chan="cxg", reads=[],
        writes=[("xg", 0), ("xg", 1)])
    dma(xg_sb[0:LEN2, D:2 * D], apin["xg"][128:LEN, :], chan="cxg", reads=[],
        writes=[("xg", 0), ("xg", 1)])
    dma(stg_state.a[0:LOWR, 0:DKH], apin["Wc"][0:LOWR, :], chan="ch_wst",
        reads=[("wst",)], writes=[("wst",)])
    dma(stg_state.a[0:LOWR, DKH:2 * DKH], apin["Wc"][LOWR:2 * LOWR, :], chan="ch_wst",
        reads=[("wst",)], writes=[("wst",)])
    vop(lambda e, stg=stg_state.a: e.tensor_copy(out=wc_sb[:], in_=stg[0:LOWR, 0:2 * DKH]),
        reads=[("wst",)], writes=[("wc",)])
    emit_ln(("xg", 0), xg_sb[0:128, 0:D], ln_ab, "lng", NT, [(("xng", 0), xng_sb[0:128, 0:D])])
    emit_ln(("xg", 1), xg_sb[0:LEN2, D:2 * D], ln_ab, "lng", NT + 1,
            [(("xng", 1), xng_sb[0:LEN2, D:2 * D])], P_=LEN2)

    ps_we = pp[0][0:LOWR, 0:DKH]
    ps_ab = pp[1]
    for h in range(H):
        j, par = h // 2, h % 2
        dma(stg_state.a[0:128, 0:LOWR], apin["WeL"][h, 0:128, :], chan="ch_wst",
            reads=[("wst",)], writes=[("wst",)])
        dma(stg_state.a[0:LEN2, LOWR:2 * LOWR], apin["WeL"][h, 128:LEN, :], chan="ch_wst",
            reads=[("wst",)], writes=[("wst",)])
        vop(lambda e, stg=stg_state.a: e.tensor_copy(out=wel_sb[:, :], in_=stg[0:128, 0:2 * LOWR]),
            reads=[("wst",)], writes=[("wel",)])
        dma(stg_state.a[0:128, 0:LOWR], apin["WrL"][h, 0:128, :], chan="ch_wst",
            reads=[("wst",)], writes=[("wst",)])
        dma(stg_state.a[0:LEN2, LOWR:2 * LOWR], apin["WrL"][h, 128:LEN, :], chan="ch_wst",
            reads=[("wst",)], writes=[("wst",)])
        vop(lambda e, stg=stg_state.a: e.tensor_copy(out=wrl_sb[:, :], in_=stg[0:128, 0:2 * LOWR]),
            reads=[("wst",)], writes=[("wrl",)])
        for src_sb, srckey, half, blk, blkkey in [
                (wel_sb, ("wel",), 0, ablk, "ablk"), (wrl_sb, ("wrl",), 1, bblk, "bblk")]:
            wcs = wc_sb[0:LOWR, 0:DKH] if half == 0 else wc_sb[0:LOWR, DKH:2 * DKH]
            mm(ps_we, src_sb[0:128, 0:LOWR], xng_sb[0:128, DKH * h:DKH * (h + 1)],
               reads=[srckey, ("xng", 0)], writes=[PPK[0]], start=True, stop=False)
            mm(ps_we, src_sb[0:LEN2, LOWR:2 * LOWR],
               xng_sb[0:LEN2, D + DKH * h:D + DKH * (h + 1)],
               reads=[srckey, ("xng", 1)], writes=[PPK[0]], start=False, stop=True)
            wt = wet_sb[:, DKH * half:DKH * (half + 1)]
            vop(lambda e, wt=wt: e.activation(out=wt, in_=ps_we, func=AF.Copy),
                reads=[PPK[0]], writes=[("wet", half)], eng="a")
            mm(ps_ab[64 * par:64 * (par + 1), 0:DKH], wt, wcs,
               reads=[("wet", half), ("wc",)], writes=[PPK[1]],
               start=True, stop=True, tile_position=(0, 64 * par))
            dst = blk[64 * par:64 * (par + 1), 128 * j + 64 * par:128 * j + 64 * (par + 1)]
            vop(lambda e, dst=dst, par=par: e.activation(
                out=dst, in_=ps_ab[64 * par:64 * (par + 1), 0:DKH], func=AF.Copy),
                reads=[PPK[1], (blkkey, "z")], writes=[(blkkey, j, par)], eng="a")

    ms.phase()
    free_group("gW1")

    qa = P()
    QA = 2048
    qa.amax = alloc("gAq", "qa_amax", (128, QA // 16), F32)
    qa.scale = alloc("gAq", "qa_scale", (128, QA // 16), F32)
    qa.v2 = [alloc("gAq", "qa_v%d" % i, (128, QA), F32) for i in range(2)]
    qa.f1 = alloc("gAq", "qa_f1", (128, QA), F32)
    qa.g = alloc("gAq", "qa_g", (128, QA), F32)
    qa.sg2 = [alloc("gAq", "qa_sg%d" % i, (128, QA), BF16) for i in range(2)]
    wstA = [alloc("gAq", "wstA%d" % i, (128, QA), F32) for i in range(2)]
    wqqA = alloc("gAq", "wqqA", (128, QA), BF16)
    recst = [alloc("gAq", "recst%d" % i, (128, 1408), BF16) for i in range(2)]
    wq_jobs = []

    def job_width_src(job):
        if job[0] == "wgu":
            return D * len(job[3])
        return 1408

    def job_stage1(job, par):
        stg = wstA[par]
        skey = ("wstA", par)
        vk, sk = ("qa.v", par), ("qa.sg", par)
        if job[0] == "wgu":
            _, srcname, base, sls = job
            for i, sl in enumerate(sls):
                dma(stg[:, D * i:D * (i + 1)], apin[srcname][128 * sl:128 * (sl + 1), :],
                    chan="ch_wstA%d" % par, reads=[skey], writes=[skey], eng="a")
            F = D * len(sls)
        else:
            _, sl, half = job
            so, F = half * 1408, 1408
            dma(stg[:, 0:F], apin["wd_nat"][128 * sl:128 * (sl + 1), so:so + F],
                chan="ch_wstA%d" % par, reads=[skey], writes=[skey], eng="a")
        emit_quant_p1(skey, stg[:, 0:F], F, 128, qa, vk, qa.v2[par][:, 0:F],
                      sk, qa.sg2[par][:, 0:F])

    def job_stage2(job, par):
        F = job_width_src(job)
        vk, sk = ("qa.v", par), ("qa.sg", par)
        emit_quant_p2(("wqqA",), wqqA[:, 0:F], F, 128, qa,
                      vk, qa.v2[par][:, 0:F], sk, qa.sg2[par][:, 0:F])
        rp = par
        rkey = ("recst", rp)
        if job[0] == "wgu":
            _, srcname, base, sls = job
            for i, sl in enumerate(sls):
                dma(recst[rp][:, 0:D].rearrange("p (j c) -> p j c", c=128),
                    wqqA[:, D * i:D * (i + 1)].rearrange("p (j c) -> p j c", c=128),
                    chan="tx_rec%d" % rp, reads=[("wqqA",), rkey], writes=[rkey],
                    transpose=True)
                dma(wgu_scr[128 * (base + sl):128 * (base + sl + 1), :],
                    recst[rp][:, 0:D],
                    chan="st_rec%d" % rp, reads=[rkey],
                    writes=[("wgu_scr", base + sl)], eng="a")
        else:
            _, sl, half = job
            so = half * 1408
            dma(recst[rp][:, 0:F].rearrange("p (j c) -> p j c", c=128),
                wqqA[:, 0:F].rearrange("p (j c) -> p j c", c=128),
                chan="tx_rec%d" % rp, reads=[("wqqA",), rkey], writes=[rkey],
                transpose=True)
            dma(wd_scr[128 * sl:128 * (sl + 1), so:so + F], recst[rp][:, 0:F],
                chan="st_rec%d" % rp, reads=[rkey], writes=[("wd_scr", sl, half)],
                eng="a")

    _g3 = [list(range(i, min(i + 2, ICH))) for i in range(0, ICH, 2)]
    for gi3, sls in enumerate(_g3):
        wq_jobs.append(("wgu", "wg_nat", 0, sls))
    for gi3, sls in enumerate(_g3):
        wq_jobs.append(("wgu", "wu_nat", ICH, sls))
    for sl in range(KC):
        for half in range(2):
            wq_jobs.append(("wd", sl, half))
    wq_jobs.reverse()
    _jobstate = {"n": 0, "pending": []}

    def quant_weight_slice():
        if wq_jobs:
            job = wq_jobs.pop()
            par = _jobstate["n"] & 1
            _jobstate["n"] += 1
            job_stage1(job, par)
            _jobstate["pending"].append((job, par))
        if len(_jobstate["pending"]) > 1 or (not wq_jobs and _jobstate["pending"]):
            job, par = _jobstate["pending"].pop(0)
            job_stage2(job, par)

    def drain_jobs():
        while wq_jobs or _jobstate["pending"]:
            quant_weight_slice()

    GA = 512
    xnT = alloc("gA2", "xnT", (128, KC * GA), BF16)
    qnT = alloc("gA2", "qnT", (128, KC * GA), BF16)
    knT = alloc("gA2", "knT", (128, KC * GA), BF16)
    mT = alloc("gA2", "mT", (128, KC * GA), BF16)
    qss = alloc("gA2", "qss", (128, H), F32)
    qrv = alloc("gA2", "qrv", (128, H), F32)
    qn_bf = alloc("gA2", "qn_bf", (128, D), BF16)
    kn_bf = alloc("gA2", "kn_bf", (128, D), BF16)
    xnbf = alloc("gA2", "xnbf", (128, D), BF16)
    qf = alloc("gA2", "qf", (128, D), F32)
    kf = qf
    s_t = qf

    def l2norm(src, srckey, dst, dstkey):
        vop(lambda e: e.tensor_tensor(out=qb.v[:, 0:D], in0=src[:], in1=src[:],
                                      op=AluOpType.mult),
            reads=[srckey], writes=[("q.v",)])
        vop(lambda e: e.tensor_reduce(out=qss[:], in_=qb.v[:, 0:D].rearrange(
            "p (h d) -> p h d", d=DKH), axis=AX.X, op=AluOpType.add),
            reads=[("q.v",)], writes=[("qss",)])
        vop(lambda e: e.activation(out=qrv[:], in_=qss[:], func=AF.Sqrt),
            reads=[("qss",)], writes=[("qrv",)], eng="a")
        vop(lambda e: e.tensor_scalar(out=qrv[:], in0=qrv[:], scalar1=1e-12,
                                      scalar2=None, op0=AluOpType.max),
            reads=[("qrv",)], writes=[("qrv",)])
        vop(lambda e: e.reciprocal(out=qrv[:], in_=qrv[:]),
            reads=[("qrv",)], writes=[("qrv",)])
        rb = qrv[:, :].unsqueeze(2).broadcast_to([128, H, DKH])
        vop(lambda e: e.tensor_tensor(out=dst.rearrange("p (h d) -> p h d", d=DKH),
                                      in0=src.rearrange("p (h d) -> p h d", d=DKH),
                                      in1=rb, op=AluOpType.mult),
            reads=[srckey, ("qrv",)], writes=[dstkey])

    for go, gn in GRA:
        ntile = gn // 128
        for ti in range(ntile):
            for _ in range(2):
                if wq_jobs:
                    quant_weight_slice()
            t0 = go + ti * 128
            tidx = t0 // 128
            par = tidx % 2
            xb = xbuf2[:, D * par:D * (par + 1)]
            xbk = ("xbuf", par)
            dma(xb, apin["xin"][t0:t0 + 128, :], chan="ch_x%d" % par,
                reads=[xbk], writes=[xbk])
            emit_ln(xbk, xb, ln_ab, "ln0", tidx, [(("xnbf",), xnbf[:])])
            wall = [("xnT", j2, ti) for j2 in range(KC)]
            dma(xnT[:, :].rearrange("p (j c) -> p j c", c=GA)[:, :, 128 * ti:128 * (ti + 1)],
                xnbf[:, :].rearrange("p (j c) -> p j c", c=128),
                chan="tx_xnT%d" % ti, reads=[("xnbf",)], writes=wall, transpose=True)
            for ww, wwkey, ppi, bap, bkey in [
                    (wq_sb, "wq", (0, 1), bq_sb, ("b_bq",)),
                    (wk_sb, "wk", (2, 3), bk_sb, ("b_bk",))]:
                for n in range(2):
                    pidx = ppi[n]
                    mm(pp[pidx], ones1[:], bap[0:1, 512 * n:512 * (n + 1)],
                       reads=[("ones1",), bkey], writes=[PPK[pidx]], start=True, stop=False)
                    for k in range(KC):
                        mm(pp[pidx], xnT[:, GA * k + 128 * ti:GA * k + 128 * (ti + 1)],
                           ww[:, D * k + 512 * n:D * k + 512 * (n + 1)],
                           reads=[("xnT", k, ti), (wwkey, k)], writes=[PPK[pidx]],
                           start=False, stop=(k == KC - 1))
            for n in range(2):
                vop(lambda e, n=n: e.activation(out=qf[:, 512 * n:512 * (n + 1)],
                                                in_=pp[n][:], func=AF.Copy),
                    reads=[PPK[n], ("qf",)], writes=[("qf",)], eng="a")
            l2norm(qf, ("qf",), qn_bf[:, :], ("qn",))
            for n in range(2):
                vop(lambda e, n=n: e.activation(out=kf[:, 512 * n:512 * (n + 1)],
                                                in_=pp[2 + n][:], func=AF.Copy),
                    reads=[PPK[2 + n], ("qf",)], writes=[("qf",)], eng="a")
            l2norm(kf, ("qf",), kn_bf[:, :], ("kn",))
            wq_all = [("qnT", j2, ti) for j2 in range(KC)]
            wk_all = [("knT", j2, ti) for j2 in range(KC)]
            dma(qnT[:, :].rearrange("p (j c) -> p j c", c=GA)[:, :, 128 * ti:128 * (ti + 1)],
                qn_bf[:, :].rearrange("p (j c) -> p j c", c=128),
                chan="tx_qnT%d" % ti, reads=[("qn",)], writes=wq_all, transpose=True)
            dma(knT[:, :].rearrange("p (j c) -> p j c", c=GA)[:, :, 128 * ti:128 * (ti + 1)],
                kn_bf[:, :].rearrange("p (j c) -> p j c", c=128),
                chan="tx_knT%d" % ti, reads=[("kn",)], writes=wk_all, transpose=True)
        for jj in range(KC):
            rq = [("qnT", jj, ti) for ti in range(ntile)]
            rk = [("knT", jj, ti) for ti in range(ntile)]
            mm(pp[4][:, 0:gn], ablk[:, 128 * jj:128 * (jj + 1)], qnT[:, GA * jj:GA * jj + gn],
               reads=rq + [("ablk", jj, 0), ("ablk", jj, 1)], writes=[PPK[4]],
               start=True, stop=False)
            mm(pp[4][:, 0:gn], bblk[:, 128 * jj:128 * (jj + 1)], knT[:, GA * jj:GA * jj + gn],
               reads=rk + [("bblk", jj, 0), ("bblk", jj, 1)], writes=[PPK[4]],
               start=False, stop=True)
            vop(lambda e, jj=jj, gn=gn: e.tensor_scalar(
                out=mT[:, GA * jj:GA * jj + gn], in0=pp[4][:, 0:gn],
                scalar1=bc_sb[:, 0:1], scalar2=None, op0=AluOpType.add),
                reads=[PPK[4], ("bc",)], writes=[("mT", jj)])
        for ti in range(ntile):
            t0 = go + ti * 128
            for n in range(2):
                pidx = 5 + n
                mm(pp[pidx], ones1[:], bo_sb[0:1, 512 * n:512 * (n + 1)],
                   reads=[("ones1",), ("b_bo",)], writes=[PPK[pidx]], start=True, stop=False)
                for k in range(KC):
                    mm(pp[pidx], mT[:, GA * k + 128 * ti:GA * k + 128 * (ti + 1)],
                       wo_sb[:, D * k + 512 * n:D * k + 512 * (n + 1)],
                       reads=[("mT", k), ("wo", k)], writes=[PPK[pidx]],
                       start=False, stop=(k == KC - 1))
            for n in range(2):
                pidx = 5 + n
                vop(lambda e, n=n, pidx=pidx: e.activation(
                    out=s_t[:, 512 * n:512 * (n + 1)], in_=pp[pidx][:], func=AF.Copy),
                    reads=[PPK[pidx], ("qf",)], writes=[("qf",)], eng="a")
            dma(s_scr[t0:t0 + 128, :], s_t[:], chan="st_s", reads=[("qf",)],
                writes=[("s_scr", t0 // 128)])
            if debug:
                dma(dbg["dbg_s"][t0:t0 + 128, :], s_t[:], chan="dbgc",
                    reads=[("qf",)], writes=[])

    drain_jobs()
    ms.phase()
    free_group("gA2")
    free_group("gAq")
    free_group("gA")

    wgT = alloc("gB2", "wgT", (128, KC * IFF), BF16)
    wuT = alloc("gB2", "wuT", (128, KC * IFF), BF16)
    wdT = alloc("gB2", "wdT", (128, ICH * D), BF16)
    for wi_, (dst_sb, dkey, base) in enumerate([(wgT, "wgT", 0), (wuT, "wuT", ICH)]):
        for sl in range(ICH):
            wall = [(dkey, j2) for j2 in range(KC)]
            dma(dst_sb[:, :].rearrange("p (j c) -> p j c", c=IFF)[:, :, 128 * sl:128 * (sl + 1)],
                wgu_scr[128 * (base + sl):128 * (base + sl + 1), :]
                .rearrange("p (j c) -> p j c", c=128),
                chan="ld_" + dkey, reads=[("wgu_scr", base + sl)], writes=wall)
    for sl in range(KC):
        wall = [("wdT", k2) for k2 in range(ICH)]
        dma(wdT[:, :].rearrange("p (j c) -> p j c", c=D)[:, :, 128 * sl:128 * (sl + 1)],
            wd_scr[128 * sl:128 * (sl + 1), :].rearrange("p (j c) -> p j c", c=128),
            chan="ld_wdT", reads=[("wd_scr", sl, 0), ("wd_scr", sl, 1)], writes=wall)
    xqbf = alloc("gB2", "xqbf", (128, D), BF16)
    tmpf = alloc("gB2", "tmpf", (128, D), F32)
    sldb = alloc("gB2", "sldb", (128, D), F32)
    xn32 = tmpf
    sload = sldb
    xqT = alloc("gB2", "xqT", (128, KC * 256), BF16)
    hqT = alloc("gB2", "hqT", (128, ICH * 256), BF16)
    h_sl = alloc("gB2", "h_sl", (128, 1024), F32)
    hq_sl = xqbf
    g_t = alloc("gB2", "g_t", (128, D), F32)
    x1_t = alloc("gB2", "x1_t", (128, D), F32)

    NSL = _split(IFF, 512)
    hq_bf = xqbf
    xb4ext = alloc("gB2", "xb4ext", (128, 2 * D), F32)

    def b_xb(tidx):
        p4 = tidx % 4
        base = xbuf2 if p4 < 2 else xb4ext
        return base[:, D * (p4 % 2):D * (p4 % 2 + 1)], ("xbuf", p4), p4

    def b_xload(tidx):
        t0 = tidx * 128
        xb, xbk, p4 = b_xb(tidx)
        dma(xb, apin["xin"][t0:t0 + 128, :], chan="ch_x%d" % p4,
            reads=[xbk], writes=[xbk])

    def b_front(tidx):
        t0 = tidx * 128
        par = tidx % 2
        xb, xbk, p4 = b_xb(tidx)
        b_xload(tidx)
        a_ap = ln_ab[:, 2 * tidx:2 * tidx + 1]
        b_ap = ln_ab[:, 2 * tidx + 1:2 * tidx + 2]
        vop(lambda e, a_ap=a_ap, b_ap=b_ap, xb=xb: e.tensor_scalar(
            out=xn32[:], in0=xb, scalar1=a_ap, scalar2=b_ap,
            op0=AluOpType.mult, op1=AluOpType.add),
            reads=[xbk], writes=[("tmpf",)])
        emit_quant(("tmpf",), xn32[:], ("xq",), xqbf[:], D)
        wall = [("xqT", j2, par) for j2 in range(KC)]
        dma(xqT[:, :].rearrange("p (j c) -> p j c", c=256)[:, :, 128 * par:128 * (par + 1)],
            xqbf[:, :].rearrange("p (j c) -> p j c", c=128),
            chan="tx_xqT%d" % par, reads=[("xq",)], writes=wall, transpose=True)

    def b_back(tidx):
        t0 = tidx * 128
        par = tidx % 2
        xb, xbk, p4 = b_xb(tidx)
        for sb_i in range(3):
            sls = NSL[2 * sb_i:2 * sb_i + 2]
            sb0 = sls[0][0]
            sbw = sum(sn for _, sn in sls)
            for k in range(KC):
                lhs = xqT[:, 256 * k + 128 * par:256 * k + 128 * (par + 1)]
                for si, (so, sn) in enumerate(sls):
                    mm(pp[si][:, 0:sn], lhs, wgT[:, IFF * k + so:IFF * k + so + sn],
                       reads=[("xqT", k, par), ("wgT", k)], writes=[PPK[si]],
                       start=(k == 0), stop=(k == KC - 1))
                    mm(pp[2 + si][:, 0:sn], lhs, wuT[:, IFF * k + so:IFF * k + so + sn],
                       reads=[("xqT", k, par), ("wuT", k)], writes=[PPK[2 + si]],
                       start=(k == 0), stop=(k == KC - 1))
            for si, (so, sn) in enumerate(sls):
                vop(lambda e, si=si, so=so, sn=sn, sb0=sb0: e.activation(
                    out=h_sl[:, so - sb0:so - sb0 + sn], in_=pp[si][:, 0:sn],
                    func=AF.Silu),
                    reads=[PPK[si]], writes=[("h_sl",)], eng="a")
                vop(lambda e, si=si, so=so, sn=sn, sb0=sb0: e.tensor_tensor(
                    out=h_sl[:, so - sb0:so - sb0 + sn],
                    in0=h_sl[:, so - sb0:so - sb0 + sn],
                    in1=pp[2 + si][:, 0:sn], op=AluOpType.mult),
                    reads=[("h_sl",), PPK[2 + si]], writes=[("h_sl",)])
            emit_quant(("h_sl",), h_sl[:, 0:sbw], ("xq",), hq_bf[:, 0:sbw], sbw)
            wall = [("hqT", k2, par) for k2 in range(ICH)]
            nblk = sbw // 128
            dma(hqT[:, :].rearrange("p (j c) -> p j c", c=256)[
                :, sb0 // 128:sb0 // 128 + nblk, 128 * par:128 * (par + 1)],
                hq_bf[:, 0:sbw].rearrange("p (j c) -> p j c", c=128),
                chan="tx_hqT%d" % par, reads=[("xq",)], writes=wall, transpose=True)
        for n in range(2):
            pidx = 6 + n
            for k in range(ICH):
                mm(pp[pidx], hqT[:, 256 * k + 128 * par:256 * k + 128 * (par + 1)],
                   wdT[:, D * k + 512 * n:D * k + 512 * (n + 1)],
                   reads=[("hqT", k, par), ("wdT", k)], writes=[PPK[pidx]],
                   start=(k == 0), stop=(k == ICH - 1))
        gk = ("g_t",)
        for n in range(2):
            pidx = 6 + n
            vop(lambda e, n=n, pidx=pidx: e.activation(
                out=g_t[:, 512 * n:512 * (n + 1)], in_=pp[pidx][:], func=AF.Silu),
                reads=[PPK[pidx]], writes=[gk], eng="a")
        dma(sload[:], s_scr[t0:t0 + 128, :], chan="ch_sload%d" % (tidx % 2),
            reads=[("s_scr", tidx), ("sldb",)], writes=[("sldb",)], eng="g")
        vop(lambda e: e.tensor_tensor(out=x1_t[:, 0:D], in0=sload[:],
                                      in1=g_t[:], op=AluOpType.mult),
            reads=[("sldb",), gk], writes=[("x1_t",)], eng="g")
        vop(lambda e, xb=xb: e.tensor_tensor(out=x1_t[:, 0:D], in0=x1_t[:, 0:D],
                                             in1=xb, op=AluOpType.add),
            reads=[("x1_t",), xbk], writes=[("x1_t",)], eng="g")
        dma(x1_scr[t0:t0 + 128, :], x1_t[:, 0:D], chan="st_x1_%d" % (tidx % 2),
            reads=[("x1_t",)], writes=[("x1_scr", tidx)], eng="g")
        if debug:
            dma(dbg["dbg_x1"][t0:t0 + 128, :], x1_t[:, 0:D], chan="dbgc",
                reads=[("x1_t",)], writes=[])

    prev_t = None
    for tidx in range(TP // 128):
        b_front(tidx)
        if prev_t is not None:
            b_back(prev_t)
        prev_t = tidx
    b_back(prev_t)

    ms.phase()
    free_group("gB2")
    free_group("gB")

    xn1T = alloc("gD", "xn1T", (128, KC * TP), BF16)
    xnbf = alloc("gD", "xnbf_c", (128, D), BF16)
    stg_state.a = alloc("gD", "wstC", (128, D), F32)
    stg_state.b = alloc("gD", "wstC2", (128, D), F32)
    z_sb = alloc("gD", "z_sb", (128, NEXP * NTM), F32)
    alp_sb = alloc("gD", "alp_sb", (128, NEXP * NTM), F32)
    zm1 = alloc("gD", "zm1", (128, 1), F32)
    zm2 = alloc("gD", "zm2", (128, 1), F32)
    znm = alloc("gD", "znm", (128, 1), F32)
    zscr = alloc("gD", "zscr", (128, NEXP), F32)

    hbuf = alloc("gC2", "hbuf", (128, KC * TP), BF16)
    slg = alloc("gC2", "slg", (128, KC * T_MAIN), BF16)
    sqacc = alloc("gC2", "sqacc", (128, T_MAIN), F32)
    wo1_sb = alloc("gC2", "wo1_sb", (128, KC * D), BF16)
    gnw_sb = alloc("gC2", "gnw_sb", (128, KC), F32)

    wim = [alloc("gC1", "wim%d" % i, (128, 3 * D), BF16) for i in range(2)]
    wg1_sb = alloc("gC1", "wg1_sb", (128, KC * NEXP), BF16)
    conv_i_sb = alloc("gC1", "conv_i_sb", (128, KC * 4), F32)
    conv_f_sb = alloc("gC1", "conv_f_sb", (128, KC * 4), F32)
    iraw = alloc("gC1", "iraw", (128, TP), F32)
    fraw = alloc("gC1", "fraw", (128, TP), F32)
    ci_b = alloc("gC1", "ci_b", (128, TP), F32)
    cf_b = alloc("gC1", "cf_b", (128, TP), F32)

    cast_weight("Wo1", wo1_sb, "wo1")
    cast_weight("WgE", wg1_sb, "wg1", width=NEXP)
    dma(conv_i_sb[:], apin["conv_i"][:], chan="c_cvi", reads=[], writes=[("convi",)])
    dma(conv_f_sb[:], apin["conv_f"][:], chan="c_cvf", reads=[], writes=[("convf",)])
    dma(gnw_sb[:], apin["gnw"][:], chan="c_gnw", reads=[], writes=[("gnw",)])

    for go, gn in GRPC:
        for ti in range(gn // 128):
            t0 = go + ti * 128
            tidx = t0 // 128
            par = tidx % 2
            xb = xbuf2[:, D * par:D * (par + 1)]
            xbk = ("xbuf", par)
            dma(xb, x1_scr[t0:t0 + 128, :], chan="ch_x%d" % par,
                reads=[xbk, ("x1_scr", tidx)], writes=[xbk])
            emit_ln(xbk, xb, ln_ab, "ln1", tidx, [(("xnbf",), xnbf[:])])
            wall = [("xn1T", j2, tidx) for j2 in range(KC)]
            dma(xn1T[:, :].rearrange("p (j c) -> p j c", c=TP)[:, :, t0:t0 + 128],
                xnbf[:, :].rearrange("p (j c) -> p j c", c=128),
                chan="tx_xn1T", reads=[("xnbf",)], writes=wall, transpose=True)

    for tm in range(NTM):
        t0 = WARM + tm * 128
        zps = pp[3][:, 0:NEXP]
        mm(zps, ones1[:], bg_sb[0:1, 0:NEXP], reads=[("ones1",), ("b_bgE",)],
           writes=[PPK[3]], start=True, stop=False)
        for k in range(KC):
            mm(zps, xn1T[:, TP * k + t0:TP * k + t0 + 128],
               wg1_sb[:, NEXP * k:NEXP * (k + 1)],
               reads=[("xn1T", k, t0 // 128), ("wg1", k)], writes=[PPK[3]],
               start=False, stop=(k == KC - 1))
        zs = z_sb[:, NEXP * tm:NEXP * (tm + 1)]
        als = alp_sb[:, NEXP * tm:NEXP * (tm + 1)]
        vop(lambda e, zs=zs: e.activation(out=zs, in_=pp[3][:, 0:NEXP], func=AF.Copy),
            reads=[PPK[3]], writes=[("z", tm)], eng="a")
        vop(lambda e, zs=zs: e.tensor_reduce(out=zm1[:], in_=zs, axis=AX.X,
                                             op=AluOpType.max),
            reads=[("z", tm)], writes=[("zm1",)])
        vop(lambda e, zs=zs: e.tensor_scalar(out=zscr[:], in0=zs, scalar1=zm1[:, 0:1],
                                             scalar2=-1e30, op0=AluOpType.is_ge,
                                             op1=AluOpType.mult),
            reads=[("z", tm), ("zm1",)], writes=[("zscr",)])
        vop(lambda e, zs=zs: e.tensor_tensor(out=zscr[:], in0=zscr[:], in1=zs,
                                             op=AluOpType.add),
            reads=[("zscr",), ("z", tm)], writes=[("zscr",)])
        vop(lambda e: e.tensor_reduce(out=zm2[:], in_=zscr[:], axis=AX.X,
                                      op=AluOpType.max),
            reads=[("zscr",)], writes=[("zm2",)])
        vop(lambda e, zs=zs, als=als: e.tensor_scalar(out=als, in0=zs,
                                                      scalar1=zm2[:, 0:1], scalar2=None,
                                                      op0=AluOpType.is_ge),
            reads=[("z", tm), ("zm2",)], writes=[("alp", tm)])
        vop(lambda e: e.tensor_scalar(out=znm[:], in0=zm1[:], scalar1=-1.0,
                                      scalar2=None, op0=AluOpType.mult),
            reads=[("zm1",)], writes=[("znm",)])
        vop(lambda e, zs=zs: e.activation(out=zscr[:], in_=zs, func=AF.Exp,
                                          bias=znm[:, 0:1], scale=1.0),
            reads=[("z", tm), ("znm",)], writes=[("zscr",)], eng="a")
        vop(lambda e: e.tensor_tensor(out=zm2[:], in0=zm2[:], in1=zm1[:],
                                      op=AluOpType.subtract),
            reads=[("zm2",), ("zm1",)], writes=[("zm2",)])
        vop(lambda e: e.activation(out=zm2[:], in_=zm2[:], func=AF.Exp),
            reads=[("zm2",)], writes=[("zm2",)], eng="a")
        vop(lambda e: e.tensor_scalar(out=zm2[:], in0=zm2[:], scalar1=1.0,
                                      scalar2=None, op0=AluOpType.add),
            reads=[("zm2",)], writes=[("zm2",)])
        vop(lambda e: e.reciprocal(out=zm2[:], in_=zm2[:]),
            reads=[("zm2",)], writes=[("zm2",)])
        vop(lambda e, als=als: e.tensor_tensor(out=als, in0=als, in1=zscr[:],
                                               op=AluOpType.mult),
            reads=[("alp", tm), ("zscr",)], writes=[("alp", tm)])
        vop(lambda e, als=als: e.tensor_scalar_mul(out=als, in0=als,
                                                   scalar1=zm2[:, 0:1]),
            reads=[("alp", tm), ("zm2",)], writes=[("alpha", tm)])
        if debug:
            vop(lambda e, als=als: e.tensor_copy(out=zscr[:], in_=als),
                reads=[("alpha", tm)], writes=[("zscr",)])
            dma(dbg["dbg_alpha"][128 * tm:128 * (tm + 1), :], zscr[:],
                chan="dbgc", reads=[("zscr",)], writes=[])

    vop(lambda e: e.memset(sqacc[:], 0.0), reads=[], writes=[("sqacc",)])
    for m in range(KC):
        par = m % 2
        for wi_, wname in enumerate(["Wi", "Wf", "Wgr"]):
            stg, skey, ch = ((stg_state.a, ("wst",), "ch_wst") if (m * 3 + wi_) % 2 == 0
                             else (stg_state.b, ("wst2",), "ch_wst2"))
            srcap = apin[wname][:, 128 * m:128 * (m + 1)].rearrange(
                "(a p) c -> p a c", p=128)
            dma(stg[:, 0:D].rearrange("p (a c) -> p a c", c=128), srcap,
                chan=ch, reads=[skey], writes=[skey], eng="a")
            vop(lambda e, par=par, wi_=wi_, stg=stg: e.tensor_copy(
                out=wim[par][:, D * wi_:D * (wi_ + 1)], in_=stg[:, 0:D]),
                reads=[skey], writes=[("wim", par, wi_)])
        for gi, (go, gn) in enumerate(GRPC):
            rx = [("xn1T", k, (go // 128) + tt) for k in range(KC)
                  for tt in range(gn // 128)]
            for pse, psk, wi_ in [(0, PPK[0], 0), (1, PPK[1], 1)]:
                for k in range(KC):
                    mm(pp[pse][:, 0:gn], wim[par][:, D * wi_ + 128 * k:D * wi_ + 128 * (k + 1)],
                       xn1T[:, TP * k + go:TP * k + go + gn],
                       reads=[("wim", par, wi_)] + rx, writes=[psk],
                       start=(k == 0), stop=(k == KC - 1))
            vop(lambda e, go=go, gn=gn: e.activation(out=iraw[:, go:go + gn],
                                                     in_=pp[0][:, 0:gn], func=AF.Copy),
                reads=[PPK[0]], writes=[("iraw",)], eng="a")
            vop(lambda e, go=go, gn=gn: e.activation(out=fraw[:, go:go + gn],
                                                     in_=pp[1][:, 0:gn], func=AF.Copy),
                reads=[PPK[1]], writes=[("fraw",)], eng="a")
        for go, gn in GRPM:
            gow = WARM + go
            rx = [("xn1T", k, (gow // 128) + tt) for k in range(KC)
                  for tt in range(gn // 128)]
            for k in range(KC):
                mm(pp[2][:, 0:gn], wim[par][:, 2 * D + 128 * k:2 * D + 128 * (k + 1)],
                   xn1T[:, TP * k + gow:TP * k + gow + gn],
                   reads=[("wim", par, 2)] + rx, writes=[PPK[2]],
                   start=(k == 0), stop=(k == KC - 1))
            vop(lambda e, m=m, go=go, gn=gn: e.activation(
                out=slg[:, T_MAIN * m + go:T_MAIN * m + go + gn],
                in_=pp[2][:, 0:gn], func=AF.Silu),
                reads=[PPK[2]], writes=[("slg", m)], eng="a")
        vop(lambda e: e.tensor_scalar_mul(out=iraw[:, 0:WARM], in0=iraw[:, 0:WARM],
                                          scalar1=mwarm_sb[:, 0:1]),
            reads=[("iraw",), ("mwarm",)], writes=[("iraw",)])
        vop(lambda e: e.tensor_scalar_mul(out=fraw[:, 0:WARM], in0=fraw[:, 0:WARM],
                                          scalar1=mwarm_sb[:, 0:1]),
            reads=[("fraw",), ("mwarm",)], writes=[("fraw",)])
        for src, skey, cout, ckey, csb, cwkey in [
                (iraw, ("iraw",), ci_b, ("ci",), conv_i_sb, ("convi",)),
                (fraw, ("fraw",), cf_b, ("cf",), conv_f_sb, ("convf",))]:
            vop(lambda e, cout=cout: e.memset(cout[:, 0:3], 0.0),
                reads=[], writes=[ckey])
            vop(lambda e, src=src, cout=cout, csb=csb, m=m: e.tensor_scalar_mul(
                out=cout[:, 3:TP], in0=src[:, 3:TP],
                scalar1=csb[:, 4 * m + 3:4 * m + 4]),
                reads=[skey, cwkey], writes=[ckey])
            for tap in range(3):
                vop(lambda e, src=src, cout=cout, csb=csb, m=m, tap=tap:
                    e.scalar_tensor_tensor(out=cout[:, 3:TP], in0=src[:, tap:TP - 3 + tap],
                                           scalar=csb[:, 4 * m + tap:4 * m + tap + 1],
                                           in1=cout[:, 3:TP],
                                           op0=AluOpType.mult, op1=AluOpType.add),
                    reads=[skey, cwkey, ckey], writes=[ckey])
        vop(lambda e: e.activation(out=cf_b[:], in_=cf_b[:], func=AF.Sigmoid),
            reads=[("cf",)], writes=[("cf",)], eng="a")
        vop(lambda e: e.activation(out=ci_b[:], in_=ci_b[:], func=AF.Silu),
            reads=[("ci",)], writes=[("ci",)], eng="a")
        vop(lambda e: e.tensor_scalar(out=iraw[:], in0=cf_b[:], scalar1=-1.0,
                                      scalar2=1.0, op0=AluOpType.mult, op1=AluOpType.add),
            reads=[("cf",)], writes=[("iraw",)])
        vop(lambda e: e.tensor_tensor(out=ci_b[:], in0=ci_b[:], in1=iraw[:],
                                      op=AluOpType.mult),
            reads=[("ci",), ("iraw",)], writes=[("ci",)])
        for so, sn in _split(TP, 544):
            init = 0.0 if so == 0 else hbuf[:, TP * m + so - 1:TP * m + so]
            vop(lambda e, m=m, so=so, sn=sn, init=init: e.tensor_tensor_scan(
                out=hbuf[:, TP * m + so:TP * m + so + sn],
                data0=cf_b[:, so:so + sn], data1=ci_b[:, so:so + sn],
                initial=init, op0=AluOpType.mult, op1=AluOpType.add),
                reads=[("cf",), ("ci",), ("h", m)], writes=[("h", m)])
        vop(lambda e, m=m: e.tensor_tensor(out=fraw[:, 0:T_MAIN],
                                           in0=hbuf[:, TP * m + WARM:TP * (m + 1)],
                                           in1=hbuf[:, TP * m + WARM:TP * (m + 1)],
                                           op=AluOpType.mult),
            reads=[("h", m), ("fraw",)], writes=[("fraw",)], eng="g")
        vop(lambda e: e.tensor_tensor(out=sqacc[:], in0=sqacc[:], in1=fraw[:, 0:T_MAIN],
                                      op=AluOpType.add),
            reads=[("sqacc",), ("fraw",)], writes=[("sqacc",)], eng="g")
        if debug and m == 0:
            vop(lambda e: e.tensor_copy(out=ci_b[:, 0:T_MAIN],
                                        in_=hbuf[:, WARM:TP]),
                reads=[("h", 0), ("ci",)], writes=[("ci",)])
            dma(dbg["dbg_h"][:, WARM:TP], ci_b[:, 0:T_MAIN], chan="dbgc",
                reads=[("ci",)], writes=[])

    ms.phase()
    free_group("gC1")

    rms_b = alloc("gC2b", "rms_b", (128, T_MAIN), F32)
    oT = alloc("gC2b", "oT", (128, KC * T_MAIN), BF16)
    s1_t = alloc("gC2b", "s1_t", (128, D), F32)
    sqtmp = alloc("gC2b", "sqtmp", (128, T_MAIN), F32)

    for gi, (go, gn) in enumerate(GRPM):
        mm(pp[4][0:1, 0:gn], onesf[:], sqacc[:, go:go + gn],
           reads=[("onesf",), ("sqacc",)], writes=[PPK[4]], start=True, stop=True)
        vop(lambda e, go=go, gn=gn: e.tensor_scalar(out=rms_b[0:1, go:go + gn],
                                                    in0=pp[4][0:1, 0:gn],
                                                    scalar1=1.0 / D, scalar2=1e-5,
                                                    op0=AluOpType.mult, op1=AluOpType.add),
            reads=[PPK[4]], writes=[("rms", 0)])
        vop(lambda e, go=go, gn=gn: e.activation(out=rms_b[0:1, go:go + gn],
                                                 in_=rms_b[0:1, go:go + gn], func=AF.Sqrt),
            reads=[("rms", 0)], writes=[("rms", 0)], eng="a")
        vop(lambda e, go=go, gn=gn: e.reciprocal(out=rms_b[0:1, go:go + gn],
                                                 in_=rms_b[0:1, go:go + gn]),
            reads=[("rms", 0)], writes=[("rms", 0)])
    kk = 1
    while kk < 128:
        dma(rms_b[kk:2 * kk, :], rms_b[0:kk, :], chan="ch_rms%d" % kk,
            reads=[("rms", kk // 2) if kk > 1 else ("rms", 0)], writes=[("rms", kk)])
        kk *= 2
    RMSD = ("rms", 64)

    for m in range(KC):
        vop(lambda e, m=m: e.tensor_tensor(out=sqtmp[:],
                                           in0=hbuf[:, TP * m + WARM:TP * (m + 1)],
                                           in1=rms_b[:], op=AluOpType.mult),
            reads=[("h", m), RMSD], writes=[("sqtmp",)], eng="g")
        vop(lambda e, m=m: e.tensor_tensor(out=sqtmp[:], in0=sqtmp[:],
                                           in1=slg[:, T_MAIN * m:T_MAIN * (m + 1)],
                                           op=AluOpType.mult),
            reads=[("sqtmp",), ("slg", m)], writes=[("sqtmp",)])
        vop(lambda e, m=m: e.tensor_scalar(out=oT[:, T_MAIN * m:T_MAIN * (m + 1)],
                                           in0=sqtmp[:],
                                           scalar1=gnw_sb[:, m:m + 1], scalar2=None,
                                           op0=AluOpType.mult),
            reads=[("sqtmp",), ("gnw",)], writes=[("oT", m)])
    for tm in range(NTM):
        for n in range(2):
            pidx = 5 + n
            for k in range(KC):
                mm(pp[pidx], oT[:, T_MAIN * k + 128 * tm:T_MAIN * k + 128 * (tm + 1)],
                   wo1_sb[:, D * k + 512 * n:D * k + 512 * (n + 1)],
                   reads=[("oT", k), ("wo1", k)], writes=[PPK[pidx]],
                   start=(k == 0), stop=(k == KC - 1))
        for n in range(2):
            pidx = 5 + n
            vop(lambda e, n=n, pidx=pidx: e.activation(
                out=s1_t[:, 512 * n:512 * (n + 1)], in_=pp[pidx][:], func=AF.Copy),
                reads=[PPK[pidx]], writes=[("s1_t",)], eng="a")
        dma(s1_scr[128 * tm:128 * (tm + 1), :], s1_t[:], chan="st_s1",
            reads=[("s1_t",)], writes=[("s1_scr", tm)])

    ms.phase()
    free_group("gC2b")
    free_group("gC2")

    weE_sb = [alloc("gDD", "weE%d" % i, (128, KC * D), BF16) for i in range(2)]
    be_sb = alloc("gDD", "be_sb", (1, 2 * D), BF16)
    acc = alloc("gDD", "acc", (128, NTM * D), F32)
    g1_t = alloc("gDD", "g1_t", (128, D), F32)
    y_t = alloc("gDD", "y_t", (128, D), F32)
    s1l = alloc("gDD", "s1l", (128, 2 * D), F32)

    for e_i in range(NEXP):
        sl = e_i % 2
        for k in range(KC):
            stg, skey, ch = ((stg_state.a, ("wst",), "ch_wst") if k % 2 == 0
                             else (stg_state.b, ("wst2",), "ch_wst2"))
            dma(stg[:, 0:D], apin["WeE"][D * e_i + 128 * k:D * e_i + 128 * (k + 1), :],
                chan=ch, reads=[skey], writes=[skey], eng="a")
            vop(lambda e, sl=sl, k=k, stg=stg: e.tensor_copy(
                out=weE_sb[sl][:, D * k:D * (k + 1)], in_=stg[:, 0:D]),
                reads=[skey], writes=[("weE", sl, k)])
        dma(stg_state.a[0:1, 0:D], apin["beE"][e_i:e_i + 1, :], chan="ch_wst",
            reads=[("wst",)], writes=[("wst",)])
        vop(lambda e, sl=sl, stg=stg_state.a: e.tensor_copy(
            out=be_sb[0:1, D * sl:D * (sl + 1)], in_=stg[0:1, 0:D]),
            reads=[("wst",)], writes=[("beE", sl)])
        pb = 2 * (e_i % 3)
        for tm in range(NTM):
            t0 = WARM + 128 * tm
            for n in range(2):
                mm(pp[pb + n], ones1[:], be_sb[0:1, D * sl + 512 * n:D * sl + 512 * (n + 1)],
                   reads=[("ones1",), ("beE", sl)], writes=[PPK[pb + n]],
                   start=True, stop=False)
                for k in range(KC):
                    mm(pp[pb + n], xn1T[:, TP * k + t0:TP * k + t0 + 128],
                       weE_sb[sl][:, D * k + 512 * n:D * k + 512 * (n + 1)],
                       reads=[("xn1T", k, t0 // 128), ("weE", sl, k)],
                       writes=[PPK[pb + n]], start=False, stop=(k == KC - 1))
            for n in range(2):
                pidx = pb + n
                alc = alp_sb[:, NEXP * tm + e_i:NEXP * tm + e_i + 1]
                if e_i == 0:
                    vop(lambda e, tm=tm, n=n, alc=alc, pidx=pidx: e.tensor_scalar(
                        out=acc[:, D * tm + 512 * n:D * tm + 512 * (n + 1)],
                        in0=pp[pidx][:], scalar1=alc, scalar2=None,
                        op0=AluOpType.mult),
                        reads=[PPK[pidx], ("alpha", tm)], writes=[("acc", tm)])
                else:
                    vop(lambda e, tm=tm, n=n, alc=alc, pidx=pidx: e.scalar_tensor_tensor(
                        out=acc[:, D * tm + 512 * n:D * tm + 512 * (n + 1)],
                        in0=pp[pidx][:], scalar=alc,
                        in1=acc[:, D * tm + 512 * n:D * tm + 512 * (n + 1)],
                        op0=AluOpType.mult, op1=AluOpType.add),
                        reads=[PPK[pidx], ("alpha", tm), ("acc", tm)],
                        writes=[("acc", tm)])

    for tm in range(NTM):
        t0 = WARM + 128 * tm
        par = tm % 2
        xb = xbuf2[:, D * par:D * (par + 1)]
        xbk = ("xbuf", par)
        dma(xb, x1_scr[t0:t0 + 128, :], chan="ch_x%d" % par,
            reads=[xbk, ("x1_scr", t0 // 128)], writes=[xbk])
        s1v = s1l[:, D * par:D * (par + 1)]
        dma(s1v, s1_scr[128 * tm:128 * (tm + 1), :], chan="ch_s1l%d" % par,
            reads=[("s1l", par), ("s1_scr", tm)], writes=[("s1l", par)])
        vop(lambda e, tm=tm: e.activation(out=g1_t[:], in_=acc[:, D * tm:D * (tm + 1)],
                                          func=AF.Silu),
            reads=[("acc", tm)], writes=[("g1_t",)], eng="a")
        if debug:
            dma(dbg["dbg_moe"][128 * tm:128 * (tm + 1), :],
                acc[:, D * tm:D * (tm + 1)], chan="dbgc",
                reads=[("acc", tm)], writes=[])
        vop(lambda e, s1v=s1v: e.tensor_tensor(out=g1_t[:], in0=g1_t[:], in1=s1v,
                                      op=AluOpType.mult),
            reads=[("g1_t",), ("s1l", par)], writes=[("g1_t",)])
        vop(lambda e, xb=xb: e.tensor_tensor(out=y_t[:], in0=g1_t[:], in1=xb,
                                             op=AluOpType.add),
            reads=[("g1_t",), xbk], writes=[("y_t",)], eng="g")
        dma(yout[128 * tm:128 * (tm + 1), :], y_t[:], chan="st_y%d" % par,
            reads=[("y_t",)], writes=[("yout", tm)], eng="g")

    ms.op("sy", lambda e: e.nop(), reads=[("yout", tm) for tm in range(NTM)], writes=[])

    with nc.Block() as block:
        ms.finalize(block)

    return nc, apin



def _host_stage(inputs, T_MAIN, L):
    x = np.asarray(inputs["L0_attn_x"], dtype=np.float32)
    B = x.shape[0]
    idx = np.linspace(0.0, L - 1, LEN).astype(np.int32)
    conv_i = np.ascontiguousarray(
        np.asarray(inputs["L1_attn_conv_i"], np.float32).reshape(KC, 128, 4)
        .transpose(1, 0, 2).reshape(128, KC * 4))
    conv_f = np.ascontiguousarray(
        np.asarray(inputs["L1_attn_conv_f"], np.float32).reshape(KC, 128, 4)
        .transpose(1, 0, 2).reshape(128, KC * 4))
    gnw = np.ascontiguousarray(
        np.asarray(inputs["L1_attn_g_norm_weight"], np.float32).reshape(KC, 128).T)
    shared = {
        "Wq": np.asarray(inputs["L0_attn_Wq"], np.float32),
        "Wk": np.asarray(inputs["L0_attn_Wk"], np.float32),
        "Wo0": np.asarray(inputs["L0_attn_Wo"], np.float32),
        "bq": np.asarray(inputs["L0_attn_bq"], np.float32).reshape(1, D),
        "bk": np.asarray(inputs["L0_attn_bk"], np.float32).reshape(1, D),
        "bo": np.asarray(inputs["L0_attn_bo"], np.float32).reshape(1, D),
        "WeL": np.asarray(inputs["L0_attn_We"], np.float32),
        "WrL": np.asarray(inputs["L0_attn_Wr"], np.float32),
        "Wc": np.asarray(inputs["L0_attn_Wc"], np.float32),
        "bc": np.tile(np.asarray(inputs["L0_attn_bc"], np.float32), 2).reshape(128, 1),
        "wg_nat": np.asarray(inputs["L0_ffn_w_gate"], np.float32),
        "wu_nat": np.asarray(inputs["L0_ffn_w_up"], np.float32),
        "wd_nat": np.asarray(inputs["L0_ffn_w_down"], np.float32),
        "Wi": np.asarray(inputs["L1_attn_Wi"], np.float32),
        "Wf": np.asarray(inputs["L1_attn_Wf"], np.float32),
        "Wgr": np.asarray(inputs["L1_attn_Wg"], np.float32),
        "Wo1": np.asarray(inputs["L1_attn_Wo"], np.float32),
        "conv_i": conv_i, "conv_f": conv_f, "gnw": gnw,
        "WeE": np.asarray(inputs["L1_ffn_We"], np.float32).reshape(NEXP * D, D),
        "beE": np.asarray(inputs["L1_ffn_be"], np.float32),
        "WgE": np.asarray(inputs["L1_ffn_Wg"], np.float32),
        "bgE": np.asarray(inputs["L1_ffn_bg"], np.float32).reshape(1, NEXP),
    }
    in_maps = []
    per_b = L // T_MAIN
    for c in range(B * per_b):
        b, pos = c // per_b, c % per_b
        t0 = pos * T_MAIN
        if pos == 0:
            xin = np.concatenate([x[b, 0:WARM], x[b, t0:t0 + T_MAIN]], axis=0)
            mw = np.zeros((128, 1), np.float32)
        else:
            xin = x[b, t0 - WARM:t0 + T_MAIN]
            mw = np.ones((128, 1), np.float32)
        m = dict(shared)
        m["xin"] = np.ascontiguousarray(xin)
        m["xg"] = np.ascontiguousarray(x[b, idx, :])
        m["mwarm"] = mw
        in_maps.append(m)
    return in_maps, B, per_b


def kernel(**inputs):
    T_MAIN, L = 2048, 8192
    in_maps, B, per_b = _host_stage(inputs, T_MAIN, L)
    nc, apin = build_program(T_MAIN)
    res = run_bass_kernel_spmd(nc, in_maps, core_ids=list(range(B * per_b)))
    shards = [np.asarray(r["y"]) for r in res.results]
    return np.stack([np.concatenate(shards[b * per_b:(b + 1) * per_b], axis=0)
                     for b in range(B)], axis=0)


# version 36
# speedup vs baseline: 1.0390x; 1.0197x over previous
import numpy as np
import ml_dtypes
import concourse.bass as bass
import concourse.mybir as mybir
from concourse.alu_op_type import AluOpType
from concourse.bass_utils import run_bass_kernel_spmd

F32 = mybir.dt.float32
BF16 = mybir.dt.bfloat16
I32 = mybir.dt.int32
AF = mybir.ActivationFunctionType
AX = mybir.AxisListType

D = 1024
KC = D // 128
H = 16
DKH = 64
LOWR = 20
LEN = 200
IFF = 2816
ICH = IFF // 128
NEXP = 8
WARM = 128
BF16NP = ml_dtypes.bfloat16
ENGINES = ("sy", "v", "a", "g", "t")


def _split(total, step):
    out, o = [], 0
    while o < total:
        n = min(step, total - o)
        out.append((o, n))
        o += n
    return out



class Op:
    __slots__ = ("eng", "fn", "idx", "pos", "is_dma", "chan", "chan_val",
                 "need_inc", "inc_seq", "waits", "drain")

    def __init__(self, eng, fn, is_dma, chan):
        self.eng = eng
        self.fn = fn
        self.is_dma = is_dma
        self.chan = chan
        self.chan_val = 0
        self.need_inc = False
        self.inc_seq = 0
        self.waits = []
        self.drain = False


class MS:
    def __init__(self, nc):
        self.nc = nc
        self.ops = []
        self.by_eng = {e: [] for e in ENGINES}
        self.res = {}
        self.chan_tot = {}
        self.sems = {}
        self.barrier_fns = {}
        self.war_keys = set()

    def _edge(self, p, c, raw=True):
        if p is c:
            return
        if p.is_dma:
            if c.is_dma and c.chan == p.chan:
                return
            c.waits.append(("ch", p.chan, p.chan_val))
        elif p.eng == c.eng:
            if (raw and p.eng in ("v", "a", "g")) or (not raw and p.eng == "a"):
                p.need_inc = True
                c.waits.append(("op", p))
        else:
            p.need_inc = True
            c.waits.append(("op", p))

    def op(self, eng, fn, reads=(), writes=(), dma_chan=None):
        o = Op(eng, fn, dma_chan is not None, dma_chan)
        if o.is_dma:
            self.chan_tot[o.chan] = self.chan_tot.get(o.chan, 0) + 16
            o.chan_val = self.chan_tot[o.chan]
        for r in reads:
            s = self.res.get(r)
            if s and s["w"] is not None:
                self._edge(s["w"], o)
        for r in writes:
            s = self.res.get(r)
            if s:
                if s["w"] is not None:
                    self._edge(s["w"], o, raw=False)
                for rd in s["r"]:
                    self._edge(rd, o, raw=(r in self.war_keys))

            self.res[r] = {"w": o, "r": []}
        for r in reads:
            self.res.setdefault(r, {"w": None, "r": []})["r"].append(o)
        o.idx = len(self.ops)
        self.ops.append(o)
        o.pos = len(self.by_eng[o.eng])
        self.by_eng[o.eng].append(o)
        return o

    def phase(self):
        lasts = []
        for e in ENGINES:
            for o in reversed(self.by_eng[e]):
                if not o.is_dma:
                    o.need_inc = True
                    lasts.append(o)
                    break
        chans = dict(self.chan_tot)
        for e in ENGINES:
            bf = self.barrier_fns.get(e)
            o = Op(e, bf() if bf else (lambda en: en.nop()), False, None)
            for p in lasts:
                if p.eng != e:
                    o.waits.append(("op", p))
            for ch, v in chans.items():
                o.waits.append(("ch", ch, v))
            o.idx = len(self.ops)
            self.ops.append(o)
            o.pos = len(self.by_eng[e])
            self.by_eng[e].append(o)
        self.res = {}

    def finalize(self, block):
        inc_cnt = {e: 0 for e in ENGINES}
        for o in self.ops:
            if o.need_inc and not o.is_dma:
                inc_cnt[o.eng] += 1
                o.inc_seq = inc_cnt[o.eng]
        for e in ENGINES:
            if inc_cnt[e]:
                self.sems["eng:" + e] = self.nc.semaphore("sm_" + e).__enter__()
        for ch in self.chan_tot:
            self.sems["ch:" + ch] = self.nc.semaphore("sc_" + ch).__enter__()
        ms = self

        def mk(eng_name):
            def inner(e):
                waited = {}
                for o in ms.by_eng[eng_name]:
                    wl = {}
                    for w in o.waits:
                        if w[0] == "ch":
                            key, val = "ch:" + w[1], w[2]
                        else:
                            key, val = "eng:" + w[1].eng, w[1].inc_seq
                        wl[key] = max(wl.get(key, 0), val)
                    for key, val in sorted(wl.items()):
                        if waited.get(key, -1) < val:
                            e.wait_ge(ms.sems[key], val)
                            waited[key] = val
                    if o.drain:
                        e.drain()
                    inst = o.fn(e)
                    if o.is_dma:
                        inst.then_inc(ms.sems["ch:" + o.chan], 16)
                    elif o.need_inc:
                        if inst is None:
                            e.sem_inc(ms.sems["eng:" + o.eng], 1)
                        else:
                            inst.then_inc(ms.sems["eng:" + o.eng], 1)
            return inner

        for name, deco in [("sy", block.sync), ("v", block.vector),
                           ("a", block.scalar), ("g", block.gpsimd),
                           ("t", block.tensor)]:
            if self.by_eng[name]:
                deco(mk(name))


class P:
    pass



def build_program(T_MAIN, debug=False):
    TP = WARM + T_MAIN
    NT = TP // 128
    NTM = T_MAIN // 128
    GRA = _split(TP, 512)
    GRP = _split(TP, 256)
    GRPC = _split(TP, 512)
    GRPM = _split(T_MAIN, 512)

    nc = bass.Bass("TRN2", target_bir_lowering=False)
    ms = MS(nc)

    apin = {}

    def din(name, shape, dtype=F32):
        apin[name] = nc.dram_tensor(name, list(shape), dtype, kind="ExternalInput").ap()

    din("xin", (TP, D)); din("xg", (LEN, D)); din("mwarm", (128, 1))
    din("Wq", (D, D)); din("Wk", (D, D)); din("Wo0", (D, D))
    din("bq", (1, D)); din("bk", (1, D)); din("bo", (1, D))
    din("WeL", (H, LEN, LOWR)); din("WrL", (H, LEN, LOWR))
    din("Wc", (2 * LOWR, DKH)); din("bc", (128, 1))
    din("wg_nat", (IFF, D)); din("wu_nat", (IFF, D)); din("wd_nat", (D, IFF))
    din("Wi", (D, D)); din("Wf", (D, D)); din("Wgr", (D, D)); din("Wo1", (D, D))
    din("conv_i", (128, KC * 4)); din("conv_f", (128, KC * 4))
    din("gnw", (128, KC))
    din("WeE", (NEXP * D, D)); din("beE", (NEXP, D))
    din("WgE", (D, NEXP)); din("bgE", (1, NEXP))

    yout = nc.dram_tensor("y", [T_MAIN, D], F32, kind="ExternalOutput").ap()
    s_scr = nc.dram_tensor("s_scr", [TP, D], F32, kind="Internal").ap()
    x1_scr = nc.dram_tensor("x1_scr", [TP, D], F32, kind="Internal").ap()
    s1_scr = nc.dram_tensor("s1_scr", [T_MAIN, D], F32, kind="Internal").ap()
    wgu_scr = nc.dram_tensor("wgu_scr", [2 * ICH * 128, KC * 128], BF16, kind="Internal").ap()
    wd_scr = nc.dram_tensor("wd_scr", [KC * 128, ICH * 128], BF16, kind="Internal").ap()
    dbg = {}
    if debug:
        for nm, shp in [("dbg_s", (TP, D)), ("dbg_x1", (TP, D)),
                        ("dbg_h", (128, TP)), ("dbg_moe", (T_MAIN, D)),
                        ("dbg_alpha", (T_MAIN, NEXP))]:
            dbg[nm] = nc.dram_tensor(nm, list(shp), F32, kind="ExternalOutput").ap()

    groups = {}

    def alloc(group, name, shape, dtype=F32):
        g = nc.sbuf_tensor(name, list(shape), dtype)
        hh = g.__enter__()
        groups.setdefault(group, []).append(g)
        return hh

    def free_group(group):
        for g in reversed(groups.pop(group, [])):
            g.__exit__(None, None, None)

    pp = []
    for i in range(8):
        g = nc.psum_tensor("pp%d" % i, [128, 512], F32)
        pp.append(g.__enter__())
    PPK = [("pp", i) for i in range(8)]

    ones1 = alloc("főr", "ones1", (1, 128), BF16)
    onesf = alloc("főr", "onesf", (128, 1), F32)
    mwarm_sb = alloc("főr", "mwarm_sb", (128, 1), F32)
    bq_sb = alloc("főr", "bq_sb", (1, D), BF16)
    bk_sb = alloc("főr", "bk_sb", (1, D), BF16)
    bo_sb = alloc("főr", "bo_sb", (1, D), BF16)
    bg_sb = alloc("főr", "bg_sb", (1, NEXP), BF16)
    bc_sb = alloc("főr", "bc_sb", (128, 1), F32)
    ln_ab = alloc("főr", "ln_ab", (128, 2 * (NT + 2)), F32)
    xbuf2 = alloc("főr", "xbuf2", (128, 2 * D), F32)
    ms.war_keys.update([("tmpf",), ("h_sl",)])
    bscr = alloc("főr", "bscr", (1, 8), F32)
    _bar_i = [0]

    def _g_barrier_fn():
        i = _bar_i[0]
        _bar_i[0] += 1
        return lambda en, i=i: en.memset(bscr[0:1, i:i + 1], 0.0)

    ms.barrier_fns["g"] = _g_barrier_fn

    stg_state = P()
    lnv = P()
    lnv.bn6 = alloc("főr", "ln_bn6", (128, 12), F32)
    lnv.mv = alloc("főr", "ln_mv", (128, 2), F32)
    lnv.var = alloc("főr", "ln_var", (128, 1), F32)

    qb = P()
    QW = 1024
    qb.amax = alloc("gB", "q_amax", (128, QW // 16), F32)
    qb.scale = alloc("gB", "q_scale", (128, QW // 16), F32)
    qb.v = alloc("gB", "q_v", (128, QW), F32)
    qb.f1 = alloc("gB", "q_f1", (128, QW), F32)
    qb.g = alloc("gB", "q_g", (128, QW), F32)
    qb.sg = alloc("gB", "q_sg", (128, QW), BF16)

    stg_state.a = alloc("gA", "wstW", (128, D), F32)
    stg_state.b = alloc("gA", "wstW2", (128, D), F32)
    wq_sb = alloc("gA", "wq_sb", (128, KC * D), BF16)
    wk_sb = alloc("gA", "wk_sb", (128, KC * D), BF16)
    wo_sb = alloc("gA", "wo_sb", (128, KC * D), BF16)
    ablk = alloc("gA", "ablk", (128, KC * 128), BF16)
    bblk = alloc("gA", "bblk", (128, KC * 128), BF16)

    def dma(dst_ap, src_ap, chan, reads, writes, transpose=False, eng="sy"):
        if transpose:
            ms.op(eng, lambda e, d=dst_ap, s=src_ap: e.dma_start_transpose(d, s),
                  reads=reads, writes=writes, dma_chan=chan)
        else:
            ms.op(eng, lambda e, d=dst_ap, s=src_ap: e.dma_start(out=d, in_=s),
                  reads=reads, writes=writes, dma_chan=chan)

    def vop(fn, reads, writes, eng="v"):
        ms.op(eng, fn, reads=reads, writes=writes)

    def mm(ps_ap, lhs_ap, rhs_ap, reads, writes, start, stop, tile_position=None):
        if not isinstance(ps_ap, bass.AP):
            ps_ap = ps_ap[:]
        if not isinstance(lhs_ap, bass.AP):
            lhs_ap = lhs_ap[:]
        if not isinstance(rhs_ap, bass.AP):
            rhs_ap = rhs_ap[:]
        ms.op("t", lambda e, o=ps_ap, l=lhs_ap, r=rhs_ap: e.matmul(
            o, l, r, start=start, stop=stop, tile_position=tile_position),
            reads=reads, writes=writes)

    def cast_weight(srcname, dst_sb, dstkey, nchunks=KC, width=D):
        for k in range(nchunks):
            stg, skey, ch = ((stg_state.a, ("wst",), "ch_wst") if k % 2 == 0
                             else (stg_state.b, ("wst2",), "ch_wst2"))
            dma(stg[:, 0:width], apin[srcname][128 * k:128 * (k + 1), :],
                chan=ch, reads=[skey], writes=[skey], eng="a")
            vop(lambda e, k=k, stg=stg: e.tensor_copy(
                out=dst_sb[:, width * k:width * (k + 1)], in_=stg[:, 0:width]),
                reads=[skey], writes=[(dstkey, k)])

    def emit_quant_p1(x_key, x_ap, F, P_, q, vkey, v_ap, sgkey, sg_ap):
        vop(lambda e: e.activation(out=v_ap, in_=x_ap, func=AF.Abs),
            reads=[x_key], writes=[vkey], eng="a")
        vop(lambda e: e.activation(out=sg_ap, in_=x_ap, func=AF.Sign),
            reads=[x_key], writes=[sgkey], eng="a")

    def emit_quant_p2(x_key, x_ap, out_key, out_ap, F, P_, q, vkey, v_ap,
                      sgkey, sg_ap):
        nb = F // 16
        vop(lambda e: e.tensor_reduce(out=q.amax[0:P_, 0:nb],
                                      in_=v_ap.rearrange("p (b w) -> p b w", w=16),
                                      axis=AX.X, op=AluOpType.max),
            reads=[vkey], writes=[("q.amax",)])
        vop(lambda e: e.tensor_scalar(out=q.scale[0:P_, 0:nb], in0=q.amax[0:P_, 0:nb],
                                      scalar1=1.0 / 6.0, scalar2=1e-12,
                                      op0=AluOpType.mult, op1=AluOpType.max),
            reads=[("q.amax",)], writes=[("q.scale",)])
        vop(lambda e: e.reciprocal(out=q.amax[0:P_, 0:nb], in_=q.scale[0:P_, 0:nb]),
            reads=[("q.scale",)], writes=[("q.amax",)])
        rb = q.amax[0:P_, 0:nb].unsqueeze(2).broadcast_to([P_, nb, 16])
        vop(lambda e: e.tensor_tensor(out=v_ap.rearrange("p (b w) -> p b w", w=16),
                                      in0=v_ap.rearrange("p (b w) -> p b w", w=16),
                                      in1=rb, op=AluOpType.mult),
            reads=[vkey, ("q.amax",)], writes=[vkey])
        vop(lambda e: e.tensor_scalar(out=q.g[0:P_, 0:F], in0=v_ap,
                                      scalar1=5.0, scalar2=2.0,
                                      op0=AluOpType.is_gt, op1=AluOpType.mult),
            reads=[vkey], writes=[("q.g",)])
        for thr in (2.5, 3.5):
            vop(lambda e, thr=thr: e.scalar_tensor_tensor(
                out=q.g[0:P_, 0:F], in0=v_ap, scalar=thr,
                in1=q.g[0:P_, 0:F], op0=AluOpType.is_gt, op1=AluOpType.add),
                reads=[vkey, ("q.g",)], writes=[("q.g",)])
        vop(lambda e: e.tensor_scalar(out=q.f1[0:P_, 0:F], in0=v_ap,
                                      scalar1=0.25, scalar2=None, op0=AluOpType.is_gt),
            reads=[vkey], writes=[("q.f1",)])
        for thr in (0.75, 1.25, 1.75):
            vop(lambda e, thr=thr: e.scalar_tensor_tensor(
                out=q.f1[0:P_, 0:F], in0=v_ap, scalar=thr,
                in1=q.f1[0:P_, 0:F], op0=AluOpType.is_gt, op1=AluOpType.add),
                reads=[vkey, ("q.f1",)], writes=[("q.f1",)])
        vop(lambda e: e.scalar_tensor_tensor(out=q.g[0:P_, 0:F], in0=q.f1[0:P_, 0:F],
                                             scalar=0.5, in1=q.g[0:P_, 0:F],
                                             op0=AluOpType.mult, op1=AluOpType.add),
            reads=[("q.f1",), ("q.g",)], writes=[("q.g",)])
        vop(lambda e: e.tensor_tensor(out=q.g[0:P_, 0:F], in0=q.g[0:P_, 0:F],
                                      in1=sg_ap, op=AluOpType.mult),
            reads=[("q.g",), sgkey], writes=[("q.g",)])
        sb2 = q.scale[0:P_, 0:nb].unsqueeze(2).broadcast_to([P_, nb, 16])
        vop(lambda e: e.tensor_tensor(out=out_ap.rearrange("p (b w) -> p b w", w=16),
                                      in0=q.g[0:P_, 0:F].rearrange("p (b w) -> p b w", w=16),
                                      in1=sb2, op=AluOpType.mult),
            reads=[("q.g",), ("q.scale",)], writes=[out_key])

    def emit_quant(x_key, x_ap, out_key, out_ap, F, P_=128, q=None):
        if q is None:
            q = qb
        vk, sk = ("q.v",), ("q.sg",)
        emit_quant_p1(x_key, x_ap, F, P_, q, vk, q.v[0:P_, 0:F], sk, q.sg[0:P_, 0:F])
        emit_quant_p2(x_key, x_ap, out_key, out_ap, F, P_, q, vk,
                      q.v[0:P_, 0:F], sk, q.sg[0:P_, 0:F])

    def emit_ln(x_key, x_ap, ab_sb, ab_key, slot, outs, P_=128,
                scr=None, scr_key=None):
        a_ap = ab_sb[0:P_, 2 * slot:2 * slot + 1]
        b_ap = ab_sb[0:P_, 2 * slot + 1:2 * slot + 2]
        vop(lambda e: e.bn_stats(out=lnv.bn6[0:P_, 0:6], in_=x_ap[:, 0:D // 2]),
            reads=[x_key], writes=[("ln.bn6",)])
        vop(lambda e: e.bn_stats(out=lnv.bn6[0:P_, 6:12], in_=x_ap[:, D // 2:D]),
            reads=[x_key], writes=[("ln.bn6",)])
        vop(lambda e: e.bn_aggr(out=lnv.mv[0:P_, :], in_=lnv.bn6[0:P_, 0:12]),
            reads=[("ln.bn6",)], writes=[("ln.mv",)])
        vop(lambda e: e.tensor_scalar(out=lnv.var[0:P_, :], in0=lnv.mv[0:P_, 1:2],
                                      scalar1=1e-5, scalar2=None, op0=AluOpType.add),
            reads=[("ln.mv",)], writes=[("ln.var",)])
        vop(lambda e: e.activation(out=lnv.var[0:P_, :], in_=lnv.var[0:P_, :],
                                   func=AF.Sqrt),
            reads=[("ln.var",)], writes=[("ln.var",)], eng="a")
        vop(lambda e: e.reciprocal(out=a_ap, in_=lnv.var[0:P_, :]),
            reads=[("ln.var",)], writes=[(ab_key, slot, "a")])
        vop(lambda e: e.scalar_tensor_tensor(out=b_ap, in0=lnv.mv[0:P_, 0:1],
                                             scalar=-1.0, in1=a_ap,
                                             op0=AluOpType.mult, op1=AluOpType.mult),
            reads=[("ln.mv",), (ab_key, slot, "a")], writes=[(ab_key, slot, "b")])
        for okey, oap in outs:
            vop(lambda e, oap=oap: e.tensor_scalar(out=oap, in_=None, in0=x_ap,
                                                   scalar1=a_ap, scalar2=b_ap,
                                                   op0=AluOpType.mult, op1=AluOpType.add)
                if False else
                e.tensor_scalar(out=oap, in0=x_ap, scalar1=a_ap, scalar2=b_ap,
                                op0=AluOpType.mult, op1=AluOpType.add),
                reads=[x_key, (ab_key, slot, "a"), (ab_key, slot, "b")],
                writes=[okey])

    dma(mwarm_sb[:], apin["mwarm"][:], chan="c_mw", reads=[], writes=[("mwarm",)])
    dma(bc_sb[:], apin["bc"][:], chan="c_bc", reads=[], writes=[("bc",)])
    vop(lambda e: e.memset(ones1[:], 1.0), reads=[], writes=[("ones1",)])
    vop(lambda e: e.memset(onesf[:], 1.0), reads=[], writes=[("onesf",)])
    vop(lambda e: e.memset(ablk[:], 0.0), reads=[], writes=[("ablk", "z")])
    vop(lambda e: e.memset(bblk[:], 0.0), reads=[], writes=[("bblk", "z")])

    for nm, dst, w in [("bq", bq_sb, D), ("bk", bk_sb, D), ("bo", bo_sb, D),
                       ("bgE", bg_sb, NEXP)]:
        dma(stg_state.a[0:1, 0:w], apin[nm][:], chan="ch_wst", reads=[("wst",)], writes=[("wst",)])
        vop(lambda e, dst=dst, w=w, stg=stg_state.a: e.tensor_copy(
            out=dst[0:1, 0:w], in_=stg[0:1, 0:w]),
            reads=[("wst",)], writes=[("b_" + nm,)])

    cast_weight("Wq", wq_sb, "wq")
    cast_weight("Wk", wk_sb, "wk")
    cast_weight("Wo0", wo_sb, "wo")

    xg_sb = alloc("gW1", "xg_sb", (128, 2 * D), F32)
    xng_sb = alloc("gW1", "xng_sb", (128, 2 * D), BF16)
    wel_sb = alloc("gW1", "wel_sb", (128, 2 * LOWR), BF16)
    wrl_sb = alloc("gW1", "wrl_sb", (128, 2 * LOWR), BF16)
    wc_sb = alloc("gW1", "wc_sb", (LOWR, 2 * DKH), BF16)
    wet_sb = alloc("gW1", "wet_sb", (LOWR, 2 * DKH), BF16)
    LEN2 = LEN - 128

    dma(xg_sb[0:128, 0:D], apin["xg"][0:128, :], chan="cxg", reads=[],
        writes=[("xg", 0), ("xg", 1)])
    dma(xg_sb[0:LEN2, D:2 * D], apin["xg"][128:LEN, :], chan="cxg", reads=[],
        writes=[("xg", 0), ("xg", 1)])
    dma(stg_state.a[0:LOWR, 0:DKH], apin["Wc"][0:LOWR, :], chan="ch_wst",
        reads=[("wst",)], writes=[("wst",)])
    dma(stg_state.a[0:LOWR, DKH:2 * DKH], apin["Wc"][LOWR:2 * LOWR, :], chan="ch_wst",
        reads=[("wst",)], writes=[("wst",)])
    vop(lambda e, stg=stg_state.a: e.tensor_copy(out=wc_sb[:], in_=stg[0:LOWR, 0:2 * DKH]),
        reads=[("wst",)], writes=[("wc",)])
    emit_ln(("xg", 0), xg_sb[0:128, 0:D], ln_ab, "lng", NT, [(("xng", 0), xng_sb[0:128, 0:D])])
    emit_ln(("xg", 1), xg_sb[0:LEN2, D:2 * D], ln_ab, "lng", NT + 1,
            [(("xng", 1), xng_sb[0:LEN2, D:2 * D])], P_=LEN2)

    ps_we = pp[0][0:LOWR, 0:DKH]
    ps_ab = pp[1]
    for h in range(H):
        j, par = h // 2, h % 2
        dma(stg_state.a[0:128, 0:LOWR], apin["WeL"][h, 0:128, :], chan="ch_wst",
            reads=[("wst",)], writes=[("wst",)])
        dma(stg_state.a[0:LEN2, LOWR:2 * LOWR], apin["WeL"][h, 128:LEN, :], chan="ch_wst",
            reads=[("wst",)], writes=[("wst",)])
        vop(lambda e, stg=stg_state.a: e.tensor_copy(out=wel_sb[:, :], in_=stg[0:128, 0:2 * LOWR]),
            reads=[("wst",)], writes=[("wel",)])
        dma(stg_state.a[0:128, 0:LOWR], apin["WrL"][h, 0:128, :], chan="ch_wst",
            reads=[("wst",)], writes=[("wst",)])
        dma(stg_state.a[0:LEN2, LOWR:2 * LOWR], apin["WrL"][h, 128:LEN, :], chan="ch_wst",
            reads=[("wst",)], writes=[("wst",)])
        vop(lambda e, stg=stg_state.a: e.tensor_copy(out=wrl_sb[:, :], in_=stg[0:128, 0:2 * LOWR]),
            reads=[("wst",)], writes=[("wrl",)])
        for src_sb, srckey, half, blk, blkkey in [
                (wel_sb, ("wel",), 0, ablk, "ablk"), (wrl_sb, ("wrl",), 1, bblk, "bblk")]:
            wcs = wc_sb[0:LOWR, 0:DKH] if half == 0 else wc_sb[0:LOWR, DKH:2 * DKH]
            mm(ps_we, src_sb[0:128, 0:LOWR], xng_sb[0:128, DKH * h:DKH * (h + 1)],
               reads=[srckey, ("xng", 0)], writes=[PPK[0]], start=True, stop=False)
            mm(ps_we, src_sb[0:LEN2, LOWR:2 * LOWR],
               xng_sb[0:LEN2, D + DKH * h:D + DKH * (h + 1)],
               reads=[srckey, ("xng", 1)], writes=[PPK[0]], start=False, stop=True)
            wt = wet_sb[:, DKH * half:DKH * (half + 1)]
            vop(lambda e, wt=wt: e.activation(out=wt, in_=ps_we, func=AF.Copy),
                reads=[PPK[0]], writes=[("wet", half)], eng="a")
            mm(ps_ab[64 * par:64 * (par + 1), 0:DKH], wt, wcs,
               reads=[("wet", half), ("wc",)], writes=[PPK[1]],
               start=True, stop=True, tile_position=(0, 64 * par))
            dst = blk[64 * par:64 * (par + 1), 128 * j + 64 * par:128 * j + 64 * (par + 1)]
            vop(lambda e, dst=dst, par=par: e.activation(
                out=dst, in_=ps_ab[64 * par:64 * (par + 1), 0:DKH], func=AF.Copy),
                reads=[PPK[1], (blkkey, "z")], writes=[(blkkey, j, par)], eng="a")

    ms.phase()
    free_group("gW1")

    qa = P()
    QA = 2048
    qa.amax = alloc("gAq", "qa_amax", (128, QA // 16), F32)
    qa.scale = alloc("gAq", "qa_scale", (128, QA // 16), F32)
    qa.v2 = [alloc("gAq", "qa_v%d" % i, (128, QA), F32) for i in range(2)]
    qa.f1 = alloc("gAq", "qa_f1", (128, QA), F32)
    qa.g = alloc("gAq", "qa_g", (128, QA), F32)
    qa.sg2 = [alloc("gAq", "qa_sg%d" % i, (128, QA), BF16) for i in range(2)]
    wstA = [alloc("gAq", "wstA%d" % i, (128, QA), F32) for i in range(2)]
    wqqA = alloc("gAq", "wqqA", (128, QA), BF16)
    recst = [alloc("gAq", "recst%d" % i, (128, 1408), BF16) for i in range(2)]
    wq_jobs = []

    def job_width_src(job):
        if job[0] == "wgu":
            return D * len(job[3])
        return 1408

    def job_stage1(job, par):
        stg = wstA[par]
        skey = ("wstA", par)
        vk, sk = ("qa.v", par), ("qa.sg", par)
        if job[0] == "wgu":
            _, srcname, base, sls = job
            for i, sl in enumerate(sls):
                dma(stg[:, D * i:D * (i + 1)], apin[srcname][128 * sl:128 * (sl + 1), :],
                    chan="ch_wstA%d" % par, reads=[skey], writes=[skey], eng="a")
            F = D * len(sls)
        else:
            _, sl, half = job
            so, F = half * 1408, 1408
            dma(stg[:, 0:F], apin["wd_nat"][128 * sl:128 * (sl + 1), so:so + F],
                chan="ch_wstA%d" % par, reads=[skey], writes=[skey], eng="a")
        emit_quant_p1(skey, stg[:, 0:F], F, 128, qa, vk, qa.v2[par][:, 0:F],
                      sk, qa.sg2[par][:, 0:F])

    def job_stage2(job, par):
        F = job_width_src(job)
        vk, sk = ("qa.v", par), ("qa.sg", par)
        emit_quant_p2(("wstA", par), wstA[par][:, 0:F], ("wqqA",), wqqA[:, 0:F],
                      F, 128, qa, vk, qa.v2[par][:, 0:F], sk, qa.sg2[par][:, 0:F])
        rp = par
        rkey = ("recst", rp)
        if job[0] == "wgu":
            _, srcname, base, sls = job
            for i, sl in enumerate(sls):
                dma(recst[rp][:, 0:D].rearrange("p (j c) -> p j c", c=128),
                    wqqA[:, D * i:D * (i + 1)].rearrange("p (j c) -> p j c", c=128),
                    chan="tx_rec%d" % rp, reads=[("wqqA",), rkey], writes=[rkey],
                    transpose=True)
                dma(wgu_scr[128 * (base + sl):128 * (base + sl + 1), :],
                    recst[rp][:, 0:D],
                    chan="st_rec%d" % rp, reads=[rkey],
                    writes=[("wgu_scr", base + sl)], eng="a")
        else:
            _, sl, half = job
            so = half * 1408
            dma(recst[rp][:, 0:F].rearrange("p (j c) -> p j c", c=128),
                wqqA[:, 0:F].rearrange("p (j c) -> p j c", c=128),
                chan="tx_rec%d" % rp, reads=[("wqqA",), rkey], writes=[rkey],
                transpose=True)
            dma(wd_scr[128 * sl:128 * (sl + 1), so:so + F], recst[rp][:, 0:F],
                chan="st_rec%d" % rp, reads=[rkey], writes=[("wd_scr", sl, half)],
                eng="a")

    _g3 = [list(range(i, min(i + 2, ICH))) for i in range(0, ICH, 2)]
    for gi3, sls in enumerate(_g3):
        wq_jobs.append(("wgu", "wg_nat", 0, sls))
    for gi3, sls in enumerate(_g3):
        wq_jobs.append(("wgu", "wu_nat", ICH, sls))
    for sl in range(KC):
        for half in range(2):
            wq_jobs.append(("wd", sl, half))
    wq_jobs.reverse()
    _jobstate = {"n": 0, "pending": []}

    def quant_weight_slice():
        if wq_jobs:
            job = wq_jobs.pop()
            par = _jobstate["n"] & 1
            _jobstate["n"] += 1
            job_stage1(job, par)
            _jobstate["pending"].append((job, par))
        if len(_jobstate["pending"]) > 1 or (not wq_jobs and _jobstate["pending"]):
            job, par = _jobstate["pending"].pop(0)
            job_stage2(job, par)

    def drain_jobs():
        while wq_jobs or _jobstate["pending"]:
            quant_weight_slice()

    GA = 512
    xnT = alloc("gA2", "xnT", (128, KC * GA), BF16)
    qnT = alloc("gA2", "qnT", (128, KC * GA), BF16)
    knT = alloc("gA2", "knT", (128, KC * GA), BF16)
    mT = alloc("gA2", "mT", (128, KC * GA), BF16)
    qss = alloc("gA2", "qss", (128, H), F32)
    qrv = alloc("gA2", "qrv", (128, H), F32)
    qn_bf = alloc("gA2", "qn_bf", (128, D), BF16)
    kn_bf = alloc("gA2", "kn_bf", (128, D), BF16)
    xnbf = alloc("gA2", "xnbf", (128, D), BF16)
    qf = alloc("gA2", "qf", (128, D), F32)
    kf = qf
    s_t = qf

    def l2norm(src, srckey, dst, dstkey):
        vop(lambda e: e.tensor_tensor(out=qb.v[:, 0:D], in0=src[:], in1=src[:],
                                      op=AluOpType.mult),
            reads=[srckey], writes=[("q.v",)])
        vop(lambda e: e.tensor_reduce(out=qss[:], in_=qb.v[:, 0:D].rearrange(
            "p (h d) -> p h d", d=DKH), axis=AX.X, op=AluOpType.add),
            reads=[("q.v",)], writes=[("qss",)])
        vop(lambda e: e.activation(out=qrv[:], in_=qss[:], func=AF.Sqrt),
            reads=[("qss",)], writes=[("qrv",)], eng="a")
        vop(lambda e: e.tensor_scalar(out=qrv[:], in0=qrv[:], scalar1=1e-12,
                                      scalar2=None, op0=AluOpType.max),
            reads=[("qrv",)], writes=[("qrv",)])
        vop(lambda e: e.reciprocal(out=qrv[:], in_=qrv[:]),
            reads=[("qrv",)], writes=[("qrv",)])
        rb = qrv[:, :].unsqueeze(2).broadcast_to([128, H, DKH])
        vop(lambda e: e.tensor_tensor(out=dst.rearrange("p (h d) -> p h d", d=DKH),
                                      in0=src.rearrange("p (h d) -> p h d", d=DKH),
                                      in1=rb, op=AluOpType.mult),
            reads=[srckey, ("qrv",)], writes=[dstkey])

    for go, gn in GRA:
        ntile = gn // 128
        for ti in range(ntile):
            for _ in range(2):
                if wq_jobs:
                    quant_weight_slice()
            t0 = go + ti * 128
            tidx = t0 // 128
            par = tidx % 2
            xb = xbuf2[:, D * par:D * (par + 1)]
            xbk = ("xbuf", par)
            dma(xb, apin["xin"][t0:t0 + 128, :], chan="ch_x%d" % par,
                reads=[xbk], writes=[xbk])
            emit_ln(xbk, xb, ln_ab, "ln0", tidx, [(("xnbf",), xnbf[:])])
            wall = [("xnT", j2, ti) for j2 in range(KC)]
            dma(xnT[:, :].rearrange("p (j c) -> p j c", c=GA)[:, :, 128 * ti:128 * (ti + 1)],
                xnbf[:, :].rearrange("p (j c) -> p j c", c=128),
                chan="tx_xnT%d" % ti, reads=[("xnbf",)], writes=wall, transpose=True)
            for ww, wwkey, ppi, bap, bkey in [
                    (wq_sb, "wq", (0, 1), bq_sb, ("b_bq",)),
                    (wk_sb, "wk", (2, 3), bk_sb, ("b_bk",))]:
                for n in range(2):
                    pidx = ppi[n]
                    mm(pp[pidx], ones1[:], bap[0:1, 512 * n:512 * (n + 1)],
                       reads=[("ones1",), bkey], writes=[PPK[pidx]], start=True, stop=False)
                    for k in range(KC):
                        mm(pp[pidx], xnT[:, GA * k + 128 * ti:GA * k + 128 * (ti + 1)],
                           ww[:, D * k + 512 * n:D * k + 512 * (n + 1)],
                           reads=[("xnT", k, ti), (wwkey, k)], writes=[PPK[pidx]],
                           start=False, stop=(k == KC - 1))
            for n in range(2):
                vop(lambda e, n=n: e.activation(out=qf[:, 512 * n:512 * (n + 1)],
                                                in_=pp[n][:], func=AF.Copy),
                    reads=[PPK[n], ("qf",)], writes=[("qf",)], eng="a")
            l2norm(qf, ("qf",), qn_bf[:, :], ("qn",))
            for n in range(2):
                vop(lambda e, n=n: e.activation(out=kf[:, 512 * n:512 * (n + 1)],
                                                in_=pp[2 + n][:], func=AF.Copy),
                    reads=[PPK[2 + n], ("qf",)], writes=[("qf",)], eng="a")
            l2norm(kf, ("qf",), kn_bf[:, :], ("kn",))
            wq_all = [("qnT", j2, ti) for j2 in range(KC)]
            wk_all = [("knT", j2, ti) for j2 in range(KC)]
            dma(qnT[:, :].rearrange("p (j c) -> p j c", c=GA)[:, :, 128 * ti:128 * (ti + 1)],
                qn_bf[:, :].rearrange("p (j c) -> p j c", c=128),
                chan="tx_qnT%d" % ti, reads=[("qn",)], writes=wq_all, transpose=True)
            dma(knT[:, :].rearrange("p (j c) -> p j c", c=GA)[:, :, 128 * ti:128 * (ti + 1)],
                kn_bf[:, :].rearrange("p (j c) -> p j c", c=128),
                chan="tx_knT%d" % ti, reads=[("kn",)], writes=wk_all, transpose=True)
        for jj in range(KC):
            rq = [("qnT", jj, ti) for ti in range(ntile)]
            rk = [("knT", jj, ti) for ti in range(ntile)]
            mm(pp[4][:, 0:gn], ablk[:, 128 * jj:128 * (jj + 1)], qnT[:, GA * jj:GA * jj + gn],
               reads=rq + [("ablk", jj, 0), ("ablk", jj, 1)], writes=[PPK[4]],
               start=True, stop=False)
            mm(pp[4][:, 0:gn], bblk[:, 128 * jj:128 * (jj + 1)], knT[:, GA * jj:GA * jj + gn],
               reads=rk + [("bblk", jj, 0), ("bblk", jj, 1)], writes=[PPK[4]],
               start=False, stop=True)
            vop(lambda e, jj=jj, gn=gn: e.tensor_scalar(
                out=mT[:, GA * jj:GA * jj + gn], in0=pp[4][:, 0:gn],
                scalar1=bc_sb[:, 0:1], scalar2=None, op0=AluOpType.add),
                reads=[PPK[4], ("bc",)], writes=[("mT", jj)])
        for ti in range(ntile):
            t0 = go + ti * 128
            for n in range(2):
                pidx = 5 + n
                mm(pp[pidx], ones1[:], bo_sb[0:1, 512 * n:512 * (n + 1)],
                   reads=[("ones1",), ("b_bo",)], writes=[PPK[pidx]], start=True, stop=False)
                for k in range(KC):
                    mm(pp[pidx], mT[:, GA * k + 128 * ti:GA * k + 128 * (ti + 1)],
                       wo_sb[:, D * k + 512 * n:D * k + 512 * (n + 1)],
                       reads=[("mT", k), ("wo", k)], writes=[PPK[pidx]],
                       start=False, stop=(k == KC - 1))
            for n in range(2):
                pidx = 5 + n
                vop(lambda e, n=n, pidx=pidx: e.activation(
                    out=s_t[:, 512 * n:512 * (n + 1)], in_=pp[pidx][:], func=AF.Copy),
                    reads=[PPK[pidx], ("qf",)], writes=[("qf",)], eng="a")
            dma(s_scr[t0:t0 + 128, :], s_t[:], chan="st_s", reads=[("qf",)],
                writes=[("s_scr", t0 // 128)])
            if debug:
                dma(dbg["dbg_s"][t0:t0 + 128, :], s_t[:], chan="dbgc",
                    reads=[("qf",)], writes=[])

    drain_jobs()
    ms.phase()
    free_group("gA2")
    free_group("gAq")
    free_group("gA")

    wgT = alloc("gB2", "wgT", (128, KC * IFF), BF16)
    wuT = alloc("gB2", "wuT", (128, KC * IFF), BF16)
    wdT = alloc("gB2", "wdT", (128, ICH * D), BF16)
    for wi_, (dst_sb, dkey, base) in enumerate([(wgT, "wgT", 0), (wuT, "wuT", ICH)]):
        for sl in range(ICH):
            wall = [(dkey, j2) for j2 in range(KC)]
            dma(dst_sb[:, :].rearrange("p (j c) -> p j c", c=IFF)[:, :, 128 * sl:128 * (sl + 1)],
                wgu_scr[128 * (base + sl):128 * (base + sl + 1), :]
                .rearrange("p (j c) -> p j c", c=128),
                chan="ld_" + dkey, reads=[("wgu_scr", base + sl)], writes=wall)
    for sl in range(KC):
        wall = [("wdT", k2) for k2 in range(ICH)]
        dma(wdT[:, :].rearrange("p (j c) -> p j c", c=D)[:, :, 128 * sl:128 * (sl + 1)],
            wd_scr[128 * sl:128 * (sl + 1), :].rearrange("p (j c) -> p j c", c=128),
            chan="ld_wdT", reads=[("wd_scr", sl, 0), ("wd_scr", sl, 1)], writes=wall)
    xqbf = alloc("gB2", "xqbf", (128, D), BF16)
    tmpf = alloc("gB2", "tmpf", (128, D), F32)
    sldb = alloc("gB2", "sldb", (128, D), F32)
    xn32 = tmpf
    sload = sldb
    xqT = alloc("gB2", "xqT", (128, KC * 256), BF16)
    hqT = alloc("gB2", "hqT", (128, ICH * 256), BF16)
    h_sl = alloc("gB2", "h_sl", (128, 1024), F32)
    hq_sl = xqbf
    g_t = alloc("gB2", "g_t", (128, D), F32)
    x1_t = alloc("gB2", "x1_t", (128, D), F32)

    NSL = _split(IFF, 512)
    hq_bf = xqbf
    xb4ext = alloc("gB2", "xb4ext", (128, 2 * D), F32)

    def b_xb(tidx):
        p4 = tidx % 4
        base = xbuf2 if p4 < 2 else xb4ext
        return base[:, D * (p4 % 2):D * (p4 % 2 + 1)], ("xbuf", p4), p4

    def b_xload(tidx):
        t0 = tidx * 128
        xb, xbk, p4 = b_xb(tidx)
        dma(xb, apin["xin"][t0:t0 + 128, :], chan="ch_x%d" % p4,
            reads=[xbk], writes=[xbk])

    def b_front(tidx):
        t0 = tidx * 128
        par = tidx % 2
        xb, xbk, p4 = b_xb(tidx)
        b_xload(tidx)
        a_ap = ln_ab[:, 2 * tidx:2 * tidx + 1]
        b_ap = ln_ab[:, 2 * tidx + 1:2 * tidx + 2]
        vop(lambda e, a_ap=a_ap, b_ap=b_ap, xb=xb: e.tensor_scalar(
            out=xn32[:], in0=xb, scalar1=a_ap, scalar2=b_ap,
            op0=AluOpType.mult, op1=AluOpType.add),
            reads=[xbk], writes=[("tmpf",)])
        emit_quant(("tmpf",), xn32[:], ("xq",), xqbf[:], D)
        wall = [("xqT", j2, par) for j2 in range(KC)]
        dma(xqT[:, :].rearrange("p (j c) -> p j c", c=256)[:, :, 128 * par:128 * (par + 1)],
            xqbf[:, :].rearrange("p (j c) -> p j c", c=128),
            chan="tx_xqT%d" % par, reads=[("xq",)], writes=wall, transpose=True)

    def b_back1(tidx):
        t0 = tidx * 128
        par = tidx % 2
        for sb_i in range(3):
            sls = NSL[2 * sb_i:2 * sb_i + 2]
            sb0 = sls[0][0]
            sbw = sum(sn for _, sn in sls)
            for k in range(KC):
                lhs = xqT[:, 256 * k + 128 * par:256 * k + 128 * (par + 1)]
                for si, (so, sn) in enumerate(sls):
                    mm(pp[si][:, 0:sn], lhs, wgT[:, IFF * k + so:IFF * k + so + sn],
                       reads=[("xqT", k, par), ("wgT", k)], writes=[PPK[si]],
                       start=(k == 0), stop=(k == KC - 1))
                    mm(pp[2 + si][:, 0:sn], lhs, wuT[:, IFF * k + so:IFF * k + so + sn],
                       reads=[("xqT", k, par), ("wuT", k)], writes=[PPK[2 + si]],
                       start=(k == 0), stop=(k == KC - 1))
            for si, (so, sn) in enumerate(sls):
                vop(lambda e, si=si, so=so, sn=sn, sb0=sb0: e.activation(
                    out=h_sl[:, so - sb0:so - sb0 + sn], in_=pp[si][:, 0:sn],
                    func=AF.Silu),
                    reads=[PPK[si]], writes=[("h_sl",)], eng="a")
                vop(lambda e, si=si, so=so, sn=sn, sb0=sb0: e.tensor_tensor(
                    out=h_sl[:, so - sb0:so - sb0 + sn],
                    in0=h_sl[:, so - sb0:so - sb0 + sn],
                    in1=pp[2 + si][:, 0:sn], op=AluOpType.mult),
                    reads=[("h_sl",), PPK[2 + si]], writes=[("h_sl",)])
            emit_quant(("h_sl",), h_sl[:, 0:sbw], ("xq",), hq_bf[:, 0:sbw], sbw)
            wall = [("hqT", k2, par) for k2 in range(ICH)]
            nblk = sbw // 128
            dma(hqT[:, :].rearrange("p (j c) -> p j c", c=256)[
                :, sb0 // 128:sb0 // 128 + nblk, 128 * par:128 * (par + 1)],
                hq_bf[:, 0:sbw].rearrange("p (j c) -> p j c", c=128),
                chan="tx_hqT%d" % par, reads=[("xq",)], writes=wall, transpose=True)

    def b_back2(tidx):
        t0 = tidx * 128
        par = tidx % 2
        xb, xbk, p4 = b_xb(tidx)
        for k in range(ICH):
            for n in range(2):
                pidx = 6 + n
                mm(pp[pidx], hqT[:, 256 * k + 128 * par:256 * k + 128 * (par + 1)],
                   wdT[:, D * k + 512 * n:D * k + 512 * (n + 1)],
                   reads=[("hqT", k, par), ("wdT", k)], writes=[PPK[pidx]],
                   start=(k == 0), stop=(k == ICH - 1))
        gk = ("g_t",)
        for n in range(2):
            pidx = 6 + n
            vop(lambda e, n=n, pidx=pidx: e.activation(
                out=g_t[:, 512 * n:512 * (n + 1)], in_=pp[pidx][:], func=AF.Silu),
                reads=[PPK[pidx]], writes=[gk], eng="a")
        dma(sload[:], s_scr[t0:t0 + 128, :], chan="ch_sload%d" % (tidx % 2),
            reads=[("s_scr", tidx), ("sldb",)], writes=[("sldb",)], eng="g")
        vop(lambda e: e.tensor_tensor(out=x1_t[:, 0:D], in0=sload[:],
                                      in1=g_t[:], op=AluOpType.mult),
            reads=[("sldb",), gk], writes=[("x1_t",)], eng="g")
        vop(lambda e, xb=xb: e.tensor_tensor(out=x1_t[:, 0:D], in0=x1_t[:, 0:D],
                                             in1=xb, op=AluOpType.add),
            reads=[("x1_t",), xbk], writes=[("x1_t",)], eng="g")
        dma(x1_scr[t0:t0 + 128, :], x1_t[:, 0:D], chan="st_x1_%d" % (tidx % 2),
            reads=[("x1_t",)], writes=[("x1_scr", tidx)], eng="g")
        if debug:
            dma(dbg["dbg_x1"][t0:t0 + 128, :], x1_t[:, 0:D], chan="dbgc",
                reads=[("x1_t",)], writes=[])

    NT_B = TP // 128
    for tidx in range(NT_B):
        b_front(tidx)
        if tidx >= 1:
            b_back1(tidx - 1)
        if tidx >= 2:
            b_back2(tidx - 2)
    b_back1(NT_B - 1)
    b_back2(NT_B - 2)
    b_back2(NT_B - 1)

    ms.phase()
    free_group("gB2")
    free_group("gB")

    xn1T = alloc("gD", "xn1T", (128, KC * TP), BF16)
    xnbf = alloc("gD", "xnbf_c", (128, D), BF16)
    stg_state.a = alloc("gD", "wstC", (128, D), F32)
    stg_state.b = alloc("gD", "wstC2", (128, D), F32)
    z_sb = alloc("gD", "z_sb", (128, NEXP * NTM), F32)
    alp_sb = alloc("gD", "alp_sb", (128, NEXP * NTM), F32)
    zm1 = alloc("gD", "zm1", (128, 1), F32)
    zm2 = alloc("gD", "zm2", (128, 1), F32)
    znm = alloc("gD", "znm", (128, 1), F32)
    zscr = alloc("gD", "zscr", (128, NEXP), F32)

    hbuf = alloc("gC2", "hbuf", (128, KC * TP), BF16)
    slg = alloc("gC2", "slg", (128, KC * T_MAIN), BF16)
    sqacc = alloc("gC2", "sqacc", (128, T_MAIN), F32)
    wo1_sb = alloc("gC2", "wo1_sb", (128, KC * D), BF16)
    gnw_sb = alloc("gC2", "gnw_sb", (128, KC), F32)

    wim = [alloc("gC1", "wim%d" % i, (128, 3 * D), BF16) for i in range(2)]
    wg1_sb = alloc("gC1", "wg1_sb", (128, KC * NEXP), BF16)
    conv_i_sb = alloc("gC1", "conv_i_sb", (128, KC * 4), F32)
    conv_f_sb = alloc("gC1", "conv_f_sb", (128, KC * 4), F32)
    iraw = alloc("gC1", "iraw", (128, TP), F32)
    fraw = alloc("gC1", "fraw", (128, TP), F32)
    ci_b = alloc("gC1", "ci_b", (128, TP), F32)
    cf_b = alloc("gC1", "cf_b", (128, TP), F32)

    cast_weight("Wo1", wo1_sb, "wo1")
    cast_weight("WgE", wg1_sb, "wg1", width=NEXP)
    dma(conv_i_sb[:], apin["conv_i"][:], chan="c_cvi", reads=[], writes=[("convi",)])
    dma(conv_f_sb[:], apin["conv_f"][:], chan="c_cvf", reads=[], writes=[("convf",)])
    dma(gnw_sb[:], apin["gnw"][:], chan="c_gnw", reads=[], writes=[("gnw",)])

    for go, gn in GRPC:
        for ti in range(gn // 128):
            t0 = go + ti * 128
            tidx = t0 // 128
            par = tidx % 2
            xb = xbuf2[:, D * par:D * (par + 1)]
            xbk = ("xbuf", par)
            dma(xb, x1_scr[t0:t0 + 128, :], chan="ch_x%d" % par,
                reads=[xbk, ("x1_scr", tidx)], writes=[xbk])
            emit_ln(xbk, xb, ln_ab, "ln1", tidx, [(("xnbf",), xnbf[:])])
            wall = [("xn1T", j2, tidx) for j2 in range(KC)]
            dma(xn1T[:, :].rearrange("p (j c) -> p j c", c=TP)[:, :, t0:t0 + 128],
                xnbf[:, :].rearrange("p (j c) -> p j c", c=128),
                chan="tx_xn1T", reads=[("xnbf",)], writes=wall, transpose=True)

    for tm in range(NTM):
        t0 = WARM + tm * 128
        zps = pp[3][:, 0:NEXP]
        mm(zps, ones1[:], bg_sb[0:1, 0:NEXP], reads=[("ones1",), ("b_bgE",)],
           writes=[PPK[3]], start=True, stop=False)
        for k in range(KC):
            mm(zps, xn1T[:, TP * k + t0:TP * k + t0 + 128],
               wg1_sb[:, NEXP * k:NEXP * (k + 1)],
               reads=[("xn1T", k, t0 // 128), ("wg1", k)], writes=[PPK[3]],
               start=False, stop=(k == KC - 1))
        zs = z_sb[:, NEXP * tm:NEXP * (tm + 1)]
        als = alp_sb[:, NEXP * tm:NEXP * (tm + 1)]
        vop(lambda e, zs=zs: e.activation(out=zs, in_=pp[3][:, 0:NEXP], func=AF.Copy),
            reads=[PPK[3]], writes=[("z", tm)], eng="a")
        vop(lambda e, zs=zs: e.tensor_reduce(out=zm1[:], in_=zs, axis=AX.X,
                                             op=AluOpType.max),
            reads=[("z", tm)], writes=[("zm1",)])
        vop(lambda e, zs=zs: e.tensor_scalar(out=zscr[:], in0=zs, scalar1=zm1[:, 0:1],
                                             scalar2=-1e30, op0=AluOpType.is_ge,
                                             op1=AluOpType.mult),
            reads=[("z", tm), ("zm1",)], writes=[("zscr",)])
        vop(lambda e, zs=zs: e.tensor_tensor(out=zscr[:], in0=zscr[:], in1=zs,
                                             op=AluOpType.add),
            reads=[("zscr",), ("z", tm)], writes=[("zscr",)])
        vop(lambda e: e.tensor_reduce(out=zm2[:], in_=zscr[:], axis=AX.X,
                                      op=AluOpType.max),
            reads=[("zscr",)], writes=[("zm2",)])
        vop(lambda e, zs=zs, als=als: e.tensor_scalar(out=als, in0=zs,
                                                      scalar1=zm2[:, 0:1], scalar2=None,
                                                      op0=AluOpType.is_ge),
            reads=[("z", tm), ("zm2",)], writes=[("alp", tm)])
        vop(lambda e: e.tensor_scalar(out=znm[:], in0=zm1[:], scalar1=-1.0,
                                      scalar2=None, op0=AluOpType.mult),
            reads=[("zm1",)], writes=[("znm",)])
        vop(lambda e, zs=zs: e.activation(out=zscr[:], in_=zs, func=AF.Exp,
                                          bias=znm[:, 0:1], scale=1.0),
            reads=[("z", tm), ("znm",)], writes=[("zscr",)], eng="a")
        vop(lambda e: e.tensor_tensor(out=zm2[:], in0=zm2[:], in1=zm1[:],
                                      op=AluOpType.subtract),
            reads=[("zm2",), ("zm1",)], writes=[("zm2",)])
        vop(lambda e: e.activation(out=zm2[:], in_=zm2[:], func=AF.Exp),
            reads=[("zm2",)], writes=[("zm2",)], eng="a")
        vop(lambda e: e.tensor_scalar(out=zm2[:], in0=zm2[:], scalar1=1.0,
                                      scalar2=None, op0=AluOpType.add),
            reads=[("zm2",)], writes=[("zm2",)])
        vop(lambda e: e.reciprocal(out=zm2[:], in_=zm2[:]),
            reads=[("zm2",)], writes=[("zm2",)])
        vop(lambda e, als=als: e.tensor_tensor(out=als, in0=als, in1=zscr[:],
                                               op=AluOpType.mult),
            reads=[("alp", tm), ("zscr",)], writes=[("alp", tm)])
        vop(lambda e, als=als: e.tensor_scalar_mul(out=als, in0=als,
                                                   scalar1=zm2[:, 0:1]),
            reads=[("alp", tm), ("zm2",)], writes=[("alpha", tm)])
        if debug:
            vop(lambda e, als=als: e.tensor_copy(out=zscr[:], in_=als),
                reads=[("alpha", tm)], writes=[("zscr",)])
            dma(dbg["dbg_alpha"][128 * tm:128 * (tm + 1), :], zscr[:],
                chan="dbgc", reads=[("zscr",)], writes=[])

    vop(lambda e: e.memset(sqacc[:], 0.0), reads=[], writes=[("sqacc",)])
    for m in range(KC):
        par = m % 2
        for wi_, wname in enumerate(["Wi", "Wf", "Wgr"]):
            stg, skey, ch = ((stg_state.a, ("wst",), "ch_wst") if (m * 3 + wi_) % 2 == 0
                             else (stg_state.b, ("wst2",), "ch_wst2"))
            srcap = apin[wname][:, 128 * m:128 * (m + 1)].rearrange(
                "(a p) c -> p a c", p=128)
            dma(stg[:, 0:D].rearrange("p (a c) -> p a c", c=128), srcap,
                chan=ch, reads=[skey], writes=[skey], eng="a")
            vop(lambda e, par=par, wi_=wi_, stg=stg: e.tensor_copy(
                out=wim[par][:, D * wi_:D * (wi_ + 1)], in_=stg[:, 0:D]),
                reads=[skey], writes=[("wim", par, wi_)])
        for gi, (go, gn) in enumerate(GRPC):
            rx = [("xn1T", k, (go // 128) + tt) for k in range(KC)
                  for tt in range(gn // 128)]
            for pse, psk, wi_ in [(0, PPK[0], 0), (1, PPK[1], 1)]:
                for k in range(KC):
                    mm(pp[pse][:, 0:gn], wim[par][:, D * wi_ + 128 * k:D * wi_ + 128 * (k + 1)],
                       xn1T[:, TP * k + go:TP * k + go + gn],
                       reads=[("wim", par, wi_)] + rx, writes=[psk],
                       start=(k == 0), stop=(k == KC - 1))
            vop(lambda e, go=go, gn=gn: e.activation(out=iraw[:, go:go + gn],
                                                     in_=pp[0][:, 0:gn], func=AF.Copy),
                reads=[PPK[0]], writes=[("iraw",)], eng="a")
            vop(lambda e, go=go, gn=gn: e.activation(out=fraw[:, go:go + gn],
                                                     in_=pp[1][:, 0:gn], func=AF.Copy),
                reads=[PPK[1]], writes=[("fraw",)], eng="a")
        for go, gn in GRPM:
            gow = WARM + go
            rx = [("xn1T", k, (gow // 128) + tt) for k in range(KC)
                  for tt in range(gn // 128)]
            for k in range(KC):
                mm(pp[2][:, 0:gn], wim[par][:, 2 * D + 128 * k:2 * D + 128 * (k + 1)],
                   xn1T[:, TP * k + gow:TP * k + gow + gn],
                   reads=[("wim", par, 2)] + rx, writes=[PPK[2]],
                   start=(k == 0), stop=(k == KC - 1))
            vop(lambda e, m=m, go=go, gn=gn: e.activation(
                out=slg[:, T_MAIN * m + go:T_MAIN * m + go + gn],
                in_=pp[2][:, 0:gn], func=AF.Silu),
                reads=[PPK[2]], writes=[("slg", m)], eng="a")
        vop(lambda e: e.tensor_scalar_mul(out=iraw[:, 0:WARM], in0=iraw[:, 0:WARM],
                                          scalar1=mwarm_sb[:, 0:1]),
            reads=[("iraw",), ("mwarm",)], writes=[("iraw",)])
        vop(lambda e: e.tensor_scalar_mul(out=fraw[:, 0:WARM], in0=fraw[:, 0:WARM],
                                          scalar1=mwarm_sb[:, 0:1]),
            reads=[("fraw",), ("mwarm",)], writes=[("fraw",)])
        for src, skey, cout, ckey, csb, cwkey in [
                (iraw, ("iraw",), ci_b, ("ci",), conv_i_sb, ("convi",)),
                (fraw, ("fraw",), cf_b, ("cf",), conv_f_sb, ("convf",))]:
            vop(lambda e, cout=cout: e.memset(cout[:, 0:3], 0.0),
                reads=[], writes=[ckey])
            vop(lambda e, src=src, cout=cout, csb=csb, m=m: e.tensor_scalar_mul(
                out=cout[:, 3:TP], in0=src[:, 3:TP],
                scalar1=csb[:, 4 * m + 3:4 * m + 4]),
                reads=[skey, cwkey], writes=[ckey])
            for tap in range(3):
                vop(lambda e, src=src, cout=cout, csb=csb, m=m, tap=tap:
                    e.scalar_tensor_tensor(out=cout[:, 3:TP], in0=src[:, tap:TP - 3 + tap],
                                           scalar=csb[:, 4 * m + tap:4 * m + tap + 1],
                                           in1=cout[:, 3:TP],
                                           op0=AluOpType.mult, op1=AluOpType.add),
                    reads=[skey, cwkey, ckey], writes=[ckey])
        vop(lambda e: e.activation(out=cf_b[:], in_=cf_b[:], func=AF.Sigmoid),
            reads=[("cf",)], writes=[("cf",)], eng="a")
        vop(lambda e: e.activation(out=ci_b[:], in_=ci_b[:], func=AF.Silu),
            reads=[("ci",)], writes=[("ci",)], eng="a")
        vop(lambda e: e.tensor_scalar(out=iraw[:], in0=cf_b[:], scalar1=-1.0,
                                      scalar2=1.0, op0=AluOpType.mult, op1=AluOpType.add),
            reads=[("cf",)], writes=[("iraw",)])
        vop(lambda e: e.tensor_tensor(out=ci_b[:], in0=ci_b[:], in1=iraw[:],
                                      op=AluOpType.mult),
            reads=[("ci",), ("iraw",)], writes=[("ci",)])
        for so, sn in _split(TP, 544):
            init = 0.0 if so == 0 else hbuf[:, TP * m + so - 1:TP * m + so]
            vop(lambda e, m=m, so=so, sn=sn, init=init: e.tensor_tensor_scan(
                out=hbuf[:, TP * m + so:TP * m + so + sn],
                data0=cf_b[:, so:so + sn], data1=ci_b[:, so:so + sn],
                initial=init, op0=AluOpType.mult, op1=AluOpType.add),
                reads=[("cf",), ("ci",), ("h", m)], writes=[("h", m)])
        vop(lambda e, m=m: e.tensor_tensor(out=fraw[:, 0:T_MAIN],
                                           in0=hbuf[:, TP * m + WARM:TP * (m + 1)],
                                           in1=hbuf[:, TP * m + WARM:TP * (m + 1)],
                                           op=AluOpType.mult),
            reads=[("h", m), ("fraw",)], writes=[("fraw",)], eng="g")
        vop(lambda e: e.tensor_tensor(out=sqacc[:], in0=sqacc[:], in1=fraw[:, 0:T_MAIN],
                                      op=AluOpType.add),
            reads=[("sqacc",), ("fraw",)], writes=[("sqacc",)], eng="g")
        if debug and m == 0:
            vop(lambda e: e.tensor_copy(out=ci_b[:, 0:T_MAIN],
                                        in_=hbuf[:, WARM:TP]),
                reads=[("h", 0), ("ci",)], writes=[("ci",)])
            dma(dbg["dbg_h"][:, WARM:TP], ci_b[:, 0:T_MAIN], chan="dbgc",
                reads=[("ci",)], writes=[])

    ms.phase()
    free_group("gC1")

    rms_b = alloc("gC2b", "rms_b", (128, T_MAIN), F32)
    oT = alloc("gC2b", "oT", (128, KC * T_MAIN), BF16)
    s1_t = alloc("gC2b", "s1_t", (128, D), F32)
    sqtmp = alloc("gC2b", "sqtmp", (128, T_MAIN), F32)

    for gi, (go, gn) in enumerate(GRPM):
        mm(pp[4][0:1, 0:gn], onesf[:], sqacc[:, go:go + gn],
           reads=[("onesf",), ("sqacc",)], writes=[PPK[4]], start=True, stop=True)
        vop(lambda e, go=go, gn=gn: e.tensor_scalar(out=rms_b[0:1, go:go + gn],
                                                    in0=pp[4][0:1, 0:gn],
                                                    scalar1=1.0 / D, scalar2=1e-5,
                                                    op0=AluOpType.mult, op1=AluOpType.add),
            reads=[PPK[4]], writes=[("rms", 0)])
        vop(lambda e, go=go, gn=gn: e.activation(out=rms_b[0:1, go:go + gn],
                                                 in_=rms_b[0:1, go:go + gn], func=AF.Sqrt),
            reads=[("rms", 0)], writes=[("rms", 0)], eng="a")
        vop(lambda e, go=go, gn=gn: e.reciprocal(out=rms_b[0:1, go:go + gn],
                                                 in_=rms_b[0:1, go:go + gn]),
            reads=[("rms", 0)], writes=[("rms", 0)])
    kk = 1
    while kk < 128:
        dma(rms_b[kk:2 * kk, :], rms_b[0:kk, :], chan="ch_rms%d" % kk,
            reads=[("rms", kk // 2) if kk > 1 else ("rms", 0)], writes=[("rms", kk)])
        kk *= 2
    RMSD = ("rms", 64)

    for m in range(KC):
        vop(lambda e, m=m: e.tensor_tensor(out=sqtmp[:],
                                           in0=hbuf[:, TP * m + WARM:TP * (m + 1)],
                                           in1=rms_b[:], op=AluOpType.mult),
            reads=[("h", m), RMSD], writes=[("sqtmp",)], eng="g")
        vop(lambda e, m=m: e.tensor_tensor(out=sqtmp[:], in0=sqtmp[:],
                                           in1=slg[:, T_MAIN * m:T_MAIN * (m + 1)],
                                           op=AluOpType.mult),
            reads=[("sqtmp",), ("slg", m)], writes=[("sqtmp",)])
        vop(lambda e, m=m: e.tensor_scalar(out=oT[:, T_MAIN * m:T_MAIN * (m + 1)],
                                           in0=sqtmp[:],
                                           scalar1=gnw_sb[:, m:m + 1], scalar2=None,
                                           op0=AluOpType.mult),
            reads=[("sqtmp",), ("gnw",)], writes=[("oT", m)])
    for tm in range(NTM):
        for n in range(2):
            pidx = 5 + n
            for k in range(KC):
                mm(pp[pidx], oT[:, T_MAIN * k + 128 * tm:T_MAIN * k + 128 * (tm + 1)],
                   wo1_sb[:, D * k + 512 * n:D * k + 512 * (n + 1)],
                   reads=[("oT", k), ("wo1", k)], writes=[PPK[pidx]],
                   start=(k == 0), stop=(k == KC - 1))
        for n in range(2):
            pidx = 5 + n
            vop(lambda e, n=n, pidx=pidx: e.activation(
                out=s1_t[:, 512 * n:512 * (n + 1)], in_=pp[pidx][:], func=AF.Copy),
                reads=[PPK[pidx]], writes=[("s1_t",)], eng="a")
        dma(s1_scr[128 * tm:128 * (tm + 1), :], s1_t[:], chan="st_s1",
            reads=[("s1_t",)], writes=[("s1_scr", tm)])

    ms.phase()
    free_group("gC2b")
    free_group("gC2")

    weE_sb = [alloc("gDD", "weE%d" % i, (128, KC * D), BF16) for i in range(2)]
    be_sb = alloc("gDD", "be_sb", (1, 2 * D), BF16)
    acc = alloc("gDD", "acc", (128, NTM * D), F32)
    g1_t = alloc("gDD", "g1_t", (128, D), F32)
    y_t = alloc("gDD", "y_t", (128, D), F32)
    s1l = alloc("gDD", "s1l", (128, 2 * D), F32)
    xde = alloc("gDD", "xde", (128, 2 * D), F32)

    for e_i in range(NEXP):
        sl = e_i % 2
        for k in range(KC):
            stg, skey, ch = ((stg_state.a, ("wst",), "ch_wst") if k % 2 == 0
                             else (stg_state.b, ("wst2",), "ch_wst2"))
            dma(stg[:, 0:D], apin["WeE"][D * e_i + 128 * k:D * e_i + 128 * (k + 1), :],
                chan=ch, reads=[skey], writes=[skey], eng="a")
            vop(lambda e, sl=sl, k=k, stg=stg: e.tensor_copy(
                out=weE_sb[sl][:, D * k:D * (k + 1)], in_=stg[:, 0:D]),
                reads=[skey], writes=[("weE", sl, k)])
        dma(stg_state.a[0:1, 0:D], apin["beE"][e_i:e_i + 1, :], chan="ch_wst",
            reads=[("wst",)], writes=[("wst",)])
        vop(lambda e, sl=sl, stg=stg_state.a: e.tensor_copy(
            out=be_sb[0:1, D * sl:D * (sl + 1)], in_=stg[0:1, 0:D]),
            reads=[("wst",)], writes=[("beE", sl)])
        pb = 2 * (e_i % 3)
        for tm in range(NTM):
            t0 = WARM + 128 * tm
            for n in range(2):
                mm(pp[pb + n], ones1[:], be_sb[0:1, D * sl + 512 * n:D * sl + 512 * (n + 1)],
                   reads=[("ones1",), ("beE", sl)], writes=[PPK[pb + n]],
                   start=True, stop=False)
                for k in range(KC):
                    mm(pp[pb + n], xn1T[:, TP * k + t0:TP * k + t0 + 128],
                       weE_sb[sl][:, D * k + 512 * n:D * k + 512 * (n + 1)],
                       reads=[("xn1T", k, t0 // 128), ("weE", sl, k)],
                       writes=[PPK[pb + n]], start=False, stop=(k == KC - 1))
            for n in range(2):
                pidx = pb + n
                alc = alp_sb[:, NEXP * tm + e_i:NEXP * tm + e_i + 1]
                if e_i == 0:
                    vop(lambda e, tm=tm, n=n, alc=alc, pidx=pidx: e.tensor_scalar(
                        out=acc[:, D * tm + 512 * n:D * tm + 512 * (n + 1)],
                        in0=pp[pidx][:], scalar1=alc, scalar2=None,
                        op0=AluOpType.mult),
                        reads=[PPK[pidx], ("alpha", tm)], writes=[("acc", tm)])
                else:
                    vop(lambda e, tm=tm, n=n, alc=alc, pidx=pidx: e.scalar_tensor_tensor(
                        out=acc[:, D * tm + 512 * n:D * tm + 512 * (n + 1)],
                        in0=pp[pidx][:], scalar=alc,
                        in1=acc[:, D * tm + 512 * n:D * tm + 512 * (n + 1)],
                        op0=AluOpType.mult, op1=AluOpType.add),
                        reads=[PPK[pidx], ("alpha", tm), ("acc", tm)],
                        writes=[("acc", tm)])

    for tm in range(NTM):
        t0 = WARM + 128 * tm
        par = tm % 2
        p4 = tm % 4
        xb = (xbuf2 if p4 < 2 else xde)[:, D * (p4 % 2):D * (p4 % 2 + 1)]
        xbk = ("xbuf", p4)
        dma(xb, x1_scr[t0:t0 + 128, :], chan="ch_x%d" % p4,
            reads=[xbk, ("x1_scr", t0 // 128)], writes=[xbk])
        s1v = s1l[:, D * par:D * (par + 1)]
        dma(s1v, s1_scr[128 * tm:128 * (tm + 1), :], chan="ch_s1l%d" % par,
            reads=[("s1l", par), ("s1_scr", tm)], writes=[("s1l", par)])
        vop(lambda e, tm=tm: e.activation(out=g1_t[:], in_=acc[:, D * tm:D * (tm + 1)],
                                          func=AF.Silu),
            reads=[("acc", tm)], writes=[("g1_t",)], eng="a")
        if debug:
            dma(dbg["dbg_moe"][128 * tm:128 * (tm + 1), :],
                acc[:, D * tm:D * (tm + 1)], chan="dbgc",
                reads=[("acc", tm)], writes=[])
        vop(lambda e, s1v=s1v: e.tensor_tensor(out=g1_t[:], in0=g1_t[:], in1=s1v,
                                      op=AluOpType.mult),
            reads=[("g1_t",), ("s1l", par)], writes=[("g1_t",)])
        vop(lambda e, xb=xb: e.tensor_tensor(out=y_t[:], in0=g1_t[:], in1=xb,
                                             op=AluOpType.add),
            reads=[("g1_t",), xbk], writes=[("y_t",)], eng="g")
        dma(yout[128 * tm:128 * (tm + 1), :], y_t[:], chan="st_y%d" % par,
            reads=[("y_t",)], writes=[("yout", tm)], eng="g")

    ms.op("sy", lambda e: e.nop(), reads=[("yout", tm) for tm in range(NTM)], writes=[])

    with nc.Block() as block:
        ms.finalize(block)

    return nc, apin



def _host_stage(inputs, T_MAIN, L):
    x = np.asarray(inputs["L0_attn_x"], dtype=np.float32)
    B = x.shape[0]
    idx = np.linspace(0.0, L - 1, LEN).astype(np.int32)
    conv_i = np.ascontiguousarray(
        np.asarray(inputs["L1_attn_conv_i"], np.float32).reshape(KC, 128, 4)
        .transpose(1, 0, 2).reshape(128, KC * 4))
    conv_f = np.ascontiguousarray(
        np.asarray(inputs["L1_attn_conv_f"], np.float32).reshape(KC, 128, 4)
        .transpose(1, 0, 2).reshape(128, KC * 4))
    gnw = np.ascontiguousarray(
        np.asarray(inputs["L1_attn_g_norm_weight"], np.float32).reshape(KC, 128).T)
    shared = {
        "Wq": np.asarray(inputs["L0_attn_Wq"], np.float32),
        "Wk": np.asarray(inputs["L0_attn_Wk"], np.float32),
        "Wo0": np.asarray(inputs["L0_attn_Wo"], np.float32),
        "bq": np.asarray(inputs["L0_attn_bq"], np.float32).reshape(1, D),
        "bk": np.asarray(inputs["L0_attn_bk"], np.float32).reshape(1, D),
        "bo": np.asarray(inputs["L0_attn_bo"], np.float32).reshape(1, D),
        "WeL": np.asarray(inputs["L0_attn_We"], np.float32),
        "WrL": np.asarray(inputs["L0_attn_Wr"], np.float32),
        "Wc": np.asarray(inputs["L0_attn_Wc"], np.float32),
        "bc": np.tile(np.asarray(inputs["L0_attn_bc"], np.float32), 2).reshape(128, 1),
        "wg_nat": np.asarray(inputs["L0_ffn_w_gate"], np.float32),
        "wu_nat": np.asarray(inputs["L0_ffn_w_up"], np.float32),
        "wd_nat": np.asarray(inputs["L0_ffn_w_down"], np.float32),
        "Wi": np.asarray(inputs["L1_attn_Wi"], np.float32),
        "Wf": np.asarray(inputs["L1_attn_Wf"], np.float32),
        "Wgr": np.asarray(inputs["L1_attn_Wg"], np.float32),
        "Wo1": np.asarray(inputs["L1_attn_Wo"], np.float32),
        "conv_i": conv_i, "conv_f": conv_f, "gnw": gnw,
        "WeE": np.asarray(inputs["L1_ffn_We"], np.float32).reshape(NEXP * D, D),
        "beE": np.asarray(inputs["L1_ffn_be"], np.float32),
        "WgE": np.asarray(inputs["L1_ffn_Wg"], np.float32),
        "bgE": np.asarray(inputs["L1_ffn_bg"], np.float32).reshape(1, NEXP),
    }
    in_maps = []
    per_b = L // T_MAIN
    for c in range(B * per_b):
        b, pos = c // per_b, c % per_b
        t0 = pos * T_MAIN
        if pos == 0:
            xin = np.concatenate([x[b, 0:WARM], x[b, t0:t0 + T_MAIN]], axis=0)
            mw = np.zeros((128, 1), np.float32)
        else:
            xin = x[b, t0 - WARM:t0 + T_MAIN]
            mw = np.ones((128, 1), np.float32)
        m = dict(shared)
        m["xin"] = np.ascontiguousarray(xin)
        m["xg"] = np.ascontiguousarray(x[b, idx, :])
        m["mwarm"] = mw
        in_maps.append(m)
    return in_maps, B, per_b


def kernel(**inputs):
    T_MAIN, L = 2048, 8192
    in_maps, B, per_b = _host_stage(inputs, T_MAIN, L)
    nc, apin = build_program(T_MAIN)
    res = run_bass_kernel_spmd(nc, in_maps, core_ids=list(range(B * per_b)))
    shards = [np.asarray(r["y"]) for r in res.results]
    return np.stack([np.concatenate(shards[b * per_b:(b + 1) * per_b], axis=0)
                     for b in range(B)], axis=0)


# version 37
# speedup vs baseline: 1.0710x; 1.0308x over previous
import numpy as np
import ml_dtypes
import concourse.bass as bass
import concourse.mybir as mybir
from concourse.alu_op_type import AluOpType
from concourse.bass_utils import run_bass_kernel_spmd

F32 = mybir.dt.float32
BF16 = mybir.dt.bfloat16
I32 = mybir.dt.int32
AF = mybir.ActivationFunctionType
AX = mybir.AxisListType

D = 1024
KC = D // 128
H = 16
DKH = 64
LOWR = 20
LEN = 200
IFF = 2816
ICH = IFF // 128
NEXP = 8
WARM = 128
BF16NP = ml_dtypes.bfloat16
ENGINES = ("sy", "v", "a", "g", "t")


def _split(total, step):
    out, o = [], 0
    while o < total:
        n = min(step, total - o)
        out.append((o, n))
        o += n
    return out



class Op:
    __slots__ = ("eng", "fn", "idx", "pos", "is_dma", "chan", "chan_val",
                 "need_inc", "inc_seq", "waits", "drain")

    def __init__(self, eng, fn, is_dma, chan):
        self.eng = eng
        self.fn = fn
        self.is_dma = is_dma
        self.chan = chan
        self.chan_val = 0
        self.need_inc = False
        self.inc_seq = 0
        self.waits = []
        self.drain = False


class MS:
    def __init__(self, nc):
        self.nc = nc
        self.ops = []
        self.by_eng = {e: [] for e in ENGINES}
        self.res = {}
        self.chan_tot = {}
        self.sems = {}
        self.barrier_fns = {}
        self.war_keys = set()

    def _edge(self, p, c, raw=True):
        if p is c:
            return
        if p.is_dma:
            if c.is_dma and c.chan == p.chan:
                return
            c.waits.append(("ch", p.chan, p.chan_val))
        elif p.eng == c.eng:
            if (raw and p.eng in ("v", "a", "g")) or (not raw and p.eng == "a"):
                p.need_inc = True
                c.waits.append(("op", p))
        else:
            p.need_inc = True
            c.waits.append(("op", p))

    def op(self, eng, fn, reads=(), writes=(), dma_chan=None):
        o = Op(eng, fn, dma_chan is not None, dma_chan)
        if o.is_dma:
            self.chan_tot[o.chan] = self.chan_tot.get(o.chan, 0) + 16
            o.chan_val = self.chan_tot[o.chan]
        for r in reads:
            s = self.res.get(r)
            if s and s["w"] is not None:
                self._edge(s["w"], o)
        for r in writes:
            s = self.res.get(r)
            if s:
                if s["w"] is not None:
                    self._edge(s["w"], o, raw=False)
                for rd in s["r"]:
                    self._edge(rd, o, raw=(r in self.war_keys))

            self.res[r] = {"w": o, "r": []}
        for r in reads:
            self.res.setdefault(r, {"w": None, "r": []})["r"].append(o)
        o.idx = len(self.ops)
        self.ops.append(o)
        o.pos = len(self.by_eng[o.eng])
        self.by_eng[o.eng].append(o)
        return o

    def phase(self):
        lasts = []
        for e in ENGINES:
            for o in reversed(self.by_eng[e]):
                if not o.is_dma:
                    o.need_inc = True
                    lasts.append(o)
                    break
        chans = dict(self.chan_tot)
        for e in ENGINES:
            bf = self.barrier_fns.get(e)
            o = Op(e, bf() if bf else (lambda en: en.nop()), False, None)
            for p in lasts:
                if p.eng != e:
                    o.waits.append(("op", p))
            for ch, v in chans.items():
                o.waits.append(("ch", ch, v))
            o.idx = len(self.ops)
            self.ops.append(o)
            o.pos = len(self.by_eng[e])
            self.by_eng[e].append(o)
        self.res = {}

    def finalize(self, block):
        inc_cnt = {e: 0 for e in ENGINES}
        for o in self.ops:
            if o.need_inc and not o.is_dma:
                inc_cnt[o.eng] += 1
                o.inc_seq = inc_cnt[o.eng]
        for e in ENGINES:
            if inc_cnt[e]:
                self.sems["eng:" + e] = self.nc.semaphore("sm_" + e).__enter__()
        for ch in self.chan_tot:
            self.sems["ch:" + ch] = self.nc.semaphore("sc_" + ch).__enter__()
        ms = self

        def mk(eng_name):
            def inner(e):
                waited = {}
                for o in ms.by_eng[eng_name]:
                    wl = {}
                    for w in o.waits:
                        if w[0] == "ch":
                            key, val = "ch:" + w[1], w[2]
                        else:
                            key, val = "eng:" + w[1].eng, w[1].inc_seq
                        wl[key] = max(wl.get(key, 0), val)
                    for key, val in sorted(wl.items()):
                        if waited.get(key, -1) < val:
                            e.wait_ge(ms.sems[key], val)
                            waited[key] = val
                    if o.drain:
                        e.drain()
                    inst = o.fn(e)
                    if o.is_dma:
                        inst.then_inc(ms.sems["ch:" + o.chan], 16)
                    elif o.need_inc:
                        if inst is None:
                            e.sem_inc(ms.sems["eng:" + o.eng], 1)
                        else:
                            inst.then_inc(ms.sems["eng:" + o.eng], 1)
            return inner

        for name, deco in [("sy", block.sync), ("v", block.vector),
                           ("a", block.scalar), ("g", block.gpsimd),
                           ("t", block.tensor)]:
            if self.by_eng[name]:
                deco(mk(name))


class P:
    pass



def build_program(T_MAIN, debug=False):
    TP = WARM + T_MAIN
    NT = TP // 128
    NTM = T_MAIN // 128
    GRA = _split(TP, 512)
    GRP = _split(TP, 256)
    GRPC = _split(TP, 512)
    GRPM = _split(T_MAIN, 512)

    nc = bass.Bass("TRN2", target_bir_lowering=False)
    ms = MS(nc)

    apin = {}

    def din(name, shape, dtype=F32):
        apin[name] = nc.dram_tensor(name, list(shape), dtype, kind="ExternalInput").ap()

    din("xin", (TP, D)); din("xg", (LEN, D)); din("mwarm", (128, 1))
    din("Wq", (D, D)); din("Wk", (D, D)); din("Wo0", (D, D))
    din("bq", (1, D)); din("bk", (1, D)); din("bo", (1, D))
    din("WeL", (H, LEN, LOWR)); din("WrL", (H, LEN, LOWR))
    din("Wc", (2 * LOWR, DKH)); din("bc", (128, 1))
    din("wg_nat", (IFF, D)); din("wu_nat", (IFF, D)); din("wd_nat", (D, IFF))
    din("Wi", (D, D)); din("Wf", (D, D)); din("Wgr", (D, D)); din("Wo1", (D, D))
    din("conv_i", (128, KC * 4)); din("conv_f", (128, KC * 4))
    din("gnw", (128, KC))
    din("WeE", (NEXP * D, D)); din("beE", (NEXP, D))
    din("WgE", (D, NEXP)); din("bgE", (1, NEXP))

    yout = nc.dram_tensor("y", [T_MAIN, D], F32, kind="ExternalOutput").ap()
    s_scr = nc.dram_tensor("s_scr", [TP, D], F32, kind="Internal").ap()
    x1_scr = nc.dram_tensor("x1_scr", [TP, D], F32, kind="Internal").ap()
    s1_scr = nc.dram_tensor("s1_scr", [T_MAIN, D], F32, kind="Internal").ap()
    wgu_scr = nc.dram_tensor("wgu_scr", [2 * ICH * 128, KC * 128], BF16, kind="Internal").ap()
    wd_scr = nc.dram_tensor("wd_scr", [KC * 128, ICH * 128], BF16, kind="Internal").ap()
    dbg = {}
    if debug:
        for nm, shp in [("dbg_s", (TP, D)), ("dbg_x1", (TP, D)),
                        ("dbg_h", (128, TP)), ("dbg_moe", (T_MAIN, D)),
                        ("dbg_alpha", (T_MAIN, NEXP))]:
            dbg[nm] = nc.dram_tensor(nm, list(shp), F32, kind="ExternalOutput").ap()

    groups = {}

    def alloc(group, name, shape, dtype=F32):
        g = nc.sbuf_tensor(name, list(shape), dtype)
        hh = g.__enter__()
        groups.setdefault(group, []).append(g)
        return hh

    def free_group(group):
        for g in reversed(groups.pop(group, [])):
            g.__exit__(None, None, None)

    pp = []
    for i in range(8):
        g = nc.psum_tensor("pp%d" % i, [128, 512], F32)
        pp.append(g.__enter__())
    PPK = [("pp", i) for i in range(8)]

    ones1 = alloc("főr", "ones1", (1, 128), BF16)
    onesf = alloc("főr", "onesf", (128, 1), F32)
    mwarm_sb = alloc("főr", "mwarm_sb", (128, 1), F32)
    bq_sb = alloc("főr", "bq_sb", (1, D), BF16)
    bk_sb = alloc("főr", "bk_sb", (1, D), BF16)
    bo_sb = alloc("főr", "bo_sb", (1, D), BF16)
    bg_sb = alloc("főr", "bg_sb", (1, NEXP), BF16)
    bc_sb = alloc("főr", "bc_sb", (128, 1), F32)
    ln_ab = alloc("főr", "ln_ab", (128, 2 * (NT + 2)), F32)
    xbuf2 = alloc("főr", "xbuf2", (128, 2 * D), F32)
    ms.war_keys.update([("tmpf",), ("h_sl",)])
    bscr = alloc("főr", "bscr", (1, 8), F32)
    _bar_i = [0]

    def _g_barrier_fn():
        i = _bar_i[0]
        _bar_i[0] += 1
        return lambda en, i=i: en.memset(bscr[0:1, i:i + 1], 0.0)

    ms.barrier_fns["g"] = _g_barrier_fn

    stg_state = P()
    lnv = P()
    lnv.bn6 = alloc("főr", "ln_bn6", (128, 12), F32)
    lnv.mv = alloc("főr", "ln_mv", (128, 2), F32)
    lnv.var = alloc("főr", "ln_var", (128, 1), F32)

    qb = P()
    QW = 1024
    qb.amax = alloc("gB", "q_amax", (128, QW // 16), F32)
    qb.scale = alloc("gB", "q_scale", (128, QW // 16), F32)
    qb.v = alloc("gB", "q_v", (128, QW), F32)
    qb.f1 = alloc("gB", "q_f1", (128, QW), F32)
    qb.g = alloc("gB", "q_g", (128, QW), F32)
    qb.sg = alloc("gB", "q_sg", (128, QW), BF16)

    stg_state.a = alloc("gA", "wstW", (128, D), F32)
    stg_state.b = alloc("gA", "wstW2", (128, D), F32)
    wq_sb = alloc("gA", "wq_sb", (128, KC * D), BF16)
    wk_sb = alloc("gA", "wk_sb", (128, KC * D), BF16)
    wo_sb = alloc("gA", "wo_sb", (128, KC * D), BF16)
    ablk = alloc("gA", "ablk", (128, KC * 128), BF16)
    bblk = alloc("gA", "bblk", (128, KC * 128), BF16)

    def dma(dst_ap, src_ap, chan, reads, writes, transpose=False, eng="sy"):
        if transpose:
            ms.op(eng, lambda e, d=dst_ap, s=src_ap: e.dma_start_transpose(d, s),
                  reads=reads, writes=writes, dma_chan=chan)
        else:
            ms.op(eng, lambda e, d=dst_ap, s=src_ap: e.dma_start(out=d, in_=s),
                  reads=reads, writes=writes, dma_chan=chan)

    def vop(fn, reads, writes, eng="v"):
        ms.op(eng, fn, reads=reads, writes=writes)

    def mm(ps_ap, lhs_ap, rhs_ap, reads, writes, start, stop, tile_position=None):
        if not isinstance(ps_ap, bass.AP):
            ps_ap = ps_ap[:]
        if not isinstance(lhs_ap, bass.AP):
            lhs_ap = lhs_ap[:]
        if not isinstance(rhs_ap, bass.AP):
            rhs_ap = rhs_ap[:]
        ms.op("t", lambda e, o=ps_ap, l=lhs_ap, r=rhs_ap: e.matmul(
            o, l, r, start=start, stop=stop, tile_position=tile_position),
            reads=reads, writes=writes)

    def cast_weight(srcname, dst_sb, dstkey, nchunks=KC, width=D):
        for k in range(nchunks):
            stg, skey, ch = ((stg_state.a, ("wst",), "ch_wst") if k % 2 == 0
                             else (stg_state.b, ("wst2",), "ch_wst2"))
            dma(stg[:, 0:width], apin[srcname][128 * k:128 * (k + 1), :],
                chan=ch, reads=[skey], writes=[skey], eng="a")
            vop(lambda e, k=k, stg=stg: e.tensor_copy(
                out=dst_sb[:, width * k:width * (k + 1)], in_=stg[:, 0:width]),
                reads=[skey], writes=[(dstkey, k)])

    def emit_quant_p1(x_key, x_ap, F, P_, q, vkey, v_ap, sgkey, sg_ap):
        vop(lambda e: e.activation(out=v_ap, in_=x_ap, func=AF.Abs),
            reads=[x_key], writes=[vkey], eng="a")
        vop(lambda e: e.activation(out=sg_ap, in_=x_ap, func=AF.Sign),
            reads=[x_key], writes=[sgkey], eng="a")

    def emit_quant_p2(x_key, x_ap, out_key, out_ap, F, P_, q, vkey, v_ap,
                      sgkey, sg_ap):
        nb = F // 16
        vop(lambda e: e.tensor_reduce(out=q.amax[0:P_, 0:nb],
                                      in_=v_ap.rearrange("p (b w) -> p b w", w=16),
                                      axis=AX.X, op=AluOpType.max),
            reads=[vkey], writes=[("q.amax",)])
        vop(lambda e: e.tensor_scalar(out=q.scale[0:P_, 0:nb], in0=q.amax[0:P_, 0:nb],
                                      scalar1=1.0 / 6.0, scalar2=1e-12,
                                      op0=AluOpType.mult, op1=AluOpType.max),
            reads=[("q.amax",)], writes=[("q.scale",)])
        vop(lambda e: e.reciprocal(out=q.amax[0:P_, 0:nb], in_=q.scale[0:P_, 0:nb]),
            reads=[("q.scale",)], writes=[("q.amax",)])
        rb = q.amax[0:P_, 0:nb].unsqueeze(2).broadcast_to([P_, nb, 16])
        vop(lambda e: e.tensor_tensor(out=v_ap.rearrange("p (b w) -> p b w", w=16),
                                      in0=v_ap.rearrange("p (b w) -> p b w", w=16),
                                      in1=rb, op=AluOpType.mult),
            reads=[vkey, ("q.amax",)], writes=[vkey])
        vop(lambda e: e.tensor_scalar(out=q.g[0:P_, 0:F], in0=v_ap,
                                      scalar1=5.0, scalar2=2.0,
                                      op0=AluOpType.is_gt, op1=AluOpType.mult),
            reads=[vkey], writes=[("q.g",)])
        for thr in (2.5, 3.5):
            vop(lambda e, thr=thr: e.scalar_tensor_tensor(
                out=q.g[0:P_, 0:F], in0=v_ap, scalar=thr,
                in1=q.g[0:P_, 0:F], op0=AluOpType.is_gt, op1=AluOpType.add),
                reads=[vkey, ("q.g",)], writes=[("q.g",)])
        vop(lambda e: e.tensor_scalar(out=q.f1[0:P_, 0:F], in0=v_ap,
                                      scalar1=0.25, scalar2=None, op0=AluOpType.is_gt),
            reads=[vkey], writes=[("q.f1",)])
        for thr in (0.75, 1.25, 1.75):
            vop(lambda e, thr=thr: e.scalar_tensor_tensor(
                out=q.f1[0:P_, 0:F], in0=v_ap, scalar=thr,
                in1=q.f1[0:P_, 0:F], op0=AluOpType.is_gt, op1=AluOpType.add),
                reads=[vkey, ("q.f1",)], writes=[("q.f1",)])
        vop(lambda e: e.scalar_tensor_tensor(out=q.g[0:P_, 0:F], in0=q.f1[0:P_, 0:F],
                                             scalar=0.5, in1=q.g[0:P_, 0:F],
                                             op0=AluOpType.mult, op1=AluOpType.add),
            reads=[("q.f1",), ("q.g",)], writes=[("q.g",)])
        vop(lambda e: e.tensor_tensor(out=q.g[0:P_, 0:F], in0=q.g[0:P_, 0:F],
                                      in1=sg_ap, op=AluOpType.mult),
            reads=[("q.g",), sgkey], writes=[("q.g",)])
        sb2 = q.scale[0:P_, 0:nb].unsqueeze(2).broadcast_to([P_, nb, 16])
        vop(lambda e: e.tensor_tensor(out=out_ap.rearrange("p (b w) -> p b w", w=16),
                                      in0=q.g[0:P_, 0:F].rearrange("p (b w) -> p b w", w=16),
                                      in1=sb2, op=AluOpType.mult),
            reads=[("q.g",), ("q.scale",)], writes=[out_key])

    def emit_quant(x_key, x_ap, out_key, out_ap, F, P_=128, q=None):
        if q is None:
            q = qb
        vk, sk = ("q.v",), ("q.sg",)
        emit_quant_p1(x_key, x_ap, F, P_, q, vk, q.v[0:P_, 0:F], sk, q.sg[0:P_, 0:F])
        emit_quant_p2(x_key, x_ap, out_key, out_ap, F, P_, q, vk,
                      q.v[0:P_, 0:F], sk, q.sg[0:P_, 0:F])

    def emit_ln(x_key, x_ap, ab_sb, ab_key, slot, outs, P_=128,
                scr=None, scr_key=None):
        a_ap = ab_sb[0:P_, 2 * slot:2 * slot + 1]
        b_ap = ab_sb[0:P_, 2 * slot + 1:2 * slot + 2]
        vop(lambda e: e.bn_stats(out=lnv.bn6[0:P_, 0:6], in_=x_ap[:, 0:D // 2]),
            reads=[x_key], writes=[("ln.bn6",)])
        vop(lambda e: e.bn_stats(out=lnv.bn6[0:P_, 6:12], in_=x_ap[:, D // 2:D]),
            reads=[x_key], writes=[("ln.bn6",)])
        vop(lambda e: e.bn_aggr(out=lnv.mv[0:P_, :], in_=lnv.bn6[0:P_, 0:12]),
            reads=[("ln.bn6",)], writes=[("ln.mv",)])
        vop(lambda e: e.tensor_scalar(out=lnv.var[0:P_, :], in0=lnv.mv[0:P_, 1:2],
                                      scalar1=1e-5, scalar2=None, op0=AluOpType.add),
            reads=[("ln.mv",)], writes=[("ln.var",)])
        vop(lambda e: e.activation(out=lnv.var[0:P_, :], in_=lnv.var[0:P_, :],
                                   func=AF.Sqrt),
            reads=[("ln.var",)], writes=[("ln.var",)], eng="a")
        vop(lambda e: e.reciprocal(out=a_ap, in_=lnv.var[0:P_, :]),
            reads=[("ln.var",)], writes=[(ab_key, slot, "a")])
        vop(lambda e: e.scalar_tensor_tensor(out=b_ap, in0=lnv.mv[0:P_, 0:1],
                                             scalar=-1.0, in1=a_ap,
                                             op0=AluOpType.mult, op1=AluOpType.mult),
            reads=[("ln.mv",), (ab_key, slot, "a")], writes=[(ab_key, slot, "b")])
        for okey, oap in outs:
            vop(lambda e, oap=oap: e.tensor_scalar(out=oap, in_=None, in0=x_ap,
                                                   scalar1=a_ap, scalar2=b_ap,
                                                   op0=AluOpType.mult, op1=AluOpType.add)
                if False else
                e.tensor_scalar(out=oap, in0=x_ap, scalar1=a_ap, scalar2=b_ap,
                                op0=AluOpType.mult, op1=AluOpType.add),
                reads=[x_key, (ab_key, slot, "a"), (ab_key, slot, "b")],
                writes=[okey])

    dma(mwarm_sb[:], apin["mwarm"][:], chan="c_mw", reads=[], writes=[("mwarm",)])
    dma(bc_sb[:], apin["bc"][:], chan="c_bc", reads=[], writes=[("bc",)])
    vop(lambda e: e.memset(ones1[:], 1.0), reads=[], writes=[("ones1",)])
    vop(lambda e: e.memset(onesf[:], 1.0), reads=[], writes=[("onesf",)])
    vop(lambda e: e.memset(ablk[:], 0.0), reads=[], writes=[("ablk", "z")])
    vop(lambda e: e.memset(bblk[:], 0.0), reads=[], writes=[("bblk", "z")])

    for nm, dst, w in [("bq", bq_sb, D), ("bk", bk_sb, D), ("bo", bo_sb, D),
                       ("bgE", bg_sb, NEXP)]:
        dma(stg_state.a[0:1, 0:w], apin[nm][:], chan="ch_wst", reads=[("wst",)], writes=[("wst",)])
        vop(lambda e, dst=dst, w=w, stg=stg_state.a: e.tensor_copy(
            out=dst[0:1, 0:w], in_=stg[0:1, 0:w]),
            reads=[("wst",)], writes=[("b_" + nm,)])

    cast_weight("Wq", wq_sb, "wq")
    cast_weight("Wk", wk_sb, "wk")
    cast_weight("Wo0", wo_sb, "wo")

    xg_sb = alloc("gW1", "xg_sb", (128, 2 * D), F32)
    xng_sb = alloc("gW1", "xng_sb", (128, 2 * D), BF16)
    wel_sb = alloc("gW1", "wel_sb", (128, 2 * LOWR), BF16)
    wrl_sb = alloc("gW1", "wrl_sb", (128, 2 * LOWR), BF16)
    wc_sb = alloc("gW1", "wc_sb", (LOWR, 2 * DKH), BF16)
    wet_sb = alloc("gW1", "wet_sb", (LOWR, 2 * DKH), BF16)
    LEN2 = LEN - 128

    dma(xg_sb[0:128, 0:D], apin["xg"][0:128, :], chan="cxg", reads=[],
        writes=[("xg", 0), ("xg", 1)])
    dma(xg_sb[0:LEN2, D:2 * D], apin["xg"][128:LEN, :], chan="cxg", reads=[],
        writes=[("xg", 0), ("xg", 1)])
    dma(stg_state.a[0:LOWR, 0:DKH], apin["Wc"][0:LOWR, :], chan="ch_wst",
        reads=[("wst",)], writes=[("wst",)])
    dma(stg_state.a[0:LOWR, DKH:2 * DKH], apin["Wc"][LOWR:2 * LOWR, :], chan="ch_wst",
        reads=[("wst",)], writes=[("wst",)])
    vop(lambda e, stg=stg_state.a: e.tensor_copy(out=wc_sb[:], in_=stg[0:LOWR, 0:2 * DKH]),
        reads=[("wst",)], writes=[("wc",)])
    emit_ln(("xg", 0), xg_sb[0:128, 0:D], ln_ab, "lng", NT, [(("xng", 0), xng_sb[0:128, 0:D])])
    emit_ln(("xg", 1), xg_sb[0:LEN2, D:2 * D], ln_ab, "lng", NT + 1,
            [(("xng", 1), xng_sb[0:LEN2, D:2 * D])], P_=LEN2)

    ps_we = pp[0][0:LOWR, 0:DKH]
    ps_ab = pp[1]
    for h in range(H):
        j, par = h // 2, h % 2
        dma(stg_state.a[0:128, 0:LOWR], apin["WeL"][h, 0:128, :], chan="ch_wst",
            reads=[("wst",)], writes=[("wst",)])
        dma(stg_state.a[0:LEN2, LOWR:2 * LOWR], apin["WeL"][h, 128:LEN, :], chan="ch_wst",
            reads=[("wst",)], writes=[("wst",)])
        vop(lambda e, stg=stg_state.a: e.tensor_copy(out=wel_sb[:, :], in_=stg[0:128, 0:2 * LOWR]),
            reads=[("wst",)], writes=[("wel",)])
        dma(stg_state.a[0:128, 0:LOWR], apin["WrL"][h, 0:128, :], chan="ch_wst",
            reads=[("wst",)], writes=[("wst",)])
        dma(stg_state.a[0:LEN2, LOWR:2 * LOWR], apin["WrL"][h, 128:LEN, :], chan="ch_wst",
            reads=[("wst",)], writes=[("wst",)])
        vop(lambda e, stg=stg_state.a: e.tensor_copy(out=wrl_sb[:, :], in_=stg[0:128, 0:2 * LOWR]),
            reads=[("wst",)], writes=[("wrl",)])
        for src_sb, srckey, half, blk, blkkey in [
                (wel_sb, ("wel",), 0, ablk, "ablk"), (wrl_sb, ("wrl",), 1, bblk, "bblk")]:
            wcs = wc_sb[0:LOWR, 0:DKH] if half == 0 else wc_sb[0:LOWR, DKH:2 * DKH]
            mm(ps_we, src_sb[0:128, 0:LOWR], xng_sb[0:128, DKH * h:DKH * (h + 1)],
               reads=[srckey, ("xng", 0)], writes=[PPK[0]], start=True, stop=False)
            mm(ps_we, src_sb[0:LEN2, LOWR:2 * LOWR],
               xng_sb[0:LEN2, D + DKH * h:D + DKH * (h + 1)],
               reads=[srckey, ("xng", 1)], writes=[PPK[0]], start=False, stop=True)
            wt = wet_sb[:, DKH * half:DKH * (half + 1)]
            vop(lambda e, wt=wt: e.activation(out=wt, in_=ps_we, func=AF.Copy),
                reads=[PPK[0]], writes=[("wet", half)], eng="a")
            mm(ps_ab[64 * par:64 * (par + 1), 0:DKH], wt, wcs,
               reads=[("wet", half), ("wc",)], writes=[PPK[1]],
               start=True, stop=True, tile_position=(0, 64 * par))
            dst = blk[64 * par:64 * (par + 1), 128 * j + 64 * par:128 * j + 64 * (par + 1)]
            vop(lambda e, dst=dst, par=par: e.activation(
                out=dst, in_=ps_ab[64 * par:64 * (par + 1), 0:DKH], func=AF.Copy),
                reads=[PPK[1], (blkkey, "z")], writes=[(blkkey, j, par)], eng="a")

    ms.phase()
    free_group("gW1")

    qa = P()
    QA = 2048
    qa.amax = alloc("gAq", "qa_amax", (128, QA // 16), F32)
    qa.scale = alloc("gAq", "qa_scale", (128, QA // 16), F32)
    qa.v2 = [alloc("gAq", "qa_v%d" % i, (128, QA), F32) for i in range(2)]
    qa.f1 = alloc("gAq", "qa_f1", (128, QA), F32)
    qa.g = alloc("gAq", "qa_g", (128, QA), F32)
    qa.sg2 = [alloc("gAq", "qa_sg%d" % i, (128, QA), BF16) for i in range(2)]
    wstA = [alloc("gAq", "wstA%d" % i, (128, QA), F32) for i in range(2)]
    wqqA = alloc("gAq", "wqqA", (128, QA), BF16)
    recst = [alloc("gAq", "recst%d" % i, (128, 1408), BF16) for i in range(2)]
    wq_jobs = []

    def job_width_src(job):
        if job[0] == "wgu":
            return D * len(job[3])
        return 1408

    def job_stage1(job, par):
        stg = wstA[par]
        skey = ("wstA", par)
        vk, sk = ("qa.v", par), ("qa.sg", par)
        if job[0] == "wgu":
            _, srcname, base, sls = job
            for i, sl in enumerate(sls):
                dma(stg[:, D * i:D * (i + 1)], apin[srcname][128 * sl:128 * (sl + 1), :],
                    chan="ch_wstA%d" % par, reads=[skey], writes=[skey], eng="a")
            F = D * len(sls)
        else:
            _, sl, half = job
            so, F = half * 1408, 1408
            dma(stg[:, 0:F], apin["wd_nat"][128 * sl:128 * (sl + 1), so:so + F],
                chan="ch_wstA%d" % par, reads=[skey], writes=[skey], eng="a")
        emit_quant_p1(skey, stg[:, 0:F], F, 128, qa, vk, qa.v2[par][:, 0:F],
                      sk, qa.sg2[par][:, 0:F])

    def job_stage2(job, par):
        F = job_width_src(job)
        vk, sk = ("qa.v", par), ("qa.sg", par)
        emit_quant_p2(("wstA", par), wstA[par][:, 0:F], ("wqqA",), wqqA[:, 0:F],
                      F, 128, qa, vk, qa.v2[par][:, 0:F], sk, qa.sg2[par][:, 0:F])
        rp = par
        rkey = ("recst", rp)
        if job[0] == "wgu":
            _, srcname, base, sls = job
            for i, sl in enumerate(sls):
                dma(recst[rp][:, 0:D].rearrange("p (j c) -> p j c", c=128),
                    wqqA[:, D * i:D * (i + 1)].rearrange("p (j c) -> p j c", c=128),
                    chan="tx_rec%d" % rp, reads=[("wqqA",), rkey], writes=[rkey],
                    transpose=True)
                dma(wgu_scr[128 * (base + sl):128 * (base + sl + 1), :],
                    recst[rp][:, 0:D],
                    chan="st_rec%d" % rp, reads=[rkey],
                    writes=[("wgu_scr", base + sl)], eng="a")
        else:
            _, sl, half = job
            so = half * 1408
            dma(recst[rp][:, 0:F].rearrange("p (j c) -> p j c", c=128),
                wqqA[:, 0:F].rearrange("p (j c) -> p j c", c=128),
                chan="tx_rec%d" % rp, reads=[("wqqA",), rkey], writes=[rkey],
                transpose=True)
            dma(wd_scr[128 * sl:128 * (sl + 1), so:so + F], recst[rp][:, 0:F],
                chan="st_rec%d" % rp, reads=[rkey], writes=[("wd_scr", sl, half)],
                eng="a")

    _g3 = [list(range(i, min(i + 2, ICH))) for i in range(0, ICH, 2)]
    for gi3, sls in enumerate(_g3):
        wq_jobs.append(("wgu", "wg_nat", 0, sls))
    for gi3, sls in enumerate(_g3):
        wq_jobs.append(("wgu", "wu_nat", ICH, sls))
    for sl in range(KC):
        for half in range(2):
            wq_jobs.append(("wd", sl, half))
    wq_jobs.reverse()
    _jobstate = {"n": 0, "pending": []}

    def quant_weight_slice():
        if wq_jobs:
            job = wq_jobs.pop()
            par = _jobstate["n"] & 1
            _jobstate["n"] += 1
            job_stage1(job, par)
            _jobstate["pending"].append((job, par))
        if len(_jobstate["pending"]) > 1 or (not wq_jobs and _jobstate["pending"]):
            job, par = _jobstate["pending"].pop(0)
            job_stage2(job, par)

    def drain_jobs():
        while wq_jobs or _jobstate["pending"]:
            quant_weight_slice()

    GA = 512
    xnT = alloc("gA2", "xnT", (128, KC * GA), BF16)
    qnT = alloc("gA2", "qnT", (128, KC * GA), BF16)
    knT = alloc("gA2", "knT", (128, KC * GA), BF16)
    mT = alloc("gA2", "mT", (128, KC * GA), BF16)
    qss = alloc("gA2", "qss", (128, H), F32)
    qrv = alloc("gA2", "qrv", (128, H), F32)
    qn_bf = alloc("gA2", "qn_bf", (128, D), BF16)
    kn_bf = alloc("gA2", "kn_bf", (128, D), BF16)
    xnbf = alloc("gA2", "xnbf", (128, D), BF16)
    qf = alloc("gA2", "qf", (128, D), F32)
    kf = qf
    s_t = qf

    def l2norm(src, srckey, dst, dstkey):
        vop(lambda e: e.tensor_tensor(out=qb.v[:, 0:D], in0=src[:], in1=src[:],
                                      op=AluOpType.mult),
            reads=[srckey], writes=[("q.v",)])
        vop(lambda e: e.tensor_reduce(out=qss[:], in_=qb.v[:, 0:D].rearrange(
            "p (h d) -> p h d", d=DKH), axis=AX.X, op=AluOpType.add),
            reads=[("q.v",)], writes=[("qss",)])
        vop(lambda e: e.activation(out=qrv[:], in_=qss[:], func=AF.Sqrt),
            reads=[("qss",)], writes=[("qrv",)], eng="a")
        vop(lambda e: e.tensor_scalar(out=qrv[:], in0=qrv[:], scalar1=1e-12,
                                      scalar2=None, op0=AluOpType.max),
            reads=[("qrv",)], writes=[("qrv",)])
        vop(lambda e: e.reciprocal(out=qrv[:], in_=qrv[:]),
            reads=[("qrv",)], writes=[("qrv",)])
        rb = qrv[:, :].unsqueeze(2).broadcast_to([128, H, DKH])
        vop(lambda e: e.tensor_tensor(out=dst.rearrange("p (h d) -> p h d", d=DKH),
                                      in0=src.rearrange("p (h d) -> p h d", d=DKH),
                                      in1=rb, op=AluOpType.mult),
            reads=[srckey, ("qrv",)], writes=[dstkey])

    def a_front(go, ti):
        for _ in range(2):
            if wq_jobs:
                quant_weight_slice()
        t0 = go + ti * 128
        tidx = t0 // 128
        par = tidx % 2
        xb = xbuf2[:, D * par:D * (par + 1)]
        xbk = ("xbuf", par)
        dma(xb, apin["xin"][t0:t0 + 128, :], chan="ch_x%d" % par,
            reads=[xbk], writes=[xbk])
        emit_ln(xbk, xb, ln_ab, "ln0", tidx, [(("xnbf",), xnbf[:])])
        wall = [("xnT", j2, ti) for j2 in range(KC)]
        dma(xnT[:, :].rearrange("p (j c) -> p j c", c=GA)[:, :, 128 * ti:128 * (ti + 1)],
            xnbf[:, :].rearrange("p (j c) -> p j c", c=128),
            chan="tx_xnT%d" % ti, reads=[("xnbf",)], writes=wall, transpose=True)

    def a_back1(go, ti):
        t0 = go + ti * 128
        for ww, wwkey, ppi, bap, bkey in [
                (wq_sb, "wq", (0, 1), bq_sb, ("b_bq",)),
                (wk_sb, "wk", (2, 3), bk_sb, ("b_bk",))]:
            for n in range(2):
                pidx = ppi[n]
                mm(pp[pidx], ones1[:], bap[0:1, 512 * n:512 * (n + 1)],
                   reads=[("ones1",), bkey], writes=[PPK[pidx]], start=True, stop=False)
                for k in range(KC):
                    mm(pp[pidx], xnT[:, GA * k + 128 * ti:GA * k + 128 * (ti + 1)],
                       ww[:, D * k + 512 * n:D * k + 512 * (n + 1)],
                       reads=[("xnT", k, ti), (wwkey, k)], writes=[PPK[pidx]],
                       start=False, stop=(k == KC - 1))
        for n in range(2):
            vop(lambda e, n=n: e.activation(out=qf[:, 512 * n:512 * (n + 1)],
                                            in_=pp[n][:], func=AF.Copy),
                reads=[PPK[n], ("qf",)], writes=[("qf",)], eng="a")
        l2norm(qf, ("qf",), qn_bf[:, :], ("qn",))
        for n in range(2):
            vop(lambda e, n=n: e.activation(out=kf[:, 512 * n:512 * (n + 1)],
                                            in_=pp[2 + n][:], func=AF.Copy),
                reads=[PPK[2 + n], ("qf",)], writes=[("qf",)], eng="a")
        l2norm(kf, ("qf",), kn_bf[:, :], ("kn",))
        wq_all = [("qnT", j2, ti) for j2 in range(KC)]
        wk_all = [("knT", j2, ti) for j2 in range(KC)]
        dma(qnT[:, :].rearrange("p (j c) -> p j c", c=GA)[:, :, 128 * ti:128 * (ti + 1)],
            qn_bf[:, :].rearrange("p (j c) -> p j c", c=128),
            chan="tx_qnT%d" % ti, reads=[("qn",)], writes=wq_all, transpose=True)
        dma(knT[:, :].rearrange("p (j c) -> p j c", c=GA)[:, :, 128 * ti:128 * (ti + 1)],
            kn_bf[:, :].rearrange("p (j c) -> p j c", c=128),
            chan="tx_knT%d" % ti, reads=[("kn",)], writes=wk_all, transpose=True)

    def a_tail(go, gn):
        ntile = gn // 128
        for jj in range(KC):
            rq = [("qnT", jj, ti) for ti in range(ntile)]
            rk = [("knT", jj, ti) for ti in range(ntile)]
            mm(pp[4][:, 0:gn], ablk[:, 128 * jj:128 * (jj + 1)], qnT[:, GA * jj:GA * jj + gn],
               reads=rq + [("ablk", jj, 0), ("ablk", jj, 1)], writes=[PPK[4]],
               start=True, stop=False)
            mm(pp[4][:, 0:gn], bblk[:, 128 * jj:128 * (jj + 1)], knT[:, GA * jj:GA * jj + gn],
               reads=rk + [("bblk", jj, 0), ("bblk", jj, 1)], writes=[PPK[4]],
               start=False, stop=True)
            vop(lambda e, jj=jj, gn=gn: e.tensor_scalar(
                out=mT[:, GA * jj:GA * jj + gn], in0=pp[4][:, 0:gn],
                scalar1=bc_sb[:, 0:1], scalar2=None, op0=AluOpType.add),
                reads=[PPK[4], ("bc",)], writes=[("mT", jj)])
        for ti in range(ntile):
            t0 = go + ti * 128
            for n in range(2):
                pidx = 5 + n
                mm(pp[pidx], ones1[:], bo_sb[0:1, 512 * n:512 * (n + 1)],
                   reads=[("ones1",), ("b_bo",)], writes=[PPK[pidx]], start=True, stop=False)
                for k in range(KC):
                    mm(pp[pidx], mT[:, GA * k + 128 * ti:GA * k + 128 * (ti + 1)],
                       wo_sb[:, D * k + 512 * n:D * k + 512 * (n + 1)],
                       reads=[("mT", k), ("wo", k)], writes=[PPK[pidx]],
                       start=False, stop=(k == KC - 1))
            for n in range(2):
                pidx = 5 + n
                vop(lambda e, n=n, pidx=pidx: e.activation(
                    out=s_t[:, 512 * n:512 * (n + 1)], in_=pp[pidx][:], func=AF.Copy),
                    reads=[PPK[pidx], ("qf",)], writes=[("qf",)], eng="a")
            dma(s_scr[t0:t0 + 128, :], s_t[:], chan="st_s", reads=[("qf",)],
                writes=[("s_scr", t0 // 128)])
            if debug:
                dma(dbg["dbg_s"][t0:t0 + 128, :], s_t[:], chan="dbgc",
                    reads=[("qf",)], writes=[])

    units = []
    for go, gn in GRA:
        nt_ = gn // 128
        for ti in range(nt_):
            units.append((go, gn, ti, ti == nt_ - 1))
    pend = None
    for u in units:
        a_front(u[0], u[2])
        if pend is not None:
            a_back1(pend[0], pend[2])
            if pend[3]:
                a_tail(pend[0], pend[1])
        pend = u
    a_back1(pend[0], pend[2])
    a_tail(pend[0], pend[1])

    drain_jobs()
    ms.phase()
    free_group("gA2")
    free_group("gAq")
    free_group("gA")

    wgT = alloc("gB2", "wgT", (128, KC * IFF), BF16)
    wuT = alloc("gB2", "wuT", (128, KC * IFF), BF16)
    wdT = alloc("gB2", "wdT", (128, ICH * D), BF16)
    for wi_, (dst_sb, dkey, base) in enumerate([(wgT, "wgT", 0), (wuT, "wuT", ICH)]):
        for sl in range(ICH):
            wall = [(dkey, j2) for j2 in range(KC)]
            dma(dst_sb[:, :].rearrange("p (j c) -> p j c", c=IFF)[:, :, 128 * sl:128 * (sl + 1)],
                wgu_scr[128 * (base + sl):128 * (base + sl + 1), :]
                .rearrange("p (j c) -> p j c", c=128),
                chan="ld_" + dkey, reads=[("wgu_scr", base + sl)], writes=wall)
    for sl in range(KC):
        wall = [("wdT", k2) for k2 in range(ICH)]
        dma(wdT[:, :].rearrange("p (j c) -> p j c", c=D)[:, :, 128 * sl:128 * (sl + 1)],
            wd_scr[128 * sl:128 * (sl + 1), :].rearrange("p (j c) -> p j c", c=128),
            chan="ld_wdT", reads=[("wd_scr", sl, 0), ("wd_scr", sl, 1)], writes=wall)
    xqbf = alloc("gB2", "xqbf", (128, D), BF16)
    tmpf = alloc("gB2", "tmpf", (128, D), F32)
    sldb = alloc("gB2", "sldb", (128, D), F32)
    xn32 = tmpf
    sload = sldb
    xqT = alloc("gB2", "xqT", (128, KC * 256), BF16)
    hqT = alloc("gB2", "hqT", (128, ICH * 256), BF16)
    h_sl = alloc("gB2", "h_sl", (128, 1024), F32)
    hq_sl = xqbf
    g_t = alloc("gB2", "g_t", (128, D), F32)
    x1_t = alloc("gB2", "x1_t", (128, D), F32)

    NSL = _split(IFF, 512)
    hq_bf = xqbf
    xb4ext = alloc("gB2", "xb4ext", (128, 2 * D), F32)

    def b_xb(tidx):
        p4 = tidx % 4
        base = xbuf2 if p4 < 2 else xb4ext
        return base[:, D * (p4 % 2):D * (p4 % 2 + 1)], ("xbuf", p4), p4

    def b_xload(tidx):
        t0 = tidx * 128
        xb, xbk, p4 = b_xb(tidx)
        dma(xb, apin["xin"][t0:t0 + 128, :], chan="ch_x%d" % p4,
            reads=[xbk], writes=[xbk])

    def b_front(tidx):
        t0 = tidx * 128
        par = tidx % 2
        xb, xbk, p4 = b_xb(tidx)
        b_xload(tidx)
        a_ap = ln_ab[:, 2 * tidx:2 * tidx + 1]
        b_ap = ln_ab[:, 2 * tidx + 1:2 * tidx + 2]
        vop(lambda e, a_ap=a_ap, b_ap=b_ap, xb=xb: e.tensor_scalar(
            out=xn32[:], in0=xb, scalar1=a_ap, scalar2=b_ap,
            op0=AluOpType.mult, op1=AluOpType.add),
            reads=[xbk], writes=[("tmpf",)])
        emit_quant(("tmpf",), xn32[:], ("xq",), xqbf[:], D)
        wall = [("xqT", j2, par) for j2 in range(KC)]
        dma(xqT[:, :].rearrange("p (j c) -> p j c", c=256)[:, :, 128 * par:128 * (par + 1)],
            xqbf[:, :].rearrange("p (j c) -> p j c", c=128),
            chan="tx_xqT%d" % par, reads=[("xq",)], writes=wall, transpose=True)

    def b_back1(tidx):
        t0 = tidx * 128
        par = tidx % 2
        for sb_i in range(3):
            sls = NSL[2 * sb_i:2 * sb_i + 2]
            sb0 = sls[0][0]
            sbw = sum(sn for _, sn in sls)
            for k in range(KC):
                lhs = xqT[:, 256 * k + 128 * par:256 * k + 128 * (par + 1)]
                for si, (so, sn) in enumerate(sls):
                    mm(pp[si][:, 0:sn], lhs, wgT[:, IFF * k + so:IFF * k + so + sn],
                       reads=[("xqT", k, par), ("wgT", k)], writes=[PPK[si]],
                       start=(k == 0), stop=(k == KC - 1))
                    mm(pp[2 + si][:, 0:sn], lhs, wuT[:, IFF * k + so:IFF * k + so + sn],
                       reads=[("xqT", k, par), ("wuT", k)], writes=[PPK[2 + si]],
                       start=(k == 0), stop=(k == KC - 1))
            for si, (so, sn) in enumerate(sls):
                vop(lambda e, si=si, so=so, sn=sn, sb0=sb0: e.activation(
                    out=h_sl[:, so - sb0:so - sb0 + sn], in_=pp[si][:, 0:sn],
                    func=AF.Silu),
                    reads=[PPK[si]], writes=[("h_sl",)], eng="a")
                vop(lambda e, si=si, so=so, sn=sn, sb0=sb0: e.tensor_tensor(
                    out=h_sl[:, so - sb0:so - sb0 + sn],
                    in0=h_sl[:, so - sb0:so - sb0 + sn],
                    in1=pp[2 + si][:, 0:sn], op=AluOpType.mult),
                    reads=[("h_sl",), PPK[2 + si]], writes=[("h_sl",)])
            emit_quant(("h_sl",), h_sl[:, 0:sbw], ("xq",), hq_bf[:, 0:sbw], sbw)
            wall = [("hqT", k2, par) for k2 in range(ICH)]
            nblk = sbw // 128
            dma(hqT[:, :].rearrange("p (j c) -> p j c", c=256)[
                :, sb0 // 128:sb0 // 128 + nblk, 128 * par:128 * (par + 1)],
                hq_bf[:, 0:sbw].rearrange("p (j c) -> p j c", c=128),
                chan="tx_hqT%d" % par, reads=[("xq",)], writes=wall, transpose=True)

    def b_back2(tidx):
        t0 = tidx * 128
        par = tidx % 2
        xb, xbk, p4 = b_xb(tidx)
        for k in range(ICH):
            for n in range(2):
                pidx = 6 + n
                mm(pp[pidx], hqT[:, 256 * k + 128 * par:256 * k + 128 * (par + 1)],
                   wdT[:, D * k + 512 * n:D * k + 512 * (n + 1)],
                   reads=[("hqT", k, par), ("wdT", k)], writes=[PPK[pidx]],
                   start=(k == 0), stop=(k == ICH - 1))
        gk = ("g_t",)
        for n in range(2):
            pidx = 6 + n
            vop(lambda e, n=n, pidx=pidx: e.activation(
                out=g_t[:, 512 * n:512 * (n + 1)], in_=pp[pidx][:], func=AF.Silu),
                reads=[PPK[pidx]], writes=[gk], eng="a")
        dma(sload[:], s_scr[t0:t0 + 128, :], chan="ch_sload%d" % (tidx % 2),
            reads=[("s_scr", tidx), ("sldb",)], writes=[("sldb",)], eng="g")
        vop(lambda e: e.tensor_tensor(out=x1_t[:, 0:D], in0=sload[:],
                                      in1=g_t[:], op=AluOpType.mult),
            reads=[("sldb",), gk], writes=[("x1_t",)], eng="g")
        vop(lambda e, xb=xb: e.tensor_tensor(out=x1_t[:, 0:D], in0=x1_t[:, 0:D],
                                             in1=xb, op=AluOpType.add),
            reads=[("x1_t",), xbk], writes=[("x1_t",)], eng="g")
        dma(x1_scr[t0:t0 + 128, :], x1_t[:, 0:D], chan="st_x1_%d" % (tidx % 2),
            reads=[("x1_t",)], writes=[("x1_scr", tidx)], eng="g")
        if debug:
            dma(dbg["dbg_x1"][t0:t0 + 128, :], x1_t[:, 0:D], chan="dbgc",
                reads=[("x1_t",)], writes=[])

    NT_B = TP // 128
    for tidx in range(NT_B):
        b_front(tidx)
        if tidx >= 1:
            b_back1(tidx - 1)
        if tidx >= 2:
            b_back2(tidx - 2)
    b_back1(NT_B - 1)
    b_back2(NT_B - 2)
    b_back2(NT_B - 1)

    ms.phase()
    free_group("gB2")
    free_group("gB")

    xn1T = alloc("gD", "xn1T", (128, KC * TP), BF16)
    xnbf = alloc("gD", "xnbf_c", (128, D), BF16)
    stg_state.a = alloc("gD", "wstC", (128, D), F32)
    stg_state.b = alloc("gD", "wstC2", (128, D), F32)
    z_sb = alloc("gD", "z_sb", (128, NEXP * NTM), F32)
    alp_sb = alloc("gD", "alp_sb", (128, NEXP * NTM), F32)
    zm1 = alloc("gD", "zm1", (128, 1), F32)
    zm2 = alloc("gD", "zm2", (128, 1), F32)
    znm = alloc("gD", "znm", (128, 1), F32)
    zscr = alloc("gD", "zscr", (128, NEXP), F32)

    hbuf = alloc("gC2", "hbuf", (128, KC * TP), BF16)
    slg = alloc("gC2", "slg", (128, KC * T_MAIN), BF16)
    sqacc = alloc("gC2", "sqacc", (128, T_MAIN), F32)
    wo1_sb = alloc("gC2", "wo1_sb", (128, KC * D), BF16)
    gnw_sb = alloc("gC2", "gnw_sb", (128, KC), F32)

    wim = [alloc("gC1", "wim%d" % i, (128, 3 * D), BF16) for i in range(2)]
    wg1_sb = alloc("gC1", "wg1_sb", (128, KC * NEXP), BF16)
    conv_i_sb = alloc("gC1", "conv_i_sb", (128, KC * 4), F32)
    conv_f_sb = alloc("gC1", "conv_f_sb", (128, KC * 4), F32)
    iraw = alloc("gC1", "iraw", (128, TP), F32)
    fraw = alloc("gC1", "fraw", (128, TP), F32)
    ci_b = alloc("gC1", "ci_b", (128, TP), F32)
    cf_b = alloc("gC1", "cf_b", (128, TP), F32)

    cast_weight("Wo1", wo1_sb, "wo1")
    cast_weight("WgE", wg1_sb, "wg1", width=NEXP)
    dma(conv_i_sb[:], apin["conv_i"][:], chan="c_cvi", reads=[], writes=[("convi",)])
    dma(conv_f_sb[:], apin["conv_f"][:], chan="c_cvf", reads=[], writes=[("convf",)])
    dma(gnw_sb[:], apin["gnw"][:], chan="c_gnw", reads=[], writes=[("gnw",)])

    for go, gn in GRPC:
        for ti in range(gn // 128):
            t0 = go + ti * 128
            tidx = t0 // 128
            par = tidx % 2
            xb = xbuf2[:, D * par:D * (par + 1)]
            xbk = ("xbuf", par)
            dma(xb, x1_scr[t0:t0 + 128, :], chan="ch_x%d" % par,
                reads=[xbk, ("x1_scr", tidx)], writes=[xbk])
            emit_ln(xbk, xb, ln_ab, "ln1", tidx, [(("xnbf",), xnbf[:])])
            wall = [("xn1T", j2, tidx) for j2 in range(KC)]
            dma(xn1T[:, :].rearrange("p (j c) -> p j c", c=TP)[:, :, t0:t0 + 128],
                xnbf[:, :].rearrange("p (j c) -> p j c", c=128),
                chan="tx_xn1T", reads=[("xnbf",)], writes=wall, transpose=True)

    for tm in range(NTM):
        t0 = WARM + tm * 128
        zps = pp[3][:, 0:NEXP]
        mm(zps, ones1[:], bg_sb[0:1, 0:NEXP], reads=[("ones1",), ("b_bgE",)],
           writes=[PPK[3]], start=True, stop=False)
        for k in range(KC):
            mm(zps, xn1T[:, TP * k + t0:TP * k + t0 + 128],
               wg1_sb[:, NEXP * k:NEXP * (k + 1)],
               reads=[("xn1T", k, t0 // 128), ("wg1", k)], writes=[PPK[3]],
               start=False, stop=(k == KC - 1))
        zs = z_sb[:, NEXP * tm:NEXP * (tm + 1)]
        als = alp_sb[:, NEXP * tm:NEXP * (tm + 1)]
        vop(lambda e, zs=zs: e.activation(out=zs, in_=pp[3][:, 0:NEXP], func=AF.Copy),
            reads=[PPK[3]], writes=[("z", tm)], eng="a")
        vop(lambda e, zs=zs: e.tensor_reduce(out=zm1[:], in_=zs, axis=AX.X,
                                             op=AluOpType.max),
            reads=[("z", tm)], writes=[("zm1",)])
        vop(lambda e, zs=zs: e.tensor_scalar(out=zscr[:], in0=zs, scalar1=zm1[:, 0:1],
                                             scalar2=-1e30, op0=AluOpType.is_ge,
                                             op1=AluOpType.mult),
            reads=[("z", tm), ("zm1",)], writes=[("zscr",)])
        vop(lambda e, zs=zs: e.tensor_tensor(out=zscr[:], in0=zscr[:], in1=zs,
                                             op=AluOpType.add),
            reads=[("zscr",), ("z", tm)], writes=[("zscr",)])
        vop(lambda e: e.tensor_reduce(out=zm2[:], in_=zscr[:], axis=AX.X,
                                      op=AluOpType.max),
            reads=[("zscr",)], writes=[("zm2",)])
        vop(lambda e, zs=zs, als=als: e.tensor_scalar(out=als, in0=zs,
                                                      scalar1=zm2[:, 0:1], scalar2=None,
                                                      op0=AluOpType.is_ge),
            reads=[("z", tm), ("zm2",)], writes=[("alp", tm)])
        vop(lambda e: e.tensor_scalar(out=znm[:], in0=zm1[:], scalar1=-1.0,
                                      scalar2=None, op0=AluOpType.mult),
            reads=[("zm1",)], writes=[("znm",)])
        vop(lambda e, zs=zs: e.activation(out=zscr[:], in_=zs, func=AF.Exp,
                                          bias=znm[:, 0:1], scale=1.0),
            reads=[("z", tm), ("znm",)], writes=[("zscr",)], eng="a")
        vop(lambda e: e.tensor_tensor(out=zm2[:], in0=zm2[:], in1=zm1[:],
                                      op=AluOpType.subtract),
            reads=[("zm2",), ("zm1",)], writes=[("zm2",)])
        vop(lambda e: e.activation(out=zm2[:], in_=zm2[:], func=AF.Exp),
            reads=[("zm2",)], writes=[("zm2",)], eng="a")
        vop(lambda e: e.tensor_scalar(out=zm2[:], in0=zm2[:], scalar1=1.0,
                                      scalar2=None, op0=AluOpType.add),
            reads=[("zm2",)], writes=[("zm2",)])
        vop(lambda e: e.reciprocal(out=zm2[:], in_=zm2[:]),
            reads=[("zm2",)], writes=[("zm2",)])
        vop(lambda e, als=als: e.tensor_tensor(out=als, in0=als, in1=zscr[:],
                                               op=AluOpType.mult),
            reads=[("alp", tm), ("zscr",)], writes=[("alp", tm)])
        vop(lambda e, als=als: e.tensor_scalar_mul(out=als, in0=als,
                                                   scalar1=zm2[:, 0:1]),
            reads=[("alp", tm), ("zm2",)], writes=[("alpha", tm)])
        if debug:
            vop(lambda e, als=als: e.tensor_copy(out=zscr[:], in_=als),
                reads=[("alpha", tm)], writes=[("zscr",)])
            dma(dbg["dbg_alpha"][128 * tm:128 * (tm + 1), :], zscr[:],
                chan="dbgc", reads=[("zscr",)], writes=[])

    vop(lambda e: e.memset(sqacc[:], 0.0), reads=[], writes=[("sqacc",)])
    for m in range(KC):
        par = m % 2
        for wi_, wname in enumerate(["Wi", "Wf", "Wgr"]):
            stg, skey, ch = ((stg_state.a, ("wst",), "ch_wst") if (m * 3 + wi_) % 2 == 0
                             else (stg_state.b, ("wst2",), "ch_wst2"))
            srcap = apin[wname][:, 128 * m:128 * (m + 1)].rearrange(
                "(a p) c -> p a c", p=128)
            dma(stg[:, 0:D].rearrange("p (a c) -> p a c", c=128), srcap,
                chan=ch, reads=[skey], writes=[skey], eng="a")
            vop(lambda e, par=par, wi_=wi_, stg=stg: e.tensor_copy(
                out=wim[par][:, D * wi_:D * (wi_ + 1)], in_=stg[:, 0:D]),
                reads=[skey], writes=[("wim", par, wi_)])
        for gi, (go, gn) in enumerate(GRPC):
            rx = [("xn1T", k, (go // 128) + tt) for k in range(KC)
                  for tt in range(gn // 128)]
            for pse, psk, wi_ in [(0, PPK[0], 0), (1, PPK[1], 1)]:
                for k in range(KC):
                    mm(pp[pse][:, 0:gn], wim[par][:, D * wi_ + 128 * k:D * wi_ + 128 * (k + 1)],
                       xn1T[:, TP * k + go:TP * k + go + gn],
                       reads=[("wim", par, wi_)] + rx, writes=[psk],
                       start=(k == 0), stop=(k == KC - 1))
            vop(lambda e, go=go, gn=gn: e.activation(out=iraw[:, go:go + gn],
                                                     in_=pp[0][:, 0:gn], func=AF.Copy),
                reads=[PPK[0]], writes=[("iraw",)], eng="a")
            vop(lambda e, go=go, gn=gn: e.activation(out=fraw[:, go:go + gn],
                                                     in_=pp[1][:, 0:gn], func=AF.Copy),
                reads=[PPK[1]], writes=[("fraw",)], eng="a")
        for go, gn in GRPM:
            gow = WARM + go
            rx = [("xn1T", k, (gow // 128) + tt) for k in range(KC)
                  for tt in range(gn // 128)]
            for k in range(KC):
                mm(pp[2][:, 0:gn], wim[par][:, 2 * D + 128 * k:2 * D + 128 * (k + 1)],
                   xn1T[:, TP * k + gow:TP * k + gow + gn],
                   reads=[("wim", par, 2)] + rx, writes=[PPK[2]],
                   start=(k == 0), stop=(k == KC - 1))
            vop(lambda e, m=m, go=go, gn=gn: e.activation(
                out=slg[:, T_MAIN * m + go:T_MAIN * m + go + gn],
                in_=pp[2][:, 0:gn], func=AF.Silu),
                reads=[PPK[2]], writes=[("slg", m)], eng="a")
        vop(lambda e: e.tensor_scalar_mul(out=iraw[:, 0:WARM], in0=iraw[:, 0:WARM],
                                          scalar1=mwarm_sb[:, 0:1]),
            reads=[("iraw",), ("mwarm",)], writes=[("iraw",)])
        vop(lambda e: e.tensor_scalar_mul(out=fraw[:, 0:WARM], in0=fraw[:, 0:WARM],
                                          scalar1=mwarm_sb[:, 0:1]),
            reads=[("fraw",), ("mwarm",)], writes=[("fraw",)])
        for src, skey, cout, ckey, csb, cwkey in [
                (iraw, ("iraw",), ci_b, ("ci",), conv_i_sb, ("convi",)),
                (fraw, ("fraw",), cf_b, ("cf",), conv_f_sb, ("convf",))]:
            vop(lambda e, cout=cout: e.memset(cout[:, 0:3], 0.0),
                reads=[], writes=[ckey])
            vop(lambda e, src=src, cout=cout, csb=csb, m=m: e.tensor_scalar_mul(
                out=cout[:, 3:TP], in0=src[:, 3:TP],
                scalar1=csb[:, 4 * m + 3:4 * m + 4]),
                reads=[skey, cwkey], writes=[ckey])
            for tap in range(3):
                vop(lambda e, src=src, cout=cout, csb=csb, m=m, tap=tap:
                    e.scalar_tensor_tensor(out=cout[:, 3:TP], in0=src[:, tap:TP - 3 + tap],
                                           scalar=csb[:, 4 * m + tap:4 * m + tap + 1],
                                           in1=cout[:, 3:TP],
                                           op0=AluOpType.mult, op1=AluOpType.add),
                    reads=[skey, cwkey, ckey], writes=[ckey])
        vop(lambda e: e.activation(out=cf_b[:], in_=cf_b[:], func=AF.Sigmoid),
            reads=[("cf",)], writes=[("cf",)], eng="a")
        vop(lambda e: e.activation(out=ci_b[:], in_=ci_b[:], func=AF.Silu),
            reads=[("ci",)], writes=[("ci",)], eng="a")
        vop(lambda e: e.tensor_scalar(out=iraw[:], in0=cf_b[:], scalar1=-1.0,
                                      scalar2=1.0, op0=AluOpType.mult, op1=AluOpType.add),
            reads=[("cf",)], writes=[("iraw",)])
        vop(lambda e: e.tensor_tensor(out=ci_b[:], in0=ci_b[:], in1=iraw[:],
                                      op=AluOpType.mult),
            reads=[("ci",), ("iraw",)], writes=[("ci",)])
        for so, sn in _split(TP, 544):
            init = 0.0 if so == 0 else hbuf[:, TP * m + so - 1:TP * m + so]
            vop(lambda e, m=m, so=so, sn=sn, init=init: e.tensor_tensor_scan(
                out=hbuf[:, TP * m + so:TP * m + so + sn],
                data0=cf_b[:, so:so + sn], data1=ci_b[:, so:so + sn],
                initial=init, op0=AluOpType.mult, op1=AluOpType.add),
                reads=[("cf",), ("ci",), ("h", m)], writes=[("h", m)])
        vop(lambda e, m=m: e.tensor_tensor(out=fraw[:, 0:T_MAIN],
                                           in0=hbuf[:, TP * m + WARM:TP * (m + 1)],
                                           in1=hbuf[:, TP * m + WARM:TP * (m + 1)],
                                           op=AluOpType.mult),
            reads=[("h", m), ("fraw",)], writes=[("fraw",)], eng="g")
        vop(lambda e: e.tensor_tensor(out=sqacc[:], in0=sqacc[:], in1=fraw[:, 0:T_MAIN],
                                      op=AluOpType.add),
            reads=[("sqacc",), ("fraw",)], writes=[("sqacc",)], eng="g")
        if debug and m == 0:
            vop(lambda e: e.tensor_copy(out=ci_b[:, 0:T_MAIN],
                                        in_=hbuf[:, WARM:TP]),
                reads=[("h", 0), ("ci",)], writes=[("ci",)])
            dma(dbg["dbg_h"][:, WARM:TP], ci_b[:, 0:T_MAIN], chan="dbgc",
                reads=[("ci",)], writes=[])

    ms.phase()
    free_group("gC1")

    rms_b = alloc("gC2b", "rms_b", (128, T_MAIN), F32)
    oT = alloc("gC2b", "oT", (128, KC * T_MAIN), BF16)
    s1_t = alloc("gC2b", "s1_t", (128, D), F32)
    sqtmp = alloc("gC2b", "sqtmp", (128, T_MAIN), F32)

    for gi, (go, gn) in enumerate(GRPM):
        mm(pp[4][0:1, 0:gn], onesf[:], sqacc[:, go:go + gn],
           reads=[("onesf",), ("sqacc",)], writes=[PPK[4]], start=True, stop=True)
        vop(lambda e, go=go, gn=gn: e.tensor_scalar(out=rms_b[0:1, go:go + gn],
                                                    in0=pp[4][0:1, 0:gn],
                                                    scalar1=1.0 / D, scalar2=1e-5,
                                                    op0=AluOpType.mult, op1=AluOpType.add),
            reads=[PPK[4]], writes=[("rms", 0)])
        vop(lambda e, go=go, gn=gn: e.activation(out=rms_b[0:1, go:go + gn],
                                                 in_=rms_b[0:1, go:go + gn], func=AF.Sqrt),
            reads=[("rms", 0)], writes=[("rms", 0)], eng="a")
        vop(lambda e, go=go, gn=gn: e.reciprocal(out=rms_b[0:1, go:go + gn],
                                                 in_=rms_b[0:1, go:go + gn]),
            reads=[("rms", 0)], writes=[("rms", 0)])
    kk = 1
    while kk < 128:
        dma(rms_b[kk:2 * kk, :], rms_b[0:kk, :], chan="ch_rms%d" % kk,
            reads=[("rms", kk // 2) if kk > 1 else ("rms", 0)], writes=[("rms", kk)])
        kk *= 2
    RMSD = ("rms", 64)

    for m in range(KC):
        vop(lambda e, m=m: e.tensor_tensor(out=sqtmp[:],
                                           in0=hbuf[:, TP * m + WARM:TP * (m + 1)],
                                           in1=rms_b[:], op=AluOpType.mult),
            reads=[("h", m), RMSD], writes=[("sqtmp",)], eng="g")
        vop(lambda e, m=m: e.tensor_tensor(out=sqtmp[:], in0=sqtmp[:],
                                           in1=slg[:, T_MAIN * m:T_MAIN * (m + 1)],
                                           op=AluOpType.mult),
            reads=[("sqtmp",), ("slg", m)], writes=[("sqtmp",)])
        vop(lambda e, m=m: e.tensor_scalar(out=oT[:, T_MAIN * m:T_MAIN * (m + 1)],
                                           in0=sqtmp[:],
                                           scalar1=gnw_sb[:, m:m + 1], scalar2=None,
                                           op0=AluOpType.mult),
            reads=[("sqtmp",), ("gnw",)], writes=[("oT", m)])
    for tm in range(NTM):
        for n in range(2):
            pidx = 5 + n
            for k in range(KC):
                mm(pp[pidx], oT[:, T_MAIN * k + 128 * tm:T_MAIN * k + 128 * (tm + 1)],
                   wo1_sb[:, D * k + 512 * n:D * k + 512 * (n + 1)],
                   reads=[("oT", k), ("wo1", k)], writes=[PPK[pidx]],
                   start=(k == 0), stop=(k == KC - 1))
        for n in range(2):
            pidx = 5 + n
            vop(lambda e, n=n, pidx=pidx: e.activation(
                out=s1_t[:, 512 * n:512 * (n + 1)], in_=pp[pidx][:], func=AF.Copy),
                reads=[PPK[pidx]], writes=[("s1_t",)], eng="a")
        dma(s1_scr[128 * tm:128 * (tm + 1), :], s1_t[:], chan="st_s1",
            reads=[("s1_t",)], writes=[("s1_scr", tm)])

    ms.phase()
    free_group("gC2b")
    free_group("gC2")

    weE_sb = [alloc("gDD", "weE%d" % i, (128, KC * D), BF16) for i in range(2)]
    be_sb = alloc("gDD", "be_sb", (1, 2 * D), BF16)
    acc = alloc("gDD", "acc", (128, NTM * D), F32)
    g1_t = alloc("gDD", "g1_t", (128, D), F32)
    y_t = alloc("gDD", "y_t", (128, D), F32)
    s1l = alloc("gDD", "s1l", (128, 2 * D), F32)
    xde = alloc("gDD", "xde", (128, 2 * D), F32)

    for e_i in range(NEXP):
        sl = e_i % 2
        for k in range(KC):
            stg, skey, ch = ((stg_state.a, ("wst",), "ch_wst") if k % 2 == 0
                             else (stg_state.b, ("wst2",), "ch_wst2"))
            dma(stg[:, 0:D], apin["WeE"][D * e_i + 128 * k:D * e_i + 128 * (k + 1), :],
                chan=ch, reads=[skey], writes=[skey], eng="a")
            vop(lambda e, sl=sl, k=k, stg=stg: e.tensor_copy(
                out=weE_sb[sl][:, D * k:D * (k + 1)], in_=stg[:, 0:D]),
                reads=[skey], writes=[("weE", sl, k)])
        dma(stg_state.a[0:1, 0:D], apin["beE"][e_i:e_i + 1, :], chan="ch_wst",
            reads=[("wst",)], writes=[("wst",)])
        vop(lambda e, sl=sl, stg=stg_state.a: e.tensor_copy(
            out=be_sb[0:1, D * sl:D * (sl + 1)], in_=stg[0:1, 0:D]),
            reads=[("wst",)], writes=[("beE", sl)])
        pb = 2 * (e_i % 3)
        for tm in range(NTM):
            t0 = WARM + 128 * tm
            for n in range(2):
                mm(pp[pb + n], ones1[:], be_sb[0:1, D * sl + 512 * n:D * sl + 512 * (n + 1)],
                   reads=[("ones1",), ("beE", sl)], writes=[PPK[pb + n]],
                   start=True, stop=False)
                for k in range(KC):
                    mm(pp[pb + n], xn1T[:, TP * k + t0:TP * k + t0 + 128],
                       weE_sb[sl][:, D * k + 512 * n:D * k + 512 * (n + 1)],
                       reads=[("xn1T", k, t0 // 128), ("weE", sl, k)],
                       writes=[PPK[pb + n]], start=False, stop=(k == KC - 1))
            for n in range(2):
                pidx = pb + n
                alc = alp_sb[:, NEXP * tm + e_i:NEXP * tm + e_i + 1]
                if e_i == 0:
                    vop(lambda e, tm=tm, n=n, alc=alc, pidx=pidx: e.tensor_scalar(
                        out=acc[:, D * tm + 512 * n:D * tm + 512 * (n + 1)],
                        in0=pp[pidx][:], scalar1=alc, scalar2=None,
                        op0=AluOpType.mult),
                        reads=[PPK[pidx], ("alpha", tm)], writes=[("acc", tm)])
                else:
                    vop(lambda e, tm=tm, n=n, alc=alc, pidx=pidx: e.scalar_tensor_tensor(
                        out=acc[:, D * tm + 512 * n:D * tm + 512 * (n + 1)],
                        in0=pp[pidx][:], scalar=alc,
                        in1=acc[:, D * tm + 512 * n:D * tm + 512 * (n + 1)],
                        op0=AluOpType.mult, op1=AluOpType.add),
                        reads=[PPK[pidx], ("alpha", tm), ("acc", tm)],
                        writes=[("acc", tm)])

    for tm in range(NTM):
        t0 = WARM + 128 * tm
        par = tm % 2
        p4 = tm % 4
        xb = (xbuf2 if p4 < 2 else xde)[:, D * (p4 % 2):D * (p4 % 2 + 1)]
        xbk = ("xbuf", p4)
        dma(xb, x1_scr[t0:t0 + 128, :], chan="ch_x%d" % p4,
            reads=[xbk, ("x1_scr", t0 // 128)], writes=[xbk])
        s1v = s1l[:, D * par:D * (par + 1)]
        dma(s1v, s1_scr[128 * tm:128 * (tm + 1), :], chan="ch_s1l%d" % par,
            reads=[("s1l", par), ("s1_scr", tm)], writes=[("s1l", par)])
        vop(lambda e, tm=tm: e.activation(out=g1_t[:], in_=acc[:, D * tm:D * (tm + 1)],
                                          func=AF.Silu),
            reads=[("acc", tm)], writes=[("g1_t",)], eng="a")
        if debug:
            dma(dbg["dbg_moe"][128 * tm:128 * (tm + 1), :],
                acc[:, D * tm:D * (tm + 1)], chan="dbgc",
                reads=[("acc", tm)], writes=[])
        vop(lambda e, s1v=s1v: e.tensor_tensor(out=g1_t[:], in0=g1_t[:], in1=s1v,
                                      op=AluOpType.mult),
            reads=[("g1_t",), ("s1l", par)], writes=[("g1_t",)])
        vop(lambda e, xb=xb: e.tensor_tensor(out=y_t[:], in0=g1_t[:], in1=xb,
                                             op=AluOpType.add),
            reads=[("g1_t",), xbk], writes=[("y_t",)], eng="g")
        dma(yout[128 * tm:128 * (tm + 1), :], y_t[:], chan="st_y%d" % par,
            reads=[("y_t",)], writes=[("yout", tm)], eng="g")

    ms.op("sy", lambda e: e.nop(), reads=[("yout", tm) for tm in range(NTM)], writes=[])

    with nc.Block() as block:
        ms.finalize(block)

    return nc, apin



def _host_stage(inputs, T_MAIN, L):
    x = np.asarray(inputs["L0_attn_x"], dtype=np.float32)
    B = x.shape[0]
    idx = np.linspace(0.0, L - 1, LEN).astype(np.int32)
    conv_i = np.ascontiguousarray(
        np.asarray(inputs["L1_attn_conv_i"], np.float32).reshape(KC, 128, 4)
        .transpose(1, 0, 2).reshape(128, KC * 4))
    conv_f = np.ascontiguousarray(
        np.asarray(inputs["L1_attn_conv_f"], np.float32).reshape(KC, 128, 4)
        .transpose(1, 0, 2).reshape(128, KC * 4))
    gnw = np.ascontiguousarray(
        np.asarray(inputs["L1_attn_g_norm_weight"], np.float32).reshape(KC, 128).T)
    shared = {
        "Wq": np.asarray(inputs["L0_attn_Wq"], np.float32),
        "Wk": np.asarray(inputs["L0_attn_Wk"], np.float32),
        "Wo0": np.asarray(inputs["L0_attn_Wo"], np.float32),
        "bq": np.asarray(inputs["L0_attn_bq"], np.float32).reshape(1, D),
        "bk": np.asarray(inputs["L0_attn_bk"], np.float32).reshape(1, D),
        "bo": np.asarray(inputs["L0_attn_bo"], np.float32).reshape(1, D),
        "WeL": np.asarray(inputs["L0_attn_We"], np.float32),
        "WrL": np.asarray(inputs["L0_attn_Wr"], np.float32),
        "Wc": np.asarray(inputs["L0_attn_Wc"], np.float32),
        "bc": np.tile(np.asarray(inputs["L0_attn_bc"], np.float32), 2).reshape(128, 1),
        "wg_nat": np.asarray(inputs["L0_ffn_w_gate"], np.float32),
        "wu_nat": np.asarray(inputs["L0_ffn_w_up"], np.float32),
        "wd_nat": np.asarray(inputs["L0_ffn_w_down"], np.float32),
        "Wi": np.asarray(inputs["L1_attn_Wi"], np.float32),
        "Wf": np.asarray(inputs["L1_attn_Wf"], np.float32),
        "Wgr": np.asarray(inputs["L1_attn_Wg"], np.float32),
        "Wo1": np.asarray(inputs["L1_attn_Wo"], np.float32),
        "conv_i": conv_i, "conv_f": conv_f, "gnw": gnw,
        "WeE": np.asarray(inputs["L1_ffn_We"], np.float32).reshape(NEXP * D, D),
        "beE": np.asarray(inputs["L1_ffn_be"], np.float32),
        "WgE": np.asarray(inputs["L1_ffn_Wg"], np.float32),
        "bgE": np.asarray(inputs["L1_ffn_bg"], np.float32).reshape(1, NEXP),
    }
    in_maps = []
    per_b = L // T_MAIN
    for c in range(B * per_b):
        b, pos = c // per_b, c % per_b
        t0 = pos * T_MAIN
        if pos == 0:
            xin = np.concatenate([x[b, 0:WARM], x[b, t0:t0 + T_MAIN]], axis=0)
            mw = np.zeros((128, 1), np.float32)
        else:
            xin = x[b, t0 - WARM:t0 + T_MAIN]
            mw = np.ones((128, 1), np.float32)
        m = dict(shared)
        m["xin"] = np.ascontiguousarray(xin)
        m["xg"] = np.ascontiguousarray(x[b, idx, :])
        m["mwarm"] = mw
        in_maps.append(m)
    return in_maps, B, per_b


def kernel(**inputs):
    T_MAIN, L = 2048, 8192
    in_maps, B, per_b = _host_stage(inputs, T_MAIN, L)
    nc, apin = build_program(T_MAIN)
    res = run_bass_kernel_spmd(nc, in_maps, core_ids=list(range(B * per_b)))
    shards = [np.asarray(r["y"]) for r in res.results]
    return np.stack([np.concatenate(shards[b * per_b:(b + 1) * per_b], axis=0)
                     for b in range(B)], axis=0)


# version 38
# speedup vs baseline: 1.0887x; 1.0165x over previous
import numpy as np
import ml_dtypes
import concourse.bass as bass
import concourse.mybir as mybir
from concourse.alu_op_type import AluOpType
from concourse.bass_utils import run_bass_kernel_spmd

F32 = mybir.dt.float32
BF16 = mybir.dt.bfloat16
I32 = mybir.dt.int32
AF = mybir.ActivationFunctionType
AX = mybir.AxisListType

D = 1024
KC = D // 128
H = 16
DKH = 64
LOWR = 20
LEN = 200
IFF = 2816
ICH = IFF // 128
NEXP = 8
WARM = 128
BF16NP = ml_dtypes.bfloat16
ENGINES = ("sy", "v", "a", "g", "t")


def _split(total, step):
    out, o = [], 0
    while o < total:
        n = min(step, total - o)
        out.append((o, n))
        o += n
    return out



class Op:
    __slots__ = ("eng", "fn", "idx", "pos", "is_dma", "chan", "chan_val",
                 "need_inc", "inc_seq", "waits", "drain")

    def __init__(self, eng, fn, is_dma, chan):
        self.eng = eng
        self.fn = fn
        self.is_dma = is_dma
        self.chan = chan
        self.chan_val = 0
        self.need_inc = False
        self.inc_seq = 0
        self.waits = []
        self.drain = False


class MS:
    def __init__(self, nc):
        self.nc = nc
        self.ops = []
        self.by_eng = {e: [] for e in ENGINES}
        self.res = {}
        self.chan_tot = {}
        self.sems = {}
        self.barrier_fns = {}
        self.war_keys = set()

    def _edge(self, p, c, raw=True):
        if p is c:
            return
        if p.is_dma:
            if c.is_dma and c.chan == p.chan:
                return
            c.waits.append(("ch", p.chan, p.chan_val))
        elif p.eng == c.eng:
            if (raw and p.eng in ("v", "a", "g")) or (not raw and p.eng == "a"):
                p.need_inc = True
                c.waits.append(("op", p))
        else:
            p.need_inc = True
            c.waits.append(("op", p))

    def op(self, eng, fn, reads=(), writes=(), dma_chan=None):
        o = Op(eng, fn, dma_chan is not None, dma_chan)
        if o.is_dma:
            self.chan_tot[o.chan] = self.chan_tot.get(o.chan, 0) + 16
            o.chan_val = self.chan_tot[o.chan]
        for r in reads:
            s = self.res.get(r)
            if s and s["w"] is not None:
                self._edge(s["w"], o)
        for r in writes:
            s = self.res.get(r)
            if s:
                if s["w"] is not None:
                    self._edge(s["w"], o, raw=False)
                for rd in s["r"]:
                    self._edge(rd, o, raw=(r in self.war_keys))

            self.res[r] = {"w": o, "r": []}
        for r in reads:
            self.res.setdefault(r, {"w": None, "r": []})["r"].append(o)
        o.idx = len(self.ops)
        self.ops.append(o)
        o.pos = len(self.by_eng[o.eng])
        self.by_eng[o.eng].append(o)
        return o

    def phase(self):
        lasts = []
        for e in ENGINES:
            for o in reversed(self.by_eng[e]):
                if not o.is_dma:
                    o.need_inc = True
                    lasts.append(o)
                    break
        chans = dict(self.chan_tot)
        for e in ENGINES:
            bf = self.barrier_fns.get(e)
            o = Op(e, bf() if bf else (lambda en: en.nop()), False, None)
            for p in lasts:
                if p.eng != e:
                    o.waits.append(("op", p))
            for ch, v in chans.items():
                o.waits.append(("ch", ch, v))
            o.idx = len(self.ops)
            self.ops.append(o)
            o.pos = len(self.by_eng[e])
            self.by_eng[e].append(o)
        self.res = {}

    def finalize(self, block):
        inc_cnt = {e: 0 for e in ENGINES}
        for o in self.ops:
            if o.need_inc and not o.is_dma:
                inc_cnt[o.eng] += 1
                o.inc_seq = inc_cnt[o.eng]
        for e in ENGINES:
            if inc_cnt[e]:
                self.sems["eng:" + e] = self.nc.semaphore("sm_" + e).__enter__()
        for ch in self.chan_tot:
            self.sems["ch:" + ch] = self.nc.semaphore("sc_" + ch).__enter__()
        ms = self

        def mk(eng_name):
            def inner(e):
                waited = {}
                for o in ms.by_eng[eng_name]:
                    wl = {}
                    for w in o.waits:
                        if w[0] == "ch":
                            key, val = "ch:" + w[1], w[2]
                        else:
                            key, val = "eng:" + w[1].eng, w[1].inc_seq
                        wl[key] = max(wl.get(key, 0), val)
                    for key, val in sorted(wl.items()):
                        if waited.get(key, -1) < val:
                            e.wait_ge(ms.sems[key], val)
                            waited[key] = val
                    if o.drain:
                        e.drain()
                    inst = o.fn(e)
                    if o.is_dma:
                        inst.then_inc(ms.sems["ch:" + o.chan], 16)
                    elif o.need_inc:
                        if inst is None:
                            e.sem_inc(ms.sems["eng:" + o.eng], 1)
                        else:
                            inst.then_inc(ms.sems["eng:" + o.eng], 1)
            return inner

        for name, deco in [("sy", block.sync), ("v", block.vector),
                           ("a", block.scalar), ("g", block.gpsimd),
                           ("t", block.tensor)]:
            if self.by_eng[name]:
                deco(mk(name))


class P:
    pass



def build_program(T_MAIN, debug=False):
    TP = WARM + T_MAIN
    NT = TP // 128
    NTM = T_MAIN // 128
    GRA = _split(TP, 512)
    GRP = _split(TP, 256)
    GRPC = _split(TP, 512)
    GRPM = _split(T_MAIN, 512)

    nc = bass.Bass("TRN2", target_bir_lowering=False)
    ms = MS(nc)

    apin = {}

    def din(name, shape, dtype=F32):
        apin[name] = nc.dram_tensor(name, list(shape), dtype, kind="ExternalInput").ap()

    din("xin", (TP, D)); din("xg", (LEN, D)); din("mwarm", (128, 1))
    din("Wq", (D, D)); din("Wk", (D, D)); din("Wo0", (D, D))
    din("bq", (1, D)); din("bk", (1, D)); din("bo", (1, D))
    din("WeL", (H, LEN, LOWR)); din("WrL", (H, LEN, LOWR))
    din("Wc", (2 * LOWR, DKH)); din("bc", (128, 1))
    din("wg_nat", (IFF, D)); din("wu_nat", (IFF, D)); din("wd_nat", (D, IFF))
    din("Wi", (D, D)); din("Wf", (D, D)); din("Wgr", (D, D)); din("Wo1", (D, D))
    din("conv_i", (128, KC * 4)); din("conv_f", (128, KC * 4))
    din("gnw", (128, KC))
    din("WeE", (NEXP * D, D)); din("beE", (NEXP, D))
    din("WgE", (D, NEXP)); din("bgE", (1, NEXP))

    yout = nc.dram_tensor("y", [T_MAIN, D], F32, kind="ExternalOutput").ap()
    s_scr = nc.dram_tensor("s_scr", [TP, D], F32, kind="Internal").ap()
    x1_scr = nc.dram_tensor("x1_scr", [TP, D], F32, kind="Internal").ap()
    s1_scr = nc.dram_tensor("s1_scr", [T_MAIN, D], F32, kind="Internal").ap()
    wgu_scr = nc.dram_tensor("wgu_scr", [2 * ICH * 128, KC * 128], BF16, kind="Internal").ap()
    wd_scr = nc.dram_tensor("wd_scr", [KC * 128, ICH * 128], BF16, kind="Internal").ap()
    dbg = {}
    if debug:
        for nm, shp in [("dbg_s", (TP, D)), ("dbg_x1", (TP, D)),
                        ("dbg_h", (128, TP)), ("dbg_moe", (T_MAIN, D)),
                        ("dbg_alpha", (T_MAIN, NEXP))]:
            dbg[nm] = nc.dram_tensor(nm, list(shp), F32, kind="ExternalOutput").ap()

    groups = {}

    def alloc(group, name, shape, dtype=F32):
        g = nc.sbuf_tensor(name, list(shape), dtype)
        hh = g.__enter__()
        groups.setdefault(group, []).append(g)
        return hh

    def free_group(group):
        for g in reversed(groups.pop(group, [])):
            g.__exit__(None, None, None)

    pp = []
    for i in range(8):
        g = nc.psum_tensor("pp%d" % i, [128, 512], F32)
        pp.append(g.__enter__())
    PPK = [("pp", i) for i in range(8)]

    ones1 = alloc("főr", "ones1", (1, 128), BF16)
    onesf = alloc("főr", "onesf", (128, 1), F32)
    mwarm_sb = alloc("főr", "mwarm_sb", (128, 1), F32)
    bq_sb = alloc("főr", "bq_sb", (1, D), BF16)
    bk_sb = alloc("főr", "bk_sb", (1, D), BF16)
    bo_sb = alloc("főr", "bo_sb", (1, D), BF16)
    bg_sb = alloc("főr", "bg_sb", (1, NEXP), BF16)
    bc_sb = alloc("főr", "bc_sb", (128, 1), F32)
    ln_ab = alloc("főr", "ln_ab", (128, 2 * (NT + 2)), F32)
    xbuf2 = alloc("főr", "xbuf2", (128, 2 * D), F32)
    ms.war_keys.update([("tmpf",), ("h_sl",)])
    bscr = alloc("főr", "bscr", (1, 8), F32)
    _bar_i = [0]

    def _g_barrier_fn():
        i = _bar_i[0]
        _bar_i[0] += 1
        return lambda en, i=i: en.memset(bscr[0:1, i:i + 1], 0.0)

    ms.barrier_fns["g"] = _g_barrier_fn

    stg_state = P()
    lnv = P()
    lnv.bn6 = alloc("főr", "ln_bn6", (128, 12), F32)
    lnv.mv = alloc("főr", "ln_mv", (128, 2), F32)
    lnv.var = alloc("főr", "ln_var", (128, 1), F32)

    qb = P()
    QW = 1024
    qb.amax = alloc("gB", "q_amax", (128, QW // 16), F32)
    qb.scale = alloc("gB", "q_scale", (128, QW // 16), F32)
    qb.v = alloc("gB", "q_v", (128, QW), F32)
    qb.f1 = alloc("gB", "q_f1", (128, QW), F32)
    qb.g = alloc("gB", "q_g", (128, QW), F32)
    qb.sg = alloc("gB", "q_sg", (128, QW), BF16)

    stg_state.a = alloc("gA", "wstW", (128, D), F32)
    stg_state.b = alloc("gA", "wstW2", (128, D), F32)
    wq_sb = alloc("gA", "wq_sb", (128, KC * D), BF16)
    wk_sb = alloc("gA", "wk_sb", (128, KC * D), BF16)
    wo_sb = alloc("gA", "wo_sb", (128, KC * D), BF16)
    ablk = alloc("gA", "ablk", (128, KC * 128), BF16)
    bblk = alloc("gA", "bblk", (128, KC * 128), BF16)

    def dma(dst_ap, src_ap, chan, reads, writes, transpose=False, eng="sy"):
        if transpose:
            ms.op(eng, lambda e, d=dst_ap, s=src_ap: e.dma_start_transpose(d, s),
                  reads=reads, writes=writes, dma_chan=chan)
        else:
            ms.op(eng, lambda e, d=dst_ap, s=src_ap: e.dma_start(out=d, in_=s),
                  reads=reads, writes=writes, dma_chan=chan)

    def vop(fn, reads, writes, eng="v"):
        ms.op(eng, fn, reads=reads, writes=writes)

    def mm(ps_ap, lhs_ap, rhs_ap, reads, writes, start, stop, tile_position=None):
        if not isinstance(ps_ap, bass.AP):
            ps_ap = ps_ap[:]
        if not isinstance(lhs_ap, bass.AP):
            lhs_ap = lhs_ap[:]
        if not isinstance(rhs_ap, bass.AP):
            rhs_ap = rhs_ap[:]
        ms.op("t", lambda e, o=ps_ap, l=lhs_ap, r=rhs_ap: e.matmul(
            o, l, r, start=start, stop=stop, tile_position=tile_position),
            reads=reads, writes=writes)

    def cast_weight(srcname, dst_sb, dstkey, nchunks=KC, width=D):
        for k in range(nchunks):
            stg, skey, ch = ((stg_state.a, ("wst",), "ch_wst") if k % 2 == 0
                             else (stg_state.b, ("wst2",), "ch_wst2"))
            dma(stg[:, 0:width], apin[srcname][128 * k:128 * (k + 1), :],
                chan=ch, reads=[skey], writes=[skey], eng="a")
            vop(lambda e, k=k, stg=stg: e.tensor_copy(
                out=dst_sb[:, width * k:width * (k + 1)], in_=stg[:, 0:width]),
                reads=[skey], writes=[(dstkey, k)])

    def emit_quant_p1(x_key, x_ap, F, P_, q, vkey, v_ap, sgkey, sg_ap):
        vop(lambda e: e.activation(out=v_ap, in_=x_ap, func=AF.Abs),
            reads=[x_key], writes=[vkey], eng="a")
        vop(lambda e: e.activation(out=sg_ap, in_=x_ap, func=AF.Sign),
            reads=[x_key], writes=[sgkey], eng="a")

    def emit_quant_p2(x_key, x_ap, out_key, out_ap, F, P_, q, vkey, v_ap,
                      sgkey, sg_ap):
        nb = F // 16
        vop(lambda e: e.tensor_reduce(out=q.amax[0:P_, 0:nb],
                                      in_=v_ap.rearrange("p (b w) -> p b w", w=16),
                                      axis=AX.X, op=AluOpType.max),
            reads=[vkey], writes=[("q.amax",)])
        vop(lambda e: e.tensor_scalar(out=q.scale[0:P_, 0:nb], in0=q.amax[0:P_, 0:nb],
                                      scalar1=1.0 / 6.0, scalar2=1e-12,
                                      op0=AluOpType.mult, op1=AluOpType.max),
            reads=[("q.amax",)], writes=[("q.scale",)])
        vop(lambda e: e.reciprocal(out=q.amax[0:P_, 0:nb], in_=q.scale[0:P_, 0:nb]),
            reads=[("q.scale",)], writes=[("q.amax",)])
        rb = q.amax[0:P_, 0:nb].unsqueeze(2).broadcast_to([P_, nb, 16])
        vop(lambda e: e.tensor_tensor(out=v_ap.rearrange("p (b w) -> p b w", w=16),
                                      in0=v_ap.rearrange("p (b w) -> p b w", w=16),
                                      in1=rb, op=AluOpType.mult),
            reads=[vkey, ("q.amax",)], writes=[vkey])
        vop(lambda e: e.tensor_scalar(out=q.g[0:P_, 0:F], in0=v_ap,
                                      scalar1=5.0, scalar2=2.0,
                                      op0=AluOpType.is_gt, op1=AluOpType.mult),
            reads=[vkey], writes=[("q.g",)])
        for thr in (2.5, 3.5):
            vop(lambda e, thr=thr: e.scalar_tensor_tensor(
                out=q.g[0:P_, 0:F], in0=v_ap, scalar=thr,
                in1=q.g[0:P_, 0:F], op0=AluOpType.is_gt, op1=AluOpType.add),
                reads=[vkey, ("q.g",)], writes=[("q.g",)])
        vop(lambda e: e.tensor_scalar(out=q.f1[0:P_, 0:F], in0=v_ap,
                                      scalar1=0.25, scalar2=None, op0=AluOpType.is_gt),
            reads=[vkey], writes=[("q.f1",)])
        for thr in (0.75, 1.25, 1.75):
            vop(lambda e, thr=thr: e.scalar_tensor_tensor(
                out=q.f1[0:P_, 0:F], in0=v_ap, scalar=thr,
                in1=q.f1[0:P_, 0:F], op0=AluOpType.is_gt, op1=AluOpType.add),
                reads=[vkey, ("q.f1",)], writes=[("q.f1",)])
        vop(lambda e: e.scalar_tensor_tensor(out=q.g[0:P_, 0:F], in0=q.f1[0:P_, 0:F],
                                             scalar=0.5, in1=q.g[0:P_, 0:F],
                                             op0=AluOpType.mult, op1=AluOpType.add),
            reads=[("q.f1",), ("q.g",)], writes=[("q.g",)])
        vop(lambda e: e.tensor_tensor(out=q.g[0:P_, 0:F], in0=q.g[0:P_, 0:F],
                                      in1=sg_ap, op=AluOpType.mult),
            reads=[("q.g",), sgkey], writes=[("q.g",)])
        sb2 = q.scale[0:P_, 0:nb].unsqueeze(2).broadcast_to([P_, nb, 16])
        vop(lambda e: e.tensor_tensor(out=out_ap.rearrange("p (b w) -> p b w", w=16),
                                      in0=q.g[0:P_, 0:F].rearrange("p (b w) -> p b w", w=16),
                                      in1=sb2, op=AluOpType.mult),
            reads=[("q.g",), ("q.scale",)], writes=[out_key])

    def emit_quant(x_key, x_ap, out_key, out_ap, F, P_=128, q=None):
        if q is None:
            q = qb
        vk, sk = ("q.v",), ("q.sg",)
        emit_quant_p1(x_key, x_ap, F, P_, q, vk, q.v[0:P_, 0:F], sk, q.sg[0:P_, 0:F])
        emit_quant_p2(x_key, x_ap, out_key, out_ap, F, P_, q, vk,
                      q.v[0:P_, 0:F], sk, q.sg[0:P_, 0:F])

    def emit_ln(x_key, x_ap, ab_sb, ab_key, slot, outs, P_=128,
                scr=None, scr_key=None):
        a_ap = ab_sb[0:P_, 2 * slot:2 * slot + 1]
        b_ap = ab_sb[0:P_, 2 * slot + 1:2 * slot + 2]
        vop(lambda e: e.bn_stats(out=lnv.bn6[0:P_, 0:6], in_=x_ap[:, 0:D // 2]),
            reads=[x_key], writes=[("ln.bn6",)])
        vop(lambda e: e.bn_stats(out=lnv.bn6[0:P_, 6:12], in_=x_ap[:, D // 2:D]),
            reads=[x_key], writes=[("ln.bn6",)])
        vop(lambda e: e.bn_aggr(out=lnv.mv[0:P_, :], in_=lnv.bn6[0:P_, 0:12]),
            reads=[("ln.bn6",)], writes=[("ln.mv",)])
        vop(lambda e: e.tensor_scalar(out=lnv.var[0:P_, :], in0=lnv.mv[0:P_, 1:2],
                                      scalar1=1e-5, scalar2=None, op0=AluOpType.add),
            reads=[("ln.mv",)], writes=[("ln.var",)])
        vop(lambda e: e.activation(out=lnv.var[0:P_, :], in_=lnv.var[0:P_, :],
                                   func=AF.Sqrt),
            reads=[("ln.var",)], writes=[("ln.var",)], eng="a")
        vop(lambda e: e.reciprocal(out=a_ap, in_=lnv.var[0:P_, :]),
            reads=[("ln.var",)], writes=[(ab_key, slot, "a")])
        vop(lambda e: e.scalar_tensor_tensor(out=b_ap, in0=lnv.mv[0:P_, 0:1],
                                             scalar=-1.0, in1=a_ap,
                                             op0=AluOpType.mult, op1=AluOpType.mult),
            reads=[("ln.mv",), (ab_key, slot, "a")], writes=[(ab_key, slot, "b")])
        for okey, oap in outs:
            vop(lambda e, oap=oap: e.tensor_scalar(out=oap, in_=None, in0=x_ap,
                                                   scalar1=a_ap, scalar2=b_ap,
                                                   op0=AluOpType.mult, op1=AluOpType.add)
                if False else
                e.tensor_scalar(out=oap, in0=x_ap, scalar1=a_ap, scalar2=b_ap,
                                op0=AluOpType.mult, op1=AluOpType.add),
                reads=[x_key, (ab_key, slot, "a"), (ab_key, slot, "b")],
                writes=[okey])

    dma(mwarm_sb[:], apin["mwarm"][:], chan="c_mw", reads=[], writes=[("mwarm",)])
    dma(bc_sb[:], apin["bc"][:], chan="c_bc", reads=[], writes=[("bc",)])
    vop(lambda e: e.memset(ones1[:], 1.0), reads=[], writes=[("ones1",)])
    vop(lambda e: e.memset(onesf[:], 1.0), reads=[], writes=[("onesf",)])
    vop(lambda e: e.memset(ablk[:], 0.0), reads=[], writes=[("ablk", "z")])
    vop(lambda e: e.memset(bblk[:], 0.0), reads=[], writes=[("bblk", "z")])

    for nm, dst, w in [("bq", bq_sb, D), ("bk", bk_sb, D), ("bo", bo_sb, D),
                       ("bgE", bg_sb, NEXP)]:
        dma(stg_state.a[0:1, 0:w], apin[nm][:], chan="ch_wst", reads=[("wst",)], writes=[("wst",)])
        vop(lambda e, dst=dst, w=w, stg=stg_state.a: e.tensor_copy(
            out=dst[0:1, 0:w], in_=stg[0:1, 0:w]),
            reads=[("wst",)], writes=[("b_" + nm,)])

    cast_weight("Wq", wq_sb, "wq")
    cast_weight("Wk", wk_sb, "wk")
    cast_weight("Wo0", wo_sb, "wo")

    xg_sb = alloc("gW1", "xg_sb", (128, 2 * D), F32)
    xng_sb = alloc("gW1", "xng_sb", (128, 2 * D), BF16)
    wel_sb = alloc("gW1", "wel_sb", (128, 2 * LOWR), BF16)
    wrl_sb = alloc("gW1", "wrl_sb", (128, 2 * LOWR), BF16)
    wc_sb = alloc("gW1", "wc_sb", (LOWR, 2 * DKH), BF16)
    wet_sb = alloc("gW1", "wet_sb", (LOWR, 2 * DKH), BF16)
    LEN2 = LEN - 128

    dma(xg_sb[0:128, 0:D], apin["xg"][0:128, :], chan="cxg", reads=[],
        writes=[("xg", 0), ("xg", 1)])
    dma(xg_sb[0:LEN2, D:2 * D], apin["xg"][128:LEN, :], chan="cxg", reads=[],
        writes=[("xg", 0), ("xg", 1)])
    dma(stg_state.a[0:LOWR, 0:DKH], apin["Wc"][0:LOWR, :], chan="ch_wst",
        reads=[("wst",)], writes=[("wst",)])
    dma(stg_state.a[0:LOWR, DKH:2 * DKH], apin["Wc"][LOWR:2 * LOWR, :], chan="ch_wst",
        reads=[("wst",)], writes=[("wst",)])
    vop(lambda e, stg=stg_state.a: e.tensor_copy(out=wc_sb[:], in_=stg[0:LOWR, 0:2 * DKH]),
        reads=[("wst",)], writes=[("wc",)])
    emit_ln(("xg", 0), xg_sb[0:128, 0:D], ln_ab, "lng", NT, [(("xng", 0), xng_sb[0:128, 0:D])])
    emit_ln(("xg", 1), xg_sb[0:LEN2, D:2 * D], ln_ab, "lng", NT + 1,
            [(("xng", 1), xng_sb[0:LEN2, D:2 * D])], P_=LEN2)

    ps_we = pp[0][0:LOWR, 0:DKH]
    ps_ab = pp[1]
    for h in range(H):
        j, par = h // 2, h % 2
        dma(stg_state.a[0:128, 0:LOWR], apin["WeL"][h, 0:128, :], chan="ch_wst",
            reads=[("wst",)], writes=[("wst",)])
        dma(stg_state.a[0:LEN2, LOWR:2 * LOWR], apin["WeL"][h, 128:LEN, :], chan="ch_wst",
            reads=[("wst",)], writes=[("wst",)])
        vop(lambda e, stg=stg_state.a: e.tensor_copy(out=wel_sb[:, :], in_=stg[0:128, 0:2 * LOWR]),
            reads=[("wst",)], writes=[("wel",)])
        dma(stg_state.a[0:128, 0:LOWR], apin["WrL"][h, 0:128, :], chan="ch_wst",
            reads=[("wst",)], writes=[("wst",)])
        dma(stg_state.a[0:LEN2, LOWR:2 * LOWR], apin["WrL"][h, 128:LEN, :], chan="ch_wst",
            reads=[("wst",)], writes=[("wst",)])
        vop(lambda e, stg=stg_state.a: e.tensor_copy(out=wrl_sb[:, :], in_=stg[0:128, 0:2 * LOWR]),
            reads=[("wst",)], writes=[("wrl",)])
        for src_sb, srckey, half, blk, blkkey in [
                (wel_sb, ("wel",), 0, ablk, "ablk"), (wrl_sb, ("wrl",), 1, bblk, "bblk")]:
            wcs = wc_sb[0:LOWR, 0:DKH] if half == 0 else wc_sb[0:LOWR, DKH:2 * DKH]
            mm(ps_we, src_sb[0:128, 0:LOWR], xng_sb[0:128, DKH * h:DKH * (h + 1)],
               reads=[srckey, ("xng", 0)], writes=[PPK[0]], start=True, stop=False)
            mm(ps_we, src_sb[0:LEN2, LOWR:2 * LOWR],
               xng_sb[0:LEN2, D + DKH * h:D + DKH * (h + 1)],
               reads=[srckey, ("xng", 1)], writes=[PPK[0]], start=False, stop=True)
            wt = wet_sb[:, DKH * half:DKH * (half + 1)]
            vop(lambda e, wt=wt: e.activation(out=wt, in_=ps_we, func=AF.Copy),
                reads=[PPK[0]], writes=[("wet", half)], eng="a")
            mm(ps_ab[64 * par:64 * (par + 1), 0:DKH], wt, wcs,
               reads=[("wet", half), ("wc",)], writes=[PPK[1]],
               start=True, stop=True, tile_position=(0, 64 * par))
            dst = blk[64 * par:64 * (par + 1), 128 * j + 64 * par:128 * j + 64 * (par + 1)]
            vop(lambda e, dst=dst, par=par: e.activation(
                out=dst, in_=ps_ab[64 * par:64 * (par + 1), 0:DKH], func=AF.Copy),
                reads=[PPK[1], (blkkey, "z")], writes=[(blkkey, j, par)], eng="a")

    ms.phase()
    free_group("gW1")

    qa = P()
    QA = 2048
    qa.amax = alloc("gAq", "qa_amax", (128, QA // 16), F32)
    qa.scale = alloc("gAq", "qa_scale", (128, QA // 16), F32)
    qa.v2 = [alloc("gAq", "qa_v%d" % i, (128, QA), F32) for i in range(2)]
    qa.f1 = alloc("gAq", "qa_f1", (128, QA), F32)
    qa.g = alloc("gAq", "qa_g", (128, QA), F32)
    qa.sg2 = [alloc("gAq", "qa_sg%d" % i, (128, QA), BF16) for i in range(2)]
    wstA = [alloc("gAq", "wstA%d" % i, (128, QA), F32) for i in range(2)]
    wqqA = alloc("gAq", "wqqA", (128, QA), BF16)
    recst = [alloc("gAq", "recst%d" % i, (128, 1408), BF16) for i in range(2)]
    wq_jobs = []

    def job_width_src(job):
        if job[0] == "wgu":
            return D * len(job[3])
        return 1408

    def job_stage1(job, par):
        stg = wstA[par]
        skey = ("wstA", par)
        vk, sk = ("qa.v", par), ("qa.sg", par)
        if job[0] == "wgu":
            _, srcname, base, sls = job
            for i, sl in enumerate(sls):
                dma(stg[:, D * i:D * (i + 1)], apin[srcname][128 * sl:128 * (sl + 1), :],
                    chan="ch_wstA%d" % par, reads=[skey], writes=[skey], eng="a")
            F = D * len(sls)
        else:
            _, sl, half = job
            so, F = half * 1408, 1408
            dma(stg[:, 0:F], apin["wd_nat"][128 * sl:128 * (sl + 1), so:so + F],
                chan="ch_wstA%d" % par, reads=[skey], writes=[skey], eng="a")
        emit_quant_p1(skey, stg[:, 0:F], F, 128, qa, vk, qa.v2[par][:, 0:F],
                      sk, qa.sg2[par][:, 0:F])

    def job_stage2(job, par):
        F = job_width_src(job)
        vk, sk = ("qa.v", par), ("qa.sg", par)
        emit_quant_p2(("wstA", par), wstA[par][:, 0:F], ("wqqA",), wqqA[:, 0:F],
                      F, 128, qa, vk, qa.v2[par][:, 0:F], sk, qa.sg2[par][:, 0:F])
        rp = par
        rkey = ("recst", rp)
        if job[0] == "wgu":
            _, srcname, base, sls = job
            for i, sl in enumerate(sls):
                dma(recst[rp][:, 0:D].rearrange("p (j c) -> p j c", c=128),
                    wqqA[:, D * i:D * (i + 1)].rearrange("p (j c) -> p j c", c=128),
                    chan="tx_rec%d" % rp, reads=[("wqqA",), rkey], writes=[rkey],
                    transpose=True)
                dma(wgu_scr[128 * (base + sl):128 * (base + sl + 1), :],
                    recst[rp][:, 0:D],
                    chan="st_rec%d" % rp, reads=[rkey],
                    writes=[("wgu_scr", base + sl)], eng="a")
        else:
            _, sl, half = job
            so = half * 1408
            dma(recst[rp][:, 0:F].rearrange("p (j c) -> p j c", c=128),
                wqqA[:, 0:F].rearrange("p (j c) -> p j c", c=128),
                chan="tx_rec%d" % rp, reads=[("wqqA",), rkey], writes=[rkey],
                transpose=True)
            dma(wd_scr[128 * sl:128 * (sl + 1), so:so + F], recst[rp][:, 0:F],
                chan="st_rec%d" % rp, reads=[rkey], writes=[("wd_scr", sl, half)],
                eng="a")

    _g3 = [list(range(i, min(i + 2, ICH))) for i in range(0, ICH, 2)]
    for gi3, sls in enumerate(_g3):
        wq_jobs.append(("wgu", "wg_nat", 0, sls))
    for gi3, sls in enumerate(_g3):
        wq_jobs.append(("wgu", "wu_nat", ICH, sls))
    for sl in range(KC):
        for half in range(2):
            wq_jobs.append(("wd", sl, half))
    wq_jobs.reverse()
    _jobstate = {"n": 0, "pending": []}

    def quant_weight_slice():
        if wq_jobs:
            job = wq_jobs.pop()
            par = _jobstate["n"] & 1
            _jobstate["n"] += 1
            job_stage1(job, par)
            _jobstate["pending"].append((job, par))
        if len(_jobstate["pending"]) > 1 or (not wq_jobs and _jobstate["pending"]):
            job, par = _jobstate["pending"].pop(0)
            job_stage2(job, par)

    def drain_jobs():
        while wq_jobs or _jobstate["pending"]:
            quant_weight_slice()

    GA = 512
    xnT = alloc("gA2", "xnT", (128, KC * GA), BF16)
    qnT = alloc("gA2", "qnT", (128, KC * GA), BF16)
    knT = alloc("gA2", "knT", (128, KC * GA), BF16)
    mT = alloc("gA2", "mT", (128, KC * GA), BF16)
    qss = alloc("gA2", "qss", (128, H), F32)
    qrv = alloc("gA2", "qrv", (128, H), F32)
    qn_bf = alloc("gA2", "qn_bf", (128, D), BF16)
    kn_bf = alloc("gA2", "kn_bf", (128, D), BF16)
    xnbf = alloc("gA2", "xnbf", (128, D), BF16)
    qf = alloc("gA2", "qf", (128, D), F32)
    kf = qf
    s_t = qf

    def l2norm(src, srckey, dst, dstkey):
        vop(lambda e: e.tensor_tensor(out=qb.v[:, 0:D], in0=src[:], in1=src[:],
                                      op=AluOpType.mult),
            reads=[srckey], writes=[("q.v",)])
        vop(lambda e: e.tensor_reduce(out=qss[:], in_=qb.v[:, 0:D].rearrange(
            "p (h d) -> p h d", d=DKH), axis=AX.X, op=AluOpType.add),
            reads=[("q.v",)], writes=[("qss",)])
        vop(lambda e: e.activation(out=qrv[:], in_=qss[:], func=AF.Sqrt),
            reads=[("qss",)], writes=[("qrv",)], eng="a")
        vop(lambda e: e.tensor_scalar(out=qrv[:], in0=qrv[:], scalar1=1e-12,
                                      scalar2=None, op0=AluOpType.max),
            reads=[("qrv",)], writes=[("qrv",)])
        vop(lambda e: e.reciprocal(out=qrv[:], in_=qrv[:]),
            reads=[("qrv",)], writes=[("qrv",)])
        rb = qrv[:, :].unsqueeze(2).broadcast_to([128, H, DKH])
        vop(lambda e: e.tensor_tensor(out=dst.rearrange("p (h d) -> p h d", d=DKH),
                                      in0=src.rearrange("p (h d) -> p h d", d=DKH),
                                      in1=rb, op=AluOpType.mult),
            reads=[srckey, ("qrv",)], writes=[dstkey])

    def a_front(go, ti):
        for _ in range(2):
            if wq_jobs:
                quant_weight_slice()
        t0 = go + ti * 128
        tidx = t0 // 128
        par = tidx % 2
        xb = xbuf2[:, D * par:D * (par + 1)]
        xbk = ("xbuf", par)
        dma(xb, apin["xin"][t0:t0 + 128, :], chan="ch_x%d" % par,
            reads=[xbk], writes=[xbk])
        emit_ln(xbk, xb, ln_ab, "ln0", tidx, [(("xnbf",), xnbf[:])])
        wall = [("xnT", j2, ti) for j2 in range(KC)]
        dma(xnT[:, :].rearrange("p (j c) -> p j c", c=GA)[:, :, 128 * ti:128 * (ti + 1)],
            xnbf[:, :].rearrange("p (j c) -> p j c", c=128),
            chan="tx_xnT%d" % ti, reads=[("xnbf",)], writes=wall, transpose=True)

    def a_back1(go, ti):
        t0 = go + ti * 128
        for ww, wwkey, ppi, bap, bkey in [
                (wq_sb, "wq", (0, 1), bq_sb, ("b_bq",)),
                (wk_sb, "wk", (2, 3), bk_sb, ("b_bk",))]:
            for n in range(2):
                pidx = ppi[n]
                mm(pp[pidx], ones1[:], bap[0:1, 512 * n:512 * (n + 1)],
                   reads=[("ones1",), bkey], writes=[PPK[pidx]], start=True, stop=False)
                for k in range(KC):
                    mm(pp[pidx], xnT[:, GA * k + 128 * ti:GA * k + 128 * (ti + 1)],
                       ww[:, D * k + 512 * n:D * k + 512 * (n + 1)],
                       reads=[("xnT", k, ti), (wwkey, k)], writes=[PPK[pidx]],
                       start=False, stop=(k == KC - 1))
        for n in range(2):
            vop(lambda e, n=n: e.activation(out=qf[:, 512 * n:512 * (n + 1)],
                                            in_=pp[n][:], func=AF.Copy),
                reads=[PPK[n], ("qf",)], writes=[("qf",)], eng="a")
        l2norm(qf, ("qf",), qn_bf[:, :], ("qn",))
        for n in range(2):
            vop(lambda e, n=n: e.activation(out=kf[:, 512 * n:512 * (n + 1)],
                                            in_=pp[2 + n][:], func=AF.Copy),
                reads=[PPK[2 + n], ("qf",)], writes=[("qf",)], eng="a")
        l2norm(kf, ("qf",), kn_bf[:, :], ("kn",))
        wq_all = [("qnT", j2, ti) for j2 in range(KC)]
        wk_all = [("knT", j2, ti) for j2 in range(KC)]
        dma(qnT[:, :].rearrange("p (j c) -> p j c", c=GA)[:, :, 128 * ti:128 * (ti + 1)],
            qn_bf[:, :].rearrange("p (j c) -> p j c", c=128),
            chan="tx_qnT%d" % ti, reads=[("qn",)], writes=wq_all, transpose=True)
        dma(knT[:, :].rearrange("p (j c) -> p j c", c=GA)[:, :, 128 * ti:128 * (ti + 1)],
            kn_bf[:, :].rearrange("p (j c) -> p j c", c=128),
            chan="tx_knT%d" % ti, reads=[("kn",)], writes=wk_all, transpose=True)

    def a_tail(go, gn):
        ntile = gn // 128
        for jj in range(KC):
            rq = [("qnT", jj, ti) for ti in range(ntile)]
            rk = [("knT", jj, ti) for ti in range(ntile)]
            mm(pp[4][:, 0:gn], ablk[:, 128 * jj:128 * (jj + 1)], qnT[:, GA * jj:GA * jj + gn],
               reads=rq + [("ablk", jj, 0), ("ablk", jj, 1)], writes=[PPK[4]],
               start=True, stop=False)
            mm(pp[4][:, 0:gn], bblk[:, 128 * jj:128 * (jj + 1)], knT[:, GA * jj:GA * jj + gn],
               reads=rk + [("bblk", jj, 0), ("bblk", jj, 1)], writes=[PPK[4]],
               start=False, stop=True)
            vop(lambda e, jj=jj, gn=gn: e.tensor_scalar(
                out=mT[:, GA * jj:GA * jj + gn], in0=pp[4][:, 0:gn],
                scalar1=bc_sb[:, 0:1], scalar2=None, op0=AluOpType.add),
                reads=[PPK[4], ("bc",)], writes=[("mT", jj)])
        for ti in range(ntile):
            t0 = go + ti * 128
            for n in range(2):
                pidx = 5 + n
                mm(pp[pidx], ones1[:], bo_sb[0:1, 512 * n:512 * (n + 1)],
                   reads=[("ones1",), ("b_bo",)], writes=[PPK[pidx]], start=True, stop=False)
                for k in range(KC):
                    mm(pp[pidx], mT[:, GA * k + 128 * ti:GA * k + 128 * (ti + 1)],
                       wo_sb[:, D * k + 512 * n:D * k + 512 * (n + 1)],
                       reads=[("mT", k), ("wo", k)], writes=[PPK[pidx]],
                       start=False, stop=(k == KC - 1))
            for n in range(2):
                pidx = 5 + n
                vop(lambda e, n=n, pidx=pidx: e.activation(
                    out=s_t[:, 512 * n:512 * (n + 1)], in_=pp[pidx][:], func=AF.Copy),
                    reads=[PPK[pidx], ("qf",)], writes=[("qf",)], eng="a")
            dma(s_scr[t0:t0 + 128, :], s_t[:], chan="st_s", reads=[("qf",)],
                writes=[("s_scr", t0 // 128)])
            if debug:
                dma(dbg["dbg_s"][t0:t0 + 128, :], s_t[:], chan="dbgc",
                    reads=[("qf",)], writes=[])

    units = []
    for go, gn in GRA:
        nt_ = gn // 128
        for ti in range(nt_):
            units.append((go, gn, ti, ti == nt_ - 1))
    pend = None
    for u in units:
        a_front(u[0], u[2])
        if pend is not None:
            a_back1(pend[0], pend[2])
            if pend[3]:
                a_tail(pend[0], pend[1])
        pend = u
    a_back1(pend[0], pend[2])
    a_tail(pend[0], pend[1])

    drain_jobs()
    ms.phase()
    free_group("gA2")
    free_group("gAq")
    free_group("gA")

    wgT = alloc("gB2", "wgT", (128, KC * IFF), BF16)
    wuT = alloc("gB2", "wuT", (128, KC * IFF), BF16)
    wdT = alloc("gB2", "wdT", (128, ICH * D), BF16)
    for wi_, (dst_sb, dkey, base) in enumerate([(wgT, "wgT", 0), (wuT, "wuT", ICH)]):
        for sl in range(ICH):
            wall = [(dkey, j2) for j2 in range(KC)]
            dma(dst_sb[:, :].rearrange("p (j c) -> p j c", c=IFF)[:, :, 128 * sl:128 * (sl + 1)],
                wgu_scr[128 * (base + sl):128 * (base + sl + 1), :]
                .rearrange("p (j c) -> p j c", c=128),
                chan="ld_" + dkey, reads=[("wgu_scr", base + sl)], writes=wall)
    for sl in range(KC):
        wall = [("wdT", k2) for k2 in range(ICH)]
        dma(wdT[:, :].rearrange("p (j c) -> p j c", c=D)[:, :, 128 * sl:128 * (sl + 1)],
            wd_scr[128 * sl:128 * (sl + 1), :].rearrange("p (j c) -> p j c", c=128),
            chan="ld_wdT", reads=[("wd_scr", sl, 0), ("wd_scr", sl, 1)], writes=wall)
    xqbf = alloc("gB2", "xqbf", (128, D), BF16)
    tmpf = alloc("gB2", "tmpf", (128, D), F32)
    sldb = alloc("gB2", "sldb", (128, D), F32)
    xn32 = tmpf
    sload = sldb
    xqT = alloc("gB2", "xqT", (128, KC * 256), BF16)
    hqT = alloc("gB2", "hqT", (128, ICH * 256), BF16)
    h_sl = alloc("gB2", "h_sl", (128, 1024), F32)
    hq_sl = xqbf
    g_t = alloc("gB2", "g_t", (128, D), F32)
    x1_t = alloc("gB2", "x1_t", (128, D), F32)

    NSL = _split(IFF, 512)
    hq_bf = xqbf
    xb4ext = alloc("gB2", "xb4ext", (128, 2 * D), F32)

    def b_xb(tidx):
        p4 = tidx % 4
        base = xbuf2 if p4 < 2 else xb4ext
        return base[:, D * (p4 % 2):D * (p4 % 2 + 1)], ("xbuf", p4), p4

    def b_xload(tidx):
        t0 = tidx * 128
        xb, xbk, p4 = b_xb(tidx)
        dma(xb, apin["xin"][t0:t0 + 128, :], chan="ch_x%d" % p4,
            reads=[xbk], writes=[xbk])

    def b_front(tidx):
        t0 = tidx * 128
        par = tidx % 2
        xb, xbk, p4 = b_xb(tidx)
        b_xload(tidx)
        a_ap = ln_ab[:, 2 * tidx:2 * tidx + 1]
        b_ap = ln_ab[:, 2 * tidx + 1:2 * tidx + 2]
        vop(lambda e, a_ap=a_ap, b_ap=b_ap, xb=xb: e.tensor_scalar(
            out=xn32[:], in0=xb, scalar1=a_ap, scalar2=b_ap,
            op0=AluOpType.mult, op1=AluOpType.add),
            reads=[xbk], writes=[("tmpf",)])
        emit_quant(("tmpf",), xn32[:], ("xq",), xqbf[:], D)
        wall = [("xqT", j2, par) for j2 in range(KC)]
        dma(xqT[:, :].rearrange("p (j c) -> p j c", c=256)[:, :, 128 * par:128 * (par + 1)],
            xqbf[:, :].rearrange("p (j c) -> p j c", c=128),
            chan="tx_xqT%d" % par, reads=[("xq",)], writes=wall, transpose=True)

    def b_back1(tidx):
        t0 = tidx * 128
        par = tidx % 2
        for sb_i in range(3):
            sls = NSL[2 * sb_i:2 * sb_i + 2]
            sb0 = sls[0][0]
            sbw = sum(sn for _, sn in sls)
            for k in range(KC):
                lhs = xqT[:, 256 * k + 128 * par:256 * k + 128 * (par + 1)]
                for si, (so, sn) in enumerate(sls):
                    mm(pp[si][:, 0:sn], lhs, wgT[:, IFF * k + so:IFF * k + so + sn],
                       reads=[("xqT", k, par), ("wgT", k)], writes=[PPK[si]],
                       start=(k == 0), stop=(k == KC - 1))
                    mm(pp[2 + si][:, 0:sn], lhs, wuT[:, IFF * k + so:IFF * k + so + sn],
                       reads=[("xqT", k, par), ("wuT", k)], writes=[PPK[2 + si]],
                       start=(k == 0), stop=(k == KC - 1))
            for si, (so, sn) in enumerate(sls):
                vop(lambda e, si=si, so=so, sn=sn, sb0=sb0: e.activation(
                    out=h_sl[:, so - sb0:so - sb0 + sn], in_=pp[si][:, 0:sn],
                    func=AF.Silu),
                    reads=[PPK[si]], writes=[("h_sl",)], eng="a")
                vop(lambda e, si=si, so=so, sn=sn, sb0=sb0: e.tensor_tensor(
                    out=h_sl[:, so - sb0:so - sb0 + sn],
                    in0=h_sl[:, so - sb0:so - sb0 + sn],
                    in1=pp[2 + si][:, 0:sn], op=AluOpType.mult),
                    reads=[("h_sl",), PPK[2 + si]], writes=[("h_sl",)])
            emit_quant(("h_sl",), h_sl[:, 0:sbw], ("xq",), hq_bf[:, 0:sbw], sbw)
            wall = [("hqT", k2, par) for k2 in range(ICH)]
            nblk = sbw // 128
            dma(hqT[:, :].rearrange("p (j c) -> p j c", c=256)[
                :, sb0 // 128:sb0 // 128 + nblk, 128 * par:128 * (par + 1)],
                hq_bf[:, 0:sbw].rearrange("p (j c) -> p j c", c=128),
                chan="tx_hqT%d" % par, reads=[("xq",)], writes=wall, transpose=True)

    def b_back2(tidx):
        t0 = tidx * 128
        par = tidx % 2
        xb, xbk, p4 = b_xb(tidx)
        for k in range(ICH):
            for n in range(2):
                pidx = 6 + n
                mm(pp[pidx], hqT[:, 256 * k + 128 * par:256 * k + 128 * (par + 1)],
                   wdT[:, D * k + 512 * n:D * k + 512 * (n + 1)],
                   reads=[("hqT", k, par), ("wdT", k)], writes=[PPK[pidx]],
                   start=(k == 0), stop=(k == ICH - 1))
        gk = ("g_t",)
        for n in range(2):
            pidx = 6 + n
            vop(lambda e, n=n, pidx=pidx: e.activation(
                out=g_t[:, 512 * n:512 * (n + 1)], in_=pp[pidx][:], func=AF.Silu),
                reads=[PPK[pidx]], writes=[gk], eng="a")
        dma(sload[:], s_scr[t0:t0 + 128, :], chan="ch_sload%d" % (tidx % 2),
            reads=[("s_scr", tidx), ("sldb",)], writes=[("sldb",)], eng="g")
        vop(lambda e: e.tensor_tensor(out=x1_t[:, 0:D], in0=sload[:],
                                      in1=g_t[:], op=AluOpType.mult),
            reads=[("sldb",), gk], writes=[("x1_t",)], eng="g")
        vop(lambda e, xb=xb: e.tensor_tensor(out=x1_t[:, 0:D], in0=x1_t[:, 0:D],
                                             in1=xb, op=AluOpType.add),
            reads=[("x1_t",), xbk], writes=[("x1_t",)], eng="g")
        dma(x1_scr[t0:t0 + 128, :], x1_t[:, 0:D], chan="st_x1_%d" % (tidx % 2),
            reads=[("x1_t",)], writes=[("x1_scr", tidx)], eng="g")
        if debug:
            dma(dbg["dbg_x1"][t0:t0 + 128, :], x1_t[:, 0:D], chan="dbgc",
                reads=[("x1_t",)], writes=[])

    NT_B = TP // 128
    for tidx in range(NT_B):
        b_front(tidx)
        if tidx >= 1:
            b_back1(tidx - 1)
        if tidx >= 2:
            b_back2(tidx - 2)
    b_back1(NT_B - 1)
    b_back2(NT_B - 2)
    b_back2(NT_B - 1)

    ms.phase()
    free_group("gB2")
    free_group("gB")

    xn1T = alloc("gD", "xn1T", (128, KC * TP), BF16)
    xnbf = alloc("gD", "xnbf_c", (128, D), BF16)
    stg_state.a = alloc("gD", "wstC", (128, D), F32)
    stg_state.b = alloc("gD", "wstC2", (128, D), F32)
    z_sb = alloc("gD", "z_sb", (128, NEXP * NTM), F32)
    alp_sb = alloc("gD", "alp_sb", (128, NEXP * NTM), F32)
    zm1 = alloc("gD", "zm1", (128, 1), F32)
    zm2 = alloc("gD", "zm2", (128, 1), F32)
    znm = alloc("gD", "znm", (128, 1), F32)
    zscr = alloc("gD", "zscr", (128, NEXP), F32)

    hbuf = alloc("gC2", "hbuf", (128, KC * TP), BF16)
    slg = alloc("gC2", "slg", (128, KC * T_MAIN), BF16)
    sqacc = alloc("gC2", "sqacc", (128, T_MAIN), F32)
    wo1_sb = alloc("gC2", "wo1_sb", (128, KC * D), BF16)
    gnw_sb = alloc("gC2", "gnw_sb", (128, KC), F32)

    wim = [alloc("gC1", "wim%d" % i, (128, 3 * D), BF16) for i in range(2)]
    wg1_sb = alloc("gC1", "wg1_sb", (128, KC * NEXP), BF16)
    conv_i_sb = alloc("gC1", "conv_i_sb", (128, KC * 4), F32)
    conv_f_sb = alloc("gC1", "conv_f_sb", (128, KC * 4), F32)
    iraw = alloc("gC1", "iraw", (128, TP), F32)
    fraw = alloc("gC1", "fraw", (128, TP), F32)
    ci_b = alloc("gC1", "ci_b", (128, TP), F32)
    cf_b = alloc("gC1", "cf_b", (128, TP), F32)

    cast_weight("Wo1", wo1_sb, "wo1")
    cast_weight("WgE", wg1_sb, "wg1", width=NEXP)
    dma(conv_i_sb[:], apin["conv_i"][:], chan="c_cvi", reads=[], writes=[("convi",)])
    dma(conv_f_sb[:], apin["conv_f"][:], chan="c_cvf", reads=[], writes=[("convf",)])
    dma(gnw_sb[:], apin["gnw"][:], chan="c_gnw", reads=[], writes=[("gnw",)])

    for go, gn in GRPC:
        for ti in range(gn // 128):
            t0 = go + ti * 128
            tidx = t0 // 128
            par = tidx % 2
            xb = xbuf2[:, D * par:D * (par + 1)]
            xbk = ("xbuf", par)
            dma(xb, x1_scr[t0:t0 + 128, :], chan="ch_x%d" % par,
                reads=[xbk, ("x1_scr", tidx)], writes=[xbk])
            emit_ln(xbk, xb, ln_ab, "ln1", tidx, [(("xnbf",), xnbf[:])])
            wall = [("xn1T", j2, tidx) for j2 in range(KC)]
            dma(xn1T[:, :].rearrange("p (j c) -> p j c", c=TP)[:, :, t0:t0 + 128],
                xnbf[:, :].rearrange("p (j c) -> p j c", c=128),
                chan="tx_xn1T", reads=[("xnbf",)], writes=wall, transpose=True)

    for tm in range(NTM):
        t0 = WARM + tm * 128
        zps = pp[3][:, 0:NEXP]
        mm(zps, ones1[:], bg_sb[0:1, 0:NEXP], reads=[("ones1",), ("b_bgE",)],
           writes=[PPK[3]], start=True, stop=False)
        for k in range(KC):
            mm(zps, xn1T[:, TP * k + t0:TP * k + t0 + 128],
               wg1_sb[:, NEXP * k:NEXP * (k + 1)],
               reads=[("xn1T", k, t0 // 128), ("wg1", k)], writes=[PPK[3]],
               start=False, stop=(k == KC - 1))
        zs = z_sb[:, NEXP * tm:NEXP * (tm + 1)]
        als = alp_sb[:, NEXP * tm:NEXP * (tm + 1)]
        vop(lambda e, zs=zs: e.activation(out=zs, in_=pp[3][:, 0:NEXP], func=AF.Copy),
            reads=[PPK[3]], writes=[("z", tm)], eng="a")
        vop(lambda e, zs=zs: e.tensor_reduce(out=zm1[:], in_=zs, axis=AX.X,
                                             op=AluOpType.max),
            reads=[("z", tm)], writes=[("zm1",)])
        vop(lambda e, zs=zs: e.tensor_scalar(out=zscr[:], in0=zs, scalar1=zm1[:, 0:1],
                                             scalar2=-1e30, op0=AluOpType.is_ge,
                                             op1=AluOpType.mult),
            reads=[("z", tm), ("zm1",)], writes=[("zscr",)])
        vop(lambda e, zs=zs: e.tensor_tensor(out=zscr[:], in0=zscr[:], in1=zs,
                                             op=AluOpType.add),
            reads=[("zscr",), ("z", tm)], writes=[("zscr",)])
        vop(lambda e: e.tensor_reduce(out=zm2[:], in_=zscr[:], axis=AX.X,
                                      op=AluOpType.max),
            reads=[("zscr",)], writes=[("zm2",)])
        vop(lambda e, zs=zs, als=als: e.tensor_scalar(out=als, in0=zs,
                                                      scalar1=zm2[:, 0:1], scalar2=None,
                                                      op0=AluOpType.is_ge),
            reads=[("z", tm), ("zm2",)], writes=[("alp", tm)])
        vop(lambda e: e.tensor_scalar(out=znm[:], in0=zm1[:], scalar1=-1.0,
                                      scalar2=None, op0=AluOpType.mult),
            reads=[("zm1",)], writes=[("znm",)])
        vop(lambda e, zs=zs: e.activation(out=zscr[:], in_=zs, func=AF.Exp,
                                          bias=znm[:, 0:1], scale=1.0),
            reads=[("z", tm), ("znm",)], writes=[("zscr",)], eng="a")
        vop(lambda e: e.tensor_tensor(out=zm2[:], in0=zm2[:], in1=zm1[:],
                                      op=AluOpType.subtract),
            reads=[("zm2",), ("zm1",)], writes=[("zm2",)])
        vop(lambda e: e.activation(out=zm2[:], in_=zm2[:], func=AF.Exp),
            reads=[("zm2",)], writes=[("zm2",)], eng="a")
        vop(lambda e: e.tensor_scalar(out=zm2[:], in0=zm2[:], scalar1=1.0,
                                      scalar2=None, op0=AluOpType.add),
            reads=[("zm2",)], writes=[("zm2",)])
        vop(lambda e: e.reciprocal(out=zm2[:], in_=zm2[:]),
            reads=[("zm2",)], writes=[("zm2",)])
        vop(lambda e, als=als: e.tensor_tensor(out=als, in0=als, in1=zscr[:],
                                               op=AluOpType.mult),
            reads=[("alp", tm), ("zscr",)], writes=[("alp", tm)])
        vop(lambda e, als=als: e.tensor_scalar_mul(out=als, in0=als,
                                                   scalar1=zm2[:, 0:1]),
            reads=[("alp", tm), ("zm2",)], writes=[("alpha", tm)])
        if debug:
            vop(lambda e, als=als: e.tensor_copy(out=zscr[:], in_=als),
                reads=[("alpha", tm)], writes=[("zscr",)])
            dma(dbg["dbg_alpha"][128 * tm:128 * (tm + 1), :], zscr[:],
                chan="dbgc", reads=[("zscr",)], writes=[])

    vop(lambda e: e.memset(sqacc[:], 0.0), reads=[], writes=[("sqacc",)])
    for m in range(KC):
        par = m % 2
        for wi_, wname in enumerate(["Wi", "Wf", "Wgr"]):
            stg, skey, ch = ((stg_state.a, ("wst",), "ch_wst") if (m * 3 + wi_) % 2 == 0
                             else (stg_state.b, ("wst2",), "ch_wst2"))
            srcap = apin[wname][:, 128 * m:128 * (m + 1)].rearrange(
                "(a p) c -> p a c", p=128)
            dma(stg[:, 0:D].rearrange("p (a c) -> p a c", c=128), srcap,
                chan=ch, reads=[skey], writes=[skey], eng="a")
            vop(lambda e, par=par, wi_=wi_, stg=stg: e.tensor_copy(
                out=wim[par][:, D * wi_:D * (wi_ + 1)], in_=stg[:, 0:D]),
                reads=[skey], writes=[("wim", par, wi_)])
        for gi, (go, gn) in enumerate(GRPC):
            rx = [("xn1T", k, (go // 128) + tt) for k in range(KC)
                  for tt in range(gn // 128)]
            for pse, psk, wi_ in [(0, PPK[0], 0), (1, PPK[1], 1)]:
                for k in range(KC):
                    mm(pp[pse][:, 0:gn], wim[par][:, D * wi_ + 128 * k:D * wi_ + 128 * (k + 1)],
                       xn1T[:, TP * k + go:TP * k + go + gn],
                       reads=[("wim", par, wi_)] + rx, writes=[psk],
                       start=(k == 0), stop=(k == KC - 1))
            vop(lambda e, go=go, gn=gn: e.activation(out=iraw[:, go:go + gn],
                                                     in_=pp[0][:, 0:gn], func=AF.Copy),
                reads=[PPK[0]], writes=[("iraw",)], eng="a")
            vop(lambda e, go=go, gn=gn: e.activation(out=fraw[:, go:go + gn],
                                                     in_=pp[1][:, 0:gn], func=AF.Copy),
                reads=[PPK[1]], writes=[("fraw",)], eng="a")
        for go, gn in GRPM:
            gow = WARM + go
            rx = [("xn1T", k, (gow // 128) + tt) for k in range(KC)
                  for tt in range(gn // 128)]
            for k in range(KC):
                mm(pp[2][:, 0:gn], wim[par][:, 2 * D + 128 * k:2 * D + 128 * (k + 1)],
                   xn1T[:, TP * k + gow:TP * k + gow + gn],
                   reads=[("wim", par, 2)] + rx, writes=[PPK[2]],
                   start=(k == 0), stop=(k == KC - 1))
            vop(lambda e, m=m, go=go, gn=gn: e.activation(
                out=slg[:, T_MAIN * m + go:T_MAIN * m + go + gn],
                in_=pp[2][:, 0:gn], func=AF.Silu),
                reads=[PPK[2]], writes=[("slg", m)], eng="a")
        vop(lambda e: e.tensor_scalar_mul(out=iraw[:, 0:WARM], in0=iraw[:, 0:WARM],
                                          scalar1=mwarm_sb[:, 0:1]),
            reads=[("iraw",), ("mwarm",)], writes=[("iraw",)])
        vop(lambda e: e.tensor_scalar_mul(out=fraw[:, 0:WARM], in0=fraw[:, 0:WARM],
                                          scalar1=mwarm_sb[:, 0:1]),
            reads=[("fraw",), ("mwarm",)], writes=[("fraw",)])
        for src, skey, cout, ckey, csb, cwkey in [
                (iraw, ("iraw",), ci_b, ("ci",), conv_i_sb, ("convi",)),
                (fraw, ("fraw",), cf_b, ("cf",), conv_f_sb, ("convf",))]:
            vop(lambda e, cout=cout: e.memset(cout[:, 0:3], 0.0),
                reads=[], writes=[ckey])
            vop(lambda e, src=src, cout=cout, csb=csb, m=m: e.tensor_scalar_mul(
                out=cout[:, 3:TP], in0=src[:, 3:TP],
                scalar1=csb[:, 4 * m + 3:4 * m + 4]),
                reads=[skey, cwkey], writes=[ckey])
            for tap in range(3):
                vop(lambda e, src=src, cout=cout, csb=csb, m=m, tap=tap:
                    e.scalar_tensor_tensor(out=cout[:, 3:TP], in0=src[:, tap:TP - 3 + tap],
                                           scalar=csb[:, 4 * m + tap:4 * m + tap + 1],
                                           in1=cout[:, 3:TP],
                                           op0=AluOpType.mult, op1=AluOpType.add),
                    reads=[skey, cwkey, ckey], writes=[ckey])
        vop(lambda e: e.activation(out=cf_b[:], in_=cf_b[:], func=AF.Sigmoid),
            reads=[("cf",)], writes=[("cf",)], eng="a")
        vop(lambda e: e.activation(out=ci_b[:], in_=ci_b[:], func=AF.Silu),
            reads=[("ci",)], writes=[("ci",)], eng="a")
        vop(lambda e: e.tensor_scalar(out=iraw[:], in0=cf_b[:], scalar1=-1.0,
                                      scalar2=1.0, op0=AluOpType.mult, op1=AluOpType.add),
            reads=[("cf",)], writes=[("iraw",)])
        vop(lambda e: e.tensor_tensor(out=ci_b[:], in0=ci_b[:], in1=iraw[:],
                                      op=AluOpType.mult),
            reads=[("ci",), ("iraw",)], writes=[("ci",)])
        for so, sn in _split(TP, 544):
            init = 0.0 if so == 0 else hbuf[:, TP * m + so - 1:TP * m + so]
            vop(lambda e, m=m, so=so, sn=sn, init=init: e.tensor_tensor_scan(
                out=hbuf[:, TP * m + so:TP * m + so + sn],
                data0=cf_b[:, so:so + sn], data1=ci_b[:, so:so + sn],
                initial=init, op0=AluOpType.mult, op1=AluOpType.add),
                reads=[("cf",), ("ci",), ("h", m)], writes=[("h", m)])
        vop(lambda e, m=m: e.tensor_tensor(out=fraw[:, 0:T_MAIN],
                                           in0=hbuf[:, TP * m + WARM:TP * (m + 1)],
                                           in1=hbuf[:, TP * m + WARM:TP * (m + 1)],
                                           op=AluOpType.mult),
            reads=[("h", m), ("fraw",)], writes=[("fraw",)], eng="g")
        vop(lambda e: e.tensor_tensor(out=sqacc[:], in0=sqacc[:], in1=fraw[:, 0:T_MAIN],
                                      op=AluOpType.add),
            reads=[("sqacc",), ("fraw",)], writes=[("sqacc",)], eng="g")
        if debug and m == 0:
            vop(lambda e: e.tensor_copy(out=ci_b[:, 0:T_MAIN],
                                        in_=hbuf[:, WARM:TP]),
                reads=[("h", 0), ("ci",)], writes=[("ci",)])
            dma(dbg["dbg_h"][:, WARM:TP], ci_b[:, 0:T_MAIN], chan="dbgc",
                reads=[("ci",)], writes=[])

    ms.phase()
    free_group("gC1")

    rms_b = alloc("gC2b", "rms_b", (128, T_MAIN), F32)
    oT = alloc("gC2b", "oT", (128, KC * T_MAIN), BF16)
    s1_t = alloc("gC2b", "s1_t", (128, D), F32)
    sqtmp = alloc("gC2b", "sqtmp", (128, T_MAIN), F32)

    for gi, (go, gn) in enumerate(GRPM):
        mm(pp[4][0:1, 0:gn], onesf[:], sqacc[:, go:go + gn],
           reads=[("onesf",), ("sqacc",)], writes=[PPK[4]], start=True, stop=True)
        vop(lambda e, go=go, gn=gn: e.tensor_scalar(out=rms_b[0:1, go:go + gn],
                                                    in0=pp[4][0:1, 0:gn],
                                                    scalar1=1.0 / D, scalar2=1e-5,
                                                    op0=AluOpType.mult, op1=AluOpType.add),
            reads=[PPK[4]], writes=[("rms", 0)])
        vop(lambda e, go=go, gn=gn: e.activation(out=rms_b[0:1, go:go + gn],
                                                 in_=rms_b[0:1, go:go + gn], func=AF.Sqrt),
            reads=[("rms", 0)], writes=[("rms", 0)], eng="a")
        vop(lambda e, go=go, gn=gn: e.reciprocal(out=rms_b[0:1, go:go + gn],
                                                 in_=rms_b[0:1, go:go + gn]),
            reads=[("rms", 0)], writes=[("rms", 0)])
    kk = 1
    while kk < 128:
        dma(rms_b[kk:2 * kk, :], rms_b[0:kk, :], chan="ch_rms%d" % kk,
            reads=[("rms", kk // 2) if kk > 1 else ("rms", 0)], writes=[("rms", kk)])
        kk *= 2
    RMSD = ("rms", 64)

    for m in range(KC):
        vop(lambda e, m=m: e.tensor_tensor(out=sqtmp[:],
                                           in0=hbuf[:, TP * m + WARM:TP * (m + 1)],
                                           in1=rms_b[:], op=AluOpType.mult),
            reads=[("h", m), RMSD], writes=[("sqtmp",)], eng="g")
        vop(lambda e, m=m: e.tensor_tensor(out=sqtmp[:], in0=sqtmp[:],
                                           in1=slg[:, T_MAIN * m:T_MAIN * (m + 1)],
                                           op=AluOpType.mult),
            reads=[("sqtmp",), ("slg", m)], writes=[("sqtmp",)])
        vop(lambda e, m=m: e.tensor_scalar(out=oT[:, T_MAIN * m:T_MAIN * (m + 1)],
                                           in0=sqtmp[:],
                                           scalar1=gnw_sb[:, m:m + 1], scalar2=None,
                                           op0=AluOpType.mult),
            reads=[("sqtmp",), ("gnw",)], writes=[("oT", m)])
    for tm in range(NTM):
        for n in range(2):
            pidx = 5 + n
            for k in range(KC):
                mm(pp[pidx], oT[:, T_MAIN * k + 128 * tm:T_MAIN * k + 128 * (tm + 1)],
                   wo1_sb[:, D * k + 512 * n:D * k + 512 * (n + 1)],
                   reads=[("oT", k), ("wo1", k)], writes=[PPK[pidx]],
                   start=(k == 0), stop=(k == KC - 1))
        for n in range(2):
            pidx = 5 + n
            vop(lambda e, n=n, pidx=pidx: e.activation(
                out=s1_t[:, 512 * n:512 * (n + 1)], in_=pp[pidx][:], func=AF.Copy),
                reads=[PPK[pidx]], writes=[("s1_t",)], eng="a")
        dma(s1_scr[128 * tm:128 * (tm + 1), :], s1_t[:], chan="st_s1",
            reads=[("s1_t",)], writes=[("s1_scr", tm)])

    ms.phase()
    free_group("gC2b")
    free_group("gC2")

    weE_sb = [alloc("gDD", "weE%d" % i, (128, KC * D), BF16) for i in range(2)]
    wstD = [alloc("gDD", "wstD%d" % i, (128, D), F32) for i in range(4)]
    be_sb = alloc("gDD", "be_sb", (1, 2 * D), BF16)
    acc = alloc("gDD", "acc", (128, NTM * D), F32)
    g1_t = alloc("gDD", "g1_t", (128, D), F32)
    y_t = alloc("gDD", "y_t", (128, D), F32)
    s1l = alloc("gDD", "s1l", (128, 4 * D), F32)
    xde = alloc("gDD", "xde", (128, 2 * D), F32)

    for e_i in range(NEXP):
        sl = e_i % 2
        for k in range(KC):
            w4 = (e_i * KC + k) % 4
            stg, skey, ch = wstD[w4], ("wstD", w4), "ch_wstD%d" % w4
            dma(stg[:, 0:D], apin["WeE"][D * e_i + 128 * k:D * e_i + 128 * (k + 1), :],
                chan=ch, reads=[skey], writes=[skey])
            vop(lambda e, sl=sl, k=k, stg=stg: e.tensor_copy(
                out=weE_sb[sl][:, D * k:D * (k + 1)], in_=stg[:, 0:D]),
                reads=[skey], writes=[("weE", sl, k)])
        dma(stg_state.a[0:1, 0:D], apin["beE"][e_i:e_i + 1, :], chan="ch_wst",
            reads=[("wst",)], writes=[("wst",)])
        vop(lambda e, sl=sl, stg=stg_state.a: e.tensor_copy(
            out=be_sb[0:1, D * sl:D * (sl + 1)], in_=stg[0:1, 0:D]),
            reads=[("wst",)], writes=[("beE", sl)])
        pb = 2 * (e_i % 3)
        for tm in range(NTM):
            t0 = WARM + 128 * tm
            for n in range(2):
                mm(pp[pb + n], ones1[:], be_sb[0:1, D * sl + 512 * n:D * sl + 512 * (n + 1)],
                   reads=[("ones1",), ("beE", sl)], writes=[PPK[pb + n]],
                   start=True, stop=False)
                for k in range(KC):
                    mm(pp[pb + n], xn1T[:, TP * k + t0:TP * k + t0 + 128],
                       weE_sb[sl][:, D * k + 512 * n:D * k + 512 * (n + 1)],
                       reads=[("xn1T", k, t0 // 128), ("weE", sl, k)],
                       writes=[PPK[pb + n]], start=False, stop=(k == KC - 1))
            for n in range(2):
                pidx = pb + n
                alc = alp_sb[:, NEXP * tm + e_i:NEXP * tm + e_i + 1]
                if e_i == 0:
                    vop(lambda e, tm=tm, n=n, alc=alc, pidx=pidx: e.tensor_scalar(
                        out=acc[:, D * tm + 512 * n:D * tm + 512 * (n + 1)],
                        in0=pp[pidx][:], scalar1=alc, scalar2=None,
                        op0=AluOpType.mult),
                        reads=[PPK[pidx], ("alpha", tm)], writes=[("acc", tm)])
                else:
                    vop(lambda e, tm=tm, n=n, alc=alc, pidx=pidx: e.scalar_tensor_tensor(
                        out=acc[:, D * tm + 512 * n:D * tm + 512 * (n + 1)],
                        in0=pp[pidx][:], scalar=alc,
                        in1=acc[:, D * tm + 512 * n:D * tm + 512 * (n + 1)],
                        op0=AluOpType.mult, op1=AluOpType.add),
                        reads=[PPK[pidx], ("alpha", tm), ("acc", tm)],
                        writes=[("acc", tm)])

    for tm in range(NTM):
        t0 = WARM + 128 * tm
        par = tm % 2
        p4 = tm % 4
        xb = (xbuf2 if p4 < 2 else xde)[:, D * (p4 % 2):D * (p4 % 2 + 1)]
        xbk = ("xbuf", p4)
        dma(xb, x1_scr[t0:t0 + 128, :], chan="ch_x%d" % p4,
            reads=[xbk, ("x1_scr", t0 // 128)], writes=[xbk])
        s1v = s1l[:, D * p4:D * (p4 + 1)]
        dma(s1v, s1_scr[128 * tm:128 * (tm + 1), :], chan="ch_s1l%d" % p4,
            reads=[("s1l", p4), ("s1_scr", tm)], writes=[("s1l", p4)])
        vop(lambda e, tm=tm: e.activation(out=g1_t[:], in_=acc[:, D * tm:D * (tm + 1)],
                                          func=AF.Silu),
            reads=[("acc", tm)], writes=[("g1_t",)], eng="a")
        if debug:
            dma(dbg["dbg_moe"][128 * tm:128 * (tm + 1), :],
                acc[:, D * tm:D * (tm + 1)], chan="dbgc",
                reads=[("acc", tm)], writes=[])
        vop(lambda e, s1v=s1v: e.tensor_tensor(out=g1_t[:], in0=g1_t[:], in1=s1v,
                                      op=AluOpType.mult),
            reads=[("g1_t",), ("s1l", p4)], writes=[("g1_t",)])
        vop(lambda e, xb=xb: e.tensor_tensor(out=y_t[:], in0=g1_t[:], in1=xb,
                                             op=AluOpType.add),
            reads=[("g1_t",), xbk], writes=[("y_t",)], eng="g")
        dma(yout[128 * tm:128 * (tm + 1), :], y_t[:], chan="st_y%d" % par,
            reads=[("y_t",)], writes=[("yout", tm)], eng="g")

    ms.op("sy", lambda e: e.nop(), reads=[("yout", tm) for tm in range(NTM)], writes=[])

    with nc.Block() as block:
        ms.finalize(block)

    return nc, apin



def _host_stage(inputs, T_MAIN, L):
    x = np.asarray(inputs["L0_attn_x"], dtype=np.float32)
    B = x.shape[0]
    idx = np.linspace(0.0, L - 1, LEN).astype(np.int32)
    conv_i = np.ascontiguousarray(
        np.asarray(inputs["L1_attn_conv_i"], np.float32).reshape(KC, 128, 4)
        .transpose(1, 0, 2).reshape(128, KC * 4))
    conv_f = np.ascontiguousarray(
        np.asarray(inputs["L1_attn_conv_f"], np.float32).reshape(KC, 128, 4)
        .transpose(1, 0, 2).reshape(128, KC * 4))
    gnw = np.ascontiguousarray(
        np.asarray(inputs["L1_attn_g_norm_weight"], np.float32).reshape(KC, 128).T)
    shared = {
        "Wq": np.asarray(inputs["L0_attn_Wq"], np.float32),
        "Wk": np.asarray(inputs["L0_attn_Wk"], np.float32),
        "Wo0": np.asarray(inputs["L0_attn_Wo"], np.float32),
        "bq": np.asarray(inputs["L0_attn_bq"], np.float32).reshape(1, D),
        "bk": np.asarray(inputs["L0_attn_bk"], np.float32).reshape(1, D),
        "bo": np.asarray(inputs["L0_attn_bo"], np.float32).reshape(1, D),
        "WeL": np.asarray(inputs["L0_attn_We"], np.float32),
        "WrL": np.asarray(inputs["L0_attn_Wr"], np.float32),
        "Wc": np.asarray(inputs["L0_attn_Wc"], np.float32),
        "bc": np.tile(np.asarray(inputs["L0_attn_bc"], np.float32), 2).reshape(128, 1),
        "wg_nat": np.asarray(inputs["L0_ffn_w_gate"], np.float32),
        "wu_nat": np.asarray(inputs["L0_ffn_w_up"], np.float32),
        "wd_nat": np.asarray(inputs["L0_ffn_w_down"], np.float32),
        "Wi": np.asarray(inputs["L1_attn_Wi"], np.float32),
        "Wf": np.asarray(inputs["L1_attn_Wf"], np.float32),
        "Wgr": np.asarray(inputs["L1_attn_Wg"], np.float32),
        "Wo1": np.asarray(inputs["L1_attn_Wo"], np.float32),
        "conv_i": conv_i, "conv_f": conv_f, "gnw": gnw,
        "WeE": np.asarray(inputs["L1_ffn_We"], np.float32).reshape(NEXP * D, D),
        "beE": np.asarray(inputs["L1_ffn_be"], np.float32),
        "WgE": np.asarray(inputs["L1_ffn_Wg"], np.float32),
        "bgE": np.asarray(inputs["L1_ffn_bg"], np.float32).reshape(1, NEXP),
    }
    in_maps = []
    per_b = L // T_MAIN
    for c in range(B * per_b):
        b, pos = c // per_b, c % per_b
        t0 = pos * T_MAIN
        if pos == 0:
            xin = np.concatenate([x[b, 0:WARM], x[b, t0:t0 + T_MAIN]], axis=0)
            mw = np.zeros((128, 1), np.float32)
        else:
            xin = x[b, t0 - WARM:t0 + T_MAIN]
            mw = np.ones((128, 1), np.float32)
        m = dict(shared)
        m["xin"] = np.ascontiguousarray(xin)
        m["xg"] = np.ascontiguousarray(x[b, idx, :])
        m["mwarm"] = mw
        in_maps.append(m)
    return in_maps, B, per_b


def kernel(**inputs):
    T_MAIN, L = 2048, 8192
    in_maps, B, per_b = _host_stage(inputs, T_MAIN, L)
    nc, apin = build_program(T_MAIN)
    res = run_bass_kernel_spmd(nc, in_maps, core_ids=list(range(B * per_b)))
    shards = [np.asarray(r["y"]) for r in res.results]
    return np.stack([np.concatenate(shards[b * per_b:(b + 1) * per_b], axis=0)
                     for b in range(B)], axis=0)
